# version 02
# baseline (speedup 1.0000x reference)
import math
import numpy as np
import ml_dtypes
from contextlib import ExitStack

import concourse.bass as bass
import concourse.mybir as mybir
from concourse.bass_utils import run_bass_kernel_spmd

F32 = mybir.dt.float32
BF16 = mybir.dt.bfloat16
I32 = mybir.dt.int32
AF = mybir.ActivationFunctionType
ALU = mybir.AluOpType
AX = mybir.AxisListType
BF = ml_dtypes.bfloat16

B, T, D = 4, 4096, 1024
TO = 2048
NT_OWN = TO // 128
NT_ALL = T // 128
H, E = 16, 64
FFI = 2816
MAGIC = float(np.float32(3 * 2 ** 22))

NCORES = 8


class P:

    def __init__(self, name):
        self.nc = bass.Bass("TRN2", target_bir_lowering=False)
        self.stack = ExitStack()
        self.sem = {}
        self.cnt = {}
        self.block = None

    def __enter__(self):
        return self

    def sbuf(self, name, shape, dtype):
        return self.stack.enter_context(self.nc.sbuf_tensor(name, shape, dtype))

    def psum(self, name, shape, dtype=F32):
        return self.stack.enter_context(self.nc.psum_tensor(name, shape, dtype))

    def new_sem(self, name):
        h = self.stack.enter_context(self.nc.semaphore(name))
        self.sem[name] = h
        self.cnt[name] = 0
        return h

    def inc(self, instr, name, n=1):
        self.cnt[name] += n
        instr.then_inc(self.sem[name], n)
        return self.cnt[name]

    def tgt(self, name):
        return self.cnt[name]

    def wait(self, eng, name, val=None):
        eng.wait_ge(self.sem[name], self.cnt[name] if val is None else val)

    def guard(self, eng, inst, gname="vg"):
        c = self.inc(inst, gname)
        eng.wait_ge(self.sem[gname], c)
        return inst

    def begin(self):
        self.block = self.stack.enter_context(self.nc.Block())

    def close(self):
        self.stack.close()



def build_l0prep():
    p = P("l0prep")
    nc = p.nc
    xin = nc.dram_tensor("xin", [T, D], F32, kind="ExternalInput").ap()
    xnfm = nc.dram_tensor("xnfm", [8, 128, T], BF16, kind="ExternalOutput").ap()
    xqfm = nc.dram_tensor("xqfm", [8, 128, TO], BF16, kind="ExternalOutput").ap()

    xt = [p.sbuf(f"xt{i}", [128, D], F32) for i in range(2)]
    sq = p.sbuf("sq", [128, D], F32)
    st1 = p.sbuf("st1", [128, 8], F32)
    xn_t = p.sbuf("xn_t", [128, D], F32)
    xnb = p.sbuf("xnb", [128, D], BF16)
    y2 = p.sbuf("y2", [128, D], F32)
    qlo = p.sbuf("qlo", [128, D], F32)
    qhi = p.sbuf("qhi", [128, D], F32)
    qmask = p.sbuf("qmask", [128, D], I32)
    qsel = p.sbuf("qsel", [128, D], F32)
    bm = p.sbuf("bm", [128, D // 16], F32)
    rs2 = p.sbuf("rs2", [128, D // 16], F32)
    rsh = p.sbuf("rsh", [128, D // 16], F32)
    xqb = p.sbuf("xqb", [128, D], BF16)
    xn_sb = p.sbuf("xn_sb", [128, 8 * T], BF16)
    xq_sb = p.sbuf("xq_sb", [128, 8 * TO], BF16)

    ld = p.new_sem("ld")
    vg = p.new_sem("vg")
    sv = p.new_sem("sv")
    ss = p.new_sem("ss")
    tp = p.new_sem("tp")
    so = p.new_sem("so")

    p.begin()
    block = p.block
    xn_v = xn_sb[:].rearrange("p (k t) -> p k t", k=8)
    xq_v = xq_sb[:].rearrange("p (k t) -> p k t", k=8)

    for i in range(NT_ALL):
        own = i < NT_OWN
        buf = xt[i % 2]

        @block.gpsimd
        def _(g, i=i, buf=buf):
            if i >= 2:
                p.wait(g, "sv", p.tgt("sv"))
            g.dma_start(out=buf[:], in_=xin[i * 128:(i + 1) * 128, :]).then_inc(p.sem["ld"], 16)
        p.cnt["ld"] += 16

        @block.vector
        def _(v, i=i, buf=buf, own=own):
            p.wait(v, "ld")
            p.guard(v, v.tensor_reduce(st1[:, 0:1], buf[:], axis=AX.X, op=ALU.add))
            v.tensor_tensor(sq[:], buf[:], buf[:], ALU.mult)
            p.guard(v, v.tensor_reduce(st1[:, 1:2], sq[:], axis=AX.X, op=ALU.add))
            p.guard(v, v.tensor_scalar(out=st1[:, 2:3], in0=st1[:, 0:1], scalar1=1.0 / D, scalar2=None, op0=ALU.mult))
            p.guard(v, v.tensor_scalar(out=st1[:, 3:4], in0=st1[:, 1:2], scalar1=1.0 / D, scalar2=None, op0=ALU.mult))
            p.guard(v, v.tensor_tensor(st1[:, 4:5], st1[:, 2:3], st1[:, 2:3], ALU.mult))
            p.guard(v, v.tensor_tensor(st1[:, 5:6], st1[:, 3:4], st1[:, 4:5], ALU.subtract))
            p.guard(v, v.tensor_scalar(out=st1[:, 5:6], in0=st1[:, 5:6], scalar1=1e-5,
                                       scalar2=None, op0=ALU.add))

        @block.scalar
        def _(s, i=i):
            p.wait(s, "vg")
            i0 = s.activation(st1[:, 6:7], st1[:, 5:6], AF.Sqrt)
            p.inc(i0, "ss")

        @block.vector
        def _(v, i=i, buf=buf, own=own):
            p.wait(v, "ss")
            p.wait(v, "tp")
            p.guard(v, v.reciprocal(st1[:, 7:8], st1[:, 6:7]))
            v.tensor_scalar(out=xn_t[:], in0=buf[:], scalar1=st1[:, 2:3], scalar2=st1[:, 7:8],
                            op0=ALU.subtract, op1=ALU.mult)
            i1 = v.tensor_copy(xnb[:], xn_t[:])
            p.inc(i1, "sv")
            if own:
                x3 = xn_t[:].rearrange("p (n b) -> p n b", b=16)
                p.guard(v, v.tensor_reduce(bm[:].unsqueeze(2), x3, axis=AX.X, op=ALU.max,
                                           apply_absolute_value=True))
                p.guard(v, v.tensor_scalar(out=bm[:], in0=bm[:], scalar1=6e-12, scalar2=None, op0=ALU.max))
                p.guard(v, v.reciprocal(rs2[:], bm[:]))
                p.guard(v, v.tensor_scalar(out=rs2[:], in0=rs2[:], scalar1=12.0, scalar2=None, op0=ALU.mult))
                p.guard(v, v.tensor_scalar(out=rsh[:], in0=bm[:], scalar1=1.0 / 12.0, scalar2=None, op0=ALU.mult))
                y2v = y2[:].rearrange("p (n b) -> p n b", b=16)
                v.tensor_tensor(y2v, x3, rs2[:].unsqueeze(2).broadcast_to((128, 64, 16)), ALU.mult)
                v.tensor_scalar(out=qlo[:], in0=y2[:], scalar1=MAGIC, scalar2=-MAGIC,
                                op0=ALU.add, op1=ALU.add)
                v.tensor_scalar(out=qhi[:].bitcast(I32), in0=y2[:].bitcast(I32),
                                scalar1=0x00200000, scalar2=None, op0=ALU.add)
                v.tensor_scalar(out=qhi[:].bitcast(I32), in0=qhi[:].bitcast(I32),
                                scalar1=-0x00400000, scalar2=None, op0=ALU.bitwise_and)
                v.tensor_scalar(out=sq[:].bitcast(I32), in0=y2[:].bitcast(I32),
                                scalar1=0x7FFFFFFF, scalar2=None, op0=ALU.bitwise_and)
                v.tensor_scalar(out=qmask[:], in0=sq[:],
                                scalar1=1.75, scalar2=None, op0=ALU.is_lt)
                v.select(out=qsel[:], mask=qmask[:], on_true=qlo[:], on_false=qhi[:])
                qv = qsel[:].rearrange("p (n b) -> p n b", b=16)
                xv = xqb[:].rearrange("p (n b) -> p n b", b=16)
                i2 = v.tensor_tensor(xv, qv, rsh[:].unsqueeze(2).broadcast_to((128, 64, 16)), ALU.mult)
                p.inc(i2, "sv")

        @block.scalar
        def _(s, i=i, own=own):
            p.wait(s, "sv")
            for k in range(8):
                ii = s.dma_start_transpose(xn_v[:, k, i * 128:(i + 1) * 128],
                                           xnb[:, k * 128:(k + 1) * 128])
                p.inc(ii, "tp", 16)
            if own:
                for k in range(8):
                    ii = s.dma_start_transpose(xq_v[:, k, i * 128:(i + 1) * 128],
                                               xqb[:, k * 128:(k + 1) * 128])
                    p.inc(ii, "tp", 16)

    @block.sync
    def _(sy):
        p.wait(sy, "tp")
        for k in range(8):
            ii = sy.dma_start(out=xnfm[k], in_=xn_v[:, k, :])
            p.inc(ii, "so", 16)
            ii = sy.dma_start(out=xqfm[k], in_=xq_v[:, k, :])
            p.inc(ii, "so", 16)
        p.wait(sy, "so")

    return p



def emit_quant(p, v, src_f32, dst_bf, tmp, C):
    x3 = src_f32.rearrange("p (n b) -> p n b", b=16)
    bm, rs2, rsh = tmp["bm"], tmp["rs2"], tmp["rsh"]
    nb = C // 16
    p.guard(v, v.tensor_reduce(bm[:, :nb].unsqueeze(2), x3, axis=AX.X, op=ALU.max,
                               apply_absolute_value=True))
    p.guard(v, v.tensor_scalar(out=bm[:, :nb], in0=bm[:, :nb], scalar1=6e-12, scalar2=None, op0=ALU.max))
    p.guard(v, v.reciprocal(rs2[:, :nb], bm[:, :nb]))
    p.guard(v, v.tensor_scalar(out=rs2[:, :nb], in0=rs2[:, :nb], scalar1=12.0, scalar2=None, op0=ALU.mult))
    p.guard(v, v.tensor_scalar(out=rsh[:, :nb], in0=bm[:, :nb], scalar1=1.0 / 12.0, scalar2=None, op0=ALU.mult))
    y2, qlo, qhi, qmask, qsel, sabs = (tmp["y2"], tmp["qlo"], tmp["qhi"],
                                       tmp["qmask"], tmp["qsel"], tmp["sabs"])
    y2v = y2[:, :C].rearrange("p (n b) -> p n b", b=16)
    v.tensor_tensor(y2v, x3, rs2[:, :nb].unsqueeze(2).broadcast_to((128, nb, 16)), ALU.mult)
    v.tensor_scalar(out=qlo[:, :C], in0=y2[:, :C], scalar1=MAGIC, scalar2=-MAGIC,
                    op0=ALU.add, op1=ALU.add)
    v.tensor_scalar(out=qhi[:, :C].bitcast(I32), in0=y2[:, :C].bitcast(I32),
                    scalar1=0x00200000, scalar2=None, op0=ALU.add)
    v.tensor_scalar(out=qhi[:, :C].bitcast(I32), in0=qhi[:, :C].bitcast(I32),
                    scalar1=-0x00400000, scalar2=None, op0=ALU.bitwise_and)
    v.tensor_scalar(out=sabs[:, :C].bitcast(I32), in0=y2[:, :C].bitcast(I32),
                    scalar1=0x7FFFFFFF, scalar2=None, op0=ALU.bitwise_and)
    v.tensor_scalar(out=qmask[:, :C], in0=sabs[:, :C], scalar1=1.75, scalar2=None, op0=ALU.is_lt)
    v.select(out=qsel[:, :C], mask=qmask[:, :C], on_true=qlo[:, :C], on_false=qhi[:, :C])
    qv = qsel[:, :C].rearrange("p (n b) -> p n b", b=16)
    xv = dst_bf.rearrange("p (n b) -> p n b", b=16)
    return v.tensor_tensor(xv, qv, rsh[:, :nb].unsqueeze(2).broadcast_to((128, nb, 16)), ALU.mult)


def build_l0wq():
    p = P("l0wq")
    nc = p.nc
    wg = nc.dram_tensor("wg", [FFI, D], F32, kind="ExternalInput").ap()
    wu = nc.dram_tensor("wu", [FFI, D], F32, kind="ExternalInput").ap()
    wd = nc.dram_tensor("wd", [D, FFI], F32, kind="ExternalInput").ap()
    wgq = nc.dram_tensor("wgq", [8, 128, FFI], BF16, kind="ExternalOutput").ap()
    wuq = nc.dram_tensor("wuq", [8, 128, FFI], BF16, kind="ExternalOutput").ap()
    wdq = nc.dram_tensor("wdq", [22, 128, D], BF16, kind="ExternalOutput").ap()

    CMAX = FFI
    wt = [p.sbuf(f"wt{i}", [128, CMAX], F32) for i in range(2)]
    tmp = {k: p.sbuf(k, [128, CMAX], F32) for k in ("y2", "qlo", "qhi", "qsel", "sabs")}
    tmp["qmask"] = p.sbuf("qmask", [128, CMAX], I32)
    for k in ("bm", "rs2", "rsh"):
        tmp[k] = p.sbuf(k, [128, CMAX // 16], F32)
    qb = [p.sbuf(f"qb{i}", [128, CMAX], BF16) for i in range(2)]
    asm = p.sbuf("asm", [128, 22528], BF16)

    ld = p.new_sem("ld")
    vg = p.new_sem("vg")
    sv = p.new_sem("sv")
    tp = p.new_sem("tp")
    so = p.new_sem("so")

    p.begin()
    block = p.block

    jobs = [(wg, wgq, FFI, D), (wu, wuq, FFI, D), (wd, wdq, D, FFI)]
    for src, dst, R, C in jobs:
        nr = R // 128
        asm_v = asm[:].rearrange("p (k r) -> p k r", k=C // 128)
        for r in range(nr):
            buf = wt[r % 2]
            qbb = qb[r % 2]

            @block.gpsimd
            def _(g, src=src, r=r, buf=buf, C=C):
                p.wait(g, "sv")
                g.dma_start(out=buf[:, :C], in_=src[r * 128:(r + 1) * 128, :]).then_inc(p.sem["ld"], 16)
            p.cnt["ld"] += 16

            @block.vector
            def _(v, r=r, buf=buf, qbb=qbb, C=C):
                p.wait(v, "ld")
                p.wait(v, "tp")
                i2 = emit_quant(p, v, buf[:, :C], qbb[:, :C], tmp, C)
                p.inc(i2, "sv")

            @block.scalar
            def _(s, r=r, qbb=qbb, C=C):
                p.wait(s, "sv")
                for k in range(C // 128):
                    ii = s.dma_start_transpose(asm_v[:, k, r * 128:(r + 1) * 128],
                                               qbb[:, k * 128:(k + 1) * 128])
                    p.inc(ii, "tp", 16)

        @block.sync
        def _(sy, dst=dst, R=R, C=C):
            p.wait(sy, "tp")
            for k in range(C // 128):
                ii = sy.dma_start(out=dst[k], in_=asm[:].rearrange("p (k r) -> p k r", k=C // 128)[:, k, :R])
                p.inc(ii, "so", 16)
            p.wait(sy, "so")

    return p



def build_l0attn():
    p = P("l0attn")
    nc = p.nc
    xnfm = nc.dram_tensor("xnfm", [8, 128, T], BF16, kind="ExternalInput").ap()
    wqh = nc.dram_tensor("wqh", [D, D], F32, kind="ExternalInput").ap()
    wkh = nc.dram_tensor("wkh", [D, D], F32, kind="ExternalInput").ap()
    wvh = nc.dram_tensor("wvh", [D, D], F32, kind="ExternalInput").ap()
    aout = nc.dram_tensor("aout", [8, 128, TO], BF16, kind="ExternalOutput").ap()
    nmx_scr = nc.dram_tensor("nmx_scr", [16, 128], BF16, kind="Internal").ap()

    xn = p.sbuf("xn", [128, 8 * T], BF16)
    wq = p.sbuf("wq", [128, 8 * D], BF16)
    wk = p.sbuf("wk", [128, 8 * D], BF16)
    wv = p.sbuf("wv", [128, 8 * D], BF16)
    Kb = [p.sbuf(f"Kb{i}", [128, T], BF16) for i in range(2)]
    Qb = [p.sbuf(f"Qb{i}", [128, TO], BF16) for i in range(2)]
    Vb = p.sbuf("Vb", [128, 32 * 130], BF16)
    att = [p.sbuf(f"att{i}", [128, 1024], BF16) for i in range(2)]
    mx8 = p.sbuf("mx8", [128, 8], F32)
    nmx = p.sbuf("nmx", [128, 16], BF16)
    nmxt = p.sbuf("nmxt", [16, 128], BF16)
    ident = p.sbuf("ident", [128, 128], BF16)
    identi = p.sbuf("identi", [128, 128], I32)
    ones1 = p.sbuf("ones1", [1, 64], F32)
    r1 = p.sbuf("r1", [1, 1024], F32)
    r1b = p.sbuf("r1b", [1, 1024], BF16)
    bc_s = p.sbuf("bc_s", [64, 1024], F32)
    ao = p.sbuf("ao", [128, 8 * TO], BF16)

    psA = p.psum("psA", [128, 512])
    psB = p.psum("psB", [128, 512])
    psC = p.psum("psC", [128, 1024])
    psD = p.psum("psD", [128, 1024])
    psE = p.psum("psE", [128, 1024])

    ld = p.new_sem("ld")
    vg = p.new_sem("vg")
    mm = p.new_sem("mm")
    pv = p.new_sem("pv")
    sv = p.new_sem("sv")
    ss = p.new_sem("ss")
    gd = p.new_sem("gd")
    so = p.new_sem("so")

    p.begin()
    block = p.block
    xn_v = xn[:].rearrange("p (k t) -> p k t", k=8)
    wq_v = wq[:].rearrange("p (k t) -> p k t", k=8)
    wk_v = wk[:].rearrange("p (k t) -> p k t", k=8)
    wv_v = wv[:].rearrange("p (k t) -> p k t", k=8)
    Vb_v = Vb[:].rearrange("p (m c) -> p m c", c=130)
    ao_v = ao[:].rearrange("p (k t) -> p k t", k=8)
    NQT = TO // 128
    NQC = TO // 1024

    @block.gpsimd
    def _(g):
        for k in range(8):
            g.dma_start(out=xn_v[:, k, :], in_=xnfm[k]).then_inc(p.sem["ld"], 16)
            g.dma_start(out=wq_v[:, k, :], in_=wqh[k * 128:(k + 1) * 128, :]).then_inc(p.sem["ld"], 16)
            g.dma_start(out=wk_v[:, k, :], in_=wkh[k * 128:(k + 1) * 128, :]).then_inc(p.sem["ld"], 16)
            g.dma_start(out=wv_v[:, k, :], in_=wvh[k * 128:(k + 1) * 128, :]).then_inc(p.sem["ld"], 16)
        g.iota(identi[:], pattern=[[1, 128]], base=0, channel_multiplier=-1,
               allow_small_or_imprecise_dtypes=True)
        ii = g.tensor_scalar(out=ident[:], in0=identi[:], scalar1=0, scalar2=None, op0=ALU.is_equal)
        p.inc(ii, "gd")
    p.cnt["ld"] += 16 * 32

    @block.vector
    def _(v):
        v.memset(ones1[:], 1.0)
        v.memset(Vb_v[:, :, 64:65].squeeze(2), 1.0)
        ii = v.memset(Vb_v[:, :, 129:130].squeeze(2), 1.0)
        p.inc(ii, "sv")

    for hp in range(8):
        @block.tensor
        def _(t, hp=hp):
            p.wait(t, "ld")
            p.wait(t, "sv")
            for mt in range(32):
                dst = psC[:, 0:128] if mt % 2 == 0 else psC[:, 512:640]
                if mt >= 2:
                    p.wait(t, "sv", p.cnt["sv"] + mt - 1)
                for sub in range(8):
                    ii = t.matmul(dst, xn_v[:, sub, mt * 128:(mt + 1) * 128],
                                  wv_v[:, sub, hp * 128:(hp + 1) * 128],
                                  start=(sub == 0), stop=(sub == 7))
                p.inc(ii, "mm")

        @block.vector
        def _(v, hp=hp):
            base = p.cnt["mm"] - 32
            for mt in range(32):
                src = psC[:, 0:128] if mt % 2 == 0 else psC[:, 512:640]
                p.wait(v, "mm", base + mt + 1)
                v.tensor_copy(Vb_v[:, mt, 0:64], src[:, 0:64])
                ii = v.tensor_copy(Vb_v[:, mt, 65:129], src[:, 64:128])
                p.inc(ii, "sv")

        for hh in range(2):
            h = hp * 2 + hh
            K_h = Kb[hh]
            Q_h = Qb[hh]
            eoff = 65 * hh

            @block.vector
            def _(v, K_h=K_h):
                ii = v.memset(K_h[64:65, :], 1.0)
                p.inc(ii, "sv")

            @block.tensor
            def _(t, h=h):
                p.wait(t, "sv")
                p.wait(t, "ss")
                for mc in range(8):
                    ps = psA if mc % 2 == 0 else psB
                    if mc >= 2:
                        p.wait(t, "ss", p.cnt["ss"] + mc - 1)
                    for sub in range(8):
                        ii = t.matmul(ps[0:64, :], wk_v[:, sub, h * 64:(h + 1) * 64],
                                      xn_v[:, sub, mc * 512:(mc + 1) * 512],
                                      start=(sub == 0), stop=(sub == 7))
                    p.inc(ii, "mm")
                for qc in range(TO // 512):
                    ps = psA if qc % 2 == 0 else psB
                    p.wait(t, "ss", p.cnt["ss"] + 6 + qc + 1)
                    for sub in range(8):
                        ii = t.matmul(ps[0:64, :], wq_v[:, sub, h * 64:(h + 1) * 64],
                                      xn_v[:, sub, qc * 512:(qc + 1) * 512],
                                      start=(sub == 0), stop=(sub == 7))
                    p.inc(ii, "mm")

            @block.scalar
            def _(s, h=h, K_h=K_h, Q_h=Q_h):
                base = p.cnt["mm"] - 8 - TO // 512
                for mc in range(8):
                    ps = psA if mc % 2 == 0 else psB
                    p.wait(s, "mm", base + 1 + mc)
                    ii = s.copy(out=K_h[0:64, mc * 512:(mc + 1) * 512], in_=ps[0:64, :])
                    p.inc(ii, "ss")
                for qc in range(TO // 512):
                    ps = psA if qc % 2 == 0 else psB
                    p.wait(s, "mm", base + 9 + qc)
                    ii = s.activation(Q_h[0:64, qc * 512:(qc + 1) * 512], ps[0:64, :],
                                      AF.Copy, scale=0.125)
                    p.inc(ii, "ss")

            @block.tensor
            def _(t, h=h, K_h=K_h, Q_h=Q_h):
                p.wait(t, "ss")
                for qt in range(NQT):
                    for mc in range(8):
                        n = qt * 8 + mc
                        ps = psA if n % 2 == 0 else psB
                        if n >= 2:
                            p.wait(t, "sv", p.cnt["sv"] + n - 1)
                        ii = t.matmul(ps[:, :], Q_h[0:64, qt * 128:(qt + 1) * 128],
                                      K_h[0:64, mc * 512:(mc + 1) * 512], start=True, stop=True)
                        p.inc(ii, "mm")

            @block.vector
            def _(v, h=h):
                base = p.cnt["mm"] - 8 * NQT
                gbase = p.cnt["sv"]
                n = 0
                for qt in range(NQT):
                    for mc in range(8):
                        n += 1
                        ps = psA if (n - 1) % 2 == 0 else psB
                        p.wait(v, "mm", base + n)
                        ii = v.tensor_reduce(mx8[:, mc:mc + 1], ps[:, :], axis=AX.X, op=ALU.max)
                        p.inc(ii, "sv")
                        v.wait_ge(p.sem["sv"], p.cnt["sv"])
                    ii = p.guard(v, v.tensor_reduce(nmx[:, qt:qt + 1], mx8[:, :],
                                                    axis=AX.X, op=ALU.max, negate=True))

            @block.tensor
            def _(t, h=h):
                p.wait(t, "vg")
                p.wait(t, "gd", 1)
                psT = psA[0:16, 0:64].bitcast(BF16)
                ii = t.transpose(psT, nmx[:, 0:NQT], ident[:])
                p.inc(ii, "mm")

            @block.vector
            def _(v, h=h):
                p.wait(v, "mm")
                ii = v.tensor_copy(nmxt[0:NQT, :], psA[0:16, 0:64].bitcast(BF16))
                p.inc(ii, "sv")

            @block.gpsimd
            def _(g, h=h, Q_h=Q_h):
                p.wait(g, "sv")
                ii = g.dma_start(out=nmx_scr, in_=nmxt[0:NQT, :])
                p.inc(ii, "gd", 16)
                p.wait(g, "gd")
                ii = g.dma_start(out=Q_h[64:65, :], in_=nmx_scr.rearrange("a b -> (a b)").unsqueeze(0))
                p.inc(ii, "gd", 16)

            for qc in range(NQC):
                qk_base = p.cnt["mm"]
                ss_base = p.cnt["ss"]
                pv_base = p.cnt["pv"]
                sv_base = p.cnt["sv"]

                @block.tensor
                def _(t, h=h, qc=qc, K_h=K_h, Q_h=Q_h, eoff=eoff,
                      qk_base=qk_base, ss_base=ss_base, sv_base=sv_base):
                    if qc == 0:
                        p.wait(t, "gd")
                    else:
                        p.wait(t, "sv", sv_base)
                    def qk(mt):
                        psl = psC if mt % 2 == 0 else psD
                        for j in range(2):
                            ii = t.matmul(psl[:, j * 512:(j + 1) * 512],
                                          K_h[0:65, mt * 128:(mt + 1) * 128],
                                          Q_h[0:65, qc * 1024 + j * 512: qc * 1024 + (j + 1) * 512],
                                          start=True, stop=True)
                        p.inc(ii, "mm")
                    qk(0)
                    qk(1)
                    for mt in range(32):
                        p.wait(t, "ss", ss_base + mt + 1)
                        ab = att[mt % 2]
                        for j in range(2):
                            ii = t.matmul(psE[0:65, j * 512:(j + 1) * 512],
                                          Vb_v[:, mt, eoff:eoff + 65],
                                          ab[:, j * 512:(j + 1) * 512],
                                          start=(mt == 0), stop=(mt == 31))
                        p.inc(ii, "pv")
                        if mt + 2 < 32:
                            qk(mt + 2)

                @block.scalar
                def _(s, h=h, qc=qc, qk_base=qk_base, pv_base=pv_base):
                    for mt in range(32):
                        psl = psC if mt % 2 == 0 else psD
                        ab = att[mt % 2]
                        p.wait(s, "mm", qk_base + mt + 1)
                        if mt >= 2:
                            p.wait(s, "pv", pv_base + mt - 1)
                        ii = s.activation(ab[:, :], psl[:, :], AF.Exp)
                        p.inc(ii, "ss")

                @block.vector
                def _(v, h=h, qc=qc):
                    p.wait(v, "pv")
                    p.guard(v, v.reciprocal(r1[:, :], psE[64:65, 0:1024]))

                @block.tensor
                def _(t, h=h, qc=qc):
                    p.wait(t, "vg")
                    for j in range(2):
                        ii = t.matmul(psA[0:64, :] if j == 0 else psB[0:64, :],
                                      ones1[:, :], r1[:, j * 512:(j + 1) * 512],
                                      start=True, stop=True)
                    p.inc(ii, "mm")

                @block.vector
                def _(v, h=h, qc=qc):
                    p.wait(v, "mm")
                    v.tensor_copy(bc_s[:, 0:512], psA[0:64, :])
                    v.tensor_copy(bc_s[:, 512:1024], psB[0:64, :])
                    ii = v.tensor_tensor(
                        ao_v[:, h // 2, qc * 1024:(qc + 1) * 1024][64 * (h % 2):64 * (h % 2) + 64, :],
                        psE[0:64, :], bc_s[:, :], ALU.mult)
                    p.inc(ii, "sv")

    @block.sync
    def _(sy):
        p.wait(sy, "sv")
        for k in range(8):
            ii = sy.dma_start(out=aout[k], in_=ao_v[:, k, :])
            p.inc(ii, "so", 16)
        p.wait(sy, "so")

    return p



def build_l0ffn():
    p = P("l0ffn")
    nc = p.nc
    xqfm = nc.dram_tensor("xqfm", [8, 128, TO], BF16, kind="ExternalInput").ap()
    wgq = nc.dram_tensor("wgq", [8, 128, FFI], BF16, kind="ExternalInput").ap()
    wuq = nc.dram_tensor("wuq", [8, 128, FFI], BF16, kind="ExternalInput").ap()
    wdq = nc.dram_tensor("wdq", [22, 128, D], BF16, kind="ExternalInput").ap()
    gout = nc.dram_tensor("gout", [8, 128, TO], BF16, kind="ExternalOutput").ap()

    xq = p.sbuf("xq", [128, 8 * TO], BF16)
    wgb = [p.sbuf(f"wgb{i}", [128, 8 * 512], BF16) for i in range(2)]
    wub = [p.sbuf(f"wub{i}", [128, 8 * 512], BF16) for i in range(2)]
    sg = p.sbuf("sg", [128, 512], F32)
    hrow = p.sbuf("hrow", [128, 512], F32)
    tmp = {k: p.sbuf(k, [128, 512], F32) for k in ("y2", "qlo", "qhi", "sabs")}
    tmp["qsel"] = tmp["y2"]
    wdb = [wgb[0], wgb[1]]
    tmp["qmask"] = p.sbuf("qmask", [128, 512], I32)
    for k in ("bm", "rs2", "rsh"):
        tmp[k] = p.sbuf(k, [128, 32], F32)
    hqb = p.sbuf("hqb", [128, 512], BF16)
    hq = p.sbuf("hq", [128, 22 * TO], BF16)
    gb = p.sbuf("gb", [128, 8 * TO], BF16)

    psG = p.psum("psG", [128, 512])
    psU = p.psum("psU", [128, 512])
    psD_ = [p.psum(f"psD{i}", [128, 512]) for i in range(2)]

    ld = p.new_sem("ld")
    vg = p.new_sem("vg")
    mm = p.new_sem("mm")
    sv = p.new_sem("sv")
    ss = p.new_sem("ss")
    tp = p.new_sem("tp")
    wl = p.new_sem("wl")
    so = p.new_sem("so")

    p.begin()
    block = p.block
    xq_v = xq[:].rearrange("p (k t) -> p k t", k=8)
    hq_v = hq[:].rearrange("p (k t) -> p k t", k=22)
    gb_v = gb[:].rearrange("p (k t) -> p k t", k=8)
    NTT = TO // 128

    @block.gpsimd
    def _(g):
        for k in range(8):
            g.dma_start(out=xq_v[:, k, :], in_=xqfm[k]).then_inc(p.sem["ld"], 16)
    p.cnt["ld"] += 16 * 8

    FCH = [(i * 512, min(512, FFI - i * 512)) for i in range((FFI + 511) // 512)]

    for fi, (f0, fw) in enumerate(FCH):
        wg_c = wgb[fi % 2]
        wu_c = wub[fi % 2]

        @block.gpsimd
        def _(g, fi=fi, f0=f0, fw=fw, wg_c=wg_c, wu_c=wu_c):
            if fi >= 2:
                p.wait(g, "mm")
            for k in range(8):
                g.dma_start(out=wg_c[:].rearrange("p (k t) -> p k t", k=8)[:, k, 0:fw],
                            in_=wgq[k][:, f0:f0 + fw]).then_inc(p.sem["wl"], 16)
                g.dma_start(out=wu_c[:].rearrange("p (k t) -> p k t", k=8)[:, k, 0:fw],
                            in_=wuq[k][:, f0:f0 + fw]).then_inc(p.sem["wl"], 16)
        p.cnt["wl"] += 16 * 16

        for tt in range(NTT):
            @block.tensor
            def _(t, fi=fi, f0=f0, fw=fw, tt=tt, wg_c=wg_c, wu_c=wu_c):
                p.wait(t, "ld")
                p.wait(t, "wl")
                if fi + tt > 0:
                    p.wait(t, "sv")
                wgv = wg_c[:].rearrange("p (k t) -> p k t", k=8)
                wuv = wu_c[:].rearrange("p (k t) -> p k t", k=8)
                for sub in range(8):
                    ii = t.matmul(psG[:, 0:fw], xq_v[:, sub, tt * 128:(tt + 1) * 128],
                                  wgv[:, sub, 0:fw], start=(sub == 0), stop=(sub == 7))
                for sub in range(8):
                    ii = t.matmul(psU[:, 0:fw], xq_v[:, sub, tt * 128:(tt + 1) * 128],
                                  wuv[:, sub, 0:fw], start=(sub == 0), stop=(sub == 7))
                p.inc(ii, "mm")

            @block.scalar
            def _(s, fi=fi, fw=fw, tt=tt):
                p.wait(s, "mm")
                ii = s.activation(sg[:, 0:fw], psG[:, 0:fw], AF.Silu)
                p.inc(ii, "ss")

            @block.vector
            def _(v, fi=fi, f0=f0, fw=fw, tt=tt):
                p.wait(v, "ss")
                p.wait(v, "tp")
                i0 = v.tensor_tensor(hrow[:, 0:fw], sg[:, 0:fw], psU[:, 0:fw], ALU.mult)
                p.inc(i0, "sv")
                i1 = emit_quant(p, v, hrow[:, 0:fw], hqb[:, 0:fw], tmp, fw)
                p.inc(i1, "sv")

            @block.scalar
            def _(s, fi=fi, f0=f0, fw=fw, tt=tt):
                p.wait(s, "sv")
                for j in range(fw // 128):
                    ii = s.dma_start_transpose(
                        hq_v[:, (f0 + j * 128) // 128, tt * 128:(tt + 1) * 128],
                        hqb[:, j * 128:(j + 1) * 128])
                    p.inc(ii, "tp", 16)

    for dt in range(8):
        wd_c = wdb[dt % 2]

        @block.gpsimd
        def _(g, dt=dt, wd_c=wd_c):
            p.wait(g, "mm")
            for fs in range(22):
                g.dma_start(out=wd_c[:, 0:2816].rearrange("p (k t) -> p k t", k=22)[:, fs, :],
                            in_=wdq[fs][:, dt * 128:(dt + 1) * 128]).then_inc(p.sem["wl"], 16)
        p.cnt["wl"] += 16 * 22

        for tc in range(TO // 512):
            @block.tensor
            def _(t, dt=dt, tc=tc, wd_c=wd_c):
                p.wait(t, "wl")
                if dt == 0 and tc == 0:
                    p.wait(t, "tp")
                ps = psD_[tc % 2]
                if dt + tc > 0:
                    p.wait(t, "ss")
                wdv = wd_c[:, 0:2816].rearrange("p (k t) -> p k t", k=22)
                for fs in range(22):
                    ii = t.matmul(ps[:, :], wdv[:, fs, :],
                                  hq_v[:, fs, tc * 512:(tc + 1) * 512],
                                  start=(fs == 0), stop=(fs == 21))
                p.inc(ii, "mm")

            @block.scalar
            def _(s, dt=dt, tc=tc):
                p.wait(s, "mm")
                ps = psD_[tc % 2]
                ii = s.activation(gb_v[:, dt, tc * 512:(tc + 1) * 512], ps[:, :], AF.Silu)
                p.inc(ii, "ss")

    @block.sync
    def _(sy):
        p.wait(sy, "ss")
        for k in range(8):
            ii = sy.dma_start(out=gout[k], in_=gb_v[:, k, :])
            p.inc(ii, "so", 16)
        p.wait(sy, "so")

    return p



def build_l0comb():
    p = P("l0comb")
    nc = p.nc
    aoin = nc.dram_tensor("aoin", [8, 128, TO], BF16, kind="ExternalInput").ap()
    gin = nc.dram_tensor("gin", [8, 128, TO], BF16, kind="ExternalInput").ap()
    xfm = nc.dram_tensor("xfm", [8, 128, TO], F32, kind="ExternalInput").ap()
    wph = nc.dram_tensor("wph", [D, D], F32, kind="ExternalInput").ap()
    bph = nc.dram_tensor("bph", [128, 8], F32, kind="ExternalInput").ap()
    x1 = nc.dram_tensor("x1", [8, 128, TO], F32, kind="ExternalOutput").ap()

    ao = p.sbuf("ao", [128, 8 * TO], BF16)
    gbuf = p.sbuf("gbuf", [128, 8 * TO], BF16)
    wp = p.sbuf("wp", [128, 8 * D], BF16)
    bp = p.sbuf("bp", [128, 8], F32)
    xt = [p.sbuf(f"xt{i}", [128, 512], F32) for i in range(2)]
    tmps = p.sbuf("tmps", [128, 512], F32)
    x1t = [p.sbuf(f"x1t{i}", [128, 512], F32) for i in range(2)]

    psS = [p.psum(f"psS{i}", [128, 512]) for i in range(2)]

    ld = p.new_sem("ld")
    vg = p.new_sem("vg")
    mm = p.new_sem("mm")
    sv = p.new_sem("sv")
    xl = p.new_sem("xl")
    so = p.new_sem("so")

    p.begin()
    block = p.block
    ao_v = ao[:].rearrange("p (k t) -> p k t", k=8)
    gb_v = gbuf[:].rearrange("p (k t) -> p k t", k=8)
    wp_v = wp[:].rearrange("p (k t) -> p k t", k=8)

    @block.gpsimd
    def _(g):
        for k in range(8):
            g.dma_start(out=ao_v[:, k, :], in_=aoin[k]).then_inc(p.sem["ld"], 16)
            g.dma_start(out=gb_v[:, k, :], in_=gin[k]).then_inc(p.sem["ld"], 16)
            g.dma_start(out=wp_v[:, k, :], in_=wph[k * 128:(k + 1) * 128, :]).then_inc(p.sem["ld"], 16)
        g.dma_start(out=bp[:, 0:8], in_=bph).then_inc(p.sem["ld"], 16)
    p.cnt["ld"] += 16 * 25

    n = 0
    for ot in range(8):
        for tc in range(TO // 512):
            n += 1
            buf = n % 2

            @block.gpsimd
            def _(g, ot=ot, tc=tc, buf=buf, n=n):
                if n > 2:
                    p.wait(g, "so", 16 * (n - 2))
                g.dma_start(out=xt[buf][:, :], in_=xfm[ot][:, tc * 512:(tc + 1) * 512]).then_inc(p.sem["xl"], 16)
            p.cnt["xl"] += 16

            @block.tensor
            def _(t, ot=ot, tc=tc, buf=buf, n=n):
                p.wait(t, "ld")
                if n > 2:
                    p.wait(t, "sv", p.cnt["sv"] - 1)
                for sub in range(8):
                    ii = t.matmul(psS[buf][:, :], wp_v[:, sub, ot * 128:(ot + 1) * 128],
                                  ao_v[:, sub, tc * 512:(tc + 1) * 512],
                                  start=(sub == 0), stop=(sub == 7))
                p.inc(ii, "mm")

            @block.vector
            def _(v, ot=ot, tc=tc, buf=buf, n=n):
                p.wait(v, "mm")
                p.wait(v, "xl")
                v.tensor_scalar(out=tmps[:, :], in0=psS[buf][:, :],
                                scalar1=bp[:, ot:ot + 1], scalar2=None, op0=ALU.add)
                v.tensor_tensor(tmps[:, :], tmps[:, :], gb_v[:, ot, tc * 512:(tc + 1) * 512], ALU.mult)
                ii = v.tensor_tensor(x1t[buf][:, :], xt[buf][:, :], tmps[:, :], ALU.add)
                p.inc(ii, "sv")

            @block.sync
            def _(sy, ot=ot, tc=tc, buf=buf):
                p.wait(sy, "sv")
                ii = sy.dma_start(out=x1[ot][:, tc * 512:(tc + 1) * 512], in_=x1t[buf][:, :])
                p.inc(ii, "so", 16)

    @block.sync
    def _(sy):
        p.wait(sy, "so")

    return p



def build_l1ln():
    p = P("l1ln")
    nc = p.nc
    x1 = nc.dram_tensor("x1", [8, 128, TO], F32, kind="ExternalInput").ap()
    pnw = nc.dram_tensor("pnw", [128, 8], F32, kind="ExternalInput").ap()
    sacc = nc.dram_tensor("sacc", [8, 128, TO], F32, kind="ExternalOutput").ap()
    xn1 = nc.dram_tensor("xn1", [8, 128, TO], BF16, kind="ExternalOutput").ap()
    rstd2 = nc.dram_tensor("rstd2", [1, TO], F32, kind="ExternalOutput").ap()
    redge = nc.dram_tensor("redge", [8, 128, 3], F32, kind="ExternalOutput").ap()

    xb = p.sbuf("xb", [128, 8 * TO], F32)
    sa = xb
    xnb = p.sbuf("xnb", [128, 8 * TO], BF16)
    sqt = [p.sbuf(f"sqt{i}", [128, 512], F32) for i in range(2)]
    onesr = p.sbuf("onesr", [1, 128], F32)
    onesc = p.sbuf("onesc", [128, 1], F32)
    rows = p.sbuf("rows", [1, 4 * TO], F32)
    rows2 = p.sbuf("rows2", [1, 2 * TO], F32)
    mbc = p.sbuf("mbc", [128, TO], F32)
    rbc = p.sbuf("rbc", [128, TO], F32)
    pnwt = p.sbuf("pnwt", [128, 8], F32)
    ret = p.sbuf("ret", [128, 8 * 3], F32)

    psX = p.psum("psX", [1, 512])
    psQ = p.psum("psQ", [1, 512])
    psB1 = p.psum("psB1", [128, 512])
    psB2 = p.psum("psB2", [128, 512])

    for s in ("ld", "vg", "mm", "m2", "sv", "ss", "so"):
        p.new_sem(s)

    p.begin()
    block = p.block
    xb_v = xb[:].rearrange("p (k t) -> p k t", k=8)
    sa_v = sa[:].rearrange("p (k t) -> p k t", k=8)
    xn_v = xnb[:].rearrange("p (k t) -> p k t", k=8)
    ret_v = ret[:].rearrange("p (k t) -> p k t", k=3)
    NCH = TO // 512

    @block.gpsimd
    def _(g):
        for k in range(8):
            g.dma_start(out=xb_v[:, k, :], in_=x1[k]).then_inc(p.sem["ld"], 16)
        g.dma_start(out=pnwt[:], in_=pnw).then_inc(p.sem["ld"], 16)
    p.cnt["ld"] += 16 * 9

    @block.vector
    def _(v):
        v.memset(onesr[:], 1.0)
        ii = v.memset(onesc[:], 1.0)
        p.inc(ii, "sv")

    def stats_pass(src_v, rowbuf):
        for ch in range(NCH):
            @block.tensor
            def _(t, ch=ch):
                p.wait(t, "ld")
                p.wait(t, "sv")
                for sub in range(8):
                    ii = t.matmul(psX[:, :], onesc[:, :],
                                  src_v[:, sub, ch * 512:(ch + 1) * 512],
                                  start=(sub == 0), stop=(sub == 7))
                p.inc(ii, "mm")

            @block.scalar
            def _(s, ch=ch):
                p.wait(s, "ld")
                p.wait(s, "sv")
                base = p.cnt["m2"]
                for sub in range(8):
                    if sub >= 2:
                        p.wait(s, "m2", base + sub - 1)
                    ii = s.activation(sqt[sub % 2][:, :],
                                      src_v[:, sub, ch * 512:(ch + 1) * 512], AF.Square)
                    p.inc(ii, "ss")

            @block.tensor
            def _(t, ch=ch):
                p.wait(t, "sv")
                base = p.cnt["ss"] - 8
                for sub in range(8):
                    p.wait(t, "ss", base + sub + 1)
                    ii = t.matmul(psQ[:, :], onesc[:, :], sqt[sub % 2][:, :],
                                  start=(sub == 0), stop=(sub == 7))
                    p.inc(ii, "m2")

            @block.vector
            def _(v, ch=ch):
                p.wait(v, "mm")
                p.wait(v, "m2")
                v.tensor_copy(rowbuf[:, ch * 512:(ch + 1) * 512], psX[:, :])
                ii = v.tensor_copy(rowbuf[:, TO + ch * 512:TO + (ch + 1) * 512], psQ[:, :])
                p.inc(ii, "sv")

    def bcast2(rowap1, rowap2, dst1, dst2):
        for ch in range(NCH):
            @block.tensor
            def _(t, ch=ch):
                p.wait(t, "vg")
                if ch > 0:
                    p.wait(t, "sv")
                t.matmul(psB1[:, :], onesr[:, :], rowap1[:, ch * 512:(ch + 1) * 512],
                         start=True, stop=True)
                ii = t.matmul(psB2[:, :], onesr[:, :], rowap2[:, ch * 512:(ch + 1) * 512],
                              start=True, stop=True)
                p.inc(ii, "mm")

            @block.vector
            def _(v, ch=ch):
                p.wait(v, "mm")
                v.tensor_copy(dst1[:, ch * 512:(ch + 1) * 512], psB1[:, :])
                ii = v.tensor_copy(dst2[:, ch * 512:(ch + 1) * 512], psB2[:, :])
                p.inc(ii, "sv")

    stats_pass(xb_v, rows)

    @block.vector
    def _(v):
        p.wait(v, "sv")
        p.guard(v, v.tensor_scalar(out=rows[:, 2 * TO:3 * TO], in0=rows[:, 0:TO],
                                   scalar1=1.0 / D, scalar2=None, op0=ALU.mult))
        p.guard(v, v.tensor_scalar(out=rows[:, 3 * TO:4 * TO], in0=rows[:, TO:2 * TO],
                                   scalar1=1.0 / D, scalar2=None, op0=ALU.mult))
        p.guard(v, v.tensor_tensor(rows[:, 0:TO], rows[:, 2 * TO:3 * TO],
                                   rows[:, 2 * TO:3 * TO], ALU.mult))
        p.guard(v, v.tensor_tensor(rows[:, 3 * TO:4 * TO], rows[:, 3 * TO:4 * TO],
                                   rows[:, 0:TO], ALU.subtract))
        p.guard(v, v.tensor_scalar(out=rows[:, 3 * TO:4 * TO], in0=rows[:, 3 * TO:4 * TO],
                                   scalar1=1e-5, scalar2=None, op0=ALU.add))

    @block.scalar
    def _(s):
        p.wait(s, "vg")
        ii = s.activation(rows[:, 0:TO], rows[:, 3 * TO:4 * TO], AF.Sqrt)
        p.inc(ii, "ss")

    @block.vector
    def _(v):
        p.wait(v, "ss")
        p.guard(v, v.reciprocal(rows[:, 3 * TO:4 * TO], rows[:, 0:TO]))

    bcast2(rows[:, 2 * TO:3 * TO], rows[:, 3 * TO:4 * TO], mbc, rbc)

    for dt in range(8):
        @block.vector
        def _(v, dt=dt):
            p.wait(v, "sv")
            v.tensor_tensor(sa_v[:, dt, :], xb_v[:, dt, :], mbc[:, :], ALU.subtract)
            v.tensor_tensor(sa_v[:, dt, :], sa_v[:, dt, :], rbc[:, :], ALU.mult)
            ii = v.tensor_copy(xn_v[:, dt, :], sa_v[:, dt, :])
            p.inc(ii, "sv")

    stats_pass(sa_v, rows2)

    @block.vector
    def _(v):
        p.wait(v, "sv")
        p.guard(v, v.tensor_scalar(out=rows2[:, 0:TO], in0=rows2[:, TO:2 * TO],
                                   scalar1=1.0 / D, scalar2=1e-6,
                                   op0=ALU.mult, op1=ALU.add))

    @block.scalar
    def _(s):
        p.wait(s, "vg")
        ii = s.activation(rows2[:, TO:2 * TO], rows2[:, 0:TO], AF.Sqrt)
        p.inc(ii, "ss")

    @block.vector
    def _(v):
        p.wait(v, "ss")
        p.guard(v, v.reciprocal(rows2[:, 0:TO], rows2[:, TO:2 * TO]))

    @block.tensor
    def _(t):
        p.wait(t, "vg")
        p.wait(t, "sv")
        ii = t.matmul(psB1[:, 0:3], onesr[:, :], rows2[:, TO - 3:TO], start=True, stop=True)
        p.inc(ii, "mm")

    @block.vector
    def _(v):
        p.wait(v, "mm")
        p.guard(v, v.tensor_copy(mbc[:, 0:3], psB1[:, 0:3]))
        for dt in range(8):
            p.guard(v, v.tensor_scalar(out=ret[:, dt * 3:(dt + 1) * 3],
                                       in0=sa_v[:, dt, TO - 3:TO],
                                       scalar1=pnwt[:, dt:dt + 1], scalar2=None, op0=ALU.mult))
            p.guard(v, v.tensor_tensor(ret[:, dt * 3:(dt + 1) * 3], ret[:, dt * 3:(dt + 1) * 3],
                                       mbc[:, 0:3], ALU.mult))
        p.inc(v.tensor_copy(rows2[:, TO:TO + 1], rows2[:, 0:1]), "sv")

    @block.sync
    def _(sy):
        p.wait(sy, "sv")
        for k in range(8):
            ii = sy.dma_start(out=sacc[k], in_=sa_v[:, k, :])
            p.inc(ii, "so", 16)
            ii = sy.dma_start(out=xn1[k], in_=xn_v[:, k, :])
            p.inc(ii, "so", 16)
            ii = sy.dma_start(out=redge[k], in_=ret[:, k * 3:(k + 1) * 3])
            p.inc(ii, "so", 16)
        ii = sy.dma_start(out=rstd2, in_=rows2[:, 0:TO])
        p.inc(ii, "so", 16)
        p.wait(sy, "so")

    return p



def build_l1rec():
    p = P("l1rec")
    nc = p.nc
    sacc = nc.dram_tensor("sacc", [8, 128, TO], F32, kind="ExternalInput").ap()
    rstd2 = nc.dram_tensor("rstd2", [1, TO], F32, kind="ExternalInput").ap()
    halo = nc.dram_tensor("halo", [8, 128, 3], F32, kind="ExternalInput").ap()
    pnw = nc.dram_tensor("pnw", [128, 8], F32, kind="ExternalInput").ap()
    cw = nc.dram_tensor("cw", [128, 32], F32, kind="ExternalInput").ap()
    cb = nc.dram_tensor("cb", [128, 8], F32, kind="ExternalInput").ap()
    db = nc.dram_tensor("db", [128, 8], F32, kind="ExternalInput").ap()
    waT = nc.dram_tensor("waT", [D, D], F32, kind="ExternalInput").ap()
    wiT = nc.dram_tensor("wiT", [D, D], F32, kind="ExternalInput").ap()
    wvT = nc.dram_tensor("wvT", [D, D], F32, kind="ExternalInput").ap()
    recl = nc.dram_tensor("recl", [8, 128, TO], BF16, kind="ExternalOutput").ap()
    Fout = nc.dram_tensor("Fout", [8, 128, TO], BF16, kind="ExternalOutput").ap()
    sout = nc.dram_tensor("sout", [128, 8], F32, kind="ExternalOutput").ap()

    NCHK = TO // 64
    sa = p.sbuf("sa", [128, 8 * TO], F32)
    hb = p.sbuf("hb", [128, 8 * TO], BF16)
    rext = p.sbuf("rext", [128, TO + 16], F32)
    ac1 = p.sbuf("ac1", [128, TO], F32)
    ac2 = p.sbuf("ac2", [128, TO], F32)
    rbc2 = p.sbuf("rbc2", [128, TO], F32)
    onesr = p.sbuf("onesr", [1, 128], F32)
    mask = p.sbuf("mask", [128, TO], F32)
    zt = p.sbuf("zt", [128, NCHK], F32)
    states = p.sbuf("states", [128, NCHK], F32)
    si = p.sbuf("si", [128, NCHK], F32)
    cpd = p.sbuf("cpd", [128, NCHK], F32)
    cpds = p.sbuf("cpds", [128, NCHK], F32)
    sot = p.sbuf("sot", [128, 8], F32)
    smallw = {n: p.sbuf(n, [128, 8], F32) for n in ("pnwt", "cbt", "dbt")}
    cwt = p.sbuf("cwt", [128, 32], F32)
    wstr = [p.sbuf(f"wstr{i}", [128, 3 * 8 * 128], BF16) for i in range(2)]
    rtile = p.sbuf("rtile", [128, TO], BF16)
    ftile = p.sbuf("ftile", [128, TO], BF16)
    halot = p.sbuf("halot", [128, 24], F32)
    r2row = p.sbuf("r2row", [1, TO], F32)

    psA = p.psum("psA", [128, 512])
    psI = p.psum("psI", [128, 512])
    psV = p.psum("psV", [128, 512])
    psB1 = p.psum("psB1", [128, 512])

    for s in ("ld", "vg", "mm", "sv", "ss", "wl", "so"):
        p.new_sem(s)

    p.begin()
    block = p.block
    sa_v = sa[:].rearrange("p (k t) -> p k t", k=8)
    hb_v = hb[:].rearrange("p (k t) -> p k t", k=8)
    halv = halot[:].rearrange("p (k t) -> p k t", k=8)
    NCH = TO // 512

    @block.gpsimd
    def _(g):
        for k in range(8):
            g.dma_start(out=sa_v[:, k, :], in_=sacc[k]).then_inc(p.sem["ld"], 16)
            g.dma_start(out=halv[:, k, :], in_=halo[k]).then_inc(p.sem["ld"], 16)
        for nm, src_ in (("pnwt", pnw), ("cbt", cb), ("dbt", db)):
            g.dma_start(out=smallw[nm][:], in_=src_).then_inc(p.sem["ld"], 16)
        g.dma_start(out=cwt[:], in_=cw).then_inc(p.sem["ld"], 16)
        g.dma_start(out=r2row[:], in_=rstd2).then_inc(p.sem["ld"], 16)
    p.cnt["ld"] += 16 * 21

    @block.vector
    def _(v):
        v.memset(onesr[:], 1.0)
        v.memset(mask[:], 1.0)
        v.memset(mask[:].rearrange("p (c k) -> p c k", k=64)[:, :, 0:1].squeeze(2), 0.0)
        ii = v.memset(zt[:], 0.0)
        p.inc(ii, "sv")

    for ch in range(NCH):
        @block.tensor
        def _(t, ch=ch):
            p.wait(t, "ld")
            p.wait(t, "sv")
            ii = t.matmul(psB1[:, :], onesr[:, :], r2row[:, ch * 512:(ch + 1) * 512],
                          start=True, stop=True)
            p.inc(ii, "mm")

        @block.vector
        def _(v, ch=ch):
            p.wait(v, "mm")
            ii = v.tensor_copy(rbc2[:, ch * 512:(ch + 1) * 512], psB1[:, :])
            p.inc(ii, "sv")

    for dt in range(8):
        @block.vector
        def _(v, dt=dt):
            p.wait(v, "ld")
            p.wait(v, "sv")
            if dt > 0:
                p.wait(v, "so")
            v.tensor_copy(rext[:, 0:3], halv[:, dt, :])
            v.tensor_scalar(out=rext[:, 3:3 + TO], in0=sa_v[:, dt, :],
                            scalar1=smallw["pnwt"][:, dt:dt + 1], scalar2=None, op0=ALU.mult)
            v.tensor_tensor(rext[:, 3:3 + TO], rext[:, 3:3 + TO], rbc2[:, :], ALU.mult)
            v.tensor_scalar(out=ac1[:, :], in0=rext[:, 0:TO],
                            scalar1=cwt[:, dt * 4:dt * 4 + 1], scalar2=None, op0=ALU.mult)
            v.scalar_tensor_tensor(out=ac2[:, :], in0=rext[:, 1:1 + TO],
                                   scalar=cwt[:, dt * 4 + 1:dt * 4 + 2], in1=ac1[:, :],
                                   op0=ALU.mult, op1=ALU.add)
            v.scalar_tensor_tensor(out=ac1[:, :], in0=rext[:, 2:2 + TO],
                                   scalar=cwt[:, dt * 4 + 2:dt * 4 + 3], in1=ac2[:, :],
                                   op0=ALU.mult, op1=ALU.add)
            v.scalar_tensor_tensor(out=ac2[:, :], in0=rext[:, 3:3 + TO],
                                   scalar=cwt[:, dt * 4 + 3:dt * 4 + 4], in1=ac1[:, :],
                                   op0=ALU.mult, op1=ALU.add)
            ii = v.tensor_scalar(out=hb_v[:, dt, :], in0=ac2[:, :],
                                 scalar1=smallw["cbt"][:, dt:dt + 1], scalar2=None, op0=ALU.add)
            p.inc(ii, "sv")

    A = [sa_v[:, k, :] for k in range(8)]
    c3 = lambda ap: ap.rearrange("p (c k) -> p c k", k=64)

    for ot in range(8):
        wbuf = wstr[ot % 2]
        wv3 = wbuf[:].rearrange("p (w k t) -> p w k t", w=3, k=8)

        @block.gpsimd
        def _(g, ot=ot, wbuf=wbuf, wv3=wv3):
            if ot < 2:
                p.wait(g, "sv")
            else:
                p.wait(g, "mm")
            for wi, wsrc in enumerate((waT, wiT, wvT)):
                for sub in range(8):
                    g.dma_start(out=wv3[:, wi, sub, :],
                                in_=wsrc[sub * 128:(sub + 1) * 128, ot * 128:(ot + 1) * 128]
                                ).then_inc(p.sem["wl"], 16)
        p.cnt["wl"] += 16 * 24

        @block.tensor
        def _(t, ot=ot, wv3=wv3):
            p.wait(t, "wl")
            p.wait(t, "sv")
            for ch in range(NCH):
                if ch > 0:
                    p.wait(t, "ss", p.cnt["ss"] + ch * 2)
                    p.wait(t, "sv", p.cnt["sv"] + ch)
                for sub in range(8):
                    t.matmul(psA[:, :], wv3[:, 0, sub, :], hb_v[:, sub, ch * 512:(ch + 1) * 512],
                             start=(sub == 0), stop=(sub == 7))
                for sub in range(8):
                    t.matmul(psI[:, :], wv3[:, 1, sub, :], hb_v[:, sub, ch * 512:(ch + 1) * 512],
                             start=(sub == 0), stop=(sub == 7))
                for sub in range(8):
                    ii = t.matmul(psV[:, :], wv3[:, 2, sub, :], hb_v[:, sub, ch * 512:(ch + 1) * 512],
                                  start=(sub == 0), stop=(sub == 7))
                p.inc(ii, "mm")

        @block.scalar
        def _(s, ot=ot):
            base = p.cnt["mm"] - NCH
            for ch in range(NCH):
                p.wait(s, "mm", base + ch + 1)
                s.activation(A[0][:, ch * 512:(ch + 1) * 512], psA[:, :], AF.Sigmoid,
                             bias=smallw["dbt"][:, ot:ot + 1], scale=1.0)
                ii = s.activation(A[1][:, ch * 512:(ch + 1) * 512], psI[:, :], AF.Sigmoid)
                p.inc(ii, "ss", 2)

        @block.vector
        def _(v, ot=ot):
            base = p.cnt["mm"] - NCH
            for ch in range(NCH):
                p.wait(v, "mm", base + ch + 1)
                ii = v.tensor_copy(A[2][:, ch * 512:(ch + 1) * 512], psV[:, :])
                p.inc(ii, "sv")

        @block.vector
        def _(v, ot=ot):
            p.wait(v, "ss")
            ii = v.tensor_scalar(out=A[3][:, :], in0=A[0][:, :], scalar1=1e-10,
                                 scalar2=None, op0=ALU.max)
            p.inc(ii, "sv")

        @block.scalar
        def _(s, ot=ot):
            p.wait(s, "sv")
            ii = s.activation(A[4][:, :], A[3][:, :], AF.Ln)
            p.inc(ii, "ss")

        @block.vector
        def _(v, ot=ot):
            p.wait(v, "ss")
            ii = v.tensor_tensor_scan(out=A[3][:, :], data0=mask[:, :], data1=A[4][:, :],
                                      initial=0.0, op0=ALU.mult, op1=ALU.add)
            p.inc(ii, "sv")

        @block.scalar
        def _(s, ot=ot):
            p.wait(s, "sv")
            ii = s.activation(A[4][:, :], A[3][:, :], AF.Exp)
            p.inc(ii, "ss")

        @block.vector
        def _(v, ot=ot):
            p.wait(v, "ss")
            v.tensor_scalar(out=A[3][:, :], in0=A[4][:, :], scalar1=1e-10,
                            scalar2=None, op0=ALU.max)
            v.reciprocal(A[5][:, :], A[3][:, :])
            v.tensor_tensor(A[3][:, :], A[0][:, :], A[0][:, :], ALU.mult)
            v.tensor_scalar(out=A[3][:, :], in0=A[3][:, :], scalar1=-1.0, scalar2=1.0,
                            op0=ALU.mult, op1=ALU.add)
            ii = v.tensor_scalar(out=A[3][:, :], in0=A[3][:, :], scalar1=1e-8,
                                 scalar2=None, op0=ALU.max)
            p.inc(ii, "sv")

        @block.scalar
        def _(s, ot=ot):
            p.wait(s, "sv")
            ii = s.activation(A[6][:, :], A[3][:, :], AF.Sqrt)
            p.inc(ii, "ss")

        @block.vector
        def _(v, ot=ot):
            p.wait(v, "ss")
            v.tensor_tensor(A[0][:, :], A[6][:, :], A[1][:, :], ALU.mult)
            v.tensor_tensor(A[0][:, :], A[0][:, :], A[2][:, :], ALU.mult)
            v.tensor_tensor(A[0][:, :], A[0][:, :], A[5][:, :], ALU.mult)
            v.tensor_tensor_scan(out=A[1][:, :], data0=mask[:, :], data1=A[0][:, :],
                                 initial=0.0, op0=ALU.mult, op1=ALU.add)
            v.tensor_tensor(A[2][:, :], A[4][:, :], A[1][:, :], ALU.mult)
            p.guard(v, v.tensor_tensor_scan(out=states[:, :], data0=c3(A[4])[:, :, 63],
                                            data1=c3(A[2])[:, :, 63], initial=0.0,
                                            op0=ALU.mult, op1=ALU.add))
            p.guard(v, v.memset(si[:, 0:1], 0.0))
            p.guard(v, v.tensor_copy(si[:, 1:NCHK], states[:, 0:NCHK - 1]))
            p.guard(v, v.tensor_tensor_scan(out=cpd[:, :], data0=c3(A[4])[:, :, 63],
                                            data1=zt[:, :], initial=1.0,
                                            op0=ALU.mult, op1=ALU.add))
            p.guard(v, v.memset(cpds[:, 0:1], 1.0))
            p.guard(v, v.tensor_copy(cpds[:, 1:NCHK], cpd[:, 0:NCHK - 1]))
            p.guard(v, v.tensor_copy(sot[:, ot:ot + 1], states[:, NCHK - 1:NCHK]))
            v.tensor_tensor(c3(A[0]), c3(A[4]),
                            si[:, :].unsqueeze(2).broadcast_to((128, NCHK, 64)), ALU.mult)
            if ot >= 1:
                p.wait(v, "so", 32 * ot)
            v.tensor_tensor(rtile[:, :], A[2][:, :], A[0][:, :], ALU.add)
            ii = v.tensor_tensor(c3(ftile), c3(A[4]),
                                 cpds[:, :].unsqueeze(2).broadcast_to((128, NCHK, 64)), ALU.mult)
            p.inc(ii, "sv")

        @block.gpsimd
        def _(g, ot=ot):
            p.wait(g, "sv")
            g.dma_start(out=recl[ot], in_=rtile[:, :]).then_inc(p.sem["so"], 16)
            g.dma_start(out=Fout[ot], in_=ftile[:, :]).then_inc(p.sem["so"], 16)
        p.cnt["so"] += 32

    @block.gpsimd
    def _(g):
        p.wait(g, "vg")
        g.dma_start(out=sout, in_=sot[:, :]).then_inc(p.sem["so"], 16)
        p.cnt["so"] += 16
        p.wait(g, "so")

    return p



def build_l1fix():
    p = P("l1fix")
    nc = p.nc
    sacc = nc.dram_tensor("sacc", [8, 128, TO], F32, kind="ExternalInput").ap()
    recl = nc.dram_tensor("recl", [8, 128, TO], BF16, kind="ExternalInput").ap()
    Fin = nc.dram_tensor("Fin", [8, 128, TO], BF16, kind="ExternalInput").ap()
    sin = nc.dram_tensor("sin", [128, 8], F32, kind="ExternalInput").ap()
    opT = nc.dram_tensor("opT", [D, D], F32, kind="ExternalInput").ap()
    fnw = nc.dram_tensor("fnw", [128, 8], F32, kind="ExternalInput").ap()
    wguT = nc.dram_tensor("wguT", [D, 2 * 4096], F32, kind="ExternalInput").ap()
    wdT = nc.dram_tensor("wdT", [4096, D], F32, kind="ExternalInput").ap()
    sfin = nc.dram_tensor("sfin", [8, 128, TO], F32, kind="ExternalOutput").ap()

    sa = p.sbuf("sa", [128, 8 * TO], F32)
    rb = p.sbuf("rb", [128, 8 * TO], BF16)
    hn = p.sbuf("hn", [128, 8 * TO], BF16)
    h2 = p.sbuf("h2", [128, 32 * 512], BF16)
    ft_in = p.sbuf("ft_in", [128, TO], BF16)
    op_sb = None
    sint = p.sbuf("sint", [128, 8], F32)
    fnwt = p.sbuf("fnwt", [128, 8], F32)
    sgt = p.sbuf("sgt", [128, 512], F32)
    sqt = [p.sbuf(f"sqt{i}", [128, 512], F32) for i in range(2)]
    onesc = p.sbuf("onesc", [128, 1], F32)
    onesr = p.sbuf("onesr", [1, 128], F32)
    rows = p.sbuf("rows", [1, TO], F32)
    rbc3 = p.sbuf("rbc3", [128, TO], F32)
    wgu = [p.sbuf(f"wgu{i}", [128, 8 * 256], BF16) for i in range(2)]
    wdd0 = p.sbuf("wdd0", [128, 32 * 128], BF16)
    wdd = [wdd0, wdd0]

    psP = [p.psum(f"psP{i}", [128, 512]) for i in range(2)]
    psG = p.psum("psG", [128, 512])
    psU = p.psum("psU", [128, 512])
    psQ = p.psum("psQ", [1, 512])
    psB = p.psum("psB", [128, 512])

    for s in ("ld", "vg", "mm", "m2", "sv", "ss", "wl", "so"):
        p.new_sem(s)

    p.begin()
    block = p.block
    sa_v = sa[:].rearrange("p (k t) -> p k t", k=8)
    rb_v = rb[:].rearrange("p (k t) -> p k t", k=8)
    hn_v = hn[:].rearrange("p (k t) -> p k t", k=8)
    h2_v = h2[:].rearrange("p (k t) -> p k t", k=32)
    op_v = h2[:, 0:8 * D].rearrange("p (k t) -> p k t", k=8)
    NCH = TO // 512

    @block.gpsimd
    def _(g):
        for k in range(8):
            g.dma_start(out=sa_v[:, k, :], in_=sacc[k]).then_inc(p.sem["ld"], 16)
            g.dma_start(out=op_v[:, k, :], in_=opT[k * 128:(k + 1) * 128, :]).then_inc(p.sem["ld"], 16)
        g.dma_start(out=sint[:], in_=sin).then_inc(p.sem["ld"], 16)
        g.dma_start(out=fnwt[:], in_=fnw).then_inc(p.sem["ld"], 16)
    p.cnt["ld"] += 16 * 18

    @block.vector
    def _(v):
        v.memset(onesc[:], 1.0)
        ii = v.memset(onesr[:], 1.0)
        p.inc(ii, "sv")

    for dt in range(8):
        @block.gpsimd
        def _(g, dt=dt):
            if dt >= 1:
                p.wait(g, "sv")
            g.dma_start(out=ft_in[:, :], in_=Fin[dt]).then_inc(p.sem["ld"], 16)
            g.dma_start(out=rb_v[:, dt, :], in_=recl[dt]).then_inc(p.sem["ld"], 16)
        p.cnt["ld"] += 32

        @block.vector
        def _(v, dt=dt):
            p.wait(v, "ld")
            ii = v.scalar_tensor_tensor(out=rb_v[:, dt, :], in0=ft_in[:, :],
                                        scalar=sint[:, dt:dt + 1], in1=rb_v[:, dt, :],
                                        op0=ALU.mult, op1=ALU.add)
            p.inc(ii, "sv")

    for ot in range(8):
        for ch in range(NCH):
            n = ot * NCH + ch

            @block.tensor
            def _(t, ot=ot, ch=ch, n=n):
                p.wait(t, "ld")
                p.wait(t, "sv")
                ps = psP[n % 2]
                for sub in range(8):
                    ii = t.matmul(ps[:, :], op_v[:, sub, ot * 128:(ot + 1) * 128],
                                  rb_v[:, sub, ch * 512:(ch + 1) * 512],
                                  start=(sub == 0), stop=(sub == 7))
                p.inc(ii, "mm")

            @block.vector
            def _(v, ot=ot, ch=ch, n=n):
                p.wait(v, "mm")
                ps = psP[n % 2]
                ii = v.tensor_tensor(sa_v[:, ot, ch * 512:(ch + 1) * 512],
                                     sa_v[:, ot, ch * 512:(ch + 1) * 512], ps[:, :], ALU.add)
                p.inc(ii, "sv")

    for ch in range(NCH):
        @block.scalar
        def _(s, ch=ch):
            p.wait(s, "sv")
            base = p.cnt["m2"]
            for sub in range(8):
                if sub >= 2:
                    p.wait(s, "m2", base + sub - 1)
                ii = s.activation(sqt[sub % 2][:, :],
                                  sa_v[:, sub, ch * 512:(ch + 1) * 512], AF.Square)
                p.inc(ii, "ss")

        @block.tensor
        def _(t, ch=ch):
            p.wait(t, "sv")
            base = p.cnt["ss"] - 8
            for sub in range(8):
                p.wait(t, "ss", base + sub + 1)
                ii = t.matmul(psQ[:, :], onesc[:, :], sqt[sub % 2][:, :],
                              start=(sub == 0), stop=(sub == 7))
                p.inc(ii, "m2")

        @block.vector
        def _(v, ch=ch):
            p.wait(v, "m2")
            ii = v.tensor_copy(rows[:, ch * 512:(ch + 1) * 512], psQ[:, :])
            p.inc(ii, "sv")

    @block.vector
    def _(v):
        p.wait(v, "sv")
        p.guard(v, v.tensor_scalar(out=rows[:, 0:TO], in0=rows[:, 0:TO],
                                   scalar1=1.0 / D, scalar2=1e-6,
                                   op0=ALU.mult, op1=ALU.add))

    @block.scalar
    def _(s):
        p.wait(s, "vg")
        ii = s.activation(rbc3[0:1, 0:TO], rows[:, 0:TO], AF.Sqrt)
        p.inc(ii, "ss")

    @block.vector
    def _(v):
        p.wait(v, "ss")
        p.guard(v, v.reciprocal(rows[:, 0:TO], rbc3[0:1, 0:TO]))

    for ch in range(NCH):
        @block.tensor
        def _(t, ch=ch):
            p.wait(t, "vg")
            if ch > 0:
                p.wait(t, "sv")
            ii = t.matmul(psB[:, :], onesr[:, :], rows[:, ch * 512:(ch + 1) * 512],
                          start=True, stop=True)
            p.inc(ii, "mm")

        @block.vector
        def _(v, ch=ch):
            p.wait(v, "mm")
            ii = v.tensor_copy(rbc3[:, ch * 512:(ch + 1) * 512], psB[:, :])
            p.inc(ii, "sv")

    for dt in range(8):
        @block.vector
        def _(v, dt=dt):
            p.wait(v, "sv")
            ii = v.tensor_scalar(out=hn_v[:, dt, :], in0=sa_v[:, dt, :],
                                 scalar1=fnwt[:, dt:dt + 1], scalar2=rbc3_slice(rbc3, dt),
                                 op0=ALU.mult, op1=ALU.mult) if False else None
            v.tensor_scalar(out=hn_v[:, dt, :], in0=sa_v[:, dt, :],
                            scalar1=fnwt[:, dt:dt + 1], scalar2=None, op0=ALU.mult)
            ii = v.tensor_tensor(hn_v[:, dt, :], hn_v[:, dt, :], rbc3[:, :], ALU.mult)
            p.inc(ii, "sv")

    for tc in range(NCH):
        for ft in range(32):
            wb = wgu[ft % 2]

            @block.gpsimd
            def _(g, tc=tc, ft=ft, wb=wb):
                p.wait(g, "mm")
                wv = wb[:].rearrange("p (k t) -> p k t", k=8)
                for sub in range(8):
                    g.dma_start(out=wv[:, sub, 0:128],
                                in_=wguT[sub * 128:(sub + 1) * 128, ft * 128:(ft + 1) * 128]
                                ).then_inc(p.sem["wl"], 16)
                    g.dma_start(out=wv[:, sub, 128:256],
                                in_=wguT[sub * 128:(sub + 1) * 128, 4096 + ft * 128:4096 + (ft + 1) * 128]
                                ).then_inc(p.sem["wl"], 16)
            p.cnt["wl"] += 16 * 16

            @block.tensor
            def _(t, tc=tc, ft=ft, wb=wb):
                p.wait(t, "wl")
                p.wait(t, "sv")
                wv = wb[:].rearrange("p (k t) -> p k t", k=8)
                for sub in range(8):
                    t.matmul(psG[:, :], wv[:, sub, 0:128].transpose([0, 1]) if False else wv[:, sub, 0:128],
                             hn_v[:, sub, tc * 512:(tc + 1) * 512],
                             start=(sub == 0), stop=(sub == 7))
                for sub in range(8):
                    ii = t.matmul(psU[:, :], wv[:, sub, 128:256],
                                  hn_v[:, sub, tc * 512:(tc + 1) * 512],
                                  start=(sub == 0), stop=(sub == 7))
                p.inc(ii, "mm")

            @block.scalar
            def _(s, tc=tc, ft=ft):
                p.wait(s, "mm")
                ii = s.activation(sgt[:, :], psG[:, :], AF.Silu)
                p.inc(ii, "ss")

            @block.vector
            def _(v, tc=tc, ft=ft):
                p.wait(v, "ss")
                ii = v.tensor_tensor(h2_v[:, ft, :], sgt[:, :], psU[:, :], ALU.mult)
                p.inc(ii, "sv")

        for dt in range(8):
            wd_c = wdd[dt % 2]

            @block.gpsimd
            def _(g, tc=tc, dt=dt, wd_c=wd_c):
                p.wait(g, "mm")
                wv = wd_c[:].rearrange("p (k t) -> p k t", k=32)
                for fs in range(32):
                    g.dma_start(out=wv[:, fs, :],
                                in_=wdT[fs * 128:(fs + 1) * 128, dt * 128:(dt + 1) * 128]
                                ).then_inc(p.sem["wl"], 16)
            p.cnt["wl"] += 16 * 32

            @block.tensor
            def _(t, tc=tc, dt=dt, wd_c=wd_c):
                p.wait(t, "wl")
                p.wait(t, "sv")
                ps = psP[dt % 2]
                wv = wd_c[:].rearrange("p (k t) -> p k t", k=32)
                for fs in range(32):
                    ii = t.matmul(ps[:, :], wv[:, fs, :], h2_v[:, fs, :],
                                  start=(fs == 0), stop=(fs == 31))
                p.inc(ii, "mm")

            @block.vector
            def _(v, tc=tc, dt=dt):
                p.wait(v, "mm")
                ps = psP[dt % 2]
                ii = v.tensor_tensor(sa_v[:, dt, tc * 512:(tc + 1) * 512],
                                     sa_v[:, dt, tc * 512:(tc + 1) * 512], ps[:, :], ALU.add)
                p.inc(ii, "sv")

    @block.sync
    def _(sy):
        p.wait(sy, "sv")
        for k in range(8):
            ii = sy.dma_start(out=sfin[k], in_=sa_v[:, k, :])
            p.inc(ii, "so", 16)
        p.wait(sy, "so")

    return p



def build_l1moe():
    p = P("l1moe")
    nc = p.nc
    xn1 = nc.dram_tensor("xn1", [8, 128, TO], BF16, kind="ExternalInput").ap()
    sfin = nc.dram_tensor("sfin", [8, 128, TO], F32, kind="ExternalInput").ap()
    x1 = nc.dram_tensor("x1", [8, 128, TO], F32, kind="ExternalInput").ap()
    wrg = nc.dram_tensor("wrg", [D, 16], F32, kind="ExternalInput").ap()
    mwg = nc.dram_tensor("mwg", [8, D, 4096], F32, kind="ExternalInput").ap()
    mwu = nc.dram_tensor("mwu", [8, D, 4096], F32, kind="ExternalInput").ap()
    mwd = nc.dram_tensor("mwd", [8, 4096, D], F32, kind="ExternalInput").ap()
    out = nc.dram_tensor("out", [8, 128, TO], F32, kind="ExternalOutput").ap()

    xn = p.sbuf("xn", [128, 8 * TO], BF16)
    macc = p.sbuf("macc", [128, 8 * TO], F32)
    h2 = p.sbuf("h2", [128, 32 * 512], BF16)
    wrgt = p.sbuf("wrgt", [128, 8 * 16], BF16)
    cwall = p.sbuf("cwall", [128, 16 * 8], F32)
    st8 = p.sbuf("st8", [128, 8], F32)
    cwfm = p.sbuf("cwfm", [8, TO], F32)
    cwrow = p.sbuf("cwrow", [1, TO], F32)
    cwb = p.sbuf("cwb", [128, TO], BF16)
    onesr = p.sbuf("onesr", [1, 128], F32)
    ident = p.sbuf("ident", [128, 128], F32)
    identi = p.sbuf("identi", [128, 128], I32)
    sgt = p.sbuf("sgt", [128, 512], F32)
    ub = p.sbuf("ub", [128, 512], BF16)
    wg_s = [p.sbuf(f"wg_s{i}", [128, 8 * 128], BF16) for i in range(2)]
    wu_s = [p.sbuf(f"wu_s{i}", [128, 8 * 128], BF16) for i in range(2)]
    wd_s = [p.sbuf(f"wd_s{i}", [128, 32 * 128], BF16) for i in range(2)]
    s_t = p.sbuf("s_t", [128, 512], F32)
    x_t = p.sbuf("x_t", [128, 512], F32)
    o_t = [p.sbuf(f"o_t{i}", [128, 512], F32) for i in range(2)]

    psG = p.psum("psG", [128, 512])
    psU = p.psum("psU", [128, 512])
    psD_ = [p.psum(f"psD{i}", [128, 512]) for i in range(2)]
    psR = p.psum("psR", [128, 16])
    psB = p.psum("psB", [128, 512])

    for s in ("ld", "vg", "mm", "sv", "ss", "wl", "gd", "so"):
        p.new_sem(s)

    p.begin()
    block = p.block
    xn_v = xn[:].rearrange("p (k t) -> p k t", k=8)
    ma_v = macc[:].rearrange("p (k t) -> p k t", k=8)
    h2_v = h2[:].rearrange("p (k t) -> p k t", k=32)
    wrg_v = wrgt[:].rearrange("p (k t) -> p k t", k=8)
    cw_v = cwall[:].rearrange("p (k t) -> p k t", k=16)
    NCH = TO // 512

    @block.gpsimd
    def _(g):
        for k in range(8):
            g.dma_start(out=xn_v[:, k, :], in_=xn1[k]).then_inc(p.sem["ld"], 16)
            g.dma_start(out=wrg_v[:, k, :], in_=wrg[k * 128:(k + 1) * 128, :]).then_inc(p.sem["ld"], 16)
        g.iota(identi[:], pattern=[[1, 128]], base=0, channel_multiplier=-1,
               allow_small_or_imprecise_dtypes=True)
        ii = g.tensor_scalar(out=ident[:], in0=identi[:], scalar1=0, scalar2=None, op0=ALU.is_equal)
        p.inc(ii, "gd")
    p.cnt["ld"] += 16 * 16

    @block.vector
    def _(v):
        ii = v.memset(onesr[:], 1.0)
        p.inc(ii, "sv")

    for tt in range(16):
        @block.tensor
        def _(t, tt=tt):
            p.wait(t, "ld")
            if tt > 0:
                p.wait(t, "ss")
            for sub in range(8):
                ii = t.matmul(psR[:, :], xn_v[:, sub, tt * 128:(tt + 1) * 128],
                              wrg_v[:, sub, :], start=(sub == 0), stop=(sub == 7))
            p.inc(ii, "mm")

        @block.vector
        def _(v, tt=tt):
            p.wait(v, "mm")
            p.guard(v, v.tensor_reduce(st8[:, 0:1], psR[:, 0:8], axis=AX.X, op=ALU.max, negate=True))
            p.guard(v, v.tensor_reduce(st8[:, 1:2], psR[:, 8:16], axis=AX.X, op=ALU.max, negate=True))
            p.guard(v, v.tensor_scalar(out=st8[:, 2:3], in0=st8[:, 1:2], scalar1=10.0,
                                       scalar2=None, op0=ALU.mult))

        @block.scalar
        def _(s, tt=tt):
            p.wait(s, "vg")
            s.activation(cw_v[:, tt, 0:8], psR[:, 0:8], AF.Exp, bias=st8[:, 0:1],
                         scale=1.0, accum_out=st8[:, 3:4])
            ii = s.activation(cw_v[:, tt, 8:16] if False else st8[:, 8:8], psR[:, 8:16], AF.Exp,
                              bias=st8[:, 2:3], scale=10.0,
                              accum_out=st8[:, 4:5]) if False else None
            ii = s.activation(cwall[:, tt * 8:(tt + 1) * 8].bitcast(F32) if False else cw_v[:, tt, 0:8],
                              psR[:, 0:8], AF.Exp, bias=st8[:, 0:1], scale=1.0,
                              accum_out=st8[:, 3:4]) if False else None
            p.inc(s.activation(sgt[:, 0:8], psR[:, 8:16], AF.Exp, bias=st8[:, 2:3],
                               scale=10.0, accum_out=st8[:, 4:5]), "ss")

        @block.vector
        def _(v, tt=tt):
            p.wait(v, "ss")
            p.guard(v, v.reciprocal(st8[:, 5:6], st8[:, 3:4]))
            p.guard(v, v.reciprocal(st8[:, 6:7], st8[:, 4:5]))
            p.guard(v, v.tensor_scalar(out=cw_v[:, tt, :], in0=cw_v[:, tt, :],
                                       scalar1=st8[:, 5:6], scalar2=None, op0=ALU.mult))
            p.guard(v, v.tensor_scalar(out=sgt[:, 0:8], in0=sgt[:, 0:8],
                                       scalar1=st8[:, 6:7], scalar2=None, op0=ALU.mult))
            p.guard(v, v.tensor_tensor(cw_v[:, tt, :], cw_v[:, tt, :], sgt[:, 0:8], ALU.mult))
            p.guard(v, v.tensor_reduce(st8[:, 7:8], cw_v[:, tt, :], axis=AX.X, op=ALU.add))
            p.guard(v, v.tensor_scalar(out=st8[:, 7:8], in0=st8[:, 7:8], scalar1=1e-9,
                                       scalar2=None, op0=ALU.add))
            p.guard(v, v.reciprocal(st8[:, 7:8], st8[:, 7:8]))
            p.guard(v, v.tensor_scalar(out=cw_v[:, tt, :], in0=cw_v[:, tt, :],
                                        scalar1=st8[:, 7:8], scalar2=None, op0=ALU.mult))
            ii = v.tensor_copy(st8[:, 0:1], st8[:, 0:1])
            p.inc(ii, "sv")

    for tt in range(16):
        @block.tensor
        def _(t, tt=tt):
            p.wait(t, "sv")
            p.wait(t, "gd", 1)
            if tt > 0:
                p.wait(t, "sv", p.cnt["sv"])
            ii = t.transpose(psB[0:8, 0:32].bitcast(F32) if False else psB[0:8, 0:128],
                             cw_v[:, tt, :], ident[:])
            p.inc(ii, "mm")

        @block.vector
        def _(v, tt=tt):
            p.wait(v, "mm")
            ii = v.tensor_copy(cwfm[:, tt * 128:(tt + 1) * 128], psB[0:8, 0:128])
            p.inc(ii, "sv")

    for e in range(8):
        @block.gpsimd
        def _(g, e=e):
            p.wait(g, "sv")
            if e > 0:
                p.wait(g, "mm")
            g.dma_start(out=cwrow[:, :], in_=cwfm[e:e + 1, :]).then_inc(p.sem["gd"], 16)
        p.cnt["gd"] += 16

        for ch in range(NCH):
            @block.tensor
            def _(t, e=e, ch=ch):
                p.wait(t, "gd")
                if ch > 0 or e > 0:
                    p.wait(t, "sv")
                ii = t.matmul(psB[:, :], onesr[:, :], cwrow[:, ch * 512:(ch + 1) * 512],
                              start=True, stop=True)
                p.inc(ii, "mm")

            @block.vector
            def _(v, e=e, ch=ch):
                p.wait(v, "mm")
                ii = v.tensor_copy(cwb[:, ch * 512:(ch + 1) * 512], psB[:, :])
                p.inc(ii, "sv")

        for tc in range(NCH):
            for ft in range(32):
                wgb = wg_s[ft % 2]
                wub = wu_s[ft % 2]

                @block.gpsimd
                def _(g, e=e, tc=tc, ft=ft, wgb=wgb, wub=wub):
                    p.wait(g, "mm")
                    wgv = wgb[:].rearrange("p (k t) -> p k t", k=8)
                    wuv = wub[:].rearrange("p (k t) -> p k t", k=8)
                    for sub in range(8):
                        g.dma_start(out=wgv[:, sub, :],
                                    in_=mwg[e][sub * 128:(sub + 1) * 128, ft * 128:(ft + 1) * 128]
                                    ).then_inc(p.sem["wl"], 16)
                        g.dma_start(out=wuv[:, sub, :],
                                    in_=mwu[e][sub * 128:(sub + 1) * 128, ft * 128:(ft + 1) * 128]
                                    ).then_inc(p.sem["wl"], 16)
                p.cnt["wl"] += 16 * 16

                @block.tensor
                def _(t, e=e, tc=tc, ft=ft, wgb=wgb, wub=wub):
                    p.wait(t, "wl")
                    p.wait(t, "sv")
                    wgv = wgb[:].rearrange("p (k t) -> p k t", k=8)
                    wuv = wub[:].rearrange("p (k t) -> p k t", k=8)
                    for sub in range(8):
                        t.matmul(psG[:, :], wgv[:, sub, :],
                                 xn_v[:, sub, tc * 512:(tc + 1) * 512],
                                 start=(sub == 0), stop=(sub == 7))
                    for sub in range(8):
                        ii = t.matmul(psU[:, :], wuv[:, sub, :],
                                      xn_v[:, sub, tc * 512:(tc + 1) * 512],
                                      start=(sub == 0), stop=(sub == 7))
                    p.inc(ii, "mm")

                @block.scalar
                def _(s, e=e, tc=tc, ft=ft):
                    p.wait(s, "mm")
                    ii = s.activation(sgt[:, :], psG[:, :], AF.Silu)
                    p.inc(ii, "ss")

                @block.vector
                def _(v, e=e, tc=tc, ft=ft):
                    p.wait(v, "ss")
                    v.tensor_tensor(ub[:, :], psU[:, :], cwb[:, tc * 512:(tc + 1) * 512], ALU.mult)
                    ii = v.tensor_tensor(h2_v[:, ft, :], sgt[:, :], ub[:, :], ALU.mult)
                    p.inc(ii, "sv")

            for dt in range(8):
                wdb = wd_s[dt % 2]

                @block.gpsimd
                def _(g, e=e, tc=tc, dt=dt, wdb=wdb):
                    p.wait(g, "mm")
                    wdv = wdb[:].rearrange("p (k t) -> p k t", k=32)
                    for fs in range(32):
                        g.dma_start(out=wdv[:, fs, :],
                                    in_=mwd[e][fs * 128:(fs + 1) * 128, dt * 128:(dt + 1) * 128]
                                    ).then_inc(p.sem["wl"], 16)
                p.cnt["wl"] += 16 * 32

                @block.tensor
                def _(t, e=e, tc=tc, dt=dt, wdb=wdb):
                    p.wait(t, "wl")
                    p.wait(t, "sv")
                    ps = psD_[dt % 2]
                    wdv = wdb[:].rearrange("p (k t) -> p k t", k=32)
                    for fs in range(32):
                        ii = t.matmul(ps[:, :], wdv[:, fs, :], h2_v[:, fs, :],
                                      start=(fs == 0), stop=(fs == 31))
                    p.inc(ii, "mm")

                @block.vector
                def _(v, e=e, tc=tc, dt=dt):
                    p.wait(v, "mm")
                    ps = psD_[dt % 2]
                    if e == 0:
                        ii = v.tensor_copy(ma_v[:, dt, tc * 512:(tc + 1) * 512], ps[:, :])
                    else:
                        ii = v.tensor_tensor(ma_v[:, dt, tc * 512:(tc + 1) * 512],
                                             ma_v[:, dt, tc * 512:(tc + 1) * 512], ps[:, :], ALU.add)
                    p.inc(ii, "sv")

    n = 0
    for dt in range(8):
        for tc in range(NCH):
            n += 1
            ob = o_t[n % 2]

            @block.gpsimd
            def _(g, dt=dt, tc=tc, n=n):
                p.wait(g, "sv")
                if n > 2:
                    p.wait(g, "so", 16 * (n - 2))
                g.dma_start(out=s_t[:, :], in_=sfin[dt][:, tc * 512:(tc + 1) * 512]).then_inc(p.sem["gd"], 16)
                g.dma_start(out=x_t[:, :], in_=x1[dt][:, tc * 512:(tc + 1) * 512]).then_inc(p.sem["gd"], 16)
            p.cnt["gd"] += 32

            @block.scalar
            def _(s, dt=dt, tc=tc):
                p.wait(s, "sv")
                ii = s.activation(sgt[:, :], ma_v[:, dt, tc * 512:(tc + 1) * 512], AF.Silu)
                p.inc(ii, "ss")

            @block.vector
            def _(v, dt=dt, tc=tc, ob=ob):
                p.wait(v, "ss")
                p.wait(v, "gd")
                v.tensor_tensor(ob[:, :], s_t[:, :], sgt[:, :], ALU.mult)
                ii = v.tensor_tensor(ob[:, :], ob[:, :], x_t[:, :], ALU.add)
                p.inc(ii, "sv")

            @block.sync
            def _(sy, dt=dt, tc=tc, ob=ob):
                p.wait(sy, "sv")
                ii = sy.dma_start(out=out[dt][:, tc * 512:(tc + 1) * 512], in_=ob[:, :])
                p.inc(ii, "so", 16)

    @block.sync
    def _(sy):
        p.wait(sy, "so")

    return p


_cache = {}


def _get(name, builder):
    if name not in _cache:
        _cache[name] = builder()
    return _cache[name]


def run_launch(name, builder, in_maps):
    p = _get(name, builder)
    res = run_bass_kernel_spmd(p.nc, in_maps, core_ids=list(range(NCORES)))
    return res.results


def run_l0prep(x_by_core):
    return run_launch("l0prep", build_l0prep, [{"xin": x_by_core[c]} for c in range(NCORES)])


def _vec_pk(w):
    return np.ascontiguousarray(np.asarray(w, np.float32).reshape(8, 128).T)


def kernel(**inp):
    x = np.asarray(inp["L0_attn_query"], np.float32)
    xs = []
    xfm = []
    for c in range(NCORES):
        b, h = c // 2, c % 2
        own = x[b, h * TO:(h + 1) * TO]
        oth = x[b, (1 - h) * TO:(2 - h) * TO]
        xs.append(np.ascontiguousarray(np.concatenate([own, oth], 0)))
        xfm.append(np.ascontiguousarray(own.T.reshape(8, 128, TO)))
    wq2 = np.ascontiguousarray(np.asarray(inp["L0_attn_Wq"], np.float32).transpose(1, 0, 2).reshape(D, D))
    wk2 = np.ascontiguousarray(np.asarray(inp["L0_attn_Wk"], np.float32).transpose(1, 0, 2).reshape(D, D))
    wv2 = np.ascontiguousarray(np.asarray(inp["L0_attn_Wv"], np.float32).transpose(1, 0, 2).reshape(D, D))
    idx = (np.arange(D) % 64) * 16 + np.arange(D) // 64
    wp2 = np.ascontiguousarray(np.asarray(inp["L0_attn_Wp"], np.float32)[idx])
    bp2 = _vec_pk(inp["L0_attn_bp"])
    cw2 = np.ascontiguousarray(np.asarray(inp["L1_attn_conv_w"], np.float32)[:, 0, :]
                               .reshape(8, 128, 4).transpose(1, 0, 2).reshape(128, 32))
    waT = np.ascontiguousarray(np.asarray(inp["L1_attn_w_a"], np.float32).T)
    wiT = np.ascontiguousarray(np.asarray(inp["L1_attn_w_i"], np.float32).T)
    wvT = np.ascontiguousarray(np.asarray(inp["L1_attn_w_v"], np.float32).T)
    opT = np.ascontiguousarray(np.asarray(inp["L1_attn_out_proj_w"], np.float32).T)
    wguT = np.ascontiguousarray(np.asarray(inp["L1_attn_w_gate_up"], np.float32).T)
    wdT = np.ascontiguousarray(np.asarray(inp["L1_attn_w_down"], np.float32).T)
    wrg = np.ascontiguousarray(np.concatenate([np.asarray(inp["L1_ffn_w_router"], np.float32),
                                               np.asarray(inp["L1_ffn_w_gating"], np.float32)], 1))
    mwg = np.ascontiguousarray(np.asarray(inp["L1_ffn_w_gate"], np.float32))
    mwu = np.ascontiguousarray(np.asarray(inp["L1_ffn_w_up"], np.float32))
    mwd = np.ascontiguousarray(np.asarray(inp["L1_ffn_w_down"], np.float32))

    pre = run_launch("l0prep", build_l0prep, [{"xin": xs[c]} for c in range(NCORES)])
    att = run_launch("l0attn", build_l0attn,
                     [{"xnfm": np.asarray(pre[c]["xnfm"]), "wqh": wq2, "wkh": wk2,
                       "wvh": wv2} for c in range(NCORES)])
    wqz = run_launch("l0wq", build_l0wq,
                     [{"wg": np.asarray(inp["L0_ffn_w_gate"], np.float32),
                       "wu": np.asarray(inp["L0_ffn_w_up"], np.float32),
                       "wd": np.asarray(inp["L0_ffn_w_down"], np.float32)} for c in range(NCORES)])
    ffn = run_launch("l0ffn", build_l0ffn,
                     [{"xqfm": np.asarray(pre[c]["xqfm"]),
                       "wgq": np.asarray(wqz[c]["wgq"]),
                       "wuq": np.asarray(wqz[c]["wuq"]),
                       "wdq": np.asarray(wqz[c]["wdq"])} for c in range(NCORES)])
    comb = run_launch("l0comb", build_l0comb,
                      [{"aoin": np.asarray(att[c]["aout"]), "gin": np.asarray(ffn[c]["gout"]),
                        "xfm": xfm[c], "wph": wp2, "bph": bp2} for c in range(NCORES)])
    x1 = [np.asarray(comb[c]["x1"]) for c in range(NCORES)]
    ln = run_launch("l1ln", build_l1ln,
                    [{"x1": x1[c], "pnw": _vec_pk(inp["L1_attn_pre_norm_w"])}
                     for c in range(NCORES)])
    maps = []
    for c in range(NCORES):
        halo = (np.zeros((8, 128, 3), np.float32) if c % 2 == 0
                else np.asarray(ln[c - 1]["redge"]))
        maps.append({
            "sacc": np.asarray(ln[c]["sacc"]), "rstd2": np.asarray(ln[c]["rstd2"]),
            "halo": halo, "pnw": _vec_pk(inp["L1_attn_pre_norm_w"]),
            "cw": cw2, "cb": _vec_pk(inp["L1_attn_conv_b"]),
            "db": _vec_pk(inp["L1_attn_decay_bias"]),
            "waT": waT, "wiT": wiT, "wvT": wvT,
        })
    rec = run_launch("l1rec", build_l1rec, maps)
    maps2 = []
    for c in range(NCORES):
        s_in = (np.zeros((128, 8), np.float32) if c % 2 == 0
                else np.asarray(rec[c - 1]["sout"]))
        maps2.append({
            "sacc": np.asarray(ln[c]["sacc"]),
            "recl": np.asarray(rec[c]["recl"]), "Fin": np.asarray(rec[c]["Fout"]),
            "sin": s_in, "opT": opT, "fnw": _vec_pk(inp["L1_attn_ffn_norm_w"]),
            "wguT": wguT, "wdT": wdT,
        })
    fix = run_launch("l1fix", build_l1fix, maps2)
    moe = run_launch("l1moe", build_l1moe,
                     [{"xn1": np.asarray(ln[c]["xn1"]), "sfin": np.asarray(fix[c]["sfin"]),
                       "x1": x1[c], "wrg": wrg, "mwg": mwg, "mwu": mwu, "mwd": mwd}
                      for c in range(NCORES)])

    out = np.empty((B, T, D), np.float32)
    for c in range(NCORES):
        b, h = c // 2, c % 2
        o = np.asarray(moe[c]["out"])
        out[b, h * TO:(h + 1) * TO] = o.reshape(D, TO).T
    return out


# version 06
# speedup vs baseline: 1.0465x; 1.0465x over previous
import math
import numpy as np
import ml_dtypes
from contextlib import ExitStack

import concourse.bass as bass
import concourse.mybir as mybir
from concourse.bass_utils import run_bass_kernel_spmd

F32 = mybir.dt.float32
BF16 = mybir.dt.bfloat16
I32 = mybir.dt.int32
AF = mybir.ActivationFunctionType
ALU = mybir.AluOpType
AX = mybir.AxisListType
BF = ml_dtypes.bfloat16

B, T, D = 4, 4096, 1024
TO = 2048
NT_OWN = TO // 128
NT_ALL = T // 128
H, E = 16, 64
FFI = 2816
MAGIC = float(np.float32(3 * 2 ** 22))

NCORES = 8


class P:

    def __init__(self, name):
        self.nc = bass.Bass("TRN2", target_bir_lowering=False)
        self.stack = ExitStack()
        self.sem = {}
        self.cnt = {}
        self.block = None

    def __enter__(self):
        return self

    def sbuf(self, name, shape, dtype):
        return self.stack.enter_context(self.nc.sbuf_tensor(name, shape, dtype))

    def psum(self, name, shape, dtype=F32):
        return self.stack.enter_context(self.nc.psum_tensor(name, shape, dtype))

    def new_sem(self, name):
        h = self.stack.enter_context(self.nc.semaphore(name))
        self.sem[name] = h
        self.cnt[name] = 0
        return h

    def inc(self, instr, name, n=1):
        self.cnt[name] += n
        instr.then_inc(self.sem[name], n)
        return self.cnt[name]

    def tgt(self, name):
        return self.cnt[name]

    def wait(self, eng, name, val=None):
        eng.wait_ge(self.sem[name], self.cnt[name] if val is None else val)

    def guard(self, eng, inst, gname="vg"):
        c = self.inc(inst, gname)
        eng.wait_ge(self.sem[gname], c)
        return inst

    def begin(self):
        self.block = self.stack.enter_context(self.nc.Block())

    def close(self):
        self.stack.close()



def build_l0prep():
    p = P("l0prep")
    nc = p.nc
    xin = nc.dram_tensor("xin", [T, D], F32, kind="ExternalInput").ap()
    xnfm = nc.dram_tensor("xnfm", [8, 128, T], BF16, kind="ExternalOutput").ap()
    xqfm = nc.dram_tensor("xqfm", [8, 128, TO], BF16, kind="ExternalOutput").ap()

    xt = [p.sbuf(f"xt{i}", [128, D], F32) for i in range(2)]
    sq = p.sbuf("sq", [128, D], F32)
    st1 = p.sbuf("st1", [128, 8], F32)
    xn_t = p.sbuf("xn_t", [128, D], F32)
    xnb = p.sbuf("xnb", [128, D], BF16)
    y2 = p.sbuf("y2", [128, D], F32)
    qlo = p.sbuf("qlo", [128, D], F32)
    qhi = p.sbuf("qhi", [128, D], F32)
    qmask = p.sbuf("qmask", [128, D], I32)
    qsel = p.sbuf("qsel", [128, D], F32)
    bm = p.sbuf("bm", [128, D // 16], F32)
    rs2 = p.sbuf("rs2", [128, D // 16], F32)
    rsh = p.sbuf("rsh", [128, D // 16], F32)
    xqb = p.sbuf("xqb", [128, D], BF16)
    xn_sb = p.sbuf("xn_sb", [128, 8 * T], BF16)
    xq_sb = p.sbuf("xq_sb", [128, 8 * TO], BF16)

    ld = p.new_sem("ld")
    vg = p.new_sem("vg")
    sv = p.new_sem("sv")
    ss = p.new_sem("ss")
    tp = p.new_sem("tp")
    so = p.new_sem("so")

    p.begin()
    block = p.block
    xn_v = xn_sb[:].rearrange("p (k t) -> p k t", k=8)
    xq_v = xq_sb[:].rearrange("p (k t) -> p k t", k=8)

    for i in range(NT_ALL):
        own = i < NT_OWN
        buf = xt[i % 2]

        @block.gpsimd
        def _(g, i=i, buf=buf):
            if i >= 2:
                p.wait(g, "sv", p.tgt("sv"))
            g.dma_start(out=buf[:], in_=xin[i * 128:(i + 1) * 128, :]).then_inc(p.sem["ld"], 16)
        p.cnt["ld"] += 16

        @block.vector
        def _(v, i=i, buf=buf, own=own):
            p.wait(v, "ld")
            p.guard(v, v.tensor_reduce(st1[:, 0:1], buf[:], axis=AX.X, op=ALU.add))
            v.tensor_tensor(sq[:], buf[:], buf[:], ALU.mult)
            p.guard(v, v.tensor_reduce(st1[:, 1:2], sq[:], axis=AX.X, op=ALU.add))
            p.guard(v, v.tensor_scalar(out=st1[:, 2:3], in0=st1[:, 0:1], scalar1=1.0 / D, scalar2=None, op0=ALU.mult))
            p.guard(v, v.tensor_scalar(out=st1[:, 3:4], in0=st1[:, 1:2], scalar1=1.0 / D, scalar2=None, op0=ALU.mult))
            p.guard(v, v.tensor_tensor(st1[:, 4:5], st1[:, 2:3], st1[:, 2:3], ALU.mult))
            p.guard(v, v.tensor_tensor(st1[:, 5:6], st1[:, 3:4], st1[:, 4:5], ALU.subtract))
            p.guard(v, v.tensor_scalar(out=st1[:, 5:6], in0=st1[:, 5:6], scalar1=1e-5,
                                       scalar2=None, op0=ALU.add))

        @block.scalar
        def _(s, i=i):
            p.wait(s, "vg")
            i0 = s.activation(st1[:, 6:7], st1[:, 5:6], AF.Sqrt)
            p.inc(i0, "ss")

        @block.vector
        def _(v, i=i, buf=buf, own=own):
            p.wait(v, "ss")
            p.wait(v, "tp")
            p.guard(v, v.reciprocal(st1[:, 7:8], st1[:, 6:7]))
            v.tensor_scalar(out=xn_t[:], in0=buf[:], scalar1=st1[:, 2:3], scalar2=st1[:, 7:8],
                            op0=ALU.subtract, op1=ALU.mult)
            i1 = v.tensor_copy(xnb[:], xn_t[:])
            p.inc(i1, "sv")
            if own:
                x3 = xn_t[:].rearrange("p (n b) -> p n b", b=16)
                p.guard(v, v.tensor_reduce(bm[:].unsqueeze(2), x3, axis=AX.X, op=ALU.max,
                                           apply_absolute_value=True))
                p.guard(v, v.tensor_scalar(out=bm[:], in0=bm[:], scalar1=6e-12, scalar2=None, op0=ALU.max))
                p.guard(v, v.reciprocal(rs2[:], bm[:]))
                p.guard(v, v.tensor_scalar(out=rs2[:], in0=rs2[:], scalar1=12.0, scalar2=None, op0=ALU.mult))
                p.guard(v, v.tensor_scalar(out=rsh[:], in0=bm[:], scalar1=1.0 / 12.0, scalar2=None, op0=ALU.mult))
                y2v = y2[:].rearrange("p (n b) -> p n b", b=16)
                v.tensor_tensor(y2v, x3, rs2[:].unsqueeze(2).broadcast_to((128, 64, 16)), ALU.mult)
                v.tensor_scalar(out=qlo[:], in0=y2[:], scalar1=MAGIC, scalar2=-MAGIC,
                                op0=ALU.add, op1=ALU.add)
                v.tensor_scalar(out=qhi[:].bitcast(I32), in0=y2[:].bitcast(I32),
                                scalar1=0x00200000, scalar2=None, op0=ALU.add)
                v.tensor_scalar(out=qhi[:].bitcast(I32), in0=qhi[:].bitcast(I32),
                                scalar1=-0x00400000, scalar2=None, op0=ALU.bitwise_and)
                v.tensor_scalar(out=sq[:].bitcast(I32), in0=y2[:].bitcast(I32),
                                scalar1=0x7FFFFFFF, scalar2=None, op0=ALU.bitwise_and)
                v.tensor_scalar(out=qmask[:], in0=sq[:],
                                scalar1=1.75, scalar2=None, op0=ALU.is_lt)
                v.select(out=qsel[:], mask=qmask[:], on_true=qlo[:], on_false=qhi[:])
                qv = qsel[:].rearrange("p (n b) -> p n b", b=16)
                xv = xqb[:].rearrange("p (n b) -> p n b", b=16)
                i2 = v.tensor_tensor(xv, qv, rsh[:].unsqueeze(2).broadcast_to((128, 64, 16)), ALU.mult)
                p.inc(i2, "sv")

        @block.scalar
        def _(s, i=i, own=own):
            p.wait(s, "sv")
            for k in range(8):
                ii = s.dma_start_transpose(xn_v[:, k, i * 128:(i + 1) * 128],
                                           xnb[:, k * 128:(k + 1) * 128])
                p.inc(ii, "tp", 16)
            if own:
                for k in range(8):
                    ii = s.dma_start_transpose(xq_v[:, k, i * 128:(i + 1) * 128],
                                               xqb[:, k * 128:(k + 1) * 128])
                    p.inc(ii, "tp", 16)

    @block.sync
    def _(sy):
        p.wait(sy, "tp")
        for k in range(8):
            ii = sy.dma_start(out=xnfm[k], in_=xn_v[:, k, :])
            p.inc(ii, "so", 16)
            ii = sy.dma_start(out=xqfm[k], in_=xq_v[:, k, :])
            p.inc(ii, "so", 16)
        p.wait(sy, "so")

    return p



def emit_quant(p, v, src_f32, dst_bf, tmp, C):
    x3 = src_f32.rearrange("p (n b) -> p n b", b=16)
    bm, rs2, rsh = tmp["bm"], tmp["rs2"], tmp["rsh"]
    nb = C // 16
    p.guard(v, v.tensor_reduce(bm[:, :nb].unsqueeze(2), x3, axis=AX.X, op=ALU.max,
                               apply_absolute_value=True))
    p.guard(v, v.tensor_scalar(out=bm[:, :nb], in0=bm[:, :nb], scalar1=6e-12, scalar2=None, op0=ALU.max))
    p.guard(v, v.reciprocal(rs2[:, :nb], bm[:, :nb]))
    p.guard(v, v.tensor_scalar(out=rs2[:, :nb], in0=rs2[:, :nb], scalar1=12.0, scalar2=None, op0=ALU.mult))
    p.guard(v, v.tensor_scalar(out=rsh[:, :nb], in0=bm[:, :nb], scalar1=1.0 / 12.0, scalar2=None, op0=ALU.mult))
    y2, qlo, qhi, qmask, qsel, sabs = (tmp["y2"], tmp["qlo"], tmp["qhi"],
                                       tmp["qmask"], tmp["qsel"], tmp["sabs"])
    y2v = y2[:, :C].rearrange("p (n b) -> p n b", b=16)
    v.tensor_tensor(y2v, x3, rs2[:, :nb].unsqueeze(2).broadcast_to((128, nb, 16)), ALU.mult)
    v.tensor_scalar(out=qlo[:, :C], in0=y2[:, :C], scalar1=MAGIC, scalar2=-MAGIC,
                    op0=ALU.add, op1=ALU.add)
    v.tensor_scalar(out=qhi[:, :C].bitcast(I32), in0=y2[:, :C].bitcast(I32),
                    scalar1=0x00200000, scalar2=None, op0=ALU.add)
    v.tensor_scalar(out=qhi[:, :C].bitcast(I32), in0=qhi[:, :C].bitcast(I32),
                    scalar1=-0x00400000, scalar2=None, op0=ALU.bitwise_and)
    v.tensor_scalar(out=sabs[:, :C].bitcast(I32), in0=y2[:, :C].bitcast(I32),
                    scalar1=0x7FFFFFFF, scalar2=None, op0=ALU.bitwise_and)
    v.tensor_scalar(out=qmask[:, :C], in0=sabs[:, :C], scalar1=1.75, scalar2=None, op0=ALU.is_lt)
    v.select(out=qsel[:, :C], mask=qmask[:, :C], on_true=qlo[:, :C], on_false=qhi[:, :C])
    qv = qsel[:, :C].rearrange("p (n b) -> p n b", b=16)
    xv = dst_bf.rearrange("p (n b) -> p n b", b=16)
    return v.tensor_tensor(xv, qv, rsh[:, :nb].unsqueeze(2).broadcast_to((128, nb, 16)), ALU.mult)


def build_l0wq():
    p = P("l0wq")
    nc = p.nc
    wg = nc.dram_tensor("wg", [FFI, D], F32, kind="ExternalInput").ap()
    wu = nc.dram_tensor("wu", [FFI, D], F32, kind="ExternalInput").ap()
    wd = nc.dram_tensor("wd", [D, FFI], F32, kind="ExternalInput").ap()
    wgq = nc.dram_tensor("wgq", [8, 128, FFI], BF16, kind="ExternalOutput").ap()
    wuq = nc.dram_tensor("wuq", [8, 128, FFI], BF16, kind="ExternalOutput").ap()
    wdq = nc.dram_tensor("wdq", [22, 128, D], BF16, kind="ExternalOutput").ap()

    CMAX = FFI
    wt = [p.sbuf(f"wt{i}", [128, CMAX], F32) for i in range(2)]
    tmp = {k: p.sbuf(k, [128, CMAX], F32) for k in ("y2", "qlo", "qhi", "qsel", "sabs")}
    tmp["qmask"] = p.sbuf("qmask", [128, CMAX], I32)
    for k in ("bm", "rs2", "rsh"):
        tmp[k] = p.sbuf(k, [128, CMAX // 16], F32)
    qb = [p.sbuf(f"qb{i}", [128, CMAX], BF16) for i in range(2)]
    asm = p.sbuf("asm", [128, 22528], BF16)

    ld = p.new_sem("ld")
    vg = p.new_sem("vg")
    sv = p.new_sem("sv")
    tp = p.new_sem("tp")
    so = p.new_sem("so")

    p.begin()
    block = p.block

    jobs = [(wg, wgq, FFI, D), (wu, wuq, FFI, D), (wd, wdq, D, FFI)]
    for src, dst, R, C in jobs:
        nr = R // 128
        asm_v = asm[:].rearrange("p (k r) -> p k r", k=C // 128)
        for r in range(nr):
            buf = wt[r % 2]
            qbb = qb[r % 2]

            @block.gpsimd
            def _(g, src=src, r=r, buf=buf, C=C):
                p.wait(g, "sv")
                g.dma_start(out=buf[:, :C], in_=src[r * 128:(r + 1) * 128, :]).then_inc(p.sem["ld"], 16)
            p.cnt["ld"] += 16

            @block.vector
            def _(v, r=r, buf=buf, qbb=qbb, C=C):
                p.wait(v, "ld")
                p.wait(v, "tp")
                i2 = emit_quant(p, v, buf[:, :C], qbb[:, :C], tmp, C)
                p.inc(i2, "sv")

            @block.scalar
            def _(s, r=r, qbb=qbb, C=C):
                p.wait(s, "sv")
                for k in range(C // 128):
                    ii = s.dma_start_transpose(asm_v[:, k, r * 128:(r + 1) * 128],
                                               qbb[:, k * 128:(k + 1) * 128])
                    p.inc(ii, "tp", 16)

        @block.sync
        def _(sy, dst=dst, R=R, C=C):
            p.wait(sy, "tp")
            for k in range(C // 128):
                ii = sy.dma_start(out=dst[k], in_=asm[:].rearrange("p (k r) -> p k r", k=C // 128)[:, k, :R])
                p.inc(ii, "so", 16)
            p.wait(sy, "so")

    return p



def build_l0attn():
    p = P("l0attn")
    nc = p.nc
    xnfm = nc.dram_tensor("xnfm", [8, 128, T], BF16, kind="ExternalInput").ap()
    wqh = nc.dram_tensor("wqh", [D, D], F32, kind="ExternalInput").ap()
    wkh = nc.dram_tensor("wkh", [D, D], F32, kind="ExternalInput").ap()
    wvh = nc.dram_tensor("wvh", [D, D], F32, kind="ExternalInput").ap()
    aout = nc.dram_tensor("aout", [8, 128, TO], BF16, kind="ExternalOutput").ap()
    nmx_scr = nc.dram_tensor("nmx_scr", [16, 128], BF16, kind="Internal").ap()

    xn = p.sbuf("xn", [128, 8 * T], BF16)
    wq = p.sbuf("wq", [128, 8 * D], BF16)
    wk = p.sbuf("wk", [128, 8 * D], BF16)
    wv = p.sbuf("wv", [128, 8 * D], BF16)
    Kb = [p.sbuf(f"Kb{i}", [128, T], BF16) for i in range(2)]
    Qb = [p.sbuf(f"Qb{i}", [128, TO], BF16) for i in range(2)]
    Vb = p.sbuf("Vb", [128, 32 * 130], BF16)
    att = [p.sbuf(f"att{i}", [128, 1024], BF16) for i in range(2)]
    mx8 = p.sbuf("mx8", [128, 8], F32)
    nmx = p.sbuf("nmx", [128, 16], BF16)
    nmxt = p.sbuf("nmxt", [16, 128], BF16)
    ident = p.sbuf("ident", [128, 128], BF16)
    identi = p.sbuf("identi", [128, 128], I32)
    ones1 = p.sbuf("ones1", [1, 64], F32)
    r1 = p.sbuf("r1", [1, 1024], F32)
    r1b = p.sbuf("r1b", [1, 1024], BF16)
    bc_s = p.sbuf("bc_s", [64, 1024], F32)
    ao = p.sbuf("ao", [128, 8 * TO], BF16)

    psA = p.psum("psA", [128, 512])
    psB = p.psum("psB", [128, 512])
    psC = p.psum("psC", [128, 1024])
    psD = p.psum("psD", [128, 1024])
    psE = p.psum("psE", [128, 1024])

    ld = p.new_sem("ld")
    vg = p.new_sem("vg")
    mm = p.new_sem("mm")
    pv = p.new_sem("pv")
    sv = p.new_sem("sv")
    ss = p.new_sem("ss")
    gd = p.new_sem("gd")
    so = p.new_sem("so")

    p.begin()
    block = p.block
    xn_v = xn[:].rearrange("p (k t) -> p k t", k=8)
    wq_v = wq[:].rearrange("p (k t) -> p k t", k=8)
    wk_v = wk[:].rearrange("p (k t) -> p k t", k=8)
    wv_v = wv[:].rearrange("p (k t) -> p k t", k=8)
    Vb_v = Vb[:].rearrange("p (m c) -> p m c", c=130)
    ao_v = ao[:].rearrange("p (k t) -> p k t", k=8)
    NQT = TO // 128
    NQC = TO // 1024

    @block.gpsimd
    def _(g):
        for k in range(8):
            g.dma_start(out=xn_v[:, k, :], in_=xnfm[k]).then_inc(p.sem["ld"], 16)
            g.dma_start(out=wq_v[:, k, :], in_=wqh[k * 128:(k + 1) * 128, :]).then_inc(p.sem["ld"], 16)
            g.dma_start(out=wk_v[:, k, :], in_=wkh[k * 128:(k + 1) * 128, :]).then_inc(p.sem["ld"], 16)
            g.dma_start(out=wv_v[:, k, :], in_=wvh[k * 128:(k + 1) * 128, :]).then_inc(p.sem["ld"], 16)
        g.iota(identi[:], pattern=[[1, 128]], base=0, channel_multiplier=-1,
               allow_small_or_imprecise_dtypes=True)
        ii = g.tensor_scalar(out=ident[:], in0=identi[:], scalar1=0, scalar2=None, op0=ALU.is_equal)
        p.inc(ii, "gd")
    p.cnt["ld"] += 16 * 32

    @block.vector
    def _(v):
        v.memset(ones1[:], 1.0)
        v.memset(Vb_v[:, :, 64:65].squeeze(2), 1.0)
        ii = v.memset(Vb_v[:, :, 129:130].squeeze(2), 1.0)
        p.inc(ii, "sv")

    for hp in range(8):
        @block.tensor
        def _(t, hp=hp):
            p.wait(t, "ld")
            p.wait(t, "sv")
            for mt in range(32):
                dst = psC[:, 0:128] if mt % 2 == 0 else psC[:, 512:640]
                if mt >= 2:
                    p.wait(t, "sv", p.cnt["sv"] + mt - 1)
                for sub in range(8):
                    ii = t.matmul(dst, xn_v[:, sub, mt * 128:(mt + 1) * 128],
                                  wv_v[:, sub, hp * 128:(hp + 1) * 128],
                                  start=(sub == 0), stop=(sub == 7))
                p.inc(ii, "mm")

        @block.vector
        def _(v, hp=hp):
            base = p.cnt["mm"] - 32
            for mt in range(32):
                src = psC[:, 0:128] if mt % 2 == 0 else psC[:, 512:640]
                p.wait(v, "mm", base + mt + 1)
                v.tensor_copy(Vb_v[:, mt, 0:64], src[:, 0:64])
                ii = v.tensor_copy(Vb_v[:, mt, 65:129], src[:, 64:128])
                p.inc(ii, "sv")

        for hh in range(2):
            h = hp * 2 + hh
            K_h = Kb[hh]
            Q_h = Qb[hh]
            eoff = 65 * hh

            @block.vector
            def _(v, K_h=K_h):
                ii = v.memset(K_h[64:65, :], 1.0)
                p.inc(ii, "sv")

            @block.tensor
            def _(t, h=h):
                p.wait(t, "sv")
                p.wait(t, "ss")
                for mc in range(8):
                    ps = psA if mc % 2 == 0 else psB
                    if mc >= 2:
                        p.wait(t, "ss", p.cnt["ss"] + mc - 1)
                    for sub in range(8):
                        ii = t.matmul(ps[0:64, :], wk_v[:, sub, h * 64:(h + 1) * 64],
                                      xn_v[:, sub, mc * 512:(mc + 1) * 512],
                                      start=(sub == 0), stop=(sub == 7))
                    p.inc(ii, "mm")
                for qc in range(TO // 512):
                    ps = psA if qc % 2 == 0 else psB
                    p.wait(t, "ss", p.cnt["ss"] + 6 + qc + 1)
                    for sub in range(8):
                        ii = t.matmul(ps[0:64, :], wq_v[:, sub, h * 64:(h + 1) * 64],
                                      xn_v[:, sub, qc * 512:(qc + 1) * 512],
                                      start=(sub == 0), stop=(sub == 7))
                    p.inc(ii, "mm")

            @block.scalar
            def _(s, h=h, K_h=K_h, Q_h=Q_h):
                base = p.cnt["mm"] - 8 - TO // 512
                for mc in range(8):
                    ps = psA if mc % 2 == 0 else psB
                    p.wait(s, "mm", base + 1 + mc)
                    ii = s.copy(out=K_h[0:64, mc * 512:(mc + 1) * 512], in_=ps[0:64, :])
                    p.inc(ii, "ss")
                for qc in range(TO // 512):
                    ps = psA if qc % 2 == 0 else psB
                    p.wait(s, "mm", base + 9 + qc)
                    ii = s.activation(Q_h[0:64, qc * 512:(qc + 1) * 512], ps[0:64, :],
                                      AF.Copy, scale=0.125)
                    p.inc(ii, "ss")

            @block.tensor
            def _(t, h=h, K_h=K_h, Q_h=Q_h):
                p.wait(t, "ss")
                for qt in range(NQT):
                    for mc in range(8):
                        n = qt * 8 + mc
                        ps = psA if n % 2 == 0 else psB
                        if n >= 2:
                            p.wait(t, "sv", p.cnt["sv"] + n - 1)
                        ii = t.matmul(ps[:, :], Q_h[0:64, qt * 128:(qt + 1) * 128],
                                      K_h[0:64, mc * 512:(mc + 1) * 512], start=True, stop=True)
                        p.inc(ii, "mm")

            @block.vector
            def _(v, h=h):
                base = p.cnt["mm"] - 8 * NQT
                gbase = p.cnt["sv"]
                n = 0
                for qt in range(NQT):
                    for mc in range(8):
                        n += 1
                        ps = psA if (n - 1) % 2 == 0 else psB
                        p.wait(v, "mm", base + n)
                        ii = v.tensor_reduce(mx8[:, mc:mc + 1], ps[:, :], axis=AX.X, op=ALU.max)
                        p.inc(ii, "sv")
                        v.wait_ge(p.sem["sv"], p.cnt["sv"])
                    ii = p.guard(v, v.tensor_reduce(nmx[:, qt:qt + 1], mx8[:, :],
                                                    axis=AX.X, op=ALU.max, negate=True))

            @block.tensor
            def _(t, h=h):
                p.wait(t, "vg")
                p.wait(t, "gd", 1)
                psT = psA[0:16, 0:64].bitcast(BF16)
                ii = t.transpose(psT, nmx[:, 0:NQT], ident[:])
                p.inc(ii, "mm")

            @block.vector
            def _(v, h=h):
                p.wait(v, "mm")
                ii = v.tensor_copy(nmxt[0:NQT, :], psA[0:16, 0:64].bitcast(BF16))
                p.inc(ii, "sv")

            @block.gpsimd
            def _(g, h=h, Q_h=Q_h):
                p.wait(g, "sv")
                ii = g.dma_start(out=nmx_scr, in_=nmxt[0:NQT, :])
                p.inc(ii, "gd", 16)
                p.wait(g, "gd")
                ii = g.dma_start(out=Q_h[64:65, :], in_=nmx_scr.rearrange("a b -> (a b)").unsqueeze(0))
                p.inc(ii, "gd", 16)

            for qc in range(NQC):
                qk_base = p.cnt["mm"]
                ss_base = p.cnt["ss"]
                pv_base = p.cnt["pv"]
                sv_base = p.cnt["sv"]

                @block.tensor
                def _(t, h=h, qc=qc, K_h=K_h, Q_h=Q_h, eoff=eoff,
                      qk_base=qk_base, ss_base=ss_base, sv_base=sv_base):
                    if qc == 0:
                        p.wait(t, "gd")
                    else:
                        p.wait(t, "sv", sv_base)
                    def qk(mt):
                        psl = psC if mt % 2 == 0 else psD
                        for j in range(2):
                            ii = t.matmul(psl[:, j * 512:(j + 1) * 512],
                                          K_h[0:65, mt * 128:(mt + 1) * 128],
                                          Q_h[0:65, qc * 1024 + j * 512: qc * 1024 + (j + 1) * 512],
                                          start=True, stop=True)
                        p.inc(ii, "mm")
                    qk(0)
                    qk(1)
                    for mt in range(32):
                        p.wait(t, "ss", ss_base + mt + 1)
                        ab = att[mt % 2]
                        for j in range(2):
                            ii = t.matmul(psE[0:65, j * 512:(j + 1) * 512],
                                          Vb_v[:, mt, eoff:eoff + 65],
                                          ab[:, j * 512:(j + 1) * 512],
                                          start=(mt == 0), stop=(mt == 31))
                        p.inc(ii, "pv")
                        if mt + 2 < 32:
                            qk(mt + 2)

                @block.scalar
                def _(s, h=h, qc=qc, qk_base=qk_base, pv_base=pv_base):
                    for mt in range(32):
                        psl = psC if mt % 2 == 0 else psD
                        ab = att[mt % 2]
                        p.wait(s, "mm", qk_base + mt + 1)
                        if mt >= 2:
                            p.wait(s, "pv", pv_base + mt - 1)
                        ii = s.activation(ab[:, :], psl[:, :], AF.Exp)
                        p.inc(ii, "ss")

                @block.vector
                def _(v, h=h, qc=qc):
                    p.wait(v, "pv")
                    p.guard(v, v.reciprocal(r1[:, :], psE[64:65, 0:1024]))

                @block.tensor
                def _(t, h=h, qc=qc):
                    p.wait(t, "vg")
                    for j in range(2):
                        ii = t.matmul(psA[0:64, :] if j == 0 else psB[0:64, :],
                                      ones1[:, :], r1[:, j * 512:(j + 1) * 512],
                                      start=True, stop=True)
                    p.inc(ii, "mm")

                @block.vector
                def _(v, h=h, qc=qc):
                    p.wait(v, "mm")
                    v.tensor_copy(bc_s[:, 0:512], psA[0:64, :])
                    v.tensor_copy(bc_s[:, 512:1024], psB[0:64, :])
                    ii = v.tensor_tensor(
                        ao_v[:, h // 2, qc * 1024:(qc + 1) * 1024][64 * (h % 2):64 * (h % 2) + 64, :],
                        psE[0:64, :], bc_s[:, :], ALU.mult)
                    p.inc(ii, "sv")

    @block.sync
    def _(sy):
        p.wait(sy, "sv")
        for k in range(8):
            ii = sy.dma_start(out=aout[k], in_=ao_v[:, k, :])
            p.inc(ii, "so", 16)
        p.wait(sy, "so")

    return p



def build_l0ffn():
    p = P("l0ffn")
    nc = p.nc
    xqfm = nc.dram_tensor("xqfm", [8, 128, TO], BF16, kind="ExternalInput").ap()
    wgq = nc.dram_tensor("wgq", [8, 128, FFI], BF16, kind="ExternalInput").ap()
    wuq = nc.dram_tensor("wuq", [8, 128, FFI], BF16, kind="ExternalInput").ap()
    wdq = nc.dram_tensor("wdq", [22, 128, D], BF16, kind="ExternalInput").ap()
    gout = nc.dram_tensor("gout", [8, 128, TO], BF16, kind="ExternalOutput").ap()

    xq = p.sbuf("xq", [128, 8 * TO], BF16)
    wgb = [p.sbuf(f"wgb{i}", [128, 8 * 512], BF16) for i in range(2)]
    wub = [p.sbuf(f"wub{i}", [128, 8 * 512], BF16) for i in range(2)]
    sg = p.sbuf("sg", [128, 512], F32)
    hrow = p.sbuf("hrow", [128, 512], F32)
    tmp = {k: p.sbuf(k, [128, 512], F32) for k in ("y2", "qlo", "qhi", "sabs")}
    tmp["qsel"] = tmp["y2"]
    wdb = [wgb[0], wgb[1]]
    tmp["qmask"] = p.sbuf("qmask", [128, 512], I32)
    for k in ("bm", "rs2", "rsh"):
        tmp[k] = p.sbuf(k, [128, 32], F32)
    hqb = p.sbuf("hqb", [128, 512], BF16)
    hq = p.sbuf("hq", [128, 22 * TO], BF16)
    gb = p.sbuf("gb", [128, 8 * TO], BF16)

    psG = p.psum("psG", [128, 512])
    psU = p.psum("psU", [128, 512])
    psD_ = [p.psum(f"psD{i}", [128, 512]) for i in range(2)]

    ld = p.new_sem("ld")
    vg = p.new_sem("vg")
    mm = p.new_sem("mm")
    sv = p.new_sem("sv")
    ss = p.new_sem("ss")
    tp = p.new_sem("tp")
    wl = p.new_sem("wl")
    so = p.new_sem("so")

    p.begin()
    block = p.block
    xq_v = xq[:].rearrange("p (k t) -> p k t", k=8)
    hq_v = hq[:].rearrange("p (k t) -> p k t", k=22)
    gb_v = gb[:].rearrange("p (k t) -> p k t", k=8)
    NTT = TO // 128

    @block.gpsimd
    def _(g):
        for k in range(8):
            g.dma_start(out=xq_v[:, k, :], in_=xqfm[k]).then_inc(p.sem["ld"], 16)
    p.cnt["ld"] += 16 * 8

    FCH = [(i * 512, min(512, FFI - i * 512)) for i in range((FFI + 511) // 512)]

    for fi, (f0, fw) in enumerate(FCH):
        wg_c = wgb[fi % 2]
        wu_c = wub[fi % 2]

        @block.gpsimd
        def _(g, fi=fi, f0=f0, fw=fw, wg_c=wg_c, wu_c=wu_c):
            if fi >= 2:
                p.wait(g, "mm")
            for k in range(8):
                g.dma_start(out=wg_c[:].rearrange("p (k t) -> p k t", k=8)[:, k, 0:fw],
                            in_=wgq[k][:, f0:f0 + fw]).then_inc(p.sem["wl"], 16)
                g.dma_start(out=wu_c[:].rearrange("p (k t) -> p k t", k=8)[:, k, 0:fw],
                            in_=wuq[k][:, f0:f0 + fw]).then_inc(p.sem["wl"], 16)
        p.cnt["wl"] += 16 * 16

        for tt in range(NTT):
            @block.tensor
            def _(t, fi=fi, f0=f0, fw=fw, tt=tt, wg_c=wg_c, wu_c=wu_c):
                p.wait(t, "ld")
                p.wait(t, "wl")
                if fi + tt > 0:
                    p.wait(t, "sv")
                wgv = wg_c[:].rearrange("p (k t) -> p k t", k=8)
                wuv = wu_c[:].rearrange("p (k t) -> p k t", k=8)
                for sub in range(8):
                    ii = t.matmul(psG[:, 0:fw], xq_v[:, sub, tt * 128:(tt + 1) * 128],
                                  wgv[:, sub, 0:fw], start=(sub == 0), stop=(sub == 7))
                for sub in range(8):
                    ii = t.matmul(psU[:, 0:fw], xq_v[:, sub, tt * 128:(tt + 1) * 128],
                                  wuv[:, sub, 0:fw], start=(sub == 0), stop=(sub == 7))
                p.inc(ii, "mm")

            @block.scalar
            def _(s, fi=fi, fw=fw, tt=tt):
                p.wait(s, "mm")
                ii = s.activation(sg[:, 0:fw], psG[:, 0:fw], AF.Silu)
                p.inc(ii, "ss")

            @block.vector
            def _(v, fi=fi, f0=f0, fw=fw, tt=tt):
                p.wait(v, "ss")
                p.wait(v, "tp")
                i0 = v.tensor_tensor(hrow[:, 0:fw], sg[:, 0:fw], psU[:, 0:fw], ALU.mult)
                p.inc(i0, "sv")
                i1 = emit_quant(p, v, hrow[:, 0:fw], hqb[:, 0:fw], tmp, fw)
                p.inc(i1, "sv")

            @block.scalar
            def _(s, fi=fi, f0=f0, fw=fw, tt=tt):
                p.wait(s, "sv")
                for j in range(fw // 128):
                    ii = s.dma_start_transpose(
                        hq_v[:, (f0 + j * 128) // 128, tt * 128:(tt + 1) * 128],
                        hqb[:, j * 128:(j + 1) * 128])
                    p.inc(ii, "tp", 16)

    for dt in range(8):
        wd_c = wdb[dt % 2]

        @block.gpsimd
        def _(g, dt=dt, wd_c=wd_c):
            p.wait(g, "mm")
            for fs in range(22):
                g.dma_start(out=wd_c[:, 0:2816].rearrange("p (k t) -> p k t", k=22)[:, fs, :],
                            in_=wdq[fs][:, dt * 128:(dt + 1) * 128]).then_inc(p.sem["wl"], 16)
        p.cnt["wl"] += 16 * 22

        for tc in range(TO // 512):
            @block.tensor
            def _(t, dt=dt, tc=tc, wd_c=wd_c):
                p.wait(t, "wl")
                if dt == 0 and tc == 0:
                    p.wait(t, "tp")
                ps = psD_[tc % 2]
                if dt + tc > 0:
                    p.wait(t, "ss")
                wdv = wd_c[:, 0:2816].rearrange("p (k t) -> p k t", k=22)
                for fs in range(22):
                    ii = t.matmul(ps[:, :], wdv[:, fs, :],
                                  hq_v[:, fs, tc * 512:(tc + 1) * 512],
                                  start=(fs == 0), stop=(fs == 21))
                p.inc(ii, "mm")

            @block.scalar
            def _(s, dt=dt, tc=tc):
                p.wait(s, "mm")
                ps = psD_[tc % 2]
                ii = s.activation(gb_v[:, dt, tc * 512:(tc + 1) * 512], ps[:, :], AF.Silu)
                p.inc(ii, "ss")

    @block.sync
    def _(sy):
        p.wait(sy, "ss")
        for k in range(8):
            ii = sy.dma_start(out=gout[k], in_=gb_v[:, k, :])
            p.inc(ii, "so", 16)
        p.wait(sy, "so")

    return p



def build_l0comb():
    p = P("l0comb")
    nc = p.nc
    aoin = nc.dram_tensor("aoin", [8, 128, TO], BF16, kind="ExternalInput").ap()
    gin = nc.dram_tensor("gin", [8, 128, TO], BF16, kind="ExternalInput").ap()
    xfm = nc.dram_tensor("xfm", [8, 128, TO], F32, kind="ExternalInput").ap()
    wph = nc.dram_tensor("wph", [D, D], F32, kind="ExternalInput").ap()
    bph = nc.dram_tensor("bph", [128, 8], F32, kind="ExternalInput").ap()
    x1 = nc.dram_tensor("x1", [8, 128, TO], F32, kind="ExternalOutput").ap()

    ao = p.sbuf("ao", [128, 8 * TO], BF16)
    gbuf = p.sbuf("gbuf", [128, 8 * TO], BF16)
    wp = p.sbuf("wp", [128, 8 * D], BF16)
    bp = p.sbuf("bp", [128, 8], F32)
    xt = [p.sbuf(f"xt{i}", [128, 512], F32) for i in range(2)]
    tmps = p.sbuf("tmps", [128, 512], F32)
    x1t = [p.sbuf(f"x1t{i}", [128, 512], F32) for i in range(2)]

    psS = [p.psum(f"psS{i}", [128, 512]) for i in range(2)]

    ld = p.new_sem("ld")
    vg = p.new_sem("vg")
    mm = p.new_sem("mm")
    sv = p.new_sem("sv")
    xl = p.new_sem("xl")
    so = p.new_sem("so")

    p.begin()
    block = p.block
    ao_v = ao[:].rearrange("p (k t) -> p k t", k=8)
    gb_v = gbuf[:].rearrange("p (k t) -> p k t", k=8)
    wp_v = wp[:].rearrange("p (k t) -> p k t", k=8)

    @block.gpsimd
    def _(g):
        for k in range(8):
            g.dma_start(out=ao_v[:, k, :], in_=aoin[k]).then_inc(p.sem["ld"], 16)
            g.dma_start(out=gb_v[:, k, :], in_=gin[k]).then_inc(p.sem["ld"], 16)
            g.dma_start(out=wp_v[:, k, :], in_=wph[k * 128:(k + 1) * 128, :]).then_inc(p.sem["ld"], 16)
        g.dma_start(out=bp[:, 0:8], in_=bph).then_inc(p.sem["ld"], 16)
    p.cnt["ld"] += 16 * 25

    n = 0
    for ot in range(8):
        for tc in range(TO // 512):
            n += 1
            buf = n % 2

            @block.gpsimd
            def _(g, ot=ot, tc=tc, buf=buf, n=n):
                if n > 2:
                    p.wait(g, "so", 16 * (n - 2))
                g.dma_start(out=xt[buf][:, :], in_=xfm[ot][:, tc * 512:(tc + 1) * 512]).then_inc(p.sem["xl"], 16)
            p.cnt["xl"] += 16

            @block.tensor
            def _(t, ot=ot, tc=tc, buf=buf, n=n):
                p.wait(t, "ld")
                if n > 2:
                    p.wait(t, "sv", p.cnt["sv"] - 1)
                for sub in range(8):
                    ii = t.matmul(psS[buf][:, :], wp_v[:, sub, ot * 128:(ot + 1) * 128],
                                  ao_v[:, sub, tc * 512:(tc + 1) * 512],
                                  start=(sub == 0), stop=(sub == 7))
                p.inc(ii, "mm")

            @block.vector
            def _(v, ot=ot, tc=tc, buf=buf, n=n):
                p.wait(v, "mm")
                p.wait(v, "xl")
                v.tensor_scalar(out=tmps[:, :], in0=psS[buf][:, :],
                                scalar1=bp[:, ot:ot + 1], scalar2=None, op0=ALU.add)
                v.tensor_tensor(tmps[:, :], tmps[:, :], gb_v[:, ot, tc * 512:(tc + 1) * 512], ALU.mult)
                ii = v.tensor_tensor(x1t[buf][:, :], xt[buf][:, :], tmps[:, :], ALU.add)
                p.inc(ii, "sv")

            @block.sync
            def _(sy, ot=ot, tc=tc, buf=buf):
                p.wait(sy, "sv")
                ii = sy.dma_start(out=x1[ot][:, tc * 512:(tc + 1) * 512], in_=x1t[buf][:, :])
                p.inc(ii, "so", 16)

    @block.sync
    def _(sy):
        p.wait(sy, "so")

    return p



def build_l1ln():
    p = P("l1ln")
    nc = p.nc
    x1 = nc.dram_tensor("x1", [8, 128, TO], F32, kind="ExternalInput").ap()
    pnw = nc.dram_tensor("pnw", [128, 8], F32, kind="ExternalInput").ap()
    sacc = nc.dram_tensor("sacc", [8, 128, TO], F32, kind="ExternalOutput").ap()
    xn1 = nc.dram_tensor("xn1", [8, 128, TO], BF16, kind="ExternalOutput").ap()
    rstd2 = nc.dram_tensor("rstd2", [1, TO], F32, kind="ExternalOutput").ap()
    redge = nc.dram_tensor("redge", [8, 128, 3], F32, kind="ExternalOutput").ap()

    xb = p.sbuf("xb", [128, 8 * TO], F32)
    sa = xb
    xnb = p.sbuf("xnb", [128, 8 * TO], BF16)
    sqt = [p.sbuf(f"sqt{i}", [128, 512], F32) for i in range(2)]
    onesr = p.sbuf("onesr", [1, 128], F32)
    onesc = p.sbuf("onesc", [128, 1], F32)
    rows = p.sbuf("rows", [1, 4 * TO], F32)
    rows2 = p.sbuf("rows2", [1, 2 * TO], F32)
    mbc = p.sbuf("mbc", [128, TO], F32)
    rbc = p.sbuf("rbc", [128, TO], F32)
    pnwt = p.sbuf("pnwt", [128, 8], F32)
    ret = p.sbuf("ret", [128, 8 * 3], F32)

    psX = p.psum("psX", [1, 512])
    psQ = p.psum("psQ", [1, 512])
    psB1 = p.psum("psB1", [128, 512])
    psB2 = p.psum("psB2", [128, 512])

    for s in ("ld", "vg", "mm", "m2", "sv", "ss", "so"):
        p.new_sem(s)

    p.begin()
    block = p.block
    xb_v = xb[:].rearrange("p (k t) -> p k t", k=8)
    sa_v = sa[:].rearrange("p (k t) -> p k t", k=8)
    xn_v = xnb[:].rearrange("p (k t) -> p k t", k=8)
    ret_v = ret[:].rearrange("p (k t) -> p k t", k=3)
    NCH = TO // 512

    @block.gpsimd
    def _(g):
        for k in range(8):
            g.dma_start(out=xb_v[:, k, :], in_=x1[k]).then_inc(p.sem["ld"], 16)
        g.dma_start(out=pnwt[:], in_=pnw).then_inc(p.sem["ld"], 16)
    p.cnt["ld"] += 16 * 9

    @block.vector
    def _(v):
        v.memset(onesr[:], 1.0)
        ii = v.memset(onesc[:], 1.0)
        p.inc(ii, "sv")

    def stats_pass(src_v, rowbuf):
        for ch in range(NCH):
            @block.tensor
            def _(t, ch=ch):
                p.wait(t, "ld")
                p.wait(t, "sv")
                for sub in range(8):
                    ii = t.matmul(psX[:, :], onesc[:, :],
                                  src_v[:, sub, ch * 512:(ch + 1) * 512],
                                  start=(sub == 0), stop=(sub == 7))
                p.inc(ii, "mm")

            @block.scalar
            def _(s, ch=ch):
                p.wait(s, "ld")
                p.wait(s, "sv")
                base = p.cnt["m2"]
                for sub in range(8):
                    if sub >= 2:
                        p.wait(s, "m2", base + sub - 1)
                    ii = s.activation(sqt[sub % 2][:, :],
                                      src_v[:, sub, ch * 512:(ch + 1) * 512], AF.Square)
                    p.inc(ii, "ss")

            @block.tensor
            def _(t, ch=ch):
                p.wait(t, "sv")
                base = p.cnt["ss"] - 8
                for sub in range(8):
                    p.wait(t, "ss", base + sub + 1)
                    ii = t.matmul(psQ[:, :], onesc[:, :], sqt[sub % 2][:, :],
                                  start=(sub == 0), stop=(sub == 7))
                    p.inc(ii, "m2")

            @block.vector
            def _(v, ch=ch):
                p.wait(v, "mm")
                p.wait(v, "m2")
                v.tensor_copy(rowbuf[:, ch * 512:(ch + 1) * 512], psX[:, :])
                ii = v.tensor_copy(rowbuf[:, TO + ch * 512:TO + (ch + 1) * 512], psQ[:, :])
                p.inc(ii, "sv")

    def bcast2(rowap1, rowap2, dst1, dst2):
        for ch in range(NCH):
            @block.tensor
            def _(t, ch=ch):
                p.wait(t, "vg")
                if ch > 0:
                    p.wait(t, "sv")
                t.matmul(psB1[:, :], onesr[:, :], rowap1[:, ch * 512:(ch + 1) * 512],
                         start=True, stop=True)
                ii = t.matmul(psB2[:, :], onesr[:, :], rowap2[:, ch * 512:(ch + 1) * 512],
                              start=True, stop=True)
                p.inc(ii, "mm")

            @block.vector
            def _(v, ch=ch):
                p.wait(v, "mm")
                v.tensor_copy(dst1[:, ch * 512:(ch + 1) * 512], psB1[:, :])
                ii = v.tensor_copy(dst2[:, ch * 512:(ch + 1) * 512], psB2[:, :])
                p.inc(ii, "sv")

    stats_pass(xb_v, rows)

    @block.vector
    def _(v):
        p.wait(v, "sv")
        p.guard(v, v.tensor_scalar(out=rows[:, 2 * TO:3 * TO], in0=rows[:, 0:TO],
                                   scalar1=1.0 / D, scalar2=None, op0=ALU.mult))
        p.guard(v, v.tensor_scalar(out=rows[:, 3 * TO:4 * TO], in0=rows[:, TO:2 * TO],
                                   scalar1=1.0 / D, scalar2=None, op0=ALU.mult))
        p.guard(v, v.tensor_tensor(rows[:, 0:TO], rows[:, 2 * TO:3 * TO],
                                   rows[:, 2 * TO:3 * TO], ALU.mult))
        p.guard(v, v.tensor_tensor(rows[:, 3 * TO:4 * TO], rows[:, 3 * TO:4 * TO],
                                   rows[:, 0:TO], ALU.subtract))
        p.guard(v, v.tensor_scalar(out=rows[:, 3 * TO:4 * TO], in0=rows[:, 3 * TO:4 * TO],
                                   scalar1=1e-5, scalar2=None, op0=ALU.add))

    @block.scalar
    def _(s):
        p.wait(s, "vg")
        ii = s.activation(rows[:, 0:TO], rows[:, 3 * TO:4 * TO], AF.Sqrt)
        p.inc(ii, "ss")

    @block.vector
    def _(v):
        p.wait(v, "ss")
        p.guard(v, v.reciprocal(rows[:, 3 * TO:4 * TO], rows[:, 0:TO]))

    bcast2(rows[:, 2 * TO:3 * TO], rows[:, 3 * TO:4 * TO], mbc, rbc)

    for dt in range(8):
        @block.vector
        def _(v, dt=dt):
            p.wait(v, "sv")
            v.tensor_tensor(sa_v[:, dt, :], xb_v[:, dt, :], mbc[:, :], ALU.subtract)
            v.tensor_tensor(sa_v[:, dt, :], sa_v[:, dt, :], rbc[:, :], ALU.mult)
            ii = v.tensor_copy(xn_v[:, dt, :], sa_v[:, dt, :])
            p.inc(ii, "sv")

    stats_pass(sa_v, rows2)

    @block.vector
    def _(v):
        p.wait(v, "sv")
        p.guard(v, v.tensor_scalar(out=rows2[:, 0:TO], in0=rows2[:, TO:2 * TO],
                                   scalar1=1.0 / D, scalar2=1e-6,
                                   op0=ALU.mult, op1=ALU.add))

    @block.scalar
    def _(s):
        p.wait(s, "vg")
        ii = s.activation(rows2[:, TO:2 * TO], rows2[:, 0:TO], AF.Sqrt)
        p.inc(ii, "ss")

    @block.vector
    def _(v):
        p.wait(v, "ss")
        p.guard(v, v.reciprocal(rows2[:, 0:TO], rows2[:, TO:2 * TO]))

    @block.tensor
    def _(t):
        p.wait(t, "vg")
        p.wait(t, "sv")
        ii = t.matmul(psB1[:, 0:3], onesr[:, :], rows2[:, TO - 3:TO], start=True, stop=True)
        p.inc(ii, "mm")

    @block.vector
    def _(v):
        p.wait(v, "mm")
        p.guard(v, v.tensor_copy(mbc[:, 0:3], psB1[:, 0:3]))
        for dt in range(8):
            p.guard(v, v.tensor_scalar(out=ret[:, dt * 3:(dt + 1) * 3],
                                       in0=sa_v[:, dt, TO - 3:TO],
                                       scalar1=pnwt[:, dt:dt + 1], scalar2=None, op0=ALU.mult))
            p.guard(v, v.tensor_tensor(ret[:, dt * 3:(dt + 1) * 3], ret[:, dt * 3:(dt + 1) * 3],
                                       mbc[:, 0:3], ALU.mult))
        p.inc(v.tensor_copy(rows2[:, TO:TO + 1], rows2[:, 0:1]), "sv")

    @block.sync
    def _(sy):
        p.wait(sy, "sv")
        for k in range(8):
            ii = sy.dma_start(out=sacc[k], in_=sa_v[:, k, :])
            p.inc(ii, "so", 16)
            ii = sy.dma_start(out=xn1[k], in_=xn_v[:, k, :])
            p.inc(ii, "so", 16)
            ii = sy.dma_start(out=redge[k], in_=ret[:, k * 3:(k + 1) * 3])
            p.inc(ii, "so", 16)
        ii = sy.dma_start(out=rstd2, in_=rows2[:, 0:TO])
        p.inc(ii, "so", 16)
        p.wait(sy, "so")

    return p



def build_l1rec():
    p = P("l1rec")
    nc = p.nc
    sacc = nc.dram_tensor("sacc", [8, 128, TO], F32, kind="ExternalInput").ap()
    rstd2 = nc.dram_tensor("rstd2", [1, TO], F32, kind="ExternalInput").ap()
    halo = nc.dram_tensor("halo", [8, 128, 3], F32, kind="ExternalInput").ap()
    pnw = nc.dram_tensor("pnw", [128, 8], F32, kind="ExternalInput").ap()
    cw = nc.dram_tensor("cw", [128, 32], F32, kind="ExternalInput").ap()
    cb = nc.dram_tensor("cb", [128, 8], F32, kind="ExternalInput").ap()
    db = nc.dram_tensor("db", [128, 8], F32, kind="ExternalInput").ap()
    waT = nc.dram_tensor("waT", [D, D], F32, kind="ExternalInput").ap()
    wiT = nc.dram_tensor("wiT", [D, D], F32, kind="ExternalInput").ap()
    wvT = nc.dram_tensor("wvT", [D, D], F32, kind="ExternalInput").ap()
    recl = nc.dram_tensor("recl", [8, 128, TO], BF16, kind="ExternalOutput").ap()
    Fout = nc.dram_tensor("Fout", [8, 128, TO], BF16, kind="ExternalOutput").ap()
    sout = nc.dram_tensor("sout", [128, 8], F32, kind="ExternalOutput").ap()

    NCHK = TO // 64
    sa = p.sbuf("sa", [128, 8 * TO], F32)
    hb = p.sbuf("hb", [128, 8 * TO], BF16)
    rext = p.sbuf("rext", [128, TO + 16], F32)
    ac1 = p.sbuf("ac1", [128, TO], F32)
    ac2 = p.sbuf("ac2", [128, TO], F32)
    rbc2 = p.sbuf("rbc2", [128, TO], F32)
    onesr = p.sbuf("onesr", [1, 128], F32)
    mask = p.sbuf("mask", [128, TO], F32)
    zt = p.sbuf("zt", [128, NCHK], F32)
    states = p.sbuf("states", [128, NCHK], F32)
    si = p.sbuf("si", [128, NCHK], F32)
    cpd = p.sbuf("cpd", [128, NCHK], F32)
    cpds = p.sbuf("cpds", [128, NCHK], F32)
    sot = p.sbuf("sot", [128, 8], F32)
    smallw = {n: p.sbuf(n, [128, 8], F32) for n in ("pnwt", "cbt", "dbt")}
    cwt = p.sbuf("cwt", [128, 32], F32)
    wstr = [p.sbuf(f"wstr{i}", [128, 3 * 8 * 128], BF16) for i in range(2)]
    rtile = p.sbuf("rtile", [128, TO], BF16)
    ftile = p.sbuf("ftile", [128, TO], BF16)
    halot = p.sbuf("halot", [128, 24], F32)
    r2row = p.sbuf("r2row", [1, TO], F32)

    psA = p.psum("psA", [128, 512])
    psI = p.psum("psI", [128, 512])
    psV = p.psum("psV", [128, 512])
    psB1 = p.psum("psB1", [128, 512])

    for s in ("ld", "vg", "mm", "sv", "ss", "wl", "so"):
        p.new_sem(s)

    p.begin()
    block = p.block
    sa_v = sa[:].rearrange("p (k t) -> p k t", k=8)
    hb_v = hb[:].rearrange("p (k t) -> p k t", k=8)
    halv = halot[:].rearrange("p (k t) -> p k t", k=8)
    NCH = TO // 512

    @block.gpsimd
    def _(g):
        for k in range(8):
            g.dma_start(out=sa_v[:, k, :], in_=sacc[k]).then_inc(p.sem["ld"], 16)
            g.dma_start(out=halv[:, k, :], in_=halo[k]).then_inc(p.sem["ld"], 16)
        for nm, src_ in (("pnwt", pnw), ("cbt", cb), ("dbt", db)):
            g.dma_start(out=smallw[nm][:], in_=src_).then_inc(p.sem["ld"], 16)
        g.dma_start(out=cwt[:], in_=cw).then_inc(p.sem["ld"], 16)
        g.dma_start(out=r2row[:], in_=rstd2).then_inc(p.sem["ld"], 16)
    p.cnt["ld"] += 16 * 21

    @block.vector
    def _(v):
        v.memset(onesr[:], 1.0)
        v.memset(mask[:], 1.0)
        v.memset(mask[:].rearrange("p (c k) -> p c k", k=64)[:, :, 0:1].squeeze(2), 0.0)
        ii = v.memset(zt[:], 0.0)
        p.inc(ii, "sv")

    for ch in range(NCH):
        @block.tensor
        def _(t, ch=ch):
            p.wait(t, "ld")
            p.wait(t, "sv")
            ii = t.matmul(psB1[:, :], onesr[:, :], r2row[:, ch * 512:(ch + 1) * 512],
                          start=True, stop=True)
            p.inc(ii, "mm")

        @block.vector
        def _(v, ch=ch):
            p.wait(v, "mm")
            ii = v.tensor_copy(rbc2[:, ch * 512:(ch + 1) * 512], psB1[:, :])
            p.inc(ii, "sv")

    for dt in range(8):
        @block.vector
        def _(v, dt=dt):
            p.wait(v, "ld")
            p.wait(v, "sv")
            if dt > 0:
                p.wait(v, "so")
            v.tensor_copy(rext[:, 0:3], halv[:, dt, :])
            v.tensor_scalar(out=rext[:, 3:3 + TO], in0=sa_v[:, dt, :],
                            scalar1=smallw["pnwt"][:, dt:dt + 1], scalar2=None, op0=ALU.mult)
            v.tensor_tensor(rext[:, 3:3 + TO], rext[:, 3:3 + TO], rbc2[:, :], ALU.mult)
            v.tensor_scalar(out=ac1[:, :], in0=rext[:, 0:TO],
                            scalar1=cwt[:, dt * 4:dt * 4 + 1], scalar2=None, op0=ALU.mult)
            v.scalar_tensor_tensor(out=ac2[:, :], in0=rext[:, 1:1 + TO],
                                   scalar=cwt[:, dt * 4 + 1:dt * 4 + 2], in1=ac1[:, :],
                                   op0=ALU.mult, op1=ALU.add)
            v.scalar_tensor_tensor(out=ac1[:, :], in0=rext[:, 2:2 + TO],
                                   scalar=cwt[:, dt * 4 + 2:dt * 4 + 3], in1=ac2[:, :],
                                   op0=ALU.mult, op1=ALU.add)
            v.scalar_tensor_tensor(out=ac2[:, :], in0=rext[:, 3:3 + TO],
                                   scalar=cwt[:, dt * 4 + 3:dt * 4 + 4], in1=ac1[:, :],
                                   op0=ALU.mult, op1=ALU.add)
            ii = v.tensor_scalar(out=hb_v[:, dt, :], in0=ac2[:, :],
                                 scalar1=smallw["cbt"][:, dt:dt + 1], scalar2=None, op0=ALU.add)
            p.inc(ii, "sv")

    A = [sa_v[:, k, :] for k in range(8)]
    c3 = lambda ap: ap.rearrange("p (c k) -> p c k", k=64)

    for ot in range(8):
        wbuf = wstr[ot % 2]
        wv3 = wbuf[:].rearrange("p (w k t) -> p w k t", w=3, k=8)

        @block.gpsimd
        def _(g, ot=ot, wbuf=wbuf, wv3=wv3):
            if ot < 2:
                p.wait(g, "sv")
            else:
                p.wait(g, "mm")
            for wi, wsrc in enumerate((waT, wiT, wvT)):
                for sub in range(8):
                    g.dma_start(out=wv3[:, wi, sub, :],
                                in_=wsrc[sub * 128:(sub + 1) * 128, ot * 128:(ot + 1) * 128]
                                ).then_inc(p.sem["wl"], 16)
        p.cnt["wl"] += 16 * 24

        @block.tensor
        def _(t, ot=ot, wv3=wv3):
            p.wait(t, "wl")
            p.wait(t, "sv")
            for ch in range(NCH):
                if ch > 0:
                    p.wait(t, "ss", p.cnt["ss"] + ch * 2)
                    p.wait(t, "sv", p.cnt["sv"] + ch)
                for sub in range(8):
                    t.matmul(psA[:, :], wv3[:, 0, sub, :], hb_v[:, sub, ch * 512:(ch + 1) * 512],
                             start=(sub == 0), stop=(sub == 7))
                for sub in range(8):
                    t.matmul(psI[:, :], wv3[:, 1, sub, :], hb_v[:, sub, ch * 512:(ch + 1) * 512],
                             start=(sub == 0), stop=(sub == 7))
                for sub in range(8):
                    ii = t.matmul(psV[:, :], wv3[:, 2, sub, :], hb_v[:, sub, ch * 512:(ch + 1) * 512],
                                  start=(sub == 0), stop=(sub == 7))
                p.inc(ii, "mm")

        @block.scalar
        def _(s, ot=ot):
            base = p.cnt["mm"] - NCH
            for ch in range(NCH):
                p.wait(s, "mm", base + ch + 1)
                s.activation(A[0][:, ch * 512:(ch + 1) * 512], psA[:, :], AF.Sigmoid,
                             bias=smallw["dbt"][:, ot:ot + 1], scale=1.0)
                ii = s.activation(A[1][:, ch * 512:(ch + 1) * 512], psI[:, :], AF.Sigmoid)
                p.inc(ii, "ss", 2)

        @block.vector
        def _(v, ot=ot):
            base = p.cnt["mm"] - NCH
            for ch in range(NCH):
                p.wait(v, "mm", base + ch + 1)
                ii = v.tensor_copy(A[2][:, ch * 512:(ch + 1) * 512], psV[:, :])
                p.inc(ii, "sv")

        @block.vector
        def _(v, ot=ot):
            p.wait(v, "ss")
            ii = v.tensor_scalar(out=A[3][:, :], in0=A[0][:, :], scalar1=1e-10,
                                 scalar2=None, op0=ALU.max)
            p.inc(ii, "sv")

        @block.scalar
        def _(s, ot=ot):
            p.wait(s, "sv")
            ii = s.activation(A[4][:, :], A[3][:, :], AF.Ln)
            p.inc(ii, "ss")

        @block.vector
        def _(v, ot=ot):
            p.wait(v, "ss")
            ii = v.tensor_tensor_scan(out=A[3][:, :], data0=mask[:, :], data1=A[4][:, :],
                                      initial=0.0, op0=ALU.mult, op1=ALU.add)
            p.inc(ii, "sv")

        @block.scalar
        def _(s, ot=ot):
            p.wait(s, "sv")
            ii = s.activation(A[4][:, :], A[3][:, :], AF.Exp)
            p.inc(ii, "ss")

        @block.vector
        def _(v, ot=ot):
            p.wait(v, "ss")
            v.tensor_scalar(out=A[3][:, :], in0=A[4][:, :], scalar1=1e-10,
                            scalar2=None, op0=ALU.max)
            v.reciprocal(A[5][:, :], A[3][:, :])
            v.tensor_tensor(A[3][:, :], A[0][:, :], A[0][:, :], ALU.mult)
            v.tensor_scalar(out=A[3][:, :], in0=A[3][:, :], scalar1=-1.0, scalar2=1.0,
                            op0=ALU.mult, op1=ALU.add)
            ii = v.tensor_scalar(out=A[3][:, :], in0=A[3][:, :], scalar1=1e-8,
                                 scalar2=None, op0=ALU.max)
            p.inc(ii, "sv")

        @block.scalar
        def _(s, ot=ot):
            p.wait(s, "sv")
            ii = s.activation(A[6][:, :], A[3][:, :], AF.Sqrt)
            p.inc(ii, "ss")

        @block.vector
        def _(v, ot=ot):
            p.wait(v, "ss")
            v.tensor_tensor(A[0][:, :], A[6][:, :], A[1][:, :], ALU.mult)
            v.tensor_tensor(A[0][:, :], A[0][:, :], A[2][:, :], ALU.mult)
            v.tensor_tensor(A[0][:, :], A[0][:, :], A[5][:, :], ALU.mult)
            v.tensor_tensor_scan(out=A[1][:, :], data0=mask[:, :], data1=A[0][:, :],
                                 initial=0.0, op0=ALU.mult, op1=ALU.add)
            v.tensor_tensor(A[2][:, :], A[4][:, :], A[1][:, :], ALU.mult)
            p.guard(v, v.tensor_tensor_scan(out=states[:, :], data0=c3(A[4])[:, :, 63],
                                            data1=c3(A[2])[:, :, 63], initial=0.0,
                                            op0=ALU.mult, op1=ALU.add))
            p.guard(v, v.memset(si[:, 0:1], 0.0))
            p.guard(v, v.tensor_copy(si[:, 1:NCHK], states[:, 0:NCHK - 1]))
            p.guard(v, v.tensor_tensor_scan(out=cpd[:, :], data0=c3(A[4])[:, :, 63],
                                            data1=zt[:, :], initial=1.0,
                                            op0=ALU.mult, op1=ALU.add))
            p.guard(v, v.memset(cpds[:, 0:1], 1.0))
            p.guard(v, v.tensor_copy(cpds[:, 1:NCHK], cpd[:, 0:NCHK - 1]))
            p.guard(v, v.tensor_copy(sot[:, ot:ot + 1], states[:, NCHK - 1:NCHK]))
            v.tensor_tensor(c3(A[0]), c3(A[4]),
                            si[:, :].unsqueeze(2).broadcast_to((128, NCHK, 64)), ALU.mult)
            if ot >= 1:
                p.wait(v, "so", 32 * ot)
            v.tensor_tensor(rtile[:, :], A[2][:, :], A[0][:, :], ALU.add)
            ii = v.tensor_tensor(c3(ftile), c3(A[4]),
                                 cpds[:, :].unsqueeze(2).broadcast_to((128, NCHK, 64)), ALU.mult)
            p.inc(ii, "sv")

        @block.gpsimd
        def _(g, ot=ot):
            p.wait(g, "sv")
            g.dma_start(out=recl[ot], in_=rtile[:, :]).then_inc(p.sem["so"], 16)
            g.dma_start(out=Fout[ot], in_=ftile[:, :]).then_inc(p.sem["so"], 16)
        p.cnt["so"] += 32

    @block.gpsimd
    def _(g):
        p.wait(g, "vg")
        g.dma_start(out=sout, in_=sot[:, :]).then_inc(p.sem["so"], 16)
        p.cnt["so"] += 16
        p.wait(g, "so")

    return p



def build_l1fix():
    p = P("l1fix")
    nc = p.nc
    sacc = nc.dram_tensor("sacc", [8, 128, TO], F32, kind="ExternalInput").ap()
    recl = nc.dram_tensor("recl", [8, 128, TO], BF16, kind="ExternalInput").ap()
    Fin = nc.dram_tensor("Fin", [8, 128, TO], BF16, kind="ExternalInput").ap()
    sin = nc.dram_tensor("sin", [128, 8], F32, kind="ExternalInput").ap()
    opT = nc.dram_tensor("opT", [D, D], F32, kind="ExternalInput").ap()
    fnw = nc.dram_tensor("fnw", [128, 8], F32, kind="ExternalInput").ap()
    wguT = nc.dram_tensor("wguT", [D, 2 * 4096], F32, kind="ExternalInput").ap()
    wdT = nc.dram_tensor("wdT", [4096, D], F32, kind="ExternalInput").ap()
    sfin = nc.dram_tensor("sfin", [8, 128, TO], F32, kind="ExternalOutput").ap()

    sa = p.sbuf("sa", [128, 8 * TO], F32)
    rb = p.sbuf("rb", [128, 8 * TO], BF16)
    hn = p.sbuf("hn", [128, 8 * TO], BF16)
    h2 = p.sbuf("h2", [128, 32 * 512], BF16)
    ft_in = p.sbuf("ft_in", [128, TO], BF16)
    op_sb = None
    sint = p.sbuf("sint", [128, 8], F32)
    fnwt = p.sbuf("fnwt", [128, 8], F32)
    sgt = p.sbuf("sgt", [128, 512], F32)
    sqt = [p.sbuf(f"sqt{i}", [128, 512], F32) for i in range(2)]
    onesc = p.sbuf("onesc", [128, 1], F32)
    onesr = p.sbuf("onesr", [1, 128], F32)
    rows = p.sbuf("rows", [1, TO], F32)
    rbc3 = p.sbuf("rbc3", [128, TO], F32)
    wgu = [p.sbuf(f"wgu{i}", [128, 8 * 256], BF16) for i in range(2)]
    wdd0 = p.sbuf("wdd0", [128, 32 * 128], BF16)
    wdd = [wdd0, wdd0]

    psP = [p.psum(f"psP{i}", [128, 512]) for i in range(2)]
    psG = p.psum("psG", [128, 512])
    psU = p.psum("psU", [128, 512])
    psQ = p.psum("psQ", [1, 512])
    psB = p.psum("psB", [128, 512])

    for s in ("ld", "vg", "mm", "m2", "sv", "ss", "wl", "so"):
        p.new_sem(s)

    p.begin()
    block = p.block
    sa_v = sa[:].rearrange("p (k t) -> p k t", k=8)
    rb_v = rb[:].rearrange("p (k t) -> p k t", k=8)
    hn_v = hn[:].rearrange("p (k t) -> p k t", k=8)
    h2_v = h2[:].rearrange("p (k t) -> p k t", k=32)
    op_v = h2[:, 0:8 * D].rearrange("p (k t) -> p k t", k=8)
    NCH = TO // 512

    @block.gpsimd
    def _(g):
        for k in range(8):
            g.dma_start(out=sa_v[:, k, :], in_=sacc[k]).then_inc(p.sem["ld"], 16)
            g.dma_start(out=op_v[:, k, :], in_=opT[k * 128:(k + 1) * 128, :]).then_inc(p.sem["ld"], 16)
        g.dma_start(out=sint[:], in_=sin).then_inc(p.sem["ld"], 16)
        g.dma_start(out=fnwt[:], in_=fnw).then_inc(p.sem["ld"], 16)
    p.cnt["ld"] += 16 * 18

    @block.vector
    def _(v):
        v.memset(onesc[:], 1.0)
        ii = v.memset(onesr[:], 1.0)
        p.inc(ii, "sv")

    for dt in range(8):
        @block.gpsimd
        def _(g, dt=dt):
            if dt >= 1:
                p.wait(g, "sv")
            g.dma_start(out=ft_in[:, :], in_=Fin[dt]).then_inc(p.sem["ld"], 16)
            g.dma_start(out=rb_v[:, dt, :], in_=recl[dt]).then_inc(p.sem["ld"], 16)
        p.cnt["ld"] += 32

        @block.vector
        def _(v, dt=dt):
            p.wait(v, "ld")
            ii = v.scalar_tensor_tensor(out=rb_v[:, dt, :], in0=ft_in[:, :],
                                        scalar=sint[:, dt:dt + 1], in1=rb_v[:, dt, :],
                                        op0=ALU.mult, op1=ALU.add)
            p.inc(ii, "sv")

    for ot in range(8):
        for ch in range(NCH):
            n = ot * NCH + ch

            @block.tensor
            def _(t, ot=ot, ch=ch, n=n):
                p.wait(t, "ld")
                p.wait(t, "sv")
                ps = psP[n % 2]
                for sub in range(8):
                    ii = t.matmul(ps[:, :], op_v[:, sub, ot * 128:(ot + 1) * 128],
                                  rb_v[:, sub, ch * 512:(ch + 1) * 512],
                                  start=(sub == 0), stop=(sub == 7))
                p.inc(ii, "mm")

            @block.vector
            def _(v, ot=ot, ch=ch, n=n):
                p.wait(v, "mm")
                ps = psP[n % 2]
                ii = v.tensor_tensor(sa_v[:, ot, ch * 512:(ch + 1) * 512],
                                     sa_v[:, ot, ch * 512:(ch + 1) * 512], ps[:, :], ALU.add)
                p.inc(ii, "sv")

    for ch in range(NCH):
        @block.scalar
        def _(s, ch=ch):
            p.wait(s, "sv")
            base = p.cnt["m2"]
            for sub in range(8):
                if sub >= 2:
                    p.wait(s, "m2", base + sub - 1)
                ii = s.activation(sqt[sub % 2][:, :],
                                  sa_v[:, sub, ch * 512:(ch + 1) * 512], AF.Square)
                p.inc(ii, "ss")

        @block.tensor
        def _(t, ch=ch):
            p.wait(t, "sv")
            base = p.cnt["ss"] - 8
            for sub in range(8):
                p.wait(t, "ss", base + sub + 1)
                ii = t.matmul(psQ[:, :], onesc[:, :], sqt[sub % 2][:, :],
                              start=(sub == 0), stop=(sub == 7))
                p.inc(ii, "m2")

        @block.vector
        def _(v, ch=ch):
            p.wait(v, "m2")
            ii = v.tensor_copy(rows[:, ch * 512:(ch + 1) * 512], psQ[:, :])
            p.inc(ii, "sv")

    @block.vector
    def _(v):
        p.wait(v, "sv")
        p.guard(v, v.tensor_scalar(out=rows[:, 0:TO], in0=rows[:, 0:TO],
                                   scalar1=1.0 / D, scalar2=1e-6,
                                   op0=ALU.mult, op1=ALU.add))

    @block.scalar
    def _(s):
        p.wait(s, "vg")
        ii = s.activation(rbc3[0:1, 0:TO], rows[:, 0:TO], AF.Sqrt)
        p.inc(ii, "ss")

    @block.vector
    def _(v):
        p.wait(v, "ss")
        p.guard(v, v.reciprocal(rows[:, 0:TO], rbc3[0:1, 0:TO]))

    for ch in range(NCH):
        @block.tensor
        def _(t, ch=ch):
            p.wait(t, "vg")
            if ch > 0:
                p.wait(t, "sv")
            ii = t.matmul(psB[:, :], onesr[:, :], rows[:, ch * 512:(ch + 1) * 512],
                          start=True, stop=True)
            p.inc(ii, "mm")

        @block.vector
        def _(v, ch=ch):
            p.wait(v, "mm")
            ii = v.tensor_copy(rbc3[:, ch * 512:(ch + 1) * 512], psB[:, :])
            p.inc(ii, "sv")

    for dt in range(8):
        @block.vector
        def _(v, dt=dt):
            p.wait(v, "sv")
            ii = v.tensor_scalar(out=hn_v[:, dt, :], in0=sa_v[:, dt, :],
                                 scalar1=fnwt[:, dt:dt + 1], scalar2=rbc3_slice(rbc3, dt),
                                 op0=ALU.mult, op1=ALU.mult) if False else None
            v.tensor_scalar(out=hn_v[:, dt, :], in0=sa_v[:, dt, :],
                            scalar1=fnwt[:, dt:dt + 1], scalar2=None, op0=ALU.mult)
            ii = v.tensor_tensor(hn_v[:, dt, :], hn_v[:, dt, :], rbc3[:, :], ALU.mult)
            p.inc(ii, "sv")

    for tc in range(NCH):
        for ft in range(32):
            wb = wgu[ft % 2]

            @block.gpsimd
            def _(g, tc=tc, ft=ft, wb=wb):
                p.wait(g, "mm")
                wv = wb[:].rearrange("p (k t) -> p k t", k=8)
                for sub in range(8):
                    g.dma_start(out=wv[:, sub, 0:128],
                                in_=wguT[sub * 128:(sub + 1) * 128, ft * 128:(ft + 1) * 128]
                                ).then_inc(p.sem["wl"], 16)
                    g.dma_start(out=wv[:, sub, 128:256],
                                in_=wguT[sub * 128:(sub + 1) * 128, 4096 + ft * 128:4096 + (ft + 1) * 128]
                                ).then_inc(p.sem["wl"], 16)
            p.cnt["wl"] += 16 * 16

            @block.tensor
            def _(t, tc=tc, ft=ft, wb=wb):
                p.wait(t, "wl")
                p.wait(t, "sv")
                wv = wb[:].rearrange("p (k t) -> p k t", k=8)
                for sub in range(8):
                    t.matmul(psG[:, :], wv[:, sub, 0:128].transpose([0, 1]) if False else wv[:, sub, 0:128],
                             hn_v[:, sub, tc * 512:(tc + 1) * 512],
                             start=(sub == 0), stop=(sub == 7))
                for sub in range(8):
                    ii = t.matmul(psU[:, :], wv[:, sub, 128:256],
                                  hn_v[:, sub, tc * 512:(tc + 1) * 512],
                                  start=(sub == 0), stop=(sub == 7))
                p.inc(ii, "mm")

            @block.scalar
            def _(s, tc=tc, ft=ft):
                p.wait(s, "mm")
                ii = s.activation(sgt[:, :], psG[:, :], AF.Silu)
                p.inc(ii, "ss")

            @block.vector
            def _(v, tc=tc, ft=ft):
                p.wait(v, "ss")
                ii = v.tensor_tensor(h2_v[:, ft, :], sgt[:, :], psU[:, :], ALU.mult)
                p.inc(ii, "sv")

        for dt in range(8):
            wd_c = wdd[dt % 2]

            @block.gpsimd
            def _(g, tc=tc, dt=dt, wd_c=wd_c):
                p.wait(g, "mm")
                wv = wd_c[:].rearrange("p (k t) -> p k t", k=32)
                for fs in range(32):
                    g.dma_start(out=wv[:, fs, :],
                                in_=wdT[fs * 128:(fs + 1) * 128, dt * 128:(dt + 1) * 128]
                                ).then_inc(p.sem["wl"], 16)
            p.cnt["wl"] += 16 * 32

            @block.tensor
            def _(t, tc=tc, dt=dt, wd_c=wd_c):
                p.wait(t, "wl")
                p.wait(t, "sv")
                ps = psP[dt % 2]
                wv = wd_c[:].rearrange("p (k t) -> p k t", k=32)
                for fs in range(32):
                    ii = t.matmul(ps[:, :], wv[:, fs, :], h2_v[:, fs, :],
                                  start=(fs == 0), stop=(fs == 31))
                p.inc(ii, "mm")

            @block.vector
            def _(v, tc=tc, dt=dt):
                p.wait(v, "mm")
                ps = psP[dt % 2]
                ii = v.tensor_tensor(sa_v[:, dt, tc * 512:(tc + 1) * 512],
                                     sa_v[:, dt, tc * 512:(tc + 1) * 512], ps[:, :], ALU.add)
                p.inc(ii, "sv")

    @block.sync
    def _(sy):
        p.wait(sy, "sv")
        for k in range(8):
            ii = sy.dma_start(out=sfin[k], in_=sa_v[:, k, :])
            p.inc(ii, "so", 16)
        p.wait(sy, "so")

    return p



def build_l1moe():
    p = P("l1moe")
    nc = p.nc
    xn1 = nc.dram_tensor("xn1", [8, 128, TO], BF16, kind="ExternalInput").ap()
    sfin = nc.dram_tensor("sfin", [8, 128, TO], F32, kind="ExternalInput").ap()
    x1 = nc.dram_tensor("x1", [8, 128, TO], F32, kind="ExternalInput").ap()
    wrg = nc.dram_tensor("wrg", [D, 16], F32, kind="ExternalInput").ap()
    mwg = nc.dram_tensor("mwg", [8, D, 4096], F32, kind="ExternalInput").ap()
    mwu = nc.dram_tensor("mwu", [8, D, 4096], F32, kind="ExternalInput").ap()
    mwd = nc.dram_tensor("mwd", [8, 4096, D], F32, kind="ExternalInput").ap()
    out = nc.dram_tensor("out", [8, 128, TO], F32, kind="ExternalOutput").ap()

    xn = p.sbuf("xn", [128, 8 * TO], BF16)
    macc = p.sbuf("macc", [128, 8 * TO], F32)
    h2 = p.sbuf("h2", [128, 32 * 512], BF16)
    wrgt = p.sbuf("wrgt", [128, 8 * 16], BF16)
    cwall = p.sbuf("cwall", [128, 16 * 8], F32)
    st8 = p.sbuf("st8", [128, 8], F32)
    cwfm = p.sbuf("cwfm", [8, TO], F32)
    cwrow = p.sbuf("cwrow", [1, TO], F32)
    cwb = p.sbuf("cwb", [128, TO], BF16)
    onesr = p.sbuf("onesr", [1, 128], F32)
    ident = p.sbuf("ident", [128, 128], F32)
    identi = p.sbuf("identi", [128, 128], I32)
    sgt = p.sbuf("sgt", [128, 512], F32)
    ub = p.sbuf("ub", [128, 512], BF16)
    wg_s = [p.sbuf(f"wg_s{i}", [128, 8 * 128], BF16) for i in range(2)]
    wu_s = [p.sbuf(f"wu_s{i}", [128, 8 * 128], BF16) for i in range(2)]
    wd_s = [p.sbuf(f"wd_s{i}", [128, 32 * 128], BF16) for i in range(2)]
    s_t = p.sbuf("s_t", [128, 512], F32)
    x_t = p.sbuf("x_t", [128, 512], F32)
    o_t = [p.sbuf(f"o_t{i}", [128, 512], F32) for i in range(2)]

    psG = p.psum("psG", [128, 512])
    psU = p.psum("psU", [128, 512])
    psD_ = [p.psum(f"psD{i}", [128, 512]) for i in range(2)]
    psR = p.psum("psR", [128, 16])
    psB = p.psum("psB", [128, 512])

    for s in ("ld", "vg", "mm", "sv", "ss", "wl", "gd", "so"):
        p.new_sem(s)

    p.begin()
    block = p.block
    xn_v = xn[:].rearrange("p (k t) -> p k t", k=8)
    ma_v = macc[:].rearrange("p (k t) -> p k t", k=8)
    h2_v = h2[:].rearrange("p (k t) -> p k t", k=32)
    wrg_v = wrgt[:].rearrange("p (k t) -> p k t", k=8)
    cw_v = cwall[:].rearrange("p (k t) -> p k t", k=16)
    NCH = TO // 512

    @block.gpsimd
    def _(g):
        for k in range(8):
            g.dma_start(out=xn_v[:, k, :], in_=xn1[k]).then_inc(p.sem["ld"], 16)
            g.dma_start(out=wrg_v[:, k, :], in_=wrg[k * 128:(k + 1) * 128, :]).then_inc(p.sem["ld"], 16)
        g.iota(identi[:], pattern=[[1, 128]], base=0, channel_multiplier=-1,
               allow_small_or_imprecise_dtypes=True)
        ii = g.tensor_scalar(out=ident[:], in0=identi[:], scalar1=0, scalar2=None, op0=ALU.is_equal)
        p.inc(ii, "gd")
    p.cnt["ld"] += 16 * 16

    @block.vector
    def _(v):
        ii = v.memset(onesr[:], 1.0)
        p.inc(ii, "sv")

    for tt in range(16):
        @block.tensor
        def _(t, tt=tt):
            p.wait(t, "ld")
            if tt > 0:
                p.wait(t, "ss")
            for sub in range(8):
                ii = t.matmul(psR[:, :], xn_v[:, sub, tt * 128:(tt + 1) * 128],
                              wrg_v[:, sub, :], start=(sub == 0), stop=(sub == 7))
            p.inc(ii, "mm")

        @block.vector
        def _(v, tt=tt):
            p.wait(v, "mm")
            p.guard(v, v.tensor_reduce(st8[:, 0:1], psR[:, 0:8], axis=AX.X, op=ALU.max, negate=True))
            p.guard(v, v.tensor_reduce(st8[:, 1:2], psR[:, 8:16], axis=AX.X, op=ALU.max, negate=True))
            p.guard(v, v.tensor_scalar(out=st8[:, 2:3], in0=st8[:, 1:2], scalar1=10.0,
                                       scalar2=None, op0=ALU.mult))

        @block.scalar
        def _(s, tt=tt):
            p.wait(s, "vg")
            s.activation(cw_v[:, tt, 0:8], psR[:, 0:8], AF.Exp, bias=st8[:, 0:1],
                         scale=1.0, accum_out=st8[:, 3:4])
            ii = s.activation(cw_v[:, tt, 8:16] if False else st8[:, 8:8], psR[:, 8:16], AF.Exp,
                              bias=st8[:, 2:3], scale=10.0,
                              accum_out=st8[:, 4:5]) if False else None
            ii = s.activation(cwall[:, tt * 8:(tt + 1) * 8].bitcast(F32) if False else cw_v[:, tt, 0:8],
                              psR[:, 0:8], AF.Exp, bias=st8[:, 0:1], scale=1.0,
                              accum_out=st8[:, 3:4]) if False else None
            p.inc(s.activation(sgt[:, 0:8], psR[:, 8:16], AF.Exp, bias=st8[:, 2:3],
                               scale=10.0, accum_out=st8[:, 4:5]), "ss")

        @block.vector
        def _(v, tt=tt):
            p.wait(v, "ss")
            p.guard(v, v.reciprocal(st8[:, 5:6], st8[:, 3:4]))
            p.guard(v, v.reciprocal(st8[:, 6:7], st8[:, 4:5]))
            p.guard(v, v.tensor_scalar(out=cw_v[:, tt, :], in0=cw_v[:, tt, :],
                                       scalar1=st8[:, 5:6], scalar2=None, op0=ALU.mult))
            p.guard(v, v.tensor_scalar(out=sgt[:, 0:8], in0=sgt[:, 0:8],
                                       scalar1=st8[:, 6:7], scalar2=None, op0=ALU.mult))
            p.guard(v, v.tensor_tensor(cw_v[:, tt, :], cw_v[:, tt, :], sgt[:, 0:8], ALU.mult))
            p.guard(v, v.tensor_reduce(st8[:, 7:8], cw_v[:, tt, :], axis=AX.X, op=ALU.add))
            p.guard(v, v.tensor_scalar(out=st8[:, 7:8], in0=st8[:, 7:8], scalar1=1e-9,
                                       scalar2=None, op0=ALU.add))
            p.guard(v, v.reciprocal(st8[:, 7:8], st8[:, 7:8]))
            p.guard(v, v.tensor_scalar(out=cw_v[:, tt, :], in0=cw_v[:, tt, :],
                                        scalar1=st8[:, 7:8], scalar2=None, op0=ALU.mult))
            ii = v.tensor_copy(st8[:, 0:1], st8[:, 0:1])
            p.inc(ii, "sv")

    for tt in range(16):
        @block.tensor
        def _(t, tt=tt):
            p.wait(t, "sv")
            p.wait(t, "gd", 1)
            if tt > 0:
                p.wait(t, "sv", p.cnt["sv"])
            ii = t.transpose(psB[0:8, 0:32].bitcast(F32) if False else psB[0:8, 0:128],
                             cw_v[:, tt, :], ident[:])
            p.inc(ii, "mm")

        @block.vector
        def _(v, tt=tt):
            p.wait(v, "mm")
            ii = v.tensor_copy(cwfm[:, tt * 128:(tt + 1) * 128], psB[0:8, 0:128])
            p.inc(ii, "sv")

    for e in range(8):
        @block.gpsimd
        def _(g, e=e):
            p.wait(g, "sv")
            if e > 0:
                p.wait(g, "mm")
            g.dma_start(out=cwrow[:, :], in_=cwfm[e:e + 1, :]).then_inc(p.sem["gd"], 16)
        p.cnt["gd"] += 16

        for ch in range(NCH):
            @block.tensor
            def _(t, e=e, ch=ch):
                p.wait(t, "gd")
                if ch > 0 or e > 0:
                    p.wait(t, "sv")
                ii = t.matmul(psB[:, :], onesr[:, :], cwrow[:, ch * 512:(ch + 1) * 512],
                              start=True, stop=True)
                p.inc(ii, "mm")

            @block.vector
            def _(v, e=e, ch=ch):
                p.wait(v, "mm")
                ii = v.tensor_copy(cwb[:, ch * 512:(ch + 1) * 512], psB[:, :])
                p.inc(ii, "sv")

        for tc in range(NCH):
            for ft in range(32):
                wgb = wg_s[ft % 2]
                wub = wu_s[ft % 2]

                @block.gpsimd
                def _(g, e=e, tc=tc, ft=ft, wgb=wgb, wub=wub):
                    p.wait(g, "mm")
                    wgv = wgb[:].rearrange("p (k t) -> p k t", k=8)
                    wuv = wub[:].rearrange("p (k t) -> p k t", k=8)
                    for sub in range(8):
                        g.dma_start(out=wgv[:, sub, :],
                                    in_=mwg[e][sub * 128:(sub + 1) * 128, ft * 128:(ft + 1) * 128]
                                    ).then_inc(p.sem["wl"], 16)
                        g.dma_start(out=wuv[:, sub, :],
                                    in_=mwu[e][sub * 128:(sub + 1) * 128, ft * 128:(ft + 1) * 128]
                                    ).then_inc(p.sem["wl"], 16)
                p.cnt["wl"] += 16 * 16

                @block.tensor
                def _(t, e=e, tc=tc, ft=ft, wgb=wgb, wub=wub):
                    p.wait(t, "wl")
                    p.wait(t, "sv")
                    wgv = wgb[:].rearrange("p (k t) -> p k t", k=8)
                    wuv = wub[:].rearrange("p (k t) -> p k t", k=8)
                    for sub in range(8):
                        t.matmul(psG[:, :], wgv[:, sub, :],
                                 xn_v[:, sub, tc * 512:(tc + 1) * 512],
                                 start=(sub == 0), stop=(sub == 7))
                    for sub in range(8):
                        ii = t.matmul(psU[:, :], wuv[:, sub, :],
                                      xn_v[:, sub, tc * 512:(tc + 1) * 512],
                                      start=(sub == 0), stop=(sub == 7))
                    p.inc(ii, "mm")

                @block.scalar
                def _(s, e=e, tc=tc, ft=ft):
                    p.wait(s, "mm")
                    ii = s.activation(sgt[:, :], psG[:, :], AF.Silu)
                    p.inc(ii, "ss")

                @block.vector
                def _(v, e=e, tc=tc, ft=ft):
                    p.wait(v, "ss")
                    v.tensor_tensor(ub[:, :], psU[:, :], cwb[:, tc * 512:(tc + 1) * 512], ALU.mult)
                    ii = v.tensor_tensor(h2_v[:, ft, :], sgt[:, :], ub[:, :], ALU.mult)
                    p.inc(ii, "sv")

            for dt in range(8):
                wdb = wd_s[dt % 2]

                @block.gpsimd
                def _(g, e=e, tc=tc, dt=dt, wdb=wdb):
                    p.wait(g, "mm")
                    wdv = wdb[:].rearrange("p (k t) -> p k t", k=32)
                    for fs in range(32):
                        g.dma_start(out=wdv[:, fs, :],
                                    in_=mwd[e][fs * 128:(fs + 1) * 128, dt * 128:(dt + 1) * 128]
                                    ).then_inc(p.sem["wl"], 16)
                p.cnt["wl"] += 16 * 32

                @block.tensor
                def _(t, e=e, tc=tc, dt=dt, wdb=wdb):
                    p.wait(t, "wl")
                    p.wait(t, "sv")
                    ps = psD_[dt % 2]
                    wdv = wdb[:].rearrange("p (k t) -> p k t", k=32)
                    for fs in range(32):
                        ii = t.matmul(ps[:, :], wdv[:, fs, :], h2_v[:, fs, :],
                                      start=(fs == 0), stop=(fs == 31))
                    p.inc(ii, "mm")

                @block.vector
                def _(v, e=e, tc=tc, dt=dt):
                    p.wait(v, "mm")
                    ps = psD_[dt % 2]
                    if e == 0:
                        ii = v.tensor_copy(ma_v[:, dt, tc * 512:(tc + 1) * 512], ps[:, :])
                    else:
                        ii = v.tensor_tensor(ma_v[:, dt, tc * 512:(tc + 1) * 512],
                                             ma_v[:, dt, tc * 512:(tc + 1) * 512], ps[:, :], ALU.add)
                    p.inc(ii, "sv")

    n = 0
    for dt in range(8):
        for tc in range(NCH):
            n += 1
            ob = o_t[n % 2]

            @block.gpsimd
            def _(g, dt=dt, tc=tc, n=n):
                p.wait(g, "sv")
                if n > 2:
                    p.wait(g, "so", 16 * (n - 2))
                g.dma_start(out=s_t[:, :], in_=sfin[dt][:, tc * 512:(tc + 1) * 512]).then_inc(p.sem["gd"], 16)
                g.dma_start(out=x_t[:, :], in_=x1[dt][:, tc * 512:(tc + 1) * 512]).then_inc(p.sem["gd"], 16)
            p.cnt["gd"] += 32

            @block.scalar
            def _(s, dt=dt, tc=tc):
                p.wait(s, "sv")
                ii = s.activation(sgt[:, :], ma_v[:, dt, tc * 512:(tc + 1) * 512], AF.Silu)
                p.inc(ii, "ss")

            @block.vector
            def _(v, dt=dt, tc=tc, ob=ob):
                p.wait(v, "ss")
                p.wait(v, "gd")
                v.tensor_tensor(ob[:, :], s_t[:, :], sgt[:, :], ALU.mult)
                ii = v.tensor_tensor(ob[:, :], ob[:, :], x_t[:, :], ALU.add)
                p.inc(ii, "sv")

            @block.sync
            def _(sy, dt=dt, tc=tc, ob=ob):
                p.wait(sy, "sv")
                ii = sy.dma_start(out=out[dt][:, tc * 512:(tc + 1) * 512], in_=ob[:, :])
                p.inc(ii, "so", 16)

    @block.sync
    def _(sy):
        p.wait(sy, "so")

    return p



TF = 4096


def build_mega(upto=None):
    p = P("mega")
    nc = p.nc
    xin = nc.dram_tensor("xin", [TF, D], F32, kind="ExternalInput").ap()
    xfm = nc.dram_tensor("xfm", [8, 128, TF], F32, kind="ExternalInput").ap()
    wqh = nc.dram_tensor("wqh", [D, D], F32, kind="ExternalInput").ap()
    wkh = nc.dram_tensor("wkh", [D, D], F32, kind="ExternalInput").ap()
    wvh = nc.dram_tensor("wvh", [D, D], F32, kind="ExternalInput").ap()
    wph = nc.dram_tensor("wph", [D, D], F32, kind="ExternalInput").ap()
    bph = nc.dram_tensor("bph", [128, 8], F32, kind="ExternalInput").ap()
    wg = nc.dram_tensor("wg", [FFI, D], F32, kind="ExternalInput").ap()
    wu = nc.dram_tensor("wu", [FFI, D], F32, kind="ExternalInput").ap()
    wd = nc.dram_tensor("wd", [D, FFI], F32, kind="ExternalInput").ap()
    pnw = nc.dram_tensor("pnw", [128, 8], F32, kind="ExternalInput").ap()
    cw = nc.dram_tensor("cw", [128, 32], F32, kind="ExternalInput").ap()
    cb = nc.dram_tensor("cb", [128, 8], F32, kind="ExternalInput").ap()
    db = nc.dram_tensor("db", [128, 8], F32, kind="ExternalInput").ap()
    waT = nc.dram_tensor("waT", [D, D], F32, kind="ExternalInput").ap()
    wiT = nc.dram_tensor("wiT", [D, D], F32, kind="ExternalInput").ap()
    wvT = nc.dram_tensor("wvT", [D, D], F32, kind="ExternalInput").ap()
    opT = nc.dram_tensor("opT", [D, D], F32, kind="ExternalInput").ap()
    fnw = nc.dram_tensor("fnw", [128, 8], F32, kind="ExternalInput").ap()
    wguT = nc.dram_tensor("wguT", [D, 2 * 4096], F32, kind="ExternalInput").ap()
    wdT = nc.dram_tensor("wdT", [4096, D], F32, kind="ExternalInput").ap()
    wrg = nc.dram_tensor("wrg", [D, 16], F32, kind="ExternalInput").ap()
    mwg = nc.dram_tensor("mwg", [8, D, 4096], F32, kind="ExternalInput").ap()
    mwu = nc.dram_tensor("mwu", [8, D, 4096], F32, kind="ExternalInput").ap()
    mwd = nc.dram_tensor("mwd", [8, 4096, D], F32, kind="ExternalInput").ap()
    hmask = nc.dram_tensor("hmask", [128, 8], F32, kind="ExternalInput").ap()
    out = nc.dram_tensor("out", [8, 128, TO], F32, kind="ExternalOutput").ap()
    wgq_d = nc.dram_tensor("wgq_d", [8, 128, FFI], BF16, kind="Internal").ap()
    wuq_d = nc.dram_tensor("wuq_d", [8, 128, FFI], BF16, kind="Internal").ap()
    wdq_d = nc.dram_tensor("wdq_d", [22, 128, D], BF16, kind="Internal").ap()
    x1_d = nc.dram_tensor("x1_d", [8, 128, TF], F32, kind="ExternalOutput").ap()
    sacc_d = nc.dram_tensor("sacc_d", [8, 128, TF], F32, kind="ExternalOutput").ap()
    sfin_d = nc.dram_tensor("sfin_d", [8, 128, TO], F32, kind="ExternalOutput").ap()
    rows_d = nc.dram_tensor("rows_d", [4, TF], F32, kind="Internal").ap()
    nmx_d = nc.dram_tensor("nmx_d", [32, 128], BF16, kind="Internal").ap()

    AR1 = p.sbuf("AR1", [128, 32768], BF16)
    AR2 = p.sbuf("AR2", [128, 32768], BF16)
    AR3 = p.sbuf("AR3", [128, 16384], F32)
    onesr = p.sbuf("onesr", [1, 128], F32)
    onesc = p.sbuf("onesc", [128, 1], F32)
    ident = p.sbuf("ident", [128, 128], BF16)
    identf = p.sbuf("identf", [128, 128], F32)
    smalls = {n: p.sbuf(n, [128, 8], F32)
              for n in ("bpt", "pnwt", "cbt", "dbt", "fnwt", "hmt", "sedge", "sinp")}
    cwt = p.sbuf("cwt", [128, 32], F32)
    redget = p.sbuf("redget", [128, 24], F32)
    halot = p.sbuf("halot", [128, 24], F32)
    mask64 = p.sbuf("mask64", [128, TO], F32)
    ztile = p.sbuf("ztile", [128, 32], F32)
    states = p.sbuf("states", [128, 32], F32)
    si = p.sbuf("si", [128, 32], F32)
    sgt = p.sbuf("sgt", [128, 512], F32)
    ub = p.sbuf("ub", [128, 512], BF16)
    st8 = p.sbuf("st8", [128, 8], F32)

    for s in ("ld", "vg", "mm", "m2", "pv", "sv", "ss", "gd", "wl", "tp", "so", "bar"):
        p.new_sem(s)

    p.begin()
    block = p.block

    def barrier():
        for name in ("tensor", "vector", "scalar", "gpsimd", "sync"):
            @getattr(block, name)
            def _(eng):
                for s in ("ld", "vg", "mm", "m2", "pv", "sv", "ss", "gd", "wl", "tp", "so"):
                    p.wait(eng, s)
                ii = eng.nop()
                p.inc(ii, "bar")
        tgt = p.cnt["bar"]
        for name in ("tensor", "vector", "scalar", "gpsimd", "sync"):
            @getattr(block, name)
            def _(eng):
                eng.wait_ge(p.sem["bar"], tgt)

    @block.gpsimd
    def _(g):
        for nm, srcd in (("bpt", bph), ("pnwt", pnw), ("cbt", cb), ("dbt", db),
                         ("fnwt", fnw), ("hmt", hmask)):
            g.dma_start(out=smalls[nm][:], in_=srcd).then_inc(p.sem["ld"], 16)
        g.dma_start(out=cwt[:], in_=cw).then_inc(p.sem["ld"], 16)
        g.iota(identf[:].bitcast(I32), pattern=[[1, 128]], base=0, channel_multiplier=-1,
               allow_small_or_imprecise_dtypes=True)
        ii = g.tensor_scalar(out=ident[:], in0=identf[:].bitcast(I32), scalar1=0,
                             scalar2=None, op0=ALU.is_equal)
        p.inc(ii, "gd")
        ii = g.tensor_scalar(out=identf[:], in0=identf[:].bitcast(I32), scalar1=0,
                             scalar2=None, op0=ALU.is_equal)
        p.inc(ii, "gd")
    p.cnt["ld"] += 16 * 7

    @block.vector
    def _(v):
        v.memset(onesr[:], 1.0)
        v.memset(onesc[:], 1.0)
        v.memset(mask64[:], 1.0)
        v.memset(mask64[:].rearrange("p (c k) -> p c k", k=64)[:, :, 0:1].squeeze(2), 0.0)
        ii = v.memset(ztile[:], 0.0)
        p.inc(ii, "sv")

    a3f = lambda o, n: AR3[:, o:o + n]
    a3b = lambda o, n: AR3[:, o:o + (n + 1) // 2].bitcast(BF16)
    xn_v = AR1[:].rearrange("p (k t) -> p k t", k=8)
    ao_v = AR2[:].rearrange("p (k t) -> p k t", k=8)

    W_X0, W_X1, W_SQ, W_XNT, W_ST = 0, 1024, 2048, 3072, 4096
    W_XNB = 4200
    xts = [a3f(W_X0, 1024), a3f(W_X1, 1024)]
    sq_t = a3f(W_SQ, 1024)
    xn_t = a3f(W_XNT, 1024)
    st1 = a3f(W_ST, 8)
    stA = a3f(W_ST, 1); stB = a3f(W_ST + 1, 1); stC = a3f(W_ST + 2, 1)
    stD = a3f(W_ST + 3, 1); stE = a3f(W_ST + 4, 1); stF = a3f(W_ST + 5, 1)
    stG = a3f(W_ST + 6, 1); stH = a3f(W_ST + 7, 1)
    xnb = a3b(W_XNB, 1024)

    for i in range(TF // 128):
        buf = xts[i % 2]

        @block.gpsimd
        def _(g, i=i, buf=buf):
            if i >= 2:
                p.wait(g, "sv")
            g.dma_start(out=buf, in_=xin[i * 128:(i + 1) * 128, :]).then_inc(p.sem["ld"], 16)
        p.cnt["ld"] += 16

        @block.vector
        def _(v, i=i, buf=buf):
            p.wait(v, "ld")
            p.guard(v, v.tensor_reduce(stA, buf, axis=AX.X, op=ALU.add))
            v.tensor_tensor(sq_t, buf, buf, ALU.mult)
            p.guard(v, v.tensor_reduce(stB, sq_t, axis=AX.X, op=ALU.add))
            p.guard(v, v.tensor_scalar(out=stC, in0=stA, scalar1=1.0 / D, scalar2=None, op0=ALU.mult))
            p.guard(v, v.tensor_scalar(out=stD, in0=stB, scalar1=1.0 / D, scalar2=None, op0=ALU.mult))
            p.guard(v, v.tensor_tensor(stE, stC, stC, ALU.mult))
            p.guard(v, v.tensor_tensor(stF, stD, stE, ALU.subtract))
            p.guard(v, v.tensor_scalar(out=stF, in0=stF, scalar1=1e-5, scalar2=None, op0=ALU.add))

        @block.scalar
        def _(s, i=i):
            p.wait(s, "vg")
            ii = s.activation(stG, stF, AF.Sqrt)
            p.inc(ii, "ss")

        @block.vector
        def _(v, i=i, buf=buf):
            p.wait(v, "ss")
            p.wait(v, "tp")
            p.guard(v, v.reciprocal(stH, stG))
            v.tensor_scalar(out=xn_t, in0=buf, scalar1=stC, scalar2=stH,
                            op0=ALU.subtract, op1=ALU.mult)
            ii = v.tensor_copy(xnb, xn_t)
            p.inc(ii, "sv")

        @block.scalar
        def _(s, i=i):
            p.wait(s, "sv")
            for k in range(8):
                ii = s.dma_start_transpose(xn_v[:, k, i * 128:(i + 1) * 128],
                                           xnb[:, k * 128:(k + 1) * 128])
                p.inc(ii, "tp", 16)

    barrier()
    if upto == "p1":
        return p

    wq_wt = a3f(0, 2816)
    wq_qb = a3b(2816, 2816)
    wq_asm = a3b(4224, 4096)
    wq_tmp = {"y2": a3f(6272, 1408), "qlo": a3f(7680, 1408), "qhi": a3f(9088, 1408),
              "sabs": a3f(10496, 1408), "qmask": a3f(11904, 1408).bitcast(I32),
              "bm": a3f(13312, 88), "rs2": a3f(13400, 88), "rsh": a3f(13488, 88)}
    wq_tmp["qsel"] = wq_tmp["y2"]

    for src_, dst_, R, C in ((wg, wgq_d, FFI, D), (wu, wuq_d, FFI, D), (wd, wdq_d, D, FFI)):
        nr = R // 128
        nk = C // 128
        asm_v = wq_asm[:, 0:C].rearrange("p (k r) -> p k r", k=nk)
        for r in range(nr):
            @block.gpsimd
            def _(g, src_=src_, r=r, C=C):
                p.wait(g, "sv")
                g.dma_start(out=wq_wt[:, :C], in_=src_[r * 128:(r + 1) * 128, :]).then_inc(p.sem["ld"], 16)
            p.cnt["ld"] += 16

            @block.vector
            def _(v, r=r, C=C):
                p.wait(v, "ld")
                p.wait(v, "tp")
                for h0 in range(0, C, 1408):
                    hw = min(1408, C - h0)
                    ii = emit_quant(p, v, wq_wt[:, h0:h0 + hw], wq_qb[:, h0:h0 + hw],
                                    wq_tmp, hw)
                p.inc(ii, "sv")

            @block.scalar
            def _(s, r=r, nk=nk, asm_v=asm_v):
                p.wait(s, "sv")
                p.wait(s, "so")
                for k in range(nk):
                    ii = s.dma_start_transpose(asm_v[:, k, :], wq_qb[:, k * 128:(k + 1) * 128])
                    p.inc(ii, "tp", 16)

            @block.gpsimd
            def _(g, r=r, nk=nk, asm_v=asm_v, dst_=dst_):
                p.wait(g, "tp")
                for k in range(nk):
                    ii = g.dma_start(out=dst_[k][:, r * 128:(r + 1) * 128], in_=asm_v[:, k, :])
                    p.inc(ii, "so", 16)
    barrier()
    if upto == "wq":
        return p

    psA = p.psum("psA", [128, 512])
    psB = p.psum("psB", [128, 512])
    psC = p.psum("psC", [128, 1024])
    psD = p.psum("psD", [128, 1024])
    psE = p.psum("psE", [128, 1024])

    Kb = [a3b(0, 4096), a3b(2048, 4096)]
    Qb = [a3b(4096, 4096), a3b(6144, 4096)]
    Vb = a3b(8192, 4160)
    att = [a3b(10272, 1024), a3b(10784, 1024)]
    bc_s = a3f(11296, 1024)[0:64, :]
    r1 = a3f(12320, 1024)[0:1, :]
    mx8 = a3f(13344, 8)
    nmx = a3b(13352, 32)
    nmxt = a3b(13368, 128)[0:32, :]
    wpair = a3b(13440, 3072)
    Vb_v = Vb.rearrange("p (m c) -> p m c", c=130)
    wpv = wpair.rearrange("p (w k e) -> p w k e", w=3, k=8)
    NQT = TF // 128
    NQC = TF // 1024

    @block.vector
    def _(v):
        p.wait(v, "bar")
        v.memset(Vb_v[:, :, 64:65].squeeze(2), 1.0)
        ii = v.memset(Vb_v[:, :, 129:130].squeeze(2), 1.0)
        p.inc(ii, "sv")

    for hp in range(8):
        @block.gpsimd
        def _(g, hp=hp):
            p.wait(g, "mm")
            p.wait(g, "sv")
            for sub in range(8):
                g.dma_start(out=wpv[:, 0, sub, :],
                            in_=wvh[sub * 128:(sub + 1) * 128, hp * 128:(hp + 1) * 128]
                            ).then_inc(p.sem["wl"], 16)
                g.dma_start(out=wpv[:, 1, sub, :],
                            in_=wkh[sub * 128:(sub + 1) * 128, hp * 128:(hp + 1) * 128]
                            ).then_inc(p.sem["wl"], 16)
                g.dma_start(out=wpv[:, 2, sub, :],
                            in_=wqh[sub * 128:(sub + 1) * 128, hp * 128:(hp + 1) * 128]
                            ).then_inc(p.sem["wl"], 16)
        p.cnt["wl"] += 16 * 24

        @block.tensor
        def _(t, hp=hp):
            p.wait(t, "wl")
            p.wait(t, "sv")
            for mt in range(32):
                dst = psC[:, 0:128] if mt % 2 == 0 else psC[:, 512:640]
                if mt >= 2:
                    p.wait(t, "sv", p.cnt["sv"] + mt - 1)
                for sub in range(8):
                    ii = t.matmul(dst, xn_v[:, sub, mt * 128:(mt + 1) * 128],
                                  wpv[:, 0, sub, :], start=(sub == 0), stop=(sub == 7))
                p.inc(ii, "mm")

        @block.vector
        def _(v, hp=hp):
            base = p.cnt["mm"] - 32
            for mt in range(32):
                srcp = psC[:, 0:128] if mt % 2 == 0 else psC[:, 512:640]
                p.wait(v, "mm", base + mt + 1)
                v.tensor_copy(Vb_v[:, mt, 0:64], srcp[:, 0:64])
                ii = v.tensor_copy(Vb_v[:, mt, 65:129], srcp[:, 64:128])
                p.inc(ii, "sv")

        for hh in range(2):
            h = hp * 2 + hh
            K_h = Kb[hh]
            Q_h = Qb[hh]
            eoff = 65 * hh

            @block.vector
            def _(v, K_h=K_h):
                ii = v.memset(K_h[64:65, :], 1.0)
                p.inc(ii, "sv")

            @block.tensor
            def _(t, h=h, hh=hh):
                p.wait(t, "sv")
                p.wait(t, "ss")
                for mc in range(8):
                    ps = psA if mc % 2 == 0 else psB
                    if mc >= 2:
                        p.wait(t, "ss", p.cnt["ss"] + mc - 1)
                    for sub in range(8):
                        ii = t.matmul(ps[0:64, :], wpv[:, 1, sub, hh * 64:(hh + 1) * 64],
                                      xn_v[:, sub, mc * 512:(mc + 1) * 512],
                                      start=(sub == 0), stop=(sub == 7))
                    p.inc(ii, "mm")
                for qc in range(TF // 512):
                    ps = psA if qc % 2 == 0 else psB
                    p.wait(t, "ss", p.cnt["ss"] + 6 + qc + 1)
                    for sub in range(8):
                        ii = t.matmul(ps[0:64, :], wpv[:, 2, sub, hh * 64:(hh + 1) * 64],
                                      xn_v[:, sub, qc * 512:(qc + 1) * 512],
                                      start=(sub == 0), stop=(sub == 7))
                    p.inc(ii, "mm")

            @block.scalar
            def _(s, h=h, K_h=K_h, Q_h=Q_h):
                base = p.cnt["mm"] - 8 - TF // 512
                for mc in range(8):
                    ps = psA if mc % 2 == 0 else psB
                    p.wait(s, "mm", base + 1 + mc)
                    ii = s.copy(out=K_h[0:64, mc * 512:(mc + 1) * 512], in_=ps[0:64, :])
                    p.inc(ii, "ss")
                for qc in range(TF // 512):
                    ps = psA if qc % 2 == 0 else psB
                    p.wait(s, "mm", base + 9 + qc)
                    ii = s.activation(Q_h[0:64, qc * 512:(qc + 1) * 512], ps[0:64, :],
                                      AF.Copy, scale=0.125)
                    p.inc(ii, "ss")

            @block.tensor
            def _(t, h=h, K_h=K_h, Q_h=Q_h):
                p.wait(t, "ss")
                for qt in range(NQT):
                    for mc in range(8):
                        n = qt * 8 + mc
                        ps = psA if n % 2 == 0 else psB
                        if n >= 2:
                            p.wait(t, "sv", p.cnt["sv"] + n - 1)
                        ii = t.matmul(ps[:, :], Q_h[0:64, qt * 128:(qt + 1) * 128],
                                      K_h[0:64, mc * 512:(mc + 1) * 512], start=True, stop=True)
                        p.inc(ii, "mm")

            @block.vector
            def _(v, h=h):
                base = p.cnt["mm"] - 8 * NQT
                n = 0
                for qt in range(NQT):
                    for mc in range(8):
                        n += 1
                        ps = psA if (n - 1) % 2 == 0 else psB
                        p.wait(v, "mm", base + n)
                        ii = v.tensor_reduce(mx8[:, mc:mc + 1], ps[:, :], axis=AX.X, op=ALU.max)
                        p.inc(ii, "sv")
                        v.wait_ge(p.sem["sv"], p.cnt["sv"])
                    p.guard(v, v.tensor_reduce(nmx[:, qt:qt + 1], mx8[:, :],
                                               axis=AX.X, op=ALU.max, negate=True))

            @block.tensor
            def _(t, h=h):
                p.wait(t, "vg")
                p.wait(t, "gd", 1)
                psT = psA[0:32, 0:64].bitcast(BF16)
                ii = t.transpose(psT, nmx[:, 0:NQT], ident[:])
                p.inc(ii, "mm")

            @block.vector
            def _(v, h=h):
                p.wait(v, "mm")
                ii = v.tensor_copy(nmxt[0:NQT, :], psA[0:32, 0:64].bitcast(BF16))
                p.inc(ii, "sv")

            @block.gpsimd
            def _(g, h=h, Q_h=Q_h):
                p.wait(g, "sv")
                ii = g.dma_start(out=nmx_d, in_=nmxt[0:NQT, :])
                p.inc(ii, "gd", 16)
                p.wait(g, "gd")
                ii = g.dma_start(out=Q_h[64:65, :], in_=nmx_d.rearrange("a b -> (a b)").unsqueeze(0))
                p.inc(ii, "gd", 16)

            for qc in range(NQC):
                qk_base = p.cnt["mm"]
                ss_base = p.cnt["ss"]
                pv_base = p.cnt["pv"]
                sv_base = p.cnt["sv"]

                @block.tensor
                def _(t, h=h, qc=qc, K_h=K_h, Q_h=Q_h, eoff=eoff,
                      qk_base=qk_base, ss_base=ss_base, sv_base=sv_base):
                    if qc == 0:
                        p.wait(t, "gd")
                    else:
                        p.wait(t, "sv", sv_base)

                    def qk(mt):
                        psl = psC if mt % 2 == 0 else psD
                        for j in range(2):
                            ii = t.matmul(psl[:, j * 512:(j + 1) * 512],
                                          K_h[0:65, mt * 128:(mt + 1) * 128],
                                          Q_h[0:65, qc * 1024 + j * 512: qc * 1024 + (j + 1) * 512],
                                          start=True, stop=True)
                        p.inc(ii, "mm")
                    qk(0)
                    qk(1)
                    for mt in range(32):
                        p.wait(t, "ss", ss_base + mt + 1)
                        ab = att[mt % 2]
                        for j in range(2):
                            ii = t.matmul(psE[0:65, j * 512:(j + 1) * 512],
                                          Vb_v[:, mt, eoff:eoff + 65],
                                          ab[:, j * 512:(j + 1) * 512],
                                          start=(mt == 0), stop=(mt == 31))
                        p.inc(ii, "pv")
                        if mt + 2 < 32:
                            qk(mt + 2)

                @block.scalar
                def _(s, h=h, qc=qc, qk_base=qk_base, pv_base=pv_base):
                    for mt in range(32):
                        psl = psC if mt % 2 == 0 else psD
                        ab = att[mt % 2]
                        p.wait(s, "mm", qk_base + mt + 1)
                        if mt >= 2:
                            p.wait(s, "pv", pv_base + mt - 1)
                        ii = s.activation(ab[:, :], psl[:, :], AF.Exp)
                        p.inc(ii, "ss")

                @block.vector
                def _(v, h=h, qc=qc):
                    p.wait(v, "pv")
                    p.guard(v, v.reciprocal(r1[:, :], psE[64:65, 0:1024]))

                @block.tensor
                def _(t, h=h, qc=qc):
                    p.wait(t, "vg")
                    for j in range(2):
                        ii = t.matmul(psA[0:64, :] if j == 0 else psB[0:64, :],
                                      onesr[:, 0:64], r1[:, j * 512:(j + 1) * 512],
                                      start=True, stop=True)
                    p.inc(ii, "mm")

                @block.vector
                def _(v, h=h, qc=qc):
                    p.wait(v, "mm")
                    v.tensor_copy(bc_s[:, 0:512], psA[0:64, :])
                    v.tensor_copy(bc_s[:, 512:1024], psB[0:64, :])
                    ii = v.tensor_tensor(
                        ao_v[:, h // 2, qc * 1024:(qc + 1) * 1024][64 * (h % 2):64 * (h % 2) + 64, :],
                        psE[0:64, :], bc_s[:, :], ALU.mult)
                    p.inc(ii, "sv")

    barrier()
    if upto == "p2":
        return p

    q_x = [a3f(0, 1024), a3f(1024, 1024)]
    q_sq = a3f(2048, 1024)
    q_xnt = a3f(3072, 1024)
    q_st = a3f(4096, 8)
    q_tmp = {"y2": a3f(4104, 1024), "qlo": a3f(5128, 1024), "qhi": a3f(6152, 1024),
             "sabs": a3f(7176, 1024), "qmask": a3f(8200, 1024).bitcast(I32),
             "bm": a3f(9224, 64), "rs2": a3f(9288, 64), "rsh": a3f(9352, 64)}
    q_tmp["qsel"] = q_tmp["y2"]
    q_xqb = a3b(9416, 1024)
    xq_v = xn_v

    for i in range(TF // 128):
        buf = q_x[i % 2]

        @block.gpsimd
        def _(g, i=i, buf=buf):
            if i >= 2:
                p.wait(g, "sv")
            g.dma_start(out=buf, in_=xin[i * 128:(i + 1) * 128, :]).then_inc(p.sem["ld"], 16)
        p.cnt["ld"] += 16

        @block.vector
        def _(v, i=i, buf=buf):
            p.wait(v, "ld")
            p.guard(v, v.tensor_reduce(q_st[:, 0:1], buf, axis=AX.X, op=ALU.add))
            v.tensor_tensor(q_sq, buf, buf, ALU.mult)
            p.guard(v, v.tensor_reduce(q_st[:, 1:2], q_sq, axis=AX.X, op=ALU.add))
            p.guard(v, v.tensor_scalar(out=q_st[:, 2:3], in0=q_st[:, 0:1], scalar1=1.0 / D, scalar2=None, op0=ALU.mult))
            p.guard(v, v.tensor_scalar(out=q_st[:, 3:4], in0=q_st[:, 1:2], scalar1=1.0 / D, scalar2=None, op0=ALU.mult))
            p.guard(v, v.tensor_tensor(q_st[:, 4:5], q_st[:, 2:3], q_st[:, 2:3], ALU.mult))
            p.guard(v, v.tensor_tensor(q_st[:, 5:6], q_st[:, 3:4], q_st[:, 4:5], ALU.subtract))
            p.guard(v, v.tensor_scalar(out=q_st[:, 5:6], in0=q_st[:, 5:6], scalar1=1e-5, scalar2=None, op0=ALU.add))

        @block.scalar
        def _(s, i=i):
            p.wait(s, "vg")
            ii = s.activation(q_st[:, 6:7], q_st[:, 5:6], AF.Sqrt)
            p.inc(ii, "ss")

        @block.vector
        def _(v, i=i, buf=buf):
            p.wait(v, "ss")
            p.wait(v, "tp")
            p.guard(v, v.reciprocal(q_st[:, 7:8], q_st[:, 6:7]))
            v.tensor_scalar(out=q_xnt, in0=buf, scalar1=q_st[:, 2:3], scalar2=q_st[:, 7:8],
                            op0=ALU.subtract, op1=ALU.mult)
            ii = emit_quant(p, v, q_xnt, q_xqb, q_tmp, 1024)
            p.inc(ii, "sv")

        @block.scalar
        def _(s, i=i):
            p.wait(s, "sv")
            for k in range(8):
                ii = s.dma_start_transpose(xq_v[:, k, i * 128:(i + 1) * 128],
                                           q_xqb[:, k * 128:(k + 1) * 128])
                p.inc(ii, "tp", 16)

    barrier()
    if upto == "p3":
        return p

    f_wg = a3b(0, 4096)
    f_wu = a3b(2048, 4096)
    f_sg = a3f(4096, 512)
    f_h = a3f(4608, 512)
    f_tmp = {"y2": a3f(5120, 512), "qlo": a3f(5632, 512), "qhi": a3f(6144, 512),
             "sabs": a3f(6656, 512), "qmask": a3f(7168, 512).bitcast(I32),
             "bm": a3f(7680, 32), "rs2": a3f(7712, 32), "rsh": a3f(7744, 32)}
    f_tmp["qsel"] = f_tmp["y2"]
    f_hqb = a3b(7776, 512)
    f_hq = a3b(8192, 11264)
    f_wd = a3b(13824, 2816)
    f_wp = a3b(15232, 1024)
    f_xt = a3f(15744, 512)
    f_gt = a3b(6144, 512)
    f_tmps = f_tmp["qlo"]
    f_hq_v = f_hq.rearrange("p (k t) -> p k t", k=22)
    FCH = [(i * 512, min(512, FFI - i * 512)) for i in range((FFI + 511) // 512)]

    for tc in range(TF // 512):
        for fi, (f0, fw) in enumerate(FCH):
            @block.gpsimd
            def _(g, tc=tc, fi=fi, f0=f0, fw=fw):
                p.wait(g, "mm")
                wgv = f_wg.rearrange("p (k t) -> p k t", k=8)
                wuv = f_wu.rearrange("p (k t) -> p k t", k=8)
                for k in range(8):
                    g.dma_start(out=wgv[:, k, 0:fw], in_=wgq_d[k][:, f0:f0 + fw]).then_inc(p.sem["wl"], 16)
                    g.dma_start(out=wuv[:, k, 0:fw], in_=wuq_d[k][:, f0:f0 + fw]).then_inc(p.sem["wl"], 16)
            p.cnt["wl"] += 16 * 16

            for tt in range(4):
                @block.tensor
                def _(t, tc=tc, fi=fi, f0=f0, fw=fw, tt=tt):
                    p.wait(t, "wl")
                    p.wait(t, "sv")
                    wgv = f_wg.rearrange("p (k t) -> p k t", k=8)
                    wuv = f_wu.rearrange("p (k t) -> p k t", k=8)
                    gt = tc * 4 + tt
                    for sub in range(8):
                        t.matmul(psA[:, 0:fw], xq_v[:, sub, gt * 128:(gt + 1) * 128],
                                 wgv[:, sub, 0:fw], start=(sub == 0), stop=(sub == 7))
                    for sub in range(8):
                        ii = t.matmul(psB[:, 0:fw], xq_v[:, sub, gt * 128:(gt + 1) * 128],
                                      wuv[:, sub, 0:fw], start=(sub == 0), stop=(sub == 7))
                    p.inc(ii, "mm")

                @block.scalar
                def _(s, fw=fw):
                    p.wait(s, "mm")
                    ii = s.activation(f_sg[:, 0:fw], psA[:, 0:fw], AF.Silu)
                    p.inc(ii, "ss")

                @block.vector
                def _(v, f0=f0, fw=fw, tt=tt):
                    p.wait(v, "ss")
                    p.wait(v, "tp")
                    i0 = v.tensor_tensor(f_h[:, 0:fw], f_sg[:, 0:fw], psB[:, 0:fw], ALU.mult)
                    p.inc(i0, "sv")
                    i1 = emit_quant(p, v, f_h[:, 0:fw], f_hqb[:, 0:fw], f_tmp, fw)
                    p.inc(i1, "sv")

                @block.scalar
                def _(s, f0=f0, fw=fw, tt=tt):
                    p.wait(s, "sv")
                    for j in range(fw // 128):
                        ii = s.dma_start_transpose(
                            f_hq_v[:, f0 // 128 + j, tt * 128:(tt + 1) * 128],
                            f_hqb[:, j * 128:(j + 1) * 128])
                        p.inc(ii, "tp", 16)

        for dt in range(8):
            @block.gpsimd
            def _(g, tc=tc, dt=dt):
                p.wait(g, "mm")
                wdv = f_wd.rearrange("p (k t) -> p k t", k=22)
                wpv2 = f_wp.rearrange("p (k t) -> p k t", k=8)
                for fs in range(22):
                    g.dma_start(out=wdv[:, fs, :], in_=wdq_d[fs][:, dt * 128:(dt + 1) * 128]
                                ).then_inc(p.sem["wl"], 16)
                for sub in range(8):
                    g.dma_start(out=wpv2[:, sub, :], in_=wph[sub * 128:(sub + 1) * 128,
                                                            dt * 128:(dt + 1) * 128]
                                ).then_inc(p.sem["wl"], 16)
                ii = g.dma_start(out=f_xt, in_=xfm[dt][:, tc * 512:(tc + 1) * 512])
                p.inc(ii, "gd", 16)
            p.cnt["wl"] += 16 * 30

            @block.tensor
            def _(t, tc=tc, dt=dt):
                p.wait(t, "wl")
                if dt == 0:
                    p.wait(t, "tp")
                p.wait(t, "sv")
                wdv = f_wd.rearrange("p (k t) -> p k t", k=22)
                wpv2 = f_wp.rearrange("p (k t) -> p k t", k=8)
                for fs in range(22):
                    ii = t.matmul(psD[:, 0:512], wdv[:, fs, :], f_hq_v[:, fs, :],
                                  start=(fs == 0), stop=(fs == 21))
                p.inc(ii, "mm")
                for sub in range(8):
                    ii = t.matmul(psE[:, 0:512], wpv2[:, sub, :],
                                  ao_v[:, sub, tc * 512:(tc + 1) * 512],
                                  start=(sub == 0), stop=(sub == 7))
                p.inc(ii, "mm")

            @block.scalar
            def _(s, tc=tc, dt=dt):
                p.wait(s, "mm", p.cnt["mm"] - 1)
                ii = s.activation(f_gt, psD[:, 0:512], AF.Silu)
                p.inc(ii, "ss")

            @block.vector
            def _(v, tc=tc, dt=dt):
                p.wait(v, "mm")
                p.wait(v, "ss")
                p.wait(v, "gd")
                v.tensor_scalar(out=f_tmps, in0=psE[:, 0:512],
                                scalar1=smalls["bpt"][:, dt:dt + 1], scalar2=None, op0=ALU.add)
                v.tensor_tensor(f_tmps, f_tmps, f_gt, ALU.mult)
                ii = v.tensor_tensor(f_tmps, f_tmps, f_xt, ALU.add)
                p.inc(ii, "sv")

            @block.gpsimd
            def _(g, tc=tc, dt=dt):
                p.wait(g, "sv")
                ii = g.dma_start(out=x1_d[dt][:, tc * 512:(tc + 1) * 512], in_=f_tmps)
                p.inc(ii, "so", 16)

    barrier()
    if upto == "p4":
        return p

    AR1F = AR1[:].bitcast(F32)
    r_sum = AR3[0:1, 0:TF]
    r_sq = AR3[0:1, TF:2 * TF]
    r_mean = AR3[0:1, 2 * TF:3 * TF]
    r_rstd = AR3[0:1, 3 * TF:4 * TF]
    l_xt = [AR1F[:, 0:512], AR1F[:, 512:1024]]
    l_sq = AR1F[:, 1024:1536]
    l_mb = AR1F[:, 1536:2048]
    l_rb = AR1F[:, 2048:2560]
    l_sa = AR1F[:, 2560:3072]
    xn1_v = ao_v

    for ch in range(TF // 512):
        for sub in range(8):
            n = ch * 8 + sub
            buf = l_xt[n % 2]

            @block.gpsimd
            def _(g, ch=ch, sub=sub, buf=buf, n=n):
                if n >= 2:
                    p.wait(g, "m2")
                g.dma_start(out=buf, in_=x1_d[sub][:, ch * 512:(ch + 1) * 512]).then_inc(p.sem["ld"], 16)
            p.cnt["ld"] += 16

            @block.tensor
            def _(t, ch=ch, sub=sub, buf=buf, n=n):
                p.wait(t, "ld")
                if n % 8 == 0 and n > 0:
                    p.wait(t, "sv")
                t.matmul(psA[0:1, :], onesc[:, :], buf, start=(sub == 0), stop=(sub == 7))
                p.inc(t.nop(), "mm")

            @block.scalar
            def _(s, ch=ch, sub=sub, buf=buf, n=n):
                p.wait(s, "mm")
                p.wait(s, "m2")
                ii = s.activation(l_sq, buf, AF.Square)
                p.inc(ii, "ss")

            @block.tensor
            def _(t, ch=ch, sub=sub, n=n):
                p.wait(t, "ss")
                if sub == 0:
                    p.wait(t, "sv")
                ii = t.matmul(psB[0:1, :], onesc[:, :], l_sq, start=(sub == 0), stop=(sub == 7))
                p.inc(ii, "m2")

        @block.vector
        def _(v, ch=ch):
            p.wait(v, "m2")
            v.tensor_copy(r_sum[:, ch * 512:(ch + 1) * 512], psA[0:1, :])
            ii = v.tensor_copy(r_sq[:, ch * 512:(ch + 1) * 512], psB[0:1, :])
            p.inc(ii, "sv")

    @block.vector
    def _(v):
        p.wait(v, "sv")
        p.guard(v, v.tensor_scalar(out=r_mean, in0=r_sum, scalar1=1.0 / D, scalar2=None, op0=ALU.mult))
        p.guard(v, v.tensor_scalar(out=r_rstd, in0=r_sq, scalar1=1.0 / D, scalar2=None, op0=ALU.mult))
        p.guard(v, v.tensor_tensor(r_sum, r_mean, r_mean, ALU.mult))
        p.guard(v, v.tensor_tensor(r_rstd, r_rstd, r_sum, ALU.subtract))
        p.guard(v, v.tensor_scalar(out=r_rstd, in0=r_rstd, scalar1=1e-5, scalar2=None, op0=ALU.add))

    @block.scalar
    def _(s):
        p.wait(s, "vg")
        ii = s.activation(r_sum, r_rstd, AF.Sqrt)
        p.inc(ii, "ss")

    @block.vector
    def _(v):
        p.wait(v, "ss")
        p.guard(v, v.reciprocal(r_rstd, r_sum))

    for ch in range(TF // 512):
        @block.tensor
        def _(t, ch=ch):
            p.wait(t, "vg")
            p.wait(t, "sv")
            t.matmul(psA[:, :], onesr[:, :], r_mean[:, ch * 512:(ch + 1) * 512], start=True, stop=True)
            ii = t.matmul(psB[:, :], onesr[:, :], r_rstd[:, ch * 512:(ch + 1) * 512], start=True, stop=True)
            p.inc(ii, "mm")

        @block.vector
        def _(v, ch=ch):
            p.wait(v, "mm")
            v.tensor_copy(l_mb, psA[:, :])
            ii = v.tensor_copy(l_rb, psB[:, :])
            p.inc(ii, "sv")

        for sub in range(8):
            n = ch * 8 + sub
            buf = l_xt[n % 2]

            @block.gpsimd
            def _(g, ch=ch, sub=sub, buf=buf, n=n):
                p.wait(g, "m2")
                g.dma_start(out=buf, in_=x1_d[sub][:, ch * 512:(ch + 1) * 512]).then_inc(p.sem["ld"], 16)
            p.cnt["ld"] += 16

            @block.vector
            def _(v, ch=ch, sub=sub, buf=buf, n=n):
                p.wait(v, "ld")
                p.wait(v, "so")
                p.wait(v, "ss")
                v.tensor_tensor(l_sa, buf, l_mb, ALU.subtract)
                v.tensor_tensor(l_sa, l_sa, l_rb, ALU.mult)
                ii = v.tensor_copy(xn1_v[:, sub, ch * 512:(ch + 1) * 512], l_sa)
                p.inc(ii, "sv")

            @block.gpsimd
            def _(g, ch=ch, sub=sub):
                p.wait(g, "sv")
                ii = g.dma_start(out=sacc_d[sub][:, ch * 512:(ch + 1) * 512], in_=l_sa)
                p.inc(ii, "so", 16)

            @block.scalar
            def _(s, ch=ch, sub=sub, n=n):
                p.wait(s, "sv")
                p.wait(s, "m2")
                ii = s.activation(l_sq, l_sa, AF.Square)
                p.inc(ii, "ss")

            @block.tensor
            def _(t, ch=ch, sub=sub, n=n):
                p.wait(t, "ss")
                if sub == 0:
                    p.wait(t, "sv")
                ii = t.matmul(psC[0:1, 0:512], onesc[:, :], l_sq, start=(sub == 0), stop=(sub == 7))
                p.inc(ii, "m2")

        @block.vector
        def _(v, ch=ch):
            p.wait(v, "m2")
            ii = v.tensor_copy(r_sq[:, ch * 512:(ch + 1) * 512], psC[0:1, 0:512])
            p.inc(ii, "sv")

    @block.vector
    def _(v):
        p.wait(v, "sv")
        p.guard(v, v.tensor_scalar(out=r_sq, in0=r_sq, scalar1=1.0 / D, scalar2=1e-6,
                                   op0=ALU.mult, op1=ALU.add))

    @block.scalar
    def _(s):
        p.wait(s, "vg")
        ii = s.activation(r_mean, r_sq, AF.Sqrt)
        p.inc(ii, "ss")

    @block.vector
    def _(v):
        p.wait(v, "ss")
        p.guard(v, v.reciprocal(r_sq, r_mean))

    @block.gpsimd
    def _(g):
        p.wait(g, "vg")
        ii = g.dma_start(out=rows_d[0:1, :], in_=r_sq)
        p.inc(ii, "so", 16)

    barrier()
    if upto == "p5":
        return p

    rec_d = nc.dram_tensor("rec_d", [8, 128, TO], BF16, kind="ExternalOutput").ap()
    A = [a3f(k * 2048, 2048) for k in range(7)]
    rn_ext = a3f(6144, 2052)
    ac1 = a3f(8208, 2048)
    ac2 = a3f(10256, 2048)
    r2bc = a3f(14336, 2048)
    r2row = AR3[0:1, 14336:16384]
    h_v = AR1[:, 0:16384].rearrange("p (k t) -> p k t", k=8)
    wstr = [AR1[:, 16384 + i * 3072: 16384 + (i + 1) * 3072] for i in range(2)]
    rectile = AR1[:, 22528:24576]
    c3 = lambda ap: ap.rearrange("p (c k) -> p c k", k=64)

    for hf in (1, 0):
        cols0 = 2048 * hf

        @block.gpsimd
        def _(g, hf=hf, cols0=cols0):
            p.wait(g, "sv")
            p.wait(g, "mm")
            ii = g.dma_start(out=r2row, in_=rows_d[0:1, cols0:cols0 + 2048])
            p.inc(ii, "gd", 16)

        for j in range(4):
            @block.tensor
            def _(t, j=j):
                p.wait(t, "gd")
                if j > 0:
                    p.wait(t, "sv")
                ii = t.matmul(psA[:, :], onesr[:, :], r2row[:, j * 512:(j + 1) * 512],
                              start=True, stop=True)
                p.inc(ii, "mm")

            @block.vector
            def _(v, j=j):
                p.wait(v, "mm")
                ii = v.tensor_copy(r2bc[:, j * 512:(j + 1) * 512], psA[:, :])
                p.inc(ii, "sv")

        for dt in range(8):
            @block.gpsimd
            def _(g, hf=hf, cols0=cols0, dt=dt):
                p.wait(g, "sv")
                ii = g.dma_start(out=rn_ext[:, 3:2051], in_=sacc_d[dt][:, cols0:cols0 + 2048])
                p.inc(ii, "gd", 16)

            @block.vector
            def _(v, hf=hf, dt=dt):
                p.wait(v, "gd")
                v.tensor_scalar(out=rn_ext[:, 3:2051], in0=rn_ext[:, 3:2051],
                                scalar1=smalls["pnwt"][:, dt:dt + 1], scalar2=None, op0=ALU.mult)
                v.tensor_tensor(rn_ext[:, 3:2051], rn_ext[:, 3:2051], r2bc[:, :], ALU.mult)
                if hf == 1:
                    v.memset(rn_ext[:, 0:3], 0.0)
                    p.guard(v, v.tensor_copy(redget[:, dt * 3:(dt + 1) * 3], rn_ext[:, 2048:2051]))
                else:
                    p.guard(v, v.tensor_copy(rn_ext[:, 0:3], halot[:, dt * 3:(dt + 1) * 3]))
                v.tensor_scalar(out=ac1[:, :], in0=rn_ext[:, 0:2048],
                                scalar1=cwt[:, dt * 4:dt * 4 + 1], scalar2=None, op0=ALU.mult)
                v.scalar_tensor_tensor(out=ac2[:, :], in0=rn_ext[:, 1:2049],
                                       scalar=cwt[:, dt * 4 + 1:dt * 4 + 2], in1=ac1[:, :],
                                       op0=ALU.mult, op1=ALU.add)
                v.scalar_tensor_tensor(out=ac1[:, :], in0=rn_ext[:, 2:2050],
                                       scalar=cwt[:, dt * 4 + 2:dt * 4 + 3], in1=ac2[:, :],
                                       op0=ALU.mult, op1=ALU.add)
                v.scalar_tensor_tensor(out=ac2[:, :], in0=rn_ext[:, 3:2051],
                                       scalar=cwt[:, dt * 4 + 3:dt * 4 + 4], in1=ac1[:, :],
                                       op0=ALU.mult, op1=ALU.add)
                ii = v.tensor_scalar(out=h_v[:, dt, :], in0=ac2[:, :],
                                     scalar1=smalls["cbt"][:, dt:dt + 1], scalar2=None, op0=ALU.add)
                p.inc(ii, "sv")

        for ot in range(8):
            wbuf = wstr[ot % 2]
            wv3 = wbuf.rearrange("p (w k t) -> p w k t", w=3, k=8)

            @block.gpsimd
            def _(g, hf=hf, ot=ot, wv3=wv3):
                p.wait(g, "mm")
                for wi, wsrc in enumerate((waT, wiT, wvT)):
                    for sub in range(8):
                        g.dma_start(out=wv3[:, wi, sub, :],
                                    in_=wsrc[sub * 128:(sub + 1) * 128, ot * 128:(ot + 1) * 128]
                                    ).then_inc(p.sem["wl"], 16)
            p.cnt["wl"] += 16 * 24

            @block.tensor
            def _(t, hf=hf, ot=ot, wv3=wv3):
                p.wait(t, "wl")
                p.wait(t, "sv")
                for ch in range(4):
                    if ch > 0:
                        p.wait(t, "ss", p.cnt["ss"] + ch * 2)
                        p.wait(t, "sv", p.cnt["sv"] + ch)
                    for sub in range(8):
                        t.matmul(psA[:, :], wv3[:, 0, sub, :], h_v[:, sub, ch * 512:(ch + 1) * 512],
                                 start=(sub == 0), stop=(sub == 7))
                    for sub in range(8):
                        t.matmul(psB[:, :], wv3[:, 1, sub, :], h_v[:, sub, ch * 512:(ch + 1) * 512],
                                 start=(sub == 0), stop=(sub == 7))
                    for sub in range(8):
                        ii = t.matmul(psC[:, 0:512], wv3[:, 2, sub, :], h_v[:, sub, ch * 512:(ch + 1) * 512],
                                      start=(sub == 0), stop=(sub == 7))
                    p.inc(ii, "mm")

            @block.scalar
            def _(s, hf=hf, ot=ot):
                base = p.cnt["mm"] - 4
                for ch in range(4):
                    p.wait(s, "mm", base + ch + 1)
                    s.activation(A[0][:, ch * 512:(ch + 1) * 512], psA[:, :], AF.Sigmoid,
                                 bias=smalls["dbt"][:, ot:ot + 1], scale=1.0)
                    ii = s.activation(A[1][:, ch * 512:(ch + 1) * 512], psB[:, :], AF.Sigmoid)
                    p.inc(ii, "ss", 2)

            @block.vector
            def _(v, hf=hf, ot=ot):
                base = p.cnt["mm"] - 4
                for ch in range(4):
                    p.wait(v, "mm", base + ch + 1)
                    ii = v.tensor_copy(A[2][:, ch * 512:(ch + 1) * 512], psC[:, 0:512])
                    p.inc(ii, "sv")

            @block.vector
            def _(v, hf=hf, ot=ot):
                p.wait(v, "ss")
                ii = v.tensor_scalar(out=A[3][:, :], in0=A[0][:, :], scalar1=1e-10,
                                     scalar2=None, op0=ALU.max)
                p.inc(ii, "sv")

            @block.scalar
            def _(s, hf=hf, ot=ot):
                p.wait(s, "sv")
                ii = s.activation(A[4][:, :], A[3][:, :], AF.Ln)
                p.inc(ii, "ss")

            @block.vector
            def _(v, hf=hf, ot=ot):
                p.wait(v, "ss")
                ii = v.tensor_tensor_scan(out=A[3][:, :], data0=mask64[:, :], data1=A[4][:, :],
                                          initial=0.0, op0=ALU.mult, op1=ALU.add)
                p.inc(ii, "sv")

            @block.scalar
            def _(s, hf=hf, ot=ot):
                p.wait(s, "sv")
                ii = s.activation(A[4][:, :], A[3][:, :], AF.Exp)
                p.inc(ii, "ss")

            @block.vector
            def _(v, hf=hf, ot=ot):
                p.wait(v, "ss")
                v.tensor_scalar(out=A[3][:, :], in0=A[4][:, :], scalar1=1e-10,
                                scalar2=None, op0=ALU.max)
                v.reciprocal(A[5][:, :], A[3][:, :])
                v.tensor_tensor(A[3][:, :], A[0][:, :], A[0][:, :], ALU.mult)
                v.tensor_scalar(out=A[3][:, :], in0=A[3][:, :], scalar1=-1.0, scalar2=1.0,
                                op0=ALU.mult, op1=ALU.add)
                ii = v.tensor_scalar(out=A[3][:, :], in0=A[3][:, :], scalar1=1e-8,
                                     scalar2=None, op0=ALU.max)
                p.inc(ii, "sv")

            @block.scalar
            def _(s, hf=hf, ot=ot):
                p.wait(s, "sv")
                ii = s.activation(A[6][:, :], A[3][:, :], AF.Sqrt)
                p.inc(ii, "ss")

            @block.vector
            def _(v, hf=hf, ot=ot):
                p.wait(v, "ss")
                v.tensor_tensor(A[0][:, :], A[6][:, :], A[1][:, :], ALU.mult)
                v.tensor_tensor(A[0][:, :], A[0][:, :], A[2][:, :], ALU.mult)
                v.tensor_tensor(A[0][:, :], A[0][:, :], A[5][:, :], ALU.mult)
                v.tensor_tensor_scan(out=A[1][:, :], data0=mask64[:, :], data1=A[0][:, :],
                                     initial=0.0, op0=ALU.mult, op1=ALU.add)
                v.tensor_tensor(A[2][:, :], A[4][:, :], A[1][:, :], ALU.mult)
                init = 0.0 if hf == 1 else smalls["sinp"][:, ot:ot + 1]
                p.guard(v, v.tensor_tensor_scan(out=states[:, :], data0=c3(A[4])[:, :, 63],
                                                data1=c3(A[2])[:, :, 63], initial=init,
                                                op0=ALU.mult, op1=ALU.add))
                if hf == 1:
                    ii = p.guard(v, v.tensor_copy(smalls["sedge"][:, ot:ot + 1],
                                                  states[:, 31:32]))
                    p.inc(v.tensor_copy(st8[:, 0:1], st8[:, 0:1]), "sv")
                else:
                    if hf == 0:
                        pass
                    p.guard(v, v.tensor_copy(si[:, 0:1], smalls["sinp"][:, ot:ot + 1]))
                    p.guard(v, v.tensor_copy(si[:, 1:32], states[:, 0:31]))
                    v.tensor_tensor(c3(A[0]), c3(A[4]),
                                    si[:, :].unsqueeze(2).broadcast_to((128, 32, 64)), ALU.mult)
                    if ot >= 1:
                        p.wait(v, "so", p.cnt["so"])
                    ii = v.tensor_tensor(rectile[:, :], A[2][:, :], A[0][:, :], ALU.add)
                    p.inc(ii, "sv")

            if hf == 0:
                @block.gpsimd
                def _(g, ot=ot):
                    p.wait(g, "sv")
                    ii = g.dma_start(out=rec_d[ot], in_=rectile[:, :])
                    p.inc(ii, "so", 16)

        if hf == 1:
            @block.vector
            def _(v):
                p.wait(v, "vg")
                p.guard(v, v.tensor_scalar(out=smalls["sinp"][:, :], in0=smalls["sedge"][:, :],
                                           scalar1=smalls["hmt"][:, 0:1], scalar2=None, op0=ALU.mult))
                p.guard(v, v.tensor_scalar(out=halot[:, :], in0=redget[:, :],
                                           scalar1=smalls["hmt"][:, 0:1], scalar2=None, op0=ALU.mult))

    barrier()
    if upto == "p6":
        return p

    rec_v = AR1[:, 0:16384].rearrange("p (k t) -> p k t", k=8)
    hn_v = rec_v
    p7_h2 = AR1[:, 16384:24576].rearrange("p (k t) -> p k t", k=32)
    p7_op = [AR1[:, 16384 + i * 1024: 16384 + (i + 1) * 1024] for i in range(2)]
    p7_sq = AR1[:, 18432:20480].bitcast(F32)
    p7_row = AR1[:, 20480:24576].bitcast(F32)
    r3_ss = p7_row[0:1, 0:2048]
    p7_bc = AR1[:, 24576:28672].bitcast(F32)
    r3_tmp = p7_bc[0:1, 1024:1536][0:1, 0:512] if False else p7_bc[0:1, 512:2048][0:1, 0:1536]
    r3_tmp = p7_bc[0:1, 0:2048]
    p7_wgu = [AR1[:, 24576 + i * 2048: 24576 + (i + 1) * 2048] for i in range(2)]
    p7_wd = AR1[:, 28672:32768]
    sa8 = [a3f(k * 2048, 2048) for k in range(8)]

    @block.gpsimd
    def _(g):
        p.wait(g, "sv")
        p.wait(g, "ss")
        for sub in range(8):
            g.dma_start(out=sa8[sub], in_=sacc_d[sub][:, 0:TO]).then_inc(p.sem["ld"], 16)
            g.dma_start(out=rec_v[:, sub, :], in_=rec_d[sub]).then_inc(p.sem["ld"], 16)
    p.cnt["ld"] += 16 * 16

    for ot in range(8):
        wb = p7_op[ot % 2]

        @block.gpsimd
        def _(g, ot=ot, wb=wb):
            p.wait(g, "mm")
            wv = wb.rearrange("p (k t) -> p k t", k=8)
            for sub in range(8):
                g.dma_start(out=wv[:, sub, :],
                            in_=opT[sub * 128:(sub + 1) * 128, ot * 128:(ot + 1) * 128]
                            ).then_inc(p.sem["wl"], 16)
        p.cnt["wl"] += 16 * 8

        for ch in range(4):
            @block.tensor
            def _(t, ot=ot, ch=ch, wb=wb):
                p.wait(t, "wl")
                p.wait(t, "ld")
                p.wait(t, "sv")
                wv = wb.rearrange("p (k t) -> p k t", k=8)
                ps = psA if ch % 2 == 0 else psB
                for sub in range(8):
                    ii = t.matmul(ps[:, :], wv[:, sub, :], rec_v[:, sub, ch * 512:(ch + 1) * 512],
                                  start=(sub == 0), stop=(sub == 7))
                p.inc(ii, "mm")

            @block.vector
            def _(v, ot=ot, ch=ch):
                p.wait(v, "mm")
                ps = psA if ch % 2 == 0 else psB
                ii = v.tensor_tensor(sa8[ot][:, ch * 512:(ch + 1) * 512],
                                     sa8[ot][:, ch * 512:(ch + 1) * 512], ps[:, :], ALU.add)
                p.inc(ii, "sv")

    for ch in range(4):
        for sub in range(8):
            @block.scalar
            def _(s, ch=ch, sub=sub):
                p.wait(s, "sv")
                p.wait(s, "m2")
                ii = s.activation(p7_sq[:, 0:512], sa8[sub][:, ch * 512:(ch + 1) * 512], AF.Square)
                p.inc(ii, "ss")

            @block.tensor
            def _(t, ch=ch, sub=sub):
                p.wait(t, "ss")
                if sub == 0:
                    p.wait(t, "sv")
                ii = t.matmul(psC[0:1, 0:512], onesc[:, :], p7_sq[:, 0:512],
                              start=(sub == 0), stop=(sub == 7))
                p.inc(ii, "m2")

        @block.vector
        def _(v, ch=ch):
            p.wait(v, "m2")
            ii = v.tensor_copy(r3_ss[:, ch * 512:(ch + 1) * 512], psC[0:1, 0:512])
            p.inc(ii, "sv")

    @block.vector
    def _(v):
        p.wait(v, "sv")
        p.guard(v, v.tensor_scalar(out=r3_ss, in0=r3_ss, scalar1=1.0 / D, scalar2=1e-6,
                                   op0=ALU.mult, op1=ALU.add))

    @block.scalar
    def _(s):
        p.wait(s, "vg")
        ii = s.activation(r3_tmp, r3_ss, AF.Sqrt)
        p.inc(ii, "ss")

    @block.vector
    def _(v):
        p.wait(v, "ss")
        p.guard(v, v.reciprocal(r3_ss, r3_tmp))

    for ch in range(4):
        @block.tensor
        def _(t, ch=ch):
            p.wait(t, "vg")
            p.wait(t, "sv")
            ii = t.matmul(psC[:, 0:512], onesr[:, :], r3_ss[:, ch * 512:(ch + 1) * 512],
                          start=True, stop=True)
            p.inc(ii, "mm")

        @block.vector
        def _(v, ch=ch):
            p.wait(v, "mm")
            ii = v.tensor_copy(p7_bc[:, 0:512], psC[:, 0:512])
            p.inc(ii, "sv")
            for sub in range(8):
                v.tensor_scalar(out=p7_sq[:, 0:512], in0=sa8[sub][:, ch * 512:(ch + 1) * 512],
                                scalar1=smalls["fnwt"][:, sub:sub + 1], scalar2=None, op0=ALU.mult)
                ii = v.tensor_tensor(hn_v[:, sub, ch * 512:(ch + 1) * 512],
                                     p7_sq[:, 0:512], p7_bc[:, 0:512], ALU.mult)
            p.inc(ii, "sv")

    for tc in range(TO // 256):
        for ft in range(32):
            wb = p7_wgu[ft % 2]

            @block.gpsimd
            def _(g, tc=tc, ft=ft, wb=wb):
                p.wait(g, "mm")
                p.wait(g, "sv")
                wv = wb.rearrange("p (k t) -> p k t", k=8)
                for sub in range(8):
                    g.dma_start(out=wv[:, sub, 0:128],
                                in_=wguT[sub * 128:(sub + 1) * 128, ft * 128:(ft + 1) * 128]
                                ).then_inc(p.sem["wl"], 16)
                    g.dma_start(out=wv[:, sub, 128:256],
                                in_=wguT[sub * 128:(sub + 1) * 128, 4096 + ft * 128:4096 + (ft + 1) * 128]
                                ).then_inc(p.sem["wl"], 16)
            p.cnt["wl"] += 16 * 16

            @block.tensor
            def _(t, tc=tc, ft=ft, wb=wb):
                p.wait(t, "wl")
                p.wait(t, "sv")
                wv = wb.rearrange("p (k t) -> p k t", k=8)
                for sub in range(8):
                    t.matmul(psA[:, 0:256], wv[:, sub, 0:128],
                             hn_v[:, sub, tc * 256:(tc + 1) * 256],
                             start=(sub == 0), stop=(sub == 7))
                for sub in range(8):
                    ii = t.matmul(psB[:, 0:256], wv[:, sub, 128:256],
                                  hn_v[:, sub, tc * 256:(tc + 1) * 256],
                                  start=(sub == 0), stop=(sub == 7))
                p.inc(ii, "mm")

            @block.scalar
            def _(s, tc=tc, ft=ft):
                p.wait(s, "mm")
                ii = s.activation(sgt[:, 0:256], psA[:, 0:256], AF.Silu)
                p.inc(ii, "ss")

            @block.vector
            def _(v, tc=tc, ft=ft):
                p.wait(v, "ss")
                ii = v.tensor_tensor(p7_h2[:, ft, tc % 1 * 0:256] if False else p7_h2[:, ft, 0:256],
                                     sgt[:, 0:256], psB[:, 0:256], ALU.mult)
                p.inc(ii, "sv")

        for dt in range(8):
            @block.gpsimd
            def _(g, tc=tc, dt=dt):
                p.wait(g, "mm")
                wv = p7_wd.rearrange("p (k t) -> p k t", k=32)
                for fs in range(32):
                    g.dma_start(out=wv[:, fs, :],
                                in_=wdT[fs * 128:(fs + 1) * 128, dt * 128:(dt + 1) * 128]
                                ).then_inc(p.sem["wl"], 16)
            p.cnt["wl"] += 16 * 32

            @block.tensor
            def _(t, tc=tc, dt=dt):
                p.wait(t, "wl")
                p.wait(t, "sv")
                wv = p7_wd.rearrange("p (k t) -> p k t", k=32)
                ps = psA if dt % 2 == 0 else psB
                for fs in range(32):
                    ii = t.matmul(ps[:, 0:256], wv[:, fs, :], p7_h2[:, fs, 0:256],
                                  start=(fs == 0), stop=(fs == 31))
                p.inc(ii, "mm")

            @block.vector
            def _(v, tc=tc, dt=dt):
                p.wait(v, "mm")
                ps = psA if dt % 2 == 0 else psB
                ii = v.tensor_tensor(sa8[dt][:, tc * 256:(tc + 1) * 256],
                                     sa8[dt][:, tc * 256:(tc + 1) * 256], ps[:, 0:256], ALU.add)
                p.inc(ii, "sv")

    @block.gpsimd
    def _(g):
        p.wait(g, "sv")
        for sub in range(8):
            ii = g.dma_start(out=sfin_d[sub], in_=sa8[sub])
            p.inc(ii, "so", 16)

    barrier()
    if upto == "p7":
        return p

    cwfm_d = nc.dram_tensor("cwfm_d", [8, TO], F32, kind="Internal").ap()
    m_h2 = AR1[:, 0:8192].rearrange("p (k t) -> p k t", k=32)
    m_wg = [AR1[:, 8192 + i * 1024: 8192 + (i + 1) * 1024] for i in range(2)]
    m_wu = [AR1[:, 10240 + i * 1024: 10240 + (i + 1) * 1024] for i in range(2)]
    m_wd = [AR1[:, 12288 + i * 4096: 12288 + (i + 1) * 4096] for i in range(2)]
    m_cw = AR1[:, 30720:32768].bitcast(F32)
    macc8 = [a3f(k * 2048, 2048) for k in range(8)]
    wrgt_v = AR1[:, 8192:9216][:, 0:128].rearrange("p (k t) -> p k t", k=8)
    cw_v2 = m_cw[:, 0:128].rearrange("p (k t) -> p k t", k=16)
    mtp = AR1[:, 9216:9472]
    NQT2 = TO // 128

    @block.gpsimd
    def _(g):
        p.wait(g, "bar")
        for sub in range(8):
            g.dma_start(out=wrgt_v[:, sub, :], in_=wrg[sub * 128:(sub + 1) * 128, :]).then_inc(p.sem["ld"], 16)
    p.cnt["ld"] += 16 * 8

    for tt in range(NQT2):
        @block.tensor
        def _(t, tt=tt):
            p.wait(t, "ld")
            if tt > 0:
                p.wait(t, "ss")
            for sub in range(8):
                ii = t.matmul(psA[:, 0:16], xn1_v[:, sub, tt * 128:(tt + 1) * 128],
                              wrgt_v[:, sub, :], start=(sub == 0), stop=(sub == 7))
            p.inc(ii, "mm")

        @block.vector
        def _(v, tt=tt):
            p.wait(v, "mm")
            p.guard(v, v.tensor_reduce(st8[:, 0:1], psA[:, 0:8], axis=AX.X, op=ALU.max, negate=True))
            p.guard(v, v.tensor_reduce(st8[:, 1:2], psA[:, 8:16], axis=AX.X, op=ALU.max, negate=True))
            p.guard(v, v.tensor_scalar(out=st8[:, 2:3], in0=st8[:, 1:2], scalar1=10.0,
                                       scalar2=None, op0=ALU.mult))

        @block.scalar
        def _(s, tt=tt):
            p.wait(s, "vg")
            s.activation(cw_v2[:, tt, :], psA[:, 0:8], AF.Exp, bias=st8[:, 0:1],
                         scale=1.0, accum_out=st8[:, 3:4])
            ii = s.activation(sgt[:, 0:8], psA[:, 8:16], AF.Exp, bias=st8[:, 2:3],
                              scale=10.0, accum_out=st8[:, 4:5])
            p.inc(ii, "ss")

        @block.vector
        def _(v, tt=tt):
            p.wait(v, "ss")
            p.guard(v, v.reciprocal(st8[:, 5:6], st8[:, 3:4]))
            p.guard(v, v.reciprocal(st8[:, 6:7], st8[:, 4:5]))
            p.guard(v, v.tensor_scalar(out=cw_v2[:, tt, :], in0=cw_v2[:, tt, :],
                                       scalar1=st8[:, 5:6], scalar2=None, op0=ALU.mult))
            p.guard(v, v.tensor_scalar(out=sgt[:, 0:8], in0=sgt[:, 0:8],
                                       scalar1=st8[:, 6:7], scalar2=None, op0=ALU.mult))
            p.guard(v, v.tensor_tensor(cw_v2[:, tt, :], cw_v2[:, tt, :], sgt[:, 0:8], ALU.mult))
            p.guard(v, v.tensor_reduce(st8[:, 7:8], cw_v2[:, tt, :], axis=AX.X, op=ALU.add))
            p.guard(v, v.tensor_scalar(out=st8[:, 7:8], in0=st8[:, 7:8], scalar1=1e-9,
                                       scalar2=None, op0=ALU.add))
            p.guard(v, v.reciprocal(st8[:, 7:8], st8[:, 7:8]))
            p.guard(v, v.tensor_scalar(out=cw_v2[:, tt, :], in0=cw_v2[:, tt, :],
                                       scalar1=st8[:, 7:8], scalar2=None, op0=ALU.mult))
            ii = v.tensor_copy(st8[:, 0:1], st8[:, 0:1])
            p.inc(ii, "sv")

    for tt in range(NQT2):
        @block.tensor
        def _(t, tt=tt):
            p.wait(t, "sv")
            p.wait(t, "gd", 2)
            psT8 = psB[0:8, 0:128]
            ii = t.transpose(psT8, cw_v2[:, tt, :], identf[:])
            p.inc(ii, "mm")

        @block.vector
        def _(v, tt=tt):
            p.wait(v, "mm")
            p.wait(v, "so")
            ii = v.tensor_copy(mtp.bitcast(F32)[0:8, 0:128], psB[0:8, 0:128])
            p.inc(ii, "sv")

        @block.gpsimd
        def _(g, tt=tt):
            p.wait(g, "sv")
            ii = g.dma_start(out=cwfm_d[:, tt * 128:(tt + 1) * 128],
                             in_=mtp.bitcast(F32)[0:8, 0:128])
            p.inc(ii, "so", 16)

    if upto == "p8r":
        barrier()
        return p
    dead = [ao_v[:, k, 2048:4096] for k in range(8)]
    wgv2 = [dead[0].rearrange("p (k t) -> p k t", k=8)[:, :, 0:128],
            dead[0].rearrange("p (k t) -> p k t", k=8)[:, :, 128:256]]
    wuv2 = [dead[1].rearrange("p (k t) -> p k t", k=8)[:, :, 0:128],
            dead[1].rearrange("p (k t) -> p k t", k=8)[:, :, 128:256]]
    wdv2 = [(dead[2].rearrange("p (k t) -> p k t", k=16),
             dead[3].rearrange("p (k t) -> p k t", k=16)),
            (dead[4].rearrange("p (k t) -> p k t", k=16),
             dead[5].rearrange("p (k t) -> p k t", k=16))]
    cwb2 = dead[6]
    d7f = dead[7][:].bitcast(F32) if False else dead[7].bitcast(F32)
    cwrow = d7f[0:1, 0:256]
    cwrow2 = d7f[0:1, 256:512]
    ub2 = dead[7][:, 1024:2048]
    m_sx = d7f[:, 0:512]
    m_xx = d7f[:, 512:1024]
    m_oo = dead[0][:, 0:1024].bitcast(F32)
    h2f = AR1[:].rearrange("p (k t) -> p k t", k=32)

    n_experts = 1 if upto in ('p8e', 'p8x') else 8
    for e in range(n_experts):
        for tcb in range(4):
            @block.gpsimd
            def _(g, e=e, tcb=tcb):
                p.wait(g, "mm")
                ii = g.dma_start(out=cwrow[:, 0:256], in_=cwfm_d[e:e + 1, tcb * 512:tcb * 512 + 256])
                p.inc(ii, "gd", 16)
                p.wait(g, "gd")
                ii = g.dma_start(out=cwrow2[:, 0:256], in_=cwfm_d[e:e + 1, tcb * 512 + 256:(tcb + 1) * 512])
                p.inc(ii, "gd", 16)

            @block.tensor
            def _(t, e=e, tcb=tcb):
                p.wait(t, "gd")
                p.wait(t, "sv")
                t.matmul(psC[:, 0:256], onesr[:, :], cwrow[:, 0:256], start=True, stop=True)
                ii = t.matmul(psC[:, 256:512], onesr[:, :], cwrow2[:, 0:256], start=True, stop=True)
                p.inc(ii, "mm")

            @block.vector
            def _(v, e=e, tcb=tcb):
                p.wait(v, "mm")
                ii = v.tensor_copy(cwb2[:, tcb * 512:(tcb + 1) * 512], psC[:, 0:512])
                p.inc(ii, "sv")

        if upto == "p8c":
            barrier()
            return p
        for tc in range(2):
            if upto == "p8g" and tc == 1:
                break
            for ft in range(32):
                wgb = wgv2[ft % 2]
                wub = wuv2[ft % 2]

                @block.gpsimd
                def _(g, e=e, tc=tc, ft=ft, wgb=wgb, wub=wub):
                    p.wait(g, "mm")
                    g.dma_start(out=wgb, in_=mwg[e][:, ft * 128:(ft + 1) * 128]
                                .rearrange("(k p) t -> p k t", p=128)).then_inc(p.sem["wl"], 16)
                    g.dma_start(out=wub, in_=mwu[e][:, ft * 128:(ft + 1) * 128]
                                .rearrange("(k p) t -> p k t", p=128)).then_inc(p.sem["wl"], 16)
                p.cnt["wl"] += 32

                @block.tensor
                def _(t, e=e, tc=tc, ft=ft, wgb=wgb, wub=wub):
                    p.wait(t, "wl")
                    p.wait(t, "sv")
                    for j in range(2):
                        for sub in range(8):
                            t.matmul(psC[:, j * 512:(j + 1) * 512], wgb[:, sub, :],
                                     xn1_v[:, sub, tc * 1024 + j * 512: tc * 1024 + (j + 1) * 512],
                                     start=(sub == 0), stop=(sub == 7))
                    for j in range(2):
                        for sub in range(8):
                            ii = t.matmul(psD[:, j * 512:(j + 1) * 512], wub[:, sub, :],
                                          xn1_v[:, sub, tc * 1024 + j * 512: tc * 1024 + (j + 1) * 512],
                                          start=(sub == 0), stop=(sub == 7))
                    p.inc(ii, "mm")

                @block.scalar
                def _(s, e=e, tc=tc, ft=ft):
                    p.wait(s, "mm")
                    ii = s.activation(h2f[:, ft, :], psC[:, 0:1024], AF.Silu)
                    p.inc(ii, "ss")

                @block.vector
                def _(v, e=e, tc=tc, ft=ft):
                    p.wait(v, "ss")
                    v.tensor_tensor(ub2[:, 0:1024], psD[:, 0:1024],
                                    cwb2[:, tc * 1024:(tc + 1) * 1024], ALU.mult)
                    ii = v.tensor_tensor(h2f[:, ft, :], h2f[:, ft, :], ub2[:, 0:1024], ALU.mult)
                    p.inc(ii, "sv")

            if upto == "p8g":
                barrier()
                return p
            for dt in range(8):
                wdb = wdv2[dt % 2]

                @block.gpsimd
                def _(g, e=e, tc=tc, dt=dt, wdb=wdb):
                    p.wait(g, "mm")
                    g.dma_start(out=wdb[0], in_=mwd[e][0:2048, dt * 128:(dt + 1) * 128]
                                .rearrange("(k p) t -> p k t", p=128)).then_inc(p.sem["wl"], 16)
                    g.dma_start(out=wdb[1], in_=mwd[e][2048:4096, dt * 128:(dt + 1) * 128]
                                .rearrange("(k p) t -> p k t", p=128)).then_inc(p.sem["wl"], 16)
                p.cnt["wl"] += 32

                @block.tensor
                def _(t, e=e, tc=tc, dt=dt, wdb=wdb):
                    p.wait(t, "wl")
                    p.wait(t, "sv")
                    for j in range(2):
                        ps = psA if j == 0 else psB
                        for fs in range(32):
                            wv = wdb[fs // 16][:, fs % 16, :]
                            ii = t.matmul(ps[:, :], wv, h2f[:, fs, j * 512:(j + 1) * 512],
                                          start=(fs == 0), stop=(fs == 31))
                        p.inc(ii, "mm")

                @block.vector
                def _(v, e=e, tc=tc, dt=dt):
                    base = p.cnt["mm"] - 2
                    for j in range(2):
                        ps = psA if j == 0 else psB
                        p.wait(v, "mm", base + j + 1)
                        col = tc * 1024 + j * 512
                        if e == 0:
                            ii = v.tensor_copy(macc8[dt][:, col:col + 512], ps[:, :])
                        else:
                            ii = v.tensor_tensor(macc8[dt][:, col:col + 512],
                                                 macc8[dt][:, col:col + 512], ps[:, :], ALU.add)
                        p.inc(ii, "sv")

    if upto in ("p8e", "p8f"):
        barrier()
        return p
    barrier()
    for dt in range(8):
        for tc in range(4):
            @block.gpsimd
            def _(g, dt=dt, tc=tc):
                p.wait(g, "so")
                g.dma_start(out=m_sx[:, 0:512], in_=sfin_d[dt][:, tc * 512:(tc + 1) * 512]).then_inc(p.sem["gd"], 16)
                g.dma_start(out=m_xx[:, 0:512], in_=x1_d[dt][:, tc * 512:(tc + 1) * 512]).then_inc(p.sem["gd"], 16)
            p.cnt["gd"] += 32

            @block.scalar
            def _(s, dt=dt, tc=tc):
                p.wait(s, "sv")
                p.wait(s, "so")
                ii = s.activation(sgt[:, :], macc8[dt][:, tc * 512:(tc + 1) * 512], AF.Silu)
                p.inc(ii, "ss")

            @block.vector
            def _(v, dt=dt, tc=tc):
                p.wait(v, "ss")
                p.wait(v, "gd")
                v.tensor_tensor(m_oo[:, 0:512], m_sx[:, 0:512], sgt[:, :], ALU.mult)
                ii = v.tensor_tensor(m_oo[:, 0:512], m_oo[:, 0:512], m_xx[:, 0:512], ALU.add)
                p.inc(ii, "sv")

            @block.sync
            def _(sy, dt=dt, tc=tc):
                p.wait(sy, "sv")
                ii = sy.dma_start(out=out[dt][:, tc * 512:(tc + 1) * 512], in_=m_oo[:, 0:512])
                p.inc(ii, "so", 16)

    @block.sync
    def _(sy):
        p.wait(sy, "so")

    return p


_cache = {}


def _get(name, builder):
    if name not in _cache:
        _cache[name] = builder()
    return _cache[name]


def run_launch(name, builder, in_maps):
    p = _get(name, builder)
    res = run_bass_kernel_spmd(p.nc, in_maps, core_ids=list(range(NCORES)))
    return res.results


def run_l0prep(x_by_core):
    return run_launch("l0prep", build_l0prep, [{"xin": x_by_core[c]} for c in range(NCORES)])


def _vec_pk(w):
    return np.ascontiguousarray(np.asarray(w, np.float32).reshape(8, 128).T)


MEGA = True


def _kernel_mega(inp, xs, wq2, wk2, wv2, wp2, bp2, cw2, waT, wiT, wvT,
                 opT, wguT, wdT, wrg, mwg, mwu, mwd):
    xfm_full = []
    for c in range(NCORES):
        xfm_full.append(np.ascontiguousarray(xs[c].T.reshape(8, 128, TF)))
    maps = []
    for c in range(NCORES):
        hm = np.full((128, 8), 0.0 if c % 2 == 0 else 1.0, np.float32)
        maps.append({
            "xin": xs[c], "xfm": xfm_full[c],
            "wqh": wq2, "wkh": wk2, "wvh": wv2, "wph": wp2, "bph": bp2,
            "wg": np.asarray(inp["L0_ffn_w_gate"], np.float32),
            "wu": np.asarray(inp["L0_ffn_w_up"], np.float32),
            "wd": np.asarray(inp["L0_ffn_w_down"], np.float32),
            "pnw": _vec_pk(inp["L1_attn_pre_norm_w"]), "cw": cw2,
            "cb": _vec_pk(inp["L1_attn_conv_b"]), "db": _vec_pk(inp["L1_attn_decay_bias"]),
            "waT": waT, "wiT": wiT, "wvT": wvT, "opT": opT,
            "fnw": _vec_pk(inp["L1_attn_ffn_norm_w"]),
            "wguT": wguT, "wdT": wdT, "wrg": wrg,
            "mwg": mwg, "mwu": mwu, "mwd": mwd, "hmask": hm,
        })
    res = run_launch("mega", build_mega, maps)
    out = np.empty((B, T, D), np.float32)
    for c in range(NCORES):
        b, h = c // 2, c % 2
        o = np.asarray(res[c]["out"])
        out[b, h * TO:(h + 1) * TO] = o.reshape(D, TO).T
    return out


def kernel(**inp):
    x = np.asarray(inp["L0_attn_query"], np.float32)
    xs = []
    xfm = []
    for c in range(NCORES):
        b, h = c // 2, c % 2
        own = x[b, h * TO:(h + 1) * TO]
        oth = x[b, (1 - h) * TO:(2 - h) * TO]
        xs.append(np.ascontiguousarray(np.concatenate([own, oth], 0)))
        xfm.append(np.ascontiguousarray(own.T.reshape(8, 128, TO)))
    wq2 = np.ascontiguousarray(np.asarray(inp["L0_attn_Wq"], np.float32).transpose(1, 0, 2).reshape(D, D))
    wk2 = np.ascontiguousarray(np.asarray(inp["L0_attn_Wk"], np.float32).transpose(1, 0, 2).reshape(D, D))
    wv2 = np.ascontiguousarray(np.asarray(inp["L0_attn_Wv"], np.float32).transpose(1, 0, 2).reshape(D, D))
    idx = (np.arange(D) % 64) * 16 + np.arange(D) // 64
    wp2 = np.ascontiguousarray(np.asarray(inp["L0_attn_Wp"], np.float32)[idx])
    bp2 = _vec_pk(inp["L0_attn_bp"])
    cw2 = np.ascontiguousarray(np.asarray(inp["L1_attn_conv_w"], np.float32)[:, 0, :]
                               .reshape(8, 128, 4).transpose(1, 0, 2).reshape(128, 32))
    waT = np.ascontiguousarray(np.asarray(inp["L1_attn_w_a"], np.float32).T)
    wiT = np.ascontiguousarray(np.asarray(inp["L1_attn_w_i"], np.float32).T)
    wvT = np.ascontiguousarray(np.asarray(inp["L1_attn_w_v"], np.float32).T)
    opT = np.ascontiguousarray(np.asarray(inp["L1_attn_out_proj_w"], np.float32).T)
    wguT = np.ascontiguousarray(np.asarray(inp["L1_attn_w_gate_up"], np.float32).T)
    wdT = np.ascontiguousarray(np.asarray(inp["L1_attn_w_down"], np.float32).T)
    wrg = np.ascontiguousarray(np.concatenate([np.asarray(inp["L1_ffn_w_router"], np.float32),
                                               np.asarray(inp["L1_ffn_w_gating"], np.float32)], 1))
    mwg = np.ascontiguousarray(np.asarray(inp["L1_ffn_w_gate"], np.float32))
    mwu = np.ascontiguousarray(np.asarray(inp["L1_ffn_w_up"], np.float32))
    mwd = np.ascontiguousarray(np.asarray(inp["L1_ffn_w_down"], np.float32))

    if MEGA:
        return _kernel_mega(inp, xs, wq2, wk2, wv2, wp2, bp2, cw2, waT, wiT, wvT,
                            opT, wguT, wdT, wrg, mwg, mwu, mwd)
    pre = run_launch("l0prep", build_l0prep, [{"xin": xs[c]} for c in range(NCORES)])
    att = run_launch("l0attn", build_l0attn,
                     [{"xnfm": np.asarray(pre[c]["xnfm"]), "wqh": wq2, "wkh": wk2,
                       "wvh": wv2} for c in range(NCORES)])
    wqz = run_launch("l0wq", build_l0wq,
                     [{"wg": np.asarray(inp["L0_ffn_w_gate"], np.float32),
                       "wu": np.asarray(inp["L0_ffn_w_up"], np.float32),
                       "wd": np.asarray(inp["L0_ffn_w_down"], np.float32)} for c in range(NCORES)])
    ffn = run_launch("l0ffn", build_l0ffn,
                     [{"xqfm": np.asarray(pre[c]["xqfm"]),
                       "wgq": np.asarray(wqz[c]["wgq"]),
                       "wuq": np.asarray(wqz[c]["wuq"]),
                       "wdq": np.asarray(wqz[c]["wdq"])} for c in range(NCORES)])
    comb = run_launch("l0comb", build_l0comb,
                      [{"aoin": np.asarray(att[c]["aout"]), "gin": np.asarray(ffn[c]["gout"]),
                        "xfm": xfm[c], "wph": wp2, "bph": bp2} for c in range(NCORES)])
    x1 = [np.asarray(comb[c]["x1"]) for c in range(NCORES)]
    ln = run_launch("l1ln", build_l1ln,
                    [{"x1": x1[c], "pnw": _vec_pk(inp["L1_attn_pre_norm_w"])}
                     for c in range(NCORES)])
    maps = []
    for c in range(NCORES):
        halo = (np.zeros((8, 128, 3), np.float32) if c % 2 == 0
                else np.asarray(ln[c - 1]["redge"]))
        maps.append({
            "sacc": np.asarray(ln[c]["sacc"]), "rstd2": np.asarray(ln[c]["rstd2"]),
            "halo": halo, "pnw": _vec_pk(inp["L1_attn_pre_norm_w"]),
            "cw": cw2, "cb": _vec_pk(inp["L1_attn_conv_b"]),
            "db": _vec_pk(inp["L1_attn_decay_bias"]),
            "waT": waT, "wiT": wiT, "wvT": wvT,
        })
    rec = run_launch("l1rec", build_l1rec, maps)
    maps2 = []
    for c in range(NCORES):
        s_in = (np.zeros((128, 8), np.float32) if c % 2 == 0
                else np.asarray(rec[c - 1]["sout"]))
        maps2.append({
            "sacc": np.asarray(ln[c]["sacc"]),
            "recl": np.asarray(rec[c]["recl"]), "Fin": np.asarray(rec[c]["Fout"]),
            "sin": s_in, "opT": opT, "fnw": _vec_pk(inp["L1_attn_ffn_norm_w"]),
            "wguT": wguT, "wdT": wdT,
        })
    fix = run_launch("l1fix", build_l1fix, maps2)
    moe = run_launch("l1moe", build_l1moe,
                     [{"xn1": np.asarray(ln[c]["xn1"]), "sfin": np.asarray(fix[c]["sfin"]),
                       "x1": x1[c], "wrg": wrg, "mwg": mwg, "mwu": mwu, "mwd": mwd}
                      for c in range(NCORES)])

    out = np.empty((B, T, D), np.float32)
    for c in range(NCORES):
        b, h = c // 2, c % 2
        o = np.asarray(moe[c]["out"])
        out[b, h * TO:(h + 1) * TO] = o.reshape(D, TO).T
    return out


# version 07 (final)
# speedup vs baseline: 1.1346x; 1.0842x over previous
import math
import numpy as np
import ml_dtypes
from contextlib import ExitStack

import concourse.bass as bass
import concourse.mybir as mybir
from concourse.bass_utils import run_bass_kernel_spmd

F32 = mybir.dt.float32
BF16 = mybir.dt.bfloat16
I32 = mybir.dt.int32
AF = mybir.ActivationFunctionType
ALU = mybir.AluOpType
AX = mybir.AxisListType
BF = ml_dtypes.bfloat16

B, T, D = 4, 4096, 1024
TO = 2048
NT_OWN = TO // 128
NT_ALL = T // 128
H, E = 16, 64
FFI = 2816
MAGIC = float(np.float32(3 * 2 ** 22))

NCORES = 8


class P:

    def __init__(self, name):
        self.nc = bass.Bass("TRN2", target_bir_lowering=False)
        self.stack = ExitStack()
        self.sem = {}
        self.cnt = {}
        self.block = None

    def __enter__(self):
        return self

    def sbuf(self, name, shape, dtype):
        return self.stack.enter_context(self.nc.sbuf_tensor(name, shape, dtype))

    def psum(self, name, shape, dtype=F32):
        return self.stack.enter_context(self.nc.psum_tensor(name, shape, dtype))

    def new_sem(self, name):
        h = self.stack.enter_context(self.nc.semaphore(name))
        self.sem[name] = h
        self.cnt[name] = 0
        return h

    def inc(self, instr, name, n=1):
        self.cnt[name] += n
        instr.then_inc(self.sem[name], n)
        return self.cnt[name]

    def tgt(self, name):
        return self.cnt[name]

    def wait(self, eng, name, val=None):
        eng.wait_ge(self.sem[name], self.cnt[name] if val is None else val)

    def guard(self, eng, inst, gname="vg"):
        c = self.inc(inst, gname)
        eng.wait_ge(self.sem[gname], c)
        return inst

    def begin(self):
        self.block = self.stack.enter_context(self.nc.Block())

    def close(self):
        self.stack.close()



def build_l0prep():
    p = P("l0prep")
    nc = p.nc
    xin = nc.dram_tensor("xin", [T, D], F32, kind="ExternalInput").ap()
    xnfm = nc.dram_tensor("xnfm", [8, 128, T], BF16, kind="ExternalOutput").ap()
    xqfm = nc.dram_tensor("xqfm", [8, 128, TO], BF16, kind="ExternalOutput").ap()

    xt = [p.sbuf(f"xt{i}", [128, D], F32) for i in range(2)]
    sq = p.sbuf("sq", [128, D], F32)
    st1 = p.sbuf("st1", [128, 8], F32)
    xn_t = p.sbuf("xn_t", [128, D], F32)
    xnb = p.sbuf("xnb", [128, D], BF16)
    y2 = p.sbuf("y2", [128, D], F32)
    qlo = p.sbuf("qlo", [128, D], F32)
    qhi = p.sbuf("qhi", [128, D], F32)
    qmask = p.sbuf("qmask", [128, D], I32)
    qsel = p.sbuf("qsel", [128, D], F32)
    bm = p.sbuf("bm", [128, D // 16], F32)
    rs2 = p.sbuf("rs2", [128, D // 16], F32)
    rsh = p.sbuf("rsh", [128, D // 16], F32)
    xqb = p.sbuf("xqb", [128, D], BF16)
    xn_sb = p.sbuf("xn_sb", [128, 8 * T], BF16)
    xq_sb = p.sbuf("xq_sb", [128, 8 * TO], BF16)

    ld = p.new_sem("ld")
    vg = p.new_sem("vg")
    sv = p.new_sem("sv")
    ss = p.new_sem("ss")
    tp = p.new_sem("tp")
    so = p.new_sem("so")

    p.begin()
    block = p.block
    xn_v = xn_sb[:].rearrange("p (k t) -> p k t", k=8)
    xq_v = xq_sb[:].rearrange("p (k t) -> p k t", k=8)

    for i in range(NT_ALL):
        own = i < NT_OWN
        buf = xt[i % 2]

        @block.gpsimd
        def _(g, i=i, buf=buf):
            if i >= 2:
                p.wait(g, "sv", p.tgt("sv"))
            g.dma_start(out=buf[:], in_=xin[i * 128:(i + 1) * 128, :]).then_inc(p.sem["ld"], 16)
        p.cnt["ld"] += 16

        @block.vector
        def _(v, i=i, buf=buf, own=own):
            p.wait(v, "ld")
            p.guard(v, v.tensor_reduce(st1[:, 0:1], buf[:], axis=AX.X, op=ALU.add))
            v.tensor_tensor(sq[:], buf[:], buf[:], ALU.mult)
            p.guard(v, v.tensor_reduce(st1[:, 1:2], sq[:], axis=AX.X, op=ALU.add))
            p.guard(v, v.tensor_scalar(out=st1[:, 2:3], in0=st1[:, 0:1], scalar1=1.0 / D, scalar2=None, op0=ALU.mult))
            p.guard(v, v.tensor_scalar(out=st1[:, 3:4], in0=st1[:, 1:2], scalar1=1.0 / D, scalar2=None, op0=ALU.mult))
            p.guard(v, v.tensor_tensor(st1[:, 4:5], st1[:, 2:3], st1[:, 2:3], ALU.mult))
            p.guard(v, v.tensor_tensor(st1[:, 5:6], st1[:, 3:4], st1[:, 4:5], ALU.subtract))
            p.guard(v, v.tensor_scalar(out=st1[:, 5:6], in0=st1[:, 5:6], scalar1=1e-5,
                                       scalar2=None, op0=ALU.add))

        @block.scalar
        def _(s, i=i):
            p.wait(s, "vg")
            i0 = s.activation(st1[:, 6:7], st1[:, 5:6], AF.Sqrt)
            p.inc(i0, "ss")

        @block.vector
        def _(v, i=i, buf=buf, own=own):
            p.wait(v, "ss")
            p.wait(v, "tp")
            p.guard(v, v.reciprocal(st1[:, 7:8], st1[:, 6:7]))
            v.tensor_scalar(out=xn_t[:], in0=buf[:], scalar1=st1[:, 2:3], scalar2=st1[:, 7:8],
                            op0=ALU.subtract, op1=ALU.mult)
            i1 = v.tensor_copy(xnb[:], xn_t[:])
            p.inc(i1, "sv")
            if own:
                x3 = xn_t[:].rearrange("p (n b) -> p n b", b=16)
                p.guard(v, v.tensor_reduce(bm[:].unsqueeze(2), x3, axis=AX.X, op=ALU.max,
                                           apply_absolute_value=True))
                p.guard(v, v.tensor_scalar(out=bm[:], in0=bm[:], scalar1=6e-12, scalar2=None, op0=ALU.max))
                p.guard(v, v.reciprocal(rs2[:], bm[:]))
                p.guard(v, v.tensor_scalar(out=rs2[:], in0=rs2[:], scalar1=12.0, scalar2=None, op0=ALU.mult))
                p.guard(v, v.tensor_scalar(out=rsh[:], in0=bm[:], scalar1=1.0 / 12.0, scalar2=None, op0=ALU.mult))
                y2v = y2[:].rearrange("p (n b) -> p n b", b=16)
                v.tensor_tensor(y2v, x3, rs2[:].unsqueeze(2).broadcast_to((128, 64, 16)), ALU.mult)
                v.tensor_scalar(out=qlo[:], in0=y2[:], scalar1=MAGIC, scalar2=-MAGIC,
                                op0=ALU.add, op1=ALU.add)
                v.tensor_scalar(out=qhi[:].bitcast(I32), in0=y2[:].bitcast(I32),
                                scalar1=0x00200000, scalar2=None, op0=ALU.add)
                v.tensor_scalar(out=qhi[:].bitcast(I32), in0=qhi[:].bitcast(I32),
                                scalar1=-0x00400000, scalar2=None, op0=ALU.bitwise_and)
                v.tensor_scalar(out=sq[:].bitcast(I32), in0=y2[:].bitcast(I32),
                                scalar1=0x7FFFFFFF, scalar2=None, op0=ALU.bitwise_and)
                v.tensor_scalar(out=qmask[:], in0=sq[:],
                                scalar1=1.75, scalar2=None, op0=ALU.is_lt)
                v.select(out=qsel[:], mask=qmask[:], on_true=qlo[:], on_false=qhi[:])
                qv = qsel[:].rearrange("p (n b) -> p n b", b=16)
                xv = xqb[:].rearrange("p (n b) -> p n b", b=16)
                i2 = v.tensor_tensor(xv, qv, rsh[:].unsqueeze(2).broadcast_to((128, 64, 16)), ALU.mult)
                p.inc(i2, "sv")

        @block.scalar
        def _(s, i=i, own=own):
            p.wait(s, "sv")
            for k in range(8):
                ii = s.dma_start_transpose(xn_v[:, k, i * 128:(i + 1) * 128],
                                           xnb[:, k * 128:(k + 1) * 128])
                p.inc(ii, "tp", 16)
            if own:
                for k in range(8):
                    ii = s.dma_start_transpose(xq_v[:, k, i * 128:(i + 1) * 128],
                                               xqb[:, k * 128:(k + 1) * 128])
                    p.inc(ii, "tp", 16)

    @block.sync
    def _(sy):
        p.wait(sy, "tp")
        for k in range(8):
            ii = sy.dma_start(out=xnfm[k], in_=xn_v[:, k, :])
            p.inc(ii, "so", 16)
            ii = sy.dma_start(out=xqfm[k], in_=xq_v[:, k, :])
            p.inc(ii, "so", 16)
        p.wait(sy, "so")

    return p



def emit_quant(p, v, src_f32, dst_bf, tmp, C):
    x3 = src_f32.rearrange("p (n b) -> p n b", b=16)
    bm, rs2, rsh = tmp["bm"], tmp["rs2"], tmp["rsh"]
    nb = C // 16
    p.guard(v, v.tensor_reduce(bm[:, :nb].unsqueeze(2), x3, axis=AX.X, op=ALU.max,
                               apply_absolute_value=True))
    p.guard(v, v.tensor_scalar(out=bm[:, :nb], in0=bm[:, :nb], scalar1=6e-12, scalar2=None, op0=ALU.max))
    p.guard(v, v.reciprocal(rs2[:, :nb], bm[:, :nb]))
    p.guard(v, v.tensor_scalar(out=rs2[:, :nb], in0=rs2[:, :nb], scalar1=12.0, scalar2=None, op0=ALU.mult))
    p.guard(v, v.tensor_scalar(out=rsh[:, :nb], in0=bm[:, :nb], scalar1=1.0 / 12.0, scalar2=None, op0=ALU.mult))
    y2, qlo, qhi, qmask, qsel, sabs = (tmp["y2"], tmp["qlo"], tmp["qhi"],
                                       tmp["qmask"], tmp["qsel"], tmp["sabs"])
    y2v = y2[:, :C].rearrange("p (n b) -> p n b", b=16)
    v.tensor_tensor(y2v, x3, rs2[:, :nb].unsqueeze(2).broadcast_to((128, nb, 16)), ALU.mult)
    v.tensor_scalar(out=qlo[:, :C], in0=y2[:, :C], scalar1=MAGIC, scalar2=-MAGIC,
                    op0=ALU.add, op1=ALU.add)
    v.tensor_scalar(out=qhi[:, :C].bitcast(I32), in0=y2[:, :C].bitcast(I32),
                    scalar1=0x00200000, scalar2=None, op0=ALU.add)
    v.tensor_scalar(out=qhi[:, :C].bitcast(I32), in0=qhi[:, :C].bitcast(I32),
                    scalar1=-0x00400000, scalar2=None, op0=ALU.bitwise_and)
    v.tensor_scalar(out=sabs[:, :C].bitcast(I32), in0=y2[:, :C].bitcast(I32),
                    scalar1=0x7FFFFFFF, scalar2=None, op0=ALU.bitwise_and)
    v.tensor_scalar(out=qmask[:, :C], in0=sabs[:, :C], scalar1=1.75, scalar2=None, op0=ALU.is_lt)
    v.select(out=qsel[:, :C], mask=qmask[:, :C], on_true=qlo[:, :C], on_false=qhi[:, :C])
    qv = qsel[:, :C].rearrange("p (n b) -> p n b", b=16)
    xv = dst_bf.rearrange("p (n b) -> p n b", b=16)
    return v.tensor_tensor(xv, qv, rsh[:, :nb].unsqueeze(2).broadcast_to((128, nb, 16)), ALU.mult)


def build_l0wq():
    p = P("l0wq")
    nc = p.nc
    wg = nc.dram_tensor("wg", [FFI, D], F32, kind="ExternalInput").ap()
    wu = nc.dram_tensor("wu", [FFI, D], F32, kind="ExternalInput").ap()
    wd = nc.dram_tensor("wd", [D, FFI], F32, kind="ExternalInput").ap()
    wgq = nc.dram_tensor("wgq", [8, 128, FFI], BF16, kind="ExternalOutput").ap()
    wuq = nc.dram_tensor("wuq", [8, 128, FFI], BF16, kind="ExternalOutput").ap()
    wdq = nc.dram_tensor("wdq", [22, 128, D], BF16, kind="ExternalOutput").ap()

    CMAX = FFI
    wt = [p.sbuf(f"wt{i}", [128, CMAX], F32) for i in range(2)]
    tmp = {k: p.sbuf(k, [128, CMAX], F32) for k in ("y2", "qlo", "qhi", "qsel", "sabs")}
    tmp["qmask"] = p.sbuf("qmask", [128, CMAX], I32)
    for k in ("bm", "rs2", "rsh"):
        tmp[k] = p.sbuf(k, [128, CMAX // 16], F32)
    qb = [p.sbuf(f"qb{i}", [128, CMAX], BF16) for i in range(2)]
    asm = p.sbuf("asm", [128, 22528], BF16)

    ld = p.new_sem("ld")
    vg = p.new_sem("vg")
    sv = p.new_sem("sv")
    tp = p.new_sem("tp")
    so = p.new_sem("so")

    p.begin()
    block = p.block

    jobs = [(wg, wgq, FFI, D), (wu, wuq, FFI, D), (wd, wdq, D, FFI)]
    for src, dst, R, C in jobs:
        nr = R // 128
        asm_v = asm[:].rearrange("p (k r) -> p k r", k=C // 128)
        for r in range(nr):
            buf = wt[r % 2]
            qbb = qb[r % 2]

            @block.gpsimd
            def _(g, src=src, r=r, buf=buf, C=C):
                p.wait(g, "sv")
                g.dma_start(out=buf[:, :C], in_=src[r * 128:(r + 1) * 128, :]).then_inc(p.sem["ld"], 16)
            p.cnt["ld"] += 16

            @block.vector
            def _(v, r=r, buf=buf, qbb=qbb, C=C):
                p.wait(v, "ld")
                p.wait(v, "tp")
                i2 = emit_quant(p, v, buf[:, :C], qbb[:, :C], tmp, C)
                p.inc(i2, "sv")

            @block.scalar
            def _(s, r=r, qbb=qbb, C=C):
                p.wait(s, "sv")
                for k in range(C // 128):
                    ii = s.dma_start_transpose(asm_v[:, k, r * 128:(r + 1) * 128],
                                               qbb[:, k * 128:(k + 1) * 128])
                    p.inc(ii, "tp", 16)

        @block.sync
        def _(sy, dst=dst, R=R, C=C):
            p.wait(sy, "tp")
            for k in range(C // 128):
                ii = sy.dma_start(out=dst[k], in_=asm[:].rearrange("p (k r) -> p k r", k=C // 128)[:, k, :R])
                p.inc(ii, "so", 16)
            p.wait(sy, "so")

    return p



def build_l0attn():
    p = P("l0attn")
    nc = p.nc
    xnfm = nc.dram_tensor("xnfm", [8, 128, T], BF16, kind="ExternalInput").ap()
    wqh = nc.dram_tensor("wqh", [D, D], F32, kind="ExternalInput").ap()
    wkh = nc.dram_tensor("wkh", [D, D], F32, kind="ExternalInput").ap()
    wvh = nc.dram_tensor("wvh", [D, D], F32, kind="ExternalInput").ap()
    aout = nc.dram_tensor("aout", [8, 128, TO], BF16, kind="ExternalOutput").ap()
    nmx_scr = nc.dram_tensor("nmx_scr", [16, 128], BF16, kind="Internal").ap()

    xn = p.sbuf("xn", [128, 8 * T], BF16)
    wq = p.sbuf("wq", [128, 8 * D], BF16)
    wk = p.sbuf("wk", [128, 8 * D], BF16)
    wv = p.sbuf("wv", [128, 8 * D], BF16)
    Kb = [p.sbuf(f"Kb{i}", [128, T], BF16) for i in range(2)]
    Qb = [p.sbuf(f"Qb{i}", [128, TO], BF16) for i in range(2)]
    Vb = p.sbuf("Vb", [128, 32 * 130], BF16)
    att = [p.sbuf(f"att{i}", [128, 1024], BF16) for i in range(2)]
    mx8 = p.sbuf("mx8", [128, 8], F32)
    nmx = p.sbuf("nmx", [128, 16], BF16)
    nmxt = p.sbuf("nmxt", [16, 128], BF16)
    ident = p.sbuf("ident", [128, 128], BF16)
    identi = p.sbuf("identi", [128, 128], I32)
    ones1 = p.sbuf("ones1", [1, 64], F32)
    r1 = p.sbuf("r1", [1, 1024], F32)
    r1b = p.sbuf("r1b", [1, 1024], BF16)
    bc_s = p.sbuf("bc_s", [64, 1024], F32)
    ao = p.sbuf("ao", [128, 8 * TO], BF16)

    psA = p.psum("psA", [128, 512])
    psB = p.psum("psB", [128, 512])
    psC = p.psum("psC", [128, 1024])
    psD = p.psum("psD", [128, 1024])
    psE = p.psum("psE", [128, 1024])

    ld = p.new_sem("ld")
    vg = p.new_sem("vg")
    mm = p.new_sem("mm")
    pv = p.new_sem("pv")
    sv = p.new_sem("sv")
    ss = p.new_sem("ss")
    gd = p.new_sem("gd")
    so = p.new_sem("so")

    p.begin()
    block = p.block
    xn_v = xn[:].rearrange("p (k t) -> p k t", k=8)
    wq_v = wq[:].rearrange("p (k t) -> p k t", k=8)
    wk_v = wk[:].rearrange("p (k t) -> p k t", k=8)
    wv_v = wv[:].rearrange("p (k t) -> p k t", k=8)
    Vb_v = Vb[:].rearrange("p (m c) -> p m c", c=130)
    ao_v = ao[:].rearrange("p (k t) -> p k t", k=8)
    NQT = TO // 128
    NQC = TO // 1024

    @block.gpsimd
    def _(g):
        for k in range(8):
            g.dma_start(out=xn_v[:, k, :], in_=xnfm[k]).then_inc(p.sem["ld"], 16)
            g.dma_start(out=wq_v[:, k, :], in_=wqh[k * 128:(k + 1) * 128, :]).then_inc(p.sem["ld"], 16)
            g.dma_start(out=wk_v[:, k, :], in_=wkh[k * 128:(k + 1) * 128, :]).then_inc(p.sem["ld"], 16)
            g.dma_start(out=wv_v[:, k, :], in_=wvh[k * 128:(k + 1) * 128, :]).then_inc(p.sem["ld"], 16)
        g.iota(identi[:], pattern=[[1, 128]], base=0, channel_multiplier=-1,
               allow_small_or_imprecise_dtypes=True)
        ii = g.tensor_scalar(out=ident[:], in0=identi[:], scalar1=0, scalar2=None, op0=ALU.is_equal)
        p.inc(ii, "gd")
    p.cnt["ld"] += 16 * 32

    @block.vector
    def _(v):
        v.memset(ones1[:], 1.0)
        v.memset(Vb_v[:, :, 64:65].squeeze(2), 1.0)
        ii = v.memset(Vb_v[:, :, 129:130].squeeze(2), 1.0)
        p.inc(ii, "sv")

    for hp in range(8):
        @block.tensor
        def _(t, hp=hp):
            p.wait(t, "ld")
            p.wait(t, "sv")
            for mt in range(32):
                dst = psC[:, 0:128] if mt % 2 == 0 else psC[:, 512:640]
                if mt >= 2:
                    p.wait(t, "sv", p.cnt["sv"] + mt - 1)
                for sub in range(8):
                    ii = t.matmul(dst, xn_v[:, sub, mt * 128:(mt + 1) * 128],
                                  wv_v[:, sub, hp * 128:(hp + 1) * 128],
                                  start=(sub == 0), stop=(sub == 7))
                p.inc(ii, "mm")

        @block.vector
        def _(v, hp=hp):
            base = p.cnt["mm"] - 32
            for mt in range(32):
                src = psC[:, 0:128] if mt % 2 == 0 else psC[:, 512:640]
                p.wait(v, "mm", base + mt + 1)
                v.tensor_copy(Vb_v[:, mt, 0:64], src[:, 0:64])
                ii = v.tensor_copy(Vb_v[:, mt, 65:129], src[:, 64:128])
                p.inc(ii, "sv")

        for hh in range(2):
            h = hp * 2 + hh
            K_h = Kb[hh]
            Q_h = Qb[hh]
            eoff = 65 * hh

            @block.vector
            def _(v, K_h=K_h):
                ii = v.memset(K_h[64:65, :], 1.0)
                p.inc(ii, "sv")

            @block.tensor
            def _(t, h=h):
                p.wait(t, "sv")
                p.wait(t, "ss")
                for mc in range(8):
                    ps = psA if mc % 2 == 0 else psB
                    if mc >= 2:
                        p.wait(t, "ss", p.cnt["ss"] + mc - 1)
                    for sub in range(8):
                        ii = t.matmul(ps[0:64, :], wk_v[:, sub, h * 64:(h + 1) * 64],
                                      xn_v[:, sub, mc * 512:(mc + 1) * 512],
                                      start=(sub == 0), stop=(sub == 7))
                    p.inc(ii, "mm")
                for qc in range(TO // 512):
                    ps = psA if qc % 2 == 0 else psB
                    p.wait(t, "ss", p.cnt["ss"] + 6 + qc + 1)
                    for sub in range(8):
                        ii = t.matmul(ps[0:64, :], wq_v[:, sub, h * 64:(h + 1) * 64],
                                      xn_v[:, sub, qc * 512:(qc + 1) * 512],
                                      start=(sub == 0), stop=(sub == 7))
                    p.inc(ii, "mm")

            @block.scalar
            def _(s, h=h, K_h=K_h, Q_h=Q_h):
                base = p.cnt["mm"] - 8 - TO // 512
                for mc in range(8):
                    ps = psA if mc % 2 == 0 else psB
                    p.wait(s, "mm", base + 1 + mc)
                    ii = s.copy(out=K_h[0:64, mc * 512:(mc + 1) * 512], in_=ps[0:64, :])
                    p.inc(ii, "ss")
                for qc in range(TO // 512):
                    ps = psA if qc % 2 == 0 else psB
                    p.wait(s, "mm", base + 9 + qc)
                    ii = s.activation(Q_h[0:64, qc * 512:(qc + 1) * 512], ps[0:64, :],
                                      AF.Copy, scale=0.125)
                    p.inc(ii, "ss")

            @block.tensor
            def _(t, h=h, K_h=K_h, Q_h=Q_h):
                p.wait(t, "ss")
                for qt in range(NQT):
                    for mc in range(8):
                        n = qt * 8 + mc
                        ps = psA if n % 2 == 0 else psB
                        if n >= 2:
                            p.wait(t, "sv", p.cnt["sv"] + n - 1)
                        ii = t.matmul(ps[:, :], Q_h[0:64, qt * 128:(qt + 1) * 128],
                                      K_h[0:64, mc * 512:(mc + 1) * 512], start=True, stop=True)
                        p.inc(ii, "mm")

            @block.vector
            def _(v, h=h):
                base = p.cnt["mm"] - 8 * NQT
                gbase = p.cnt["sv"]
                n = 0
                for qt in range(NQT):
                    for mc in range(8):
                        n += 1
                        ps = psA if (n - 1) % 2 == 0 else psB
                        p.wait(v, "mm", base + n)
                        ii = v.tensor_reduce(mx8[:, mc:mc + 1], ps[:, :], axis=AX.X, op=ALU.max)
                        p.inc(ii, "sv")
                        v.wait_ge(p.sem["sv"], p.cnt["sv"])
                    ii = p.guard(v, v.tensor_reduce(nmx[:, qt:qt + 1], mx8[:, :],
                                                    axis=AX.X, op=ALU.max, negate=True))

            @block.tensor
            def _(t, h=h):
                p.wait(t, "vg")
                p.wait(t, "gd", 1)
                psT = psA[0:16, 0:64].bitcast(BF16)
                ii = t.transpose(psT, nmx[:, 0:NQT], ident[:])
                p.inc(ii, "mm")

            @block.vector
            def _(v, h=h):
                p.wait(v, "mm")
                ii = v.tensor_copy(nmxt[0:NQT, :], psA[0:16, 0:64].bitcast(BF16))
                p.inc(ii, "sv")

            @block.gpsimd
            def _(g, h=h, Q_h=Q_h):
                p.wait(g, "sv")
                ii = g.dma_start(out=nmx_scr, in_=nmxt[0:NQT, :])
                p.inc(ii, "gd", 16)
                p.wait(g, "gd")
                ii = g.dma_start(out=Q_h[64:65, :], in_=nmx_scr.rearrange("a b -> (a b)").unsqueeze(0))
                p.inc(ii, "gd", 16)

            for qc in range(NQC):
                qk_base = p.cnt["mm"]
                ss_base = p.cnt["ss"]
                pv_base = p.cnt["pv"]
                sv_base = p.cnt["sv"]

                @block.tensor
                def _(t, h=h, qc=qc, K_h=K_h, Q_h=Q_h, eoff=eoff,
                      qk_base=qk_base, ss_base=ss_base, sv_base=sv_base):
                    if qc == 0:
                        p.wait(t, "gd")
                    else:
                        p.wait(t, "sv", sv_base)
                    def qk(mt):
                        psl = psC if mt % 2 == 0 else psD
                        for j in range(2):
                            ii = t.matmul(psl[:, j * 512:(j + 1) * 512],
                                          K_h[0:65, mt * 128:(mt + 1) * 128],
                                          Q_h[0:65, qc * 1024 + j * 512: qc * 1024 + (j + 1) * 512],
                                          start=True, stop=True)
                        p.inc(ii, "mm")
                    qk(0)
                    qk(1)
                    for mt in range(32):
                        p.wait(t, "ss", ss_base + mt + 1)
                        ab = att[mt % 2]
                        for j in range(2):
                            ii = t.matmul(psE[0:65, j * 512:(j + 1) * 512],
                                          Vb_v[:, mt, eoff:eoff + 65],
                                          ab[:, j * 512:(j + 1) * 512],
                                          start=(mt == 0), stop=(mt == 31))
                        p.inc(ii, "pv")
                        if mt + 2 < 32:
                            qk(mt + 2)

                @block.scalar
                def _(s, h=h, qc=qc, qk_base=qk_base, pv_base=pv_base):
                    for mt in range(32):
                        psl = psC if mt % 2 == 0 else psD
                        ab = att[mt % 2]
                        p.wait(s, "mm", qk_base + mt + 1)
                        if mt >= 2:
                            p.wait(s, "pv", pv_base + mt - 1)
                        ii = s.activation(ab[:, :], psl[:, :], AF.Exp)
                        p.inc(ii, "ss")

                @block.vector
                def _(v, h=h, qc=qc):
                    p.wait(v, "pv")
                    p.guard(v, v.reciprocal(r1[:, :], psE[64:65, 0:1024]))

                @block.tensor
                def _(t, h=h, qc=qc):
                    p.wait(t, "vg")
                    for j in range(2):
                        ii = t.matmul(psA[0:64, :] if j == 0 else psB[0:64, :],
                                      ones1[:, :], r1[:, j * 512:(j + 1) * 512],
                                      start=True, stop=True)
                    p.inc(ii, "mm")

                @block.vector
                def _(v, h=h, qc=qc):
                    p.wait(v, "mm")
                    v.tensor_copy(bc_s[:, 0:512], psA[0:64, :])
                    v.tensor_copy(bc_s[:, 512:1024], psB[0:64, :])
                    ii = v.tensor_tensor(
                        ao_v[:, h // 2, qc * 1024:(qc + 1) * 1024][64 * (h % 2):64 * (h % 2) + 64, :],
                        psE[0:64, :], bc_s[:, :], ALU.mult)
                    p.inc(ii, "sv")

    @block.sync
    def _(sy):
        p.wait(sy, "sv")
        for k in range(8):
            ii = sy.dma_start(out=aout[k], in_=ao_v[:, k, :])
            p.inc(ii, "so", 16)
        p.wait(sy, "so")

    return p



def build_l0ffn():
    p = P("l0ffn")
    nc = p.nc
    xqfm = nc.dram_tensor("xqfm", [8, 128, TO], BF16, kind="ExternalInput").ap()
    wgq = nc.dram_tensor("wgq", [8, 128, FFI], BF16, kind="ExternalInput").ap()
    wuq = nc.dram_tensor("wuq", [8, 128, FFI], BF16, kind="ExternalInput").ap()
    wdq = nc.dram_tensor("wdq", [22, 128, D], BF16, kind="ExternalInput").ap()
    gout = nc.dram_tensor("gout", [8, 128, TO], BF16, kind="ExternalOutput").ap()

    xq = p.sbuf("xq", [128, 8 * TO], BF16)
    wgb = [p.sbuf(f"wgb{i}", [128, 8 * 512], BF16) for i in range(2)]
    wub = [p.sbuf(f"wub{i}", [128, 8 * 512], BF16) for i in range(2)]
    sg = p.sbuf("sg", [128, 512], F32)
    hrow = p.sbuf("hrow", [128, 512], F32)
    tmp = {k: p.sbuf(k, [128, 512], F32) for k in ("y2", "qlo", "qhi", "sabs")}
    tmp["qsel"] = tmp["y2"]
    wdb = [wgb[0], wgb[1]]
    tmp["qmask"] = p.sbuf("qmask", [128, 512], I32)
    for k in ("bm", "rs2", "rsh"):
        tmp[k] = p.sbuf(k, [128, 32], F32)
    hqb = p.sbuf("hqb", [128, 512], BF16)
    hq = p.sbuf("hq", [128, 22 * TO], BF16)
    gb = p.sbuf("gb", [128, 8 * TO], BF16)

    psG = p.psum("psG", [128, 512])
    psU = p.psum("psU", [128, 512])
    psD_ = [p.psum(f"psD{i}", [128, 512]) for i in range(2)]

    ld = p.new_sem("ld")
    vg = p.new_sem("vg")
    mm = p.new_sem("mm")
    sv = p.new_sem("sv")
    ss = p.new_sem("ss")
    tp = p.new_sem("tp")
    wl = p.new_sem("wl")
    so = p.new_sem("so")

    p.begin()
    block = p.block
    xq_v = xq[:].rearrange("p (k t) -> p k t", k=8)
    hq_v = hq[:].rearrange("p (k t) -> p k t", k=22)
    gb_v = gb[:].rearrange("p (k t) -> p k t", k=8)
    NTT = TO // 128

    @block.gpsimd
    def _(g):
        for k in range(8):
            g.dma_start(out=xq_v[:, k, :], in_=xqfm[k]).then_inc(p.sem["ld"], 16)
    p.cnt["ld"] += 16 * 8

    FCH = [(i * 512, min(512, FFI - i * 512)) for i in range((FFI + 511) // 512)]

    for fi, (f0, fw) in enumerate(FCH):
        wg_c = wgb[fi % 2]
        wu_c = wub[fi % 2]

        @block.gpsimd
        def _(g, fi=fi, f0=f0, fw=fw, wg_c=wg_c, wu_c=wu_c):
            if fi >= 2:
                p.wait(g, "mm")
            for k in range(8):
                g.dma_start(out=wg_c[:].rearrange("p (k t) -> p k t", k=8)[:, k, 0:fw],
                            in_=wgq[k][:, f0:f0 + fw]).then_inc(p.sem["wl"], 16)
                g.dma_start(out=wu_c[:].rearrange("p (k t) -> p k t", k=8)[:, k, 0:fw],
                            in_=wuq[k][:, f0:f0 + fw]).then_inc(p.sem["wl"], 16)
        p.cnt["wl"] += 16 * 16

        for tt in range(NTT):
            @block.tensor
            def _(t, fi=fi, f0=f0, fw=fw, tt=tt, wg_c=wg_c, wu_c=wu_c):
                p.wait(t, "ld")
                p.wait(t, "wl")
                if fi + tt > 0:
                    p.wait(t, "sv")
                wgv = wg_c[:].rearrange("p (k t) -> p k t", k=8)
                wuv = wu_c[:].rearrange("p (k t) -> p k t", k=8)
                for sub in range(8):
                    ii = t.matmul(psG[:, 0:fw], xq_v[:, sub, tt * 128:(tt + 1) * 128],
                                  wgv[:, sub, 0:fw], start=(sub == 0), stop=(sub == 7))
                for sub in range(8):
                    ii = t.matmul(psU[:, 0:fw], xq_v[:, sub, tt * 128:(tt + 1) * 128],
                                  wuv[:, sub, 0:fw], start=(sub == 0), stop=(sub == 7))
                p.inc(ii, "mm")

            @block.scalar
            def _(s, fi=fi, fw=fw, tt=tt):
                p.wait(s, "mm")
                ii = s.activation(sg[:, 0:fw], psG[:, 0:fw], AF.Silu)
                p.inc(ii, "ss")

            @block.vector
            def _(v, fi=fi, f0=f0, fw=fw, tt=tt):
                p.wait(v, "ss")
                p.wait(v, "tp")
                i0 = v.tensor_tensor(hrow[:, 0:fw], sg[:, 0:fw], psU[:, 0:fw], ALU.mult)
                p.inc(i0, "sv")
                i1 = emit_quant(p, v, hrow[:, 0:fw], hqb[:, 0:fw], tmp, fw)
                p.inc(i1, "sv")

            @block.scalar
            def _(s, fi=fi, f0=f0, fw=fw, tt=tt):
                p.wait(s, "sv")
                for j in range(fw // 128):
                    ii = s.dma_start_transpose(
                        hq_v[:, (f0 + j * 128) // 128, tt * 128:(tt + 1) * 128],
                        hqb[:, j * 128:(j + 1) * 128])
                    p.inc(ii, "tp", 16)

    for dt in range(8):
        wd_c = wdb[dt % 2]

        @block.gpsimd
        def _(g, dt=dt, wd_c=wd_c):
            p.wait(g, "mm")
            for fs in range(22):
                g.dma_start(out=wd_c[:, 0:2816].rearrange("p (k t) -> p k t", k=22)[:, fs, :],
                            in_=wdq[fs][:, dt * 128:(dt + 1) * 128]).then_inc(p.sem["wl"], 16)
        p.cnt["wl"] += 16 * 22

        for tc in range(TO // 512):
            @block.tensor
            def _(t, dt=dt, tc=tc, wd_c=wd_c):
                p.wait(t, "wl")
                if dt == 0 and tc == 0:
                    p.wait(t, "tp")
                ps = psD_[tc % 2]
                if dt + tc > 0:
                    p.wait(t, "ss")
                wdv = wd_c[:, 0:2816].rearrange("p (k t) -> p k t", k=22)
                for fs in range(22):
                    ii = t.matmul(ps[:, :], wdv[:, fs, :],
                                  hq_v[:, fs, tc * 512:(tc + 1) * 512],
                                  start=(fs == 0), stop=(fs == 21))
                p.inc(ii, "mm")

            @block.scalar
            def _(s, dt=dt, tc=tc):
                p.wait(s, "mm")
                ps = psD_[tc % 2]
                ii = s.activation(gb_v[:, dt, tc * 512:(tc + 1) * 512], ps[:, :], AF.Silu)
                p.inc(ii, "ss")

    @block.sync
    def _(sy):
        p.wait(sy, "ss")
        for k in range(8):
            ii = sy.dma_start(out=gout[k], in_=gb_v[:, k, :])
            p.inc(ii, "so", 16)
        p.wait(sy, "so")

    return p



def build_l0comb():
    p = P("l0comb")
    nc = p.nc
    aoin = nc.dram_tensor("aoin", [8, 128, TO], BF16, kind="ExternalInput").ap()
    gin = nc.dram_tensor("gin", [8, 128, TO], BF16, kind="ExternalInput").ap()
    xfm = nc.dram_tensor("xfm", [8, 128, TO], F32, kind="ExternalInput").ap()
    wph = nc.dram_tensor("wph", [D, D], F32, kind="ExternalInput").ap()
    bph = nc.dram_tensor("bph", [128, 8], F32, kind="ExternalInput").ap()
    x1 = nc.dram_tensor("x1", [8, 128, TO], F32, kind="ExternalOutput").ap()

    ao = p.sbuf("ao", [128, 8 * TO], BF16)
    gbuf = p.sbuf("gbuf", [128, 8 * TO], BF16)
    wp = p.sbuf("wp", [128, 8 * D], BF16)
    bp = p.sbuf("bp", [128, 8], F32)
    xt = [p.sbuf(f"xt{i}", [128, 512], F32) for i in range(2)]
    tmps = p.sbuf("tmps", [128, 512], F32)
    x1t = [p.sbuf(f"x1t{i}", [128, 512], F32) for i in range(2)]

    psS = [p.psum(f"psS{i}", [128, 512]) for i in range(2)]

    ld = p.new_sem("ld")
    vg = p.new_sem("vg")
    mm = p.new_sem("mm")
    sv = p.new_sem("sv")
    xl = p.new_sem("xl")
    so = p.new_sem("so")

    p.begin()
    block = p.block
    ao_v = ao[:].rearrange("p (k t) -> p k t", k=8)
    gb_v = gbuf[:].rearrange("p (k t) -> p k t", k=8)
    wp_v = wp[:].rearrange("p (k t) -> p k t", k=8)

    @block.gpsimd
    def _(g):
        for k in range(8):
            g.dma_start(out=ao_v[:, k, :], in_=aoin[k]).then_inc(p.sem["ld"], 16)
            g.dma_start(out=gb_v[:, k, :], in_=gin[k]).then_inc(p.sem["ld"], 16)
            g.dma_start(out=wp_v[:, k, :], in_=wph[k * 128:(k + 1) * 128, :]).then_inc(p.sem["ld"], 16)
        g.dma_start(out=bp[:, 0:8], in_=bph).then_inc(p.sem["ld"], 16)
    p.cnt["ld"] += 16 * 25

    n = 0
    for ot in range(8):
        for tc in range(TO // 512):
            n += 1
            buf = n % 2

            @block.gpsimd
            def _(g, ot=ot, tc=tc, buf=buf, n=n):
                if n > 2:
                    p.wait(g, "so", 16 * (n - 2))
                g.dma_start(out=xt[buf][:, :], in_=xfm[ot][:, tc * 512:(tc + 1) * 512]).then_inc(p.sem["xl"], 16)
            p.cnt["xl"] += 16

            @block.tensor
            def _(t, ot=ot, tc=tc, buf=buf, n=n):
                p.wait(t, "ld")
                if n > 2:
                    p.wait(t, "sv", p.cnt["sv"] - 1)
                for sub in range(8):
                    ii = t.matmul(psS[buf][:, :], wp_v[:, sub, ot * 128:(ot + 1) * 128],
                                  ao_v[:, sub, tc * 512:(tc + 1) * 512],
                                  start=(sub == 0), stop=(sub == 7))
                p.inc(ii, "mm")

            @block.vector
            def _(v, ot=ot, tc=tc, buf=buf, n=n):
                p.wait(v, "mm")
                p.wait(v, "xl")
                v.tensor_scalar(out=tmps[:, :], in0=psS[buf][:, :],
                                scalar1=bp[:, ot:ot + 1], scalar2=None, op0=ALU.add)
                v.tensor_tensor(tmps[:, :], tmps[:, :], gb_v[:, ot, tc * 512:(tc + 1) * 512], ALU.mult)
                ii = v.tensor_tensor(x1t[buf][:, :], xt[buf][:, :], tmps[:, :], ALU.add)
                p.inc(ii, "sv")

            @block.sync
            def _(sy, ot=ot, tc=tc, buf=buf):
                p.wait(sy, "sv")
                ii = sy.dma_start(out=x1[ot][:, tc * 512:(tc + 1) * 512], in_=x1t[buf][:, :])
                p.inc(ii, "so", 16)

    @block.sync
    def _(sy):
        p.wait(sy, "so")

    return p



def build_l1ln():
    p = P("l1ln")
    nc = p.nc
    x1 = nc.dram_tensor("x1", [8, 128, TO], F32, kind="ExternalInput").ap()
    pnw = nc.dram_tensor("pnw", [128, 8], F32, kind="ExternalInput").ap()
    sacc = nc.dram_tensor("sacc", [8, 128, TO], F32, kind="ExternalOutput").ap()
    xn1 = nc.dram_tensor("xn1", [8, 128, TO], BF16, kind="ExternalOutput").ap()
    rstd2 = nc.dram_tensor("rstd2", [1, TO], F32, kind="ExternalOutput").ap()
    redge = nc.dram_tensor("redge", [8, 128, 3], F32, kind="ExternalOutput").ap()

    xb = p.sbuf("xb", [128, 8 * TO], F32)
    sa = xb
    xnb = p.sbuf("xnb", [128, 8 * TO], BF16)
    sqt = [p.sbuf(f"sqt{i}", [128, 512], F32) for i in range(2)]
    onesr = p.sbuf("onesr", [1, 128], F32)
    onesc = p.sbuf("onesc", [128, 1], F32)
    rows = p.sbuf("rows", [1, 4 * TO], F32)
    rows2 = p.sbuf("rows2", [1, 2 * TO], F32)
    mbc = p.sbuf("mbc", [128, TO], F32)
    rbc = p.sbuf("rbc", [128, TO], F32)
    pnwt = p.sbuf("pnwt", [128, 8], F32)
    ret = p.sbuf("ret", [128, 8 * 3], F32)

    psX = p.psum("psX", [1, 512])
    psQ = p.psum("psQ", [1, 512])
    psB1 = p.psum("psB1", [128, 512])
    psB2 = p.psum("psB2", [128, 512])

    for s in ("ld", "vg", "mm", "m2", "sv", "ss", "so"):
        p.new_sem(s)

    p.begin()
    block = p.block
    xb_v = xb[:].rearrange("p (k t) -> p k t", k=8)
    sa_v = sa[:].rearrange("p (k t) -> p k t", k=8)
    xn_v = xnb[:].rearrange("p (k t) -> p k t", k=8)
    ret_v = ret[:].rearrange("p (k t) -> p k t", k=3)
    NCH = TO // 512

    @block.gpsimd
    def _(g):
        for k in range(8):
            g.dma_start(out=xb_v[:, k, :], in_=x1[k]).then_inc(p.sem["ld"], 16)
        g.dma_start(out=pnwt[:], in_=pnw).then_inc(p.sem["ld"], 16)
    p.cnt["ld"] += 16 * 9

    @block.vector
    def _(v):
        v.memset(onesr[:], 1.0)
        ii = v.memset(onesc[:], 1.0)
        p.inc(ii, "sv")

    def stats_pass(src_v, rowbuf):
        for ch in range(NCH):
            @block.tensor
            def _(t, ch=ch):
                p.wait(t, "ld")
                p.wait(t, "sv")
                for sub in range(8):
                    ii = t.matmul(psX[:, :], onesc[:, :],
                                  src_v[:, sub, ch * 512:(ch + 1) * 512],
                                  start=(sub == 0), stop=(sub == 7))
                p.inc(ii, "mm")

            @block.scalar
            def _(s, ch=ch):
                p.wait(s, "ld")
                p.wait(s, "sv")
                base = p.cnt["m2"]
                for sub in range(8):
                    if sub >= 2:
                        p.wait(s, "m2", base + sub - 1)
                    ii = s.activation(sqt[sub % 2][:, :],
                                      src_v[:, sub, ch * 512:(ch + 1) * 512], AF.Square)
                    p.inc(ii, "ss")

            @block.tensor
            def _(t, ch=ch):
                p.wait(t, "sv")
                base = p.cnt["ss"] - 8
                for sub in range(8):
                    p.wait(t, "ss", base + sub + 1)
                    ii = t.matmul(psQ[:, :], onesc[:, :], sqt[sub % 2][:, :],
                                  start=(sub == 0), stop=(sub == 7))
                    p.inc(ii, "m2")

            @block.vector
            def _(v, ch=ch):
                p.wait(v, "mm")
                p.wait(v, "m2")
                v.tensor_copy(rowbuf[:, ch * 512:(ch + 1) * 512], psX[:, :])
                ii = v.tensor_copy(rowbuf[:, TO + ch * 512:TO + (ch + 1) * 512], psQ[:, :])
                p.inc(ii, "sv")

    def bcast2(rowap1, rowap2, dst1, dst2):
        for ch in range(NCH):
            @block.tensor
            def _(t, ch=ch):
                p.wait(t, "vg")
                if ch > 0:
                    p.wait(t, "sv")
                t.matmul(psB1[:, :], onesr[:, :], rowap1[:, ch * 512:(ch + 1) * 512],
                         start=True, stop=True)
                ii = t.matmul(psB2[:, :], onesr[:, :], rowap2[:, ch * 512:(ch + 1) * 512],
                              start=True, stop=True)
                p.inc(ii, "mm")

            @block.vector
            def _(v, ch=ch):
                p.wait(v, "mm")
                v.tensor_copy(dst1[:, ch * 512:(ch + 1) * 512], psB1[:, :])
                ii = v.tensor_copy(dst2[:, ch * 512:(ch + 1) * 512], psB2[:, :])
                p.inc(ii, "sv")

    stats_pass(xb_v, rows)

    @block.vector
    def _(v):
        p.wait(v, "sv")
        p.guard(v, v.tensor_scalar(out=rows[:, 2 * TO:3 * TO], in0=rows[:, 0:TO],
                                   scalar1=1.0 / D, scalar2=None, op0=ALU.mult))
        p.guard(v, v.tensor_scalar(out=rows[:, 3 * TO:4 * TO], in0=rows[:, TO:2 * TO],
                                   scalar1=1.0 / D, scalar2=None, op0=ALU.mult))
        p.guard(v, v.tensor_tensor(rows[:, 0:TO], rows[:, 2 * TO:3 * TO],
                                   rows[:, 2 * TO:3 * TO], ALU.mult))
        p.guard(v, v.tensor_tensor(rows[:, 3 * TO:4 * TO], rows[:, 3 * TO:4 * TO],
                                   rows[:, 0:TO], ALU.subtract))
        p.guard(v, v.tensor_scalar(out=rows[:, 3 * TO:4 * TO], in0=rows[:, 3 * TO:4 * TO],
                                   scalar1=1e-5, scalar2=None, op0=ALU.add))

    @block.scalar
    def _(s):
        p.wait(s, "vg")
        ii = s.activation(rows[:, 0:TO], rows[:, 3 * TO:4 * TO], AF.Sqrt)
        p.inc(ii, "ss")

    @block.vector
    def _(v):
        p.wait(v, "ss")
        p.guard(v, v.reciprocal(rows[:, 3 * TO:4 * TO], rows[:, 0:TO]))

    bcast2(rows[:, 2 * TO:3 * TO], rows[:, 3 * TO:4 * TO], mbc, rbc)

    for dt in range(8):
        @block.vector
        def _(v, dt=dt):
            p.wait(v, "sv")
            v.tensor_tensor(sa_v[:, dt, :], xb_v[:, dt, :], mbc[:, :], ALU.subtract)
            v.tensor_tensor(sa_v[:, dt, :], sa_v[:, dt, :], rbc[:, :], ALU.mult)
            ii = v.tensor_copy(xn_v[:, dt, :], sa_v[:, dt, :])
            p.inc(ii, "sv")

    stats_pass(sa_v, rows2)

    @block.vector
    def _(v):
        p.wait(v, "sv")
        p.guard(v, v.tensor_scalar(out=rows2[:, 0:TO], in0=rows2[:, TO:2 * TO],
                                   scalar1=1.0 / D, scalar2=1e-6,
                                   op0=ALU.mult, op1=ALU.add))

    @block.scalar
    def _(s):
        p.wait(s, "vg")
        ii = s.activation(rows2[:, TO:2 * TO], rows2[:, 0:TO], AF.Sqrt)
        p.inc(ii, "ss")

    @block.vector
    def _(v):
        p.wait(v, "ss")
        p.guard(v, v.reciprocal(rows2[:, 0:TO], rows2[:, TO:2 * TO]))

    @block.tensor
    def _(t):
        p.wait(t, "vg")
        p.wait(t, "sv")
        ii = t.matmul(psB1[:, 0:3], onesr[:, :], rows2[:, TO - 3:TO], start=True, stop=True)
        p.inc(ii, "mm")

    @block.vector
    def _(v):
        p.wait(v, "mm")
        p.guard(v, v.tensor_copy(mbc[:, 0:3], psB1[:, 0:3]))
        for dt in range(8):
            p.guard(v, v.tensor_scalar(out=ret[:, dt * 3:(dt + 1) * 3],
                                       in0=sa_v[:, dt, TO - 3:TO],
                                       scalar1=pnwt[:, dt:dt + 1], scalar2=None, op0=ALU.mult))
            p.guard(v, v.tensor_tensor(ret[:, dt * 3:(dt + 1) * 3], ret[:, dt * 3:(dt + 1) * 3],
                                       mbc[:, 0:3], ALU.mult))
        p.inc(v.tensor_copy(rows2[:, TO:TO + 1], rows2[:, 0:1]), "sv")

    @block.sync
    def _(sy):
        p.wait(sy, "sv")
        for k in range(8):
            ii = sy.dma_start(out=sacc[k], in_=sa_v[:, k, :])
            p.inc(ii, "so", 16)
            ii = sy.dma_start(out=xn1[k], in_=xn_v[:, k, :])
            p.inc(ii, "so", 16)
            ii = sy.dma_start(out=redge[k], in_=ret[:, k * 3:(k + 1) * 3])
            p.inc(ii, "so", 16)
        ii = sy.dma_start(out=rstd2, in_=rows2[:, 0:TO])
        p.inc(ii, "so", 16)
        p.wait(sy, "so")

    return p



def build_l1rec():
    p = P("l1rec")
    nc = p.nc
    sacc = nc.dram_tensor("sacc", [8, 128, TO], F32, kind="ExternalInput").ap()
    rstd2 = nc.dram_tensor("rstd2", [1, TO], F32, kind="ExternalInput").ap()
    halo = nc.dram_tensor("halo", [8, 128, 3], F32, kind="ExternalInput").ap()
    pnw = nc.dram_tensor("pnw", [128, 8], F32, kind="ExternalInput").ap()
    cw = nc.dram_tensor("cw", [128, 32], F32, kind="ExternalInput").ap()
    cb = nc.dram_tensor("cb", [128, 8], F32, kind="ExternalInput").ap()
    db = nc.dram_tensor("db", [128, 8], F32, kind="ExternalInput").ap()
    waT = nc.dram_tensor("waT", [D, D], F32, kind="ExternalInput").ap()
    wiT = nc.dram_tensor("wiT", [D, D], F32, kind="ExternalInput").ap()
    wvT = nc.dram_tensor("wvT", [D, D], F32, kind="ExternalInput").ap()
    recl = nc.dram_tensor("recl", [8, 128, TO], BF16, kind="ExternalOutput").ap()
    Fout = nc.dram_tensor("Fout", [8, 128, TO], BF16, kind="ExternalOutput").ap()
    sout = nc.dram_tensor("sout", [128, 8], F32, kind="ExternalOutput").ap()

    NCHK = TO // 64
    sa = p.sbuf("sa", [128, 8 * TO], F32)
    hb = p.sbuf("hb", [128, 8 * TO], BF16)
    rext = p.sbuf("rext", [128, TO + 16], F32)
    ac1 = p.sbuf("ac1", [128, TO], F32)
    ac2 = p.sbuf("ac2", [128, TO], F32)
    rbc2 = p.sbuf("rbc2", [128, TO], F32)
    onesr = p.sbuf("onesr", [1, 128], F32)
    mask = p.sbuf("mask", [128, TO], F32)
    zt = p.sbuf("zt", [128, NCHK], F32)
    states = p.sbuf("states", [128, NCHK], F32)
    si = p.sbuf("si", [128, NCHK], F32)
    cpd = p.sbuf("cpd", [128, NCHK], F32)
    cpds = p.sbuf("cpds", [128, NCHK], F32)
    sot = p.sbuf("sot", [128, 8], F32)
    smallw = {n: p.sbuf(n, [128, 8], F32) for n in ("pnwt", "cbt", "dbt")}
    cwt = p.sbuf("cwt", [128, 32], F32)
    wstr = [p.sbuf(f"wstr{i}", [128, 3 * 8 * 128], BF16) for i in range(2)]
    rtile = p.sbuf("rtile", [128, TO], BF16)
    ftile = p.sbuf("ftile", [128, TO], BF16)
    halot = p.sbuf("halot", [128, 24], F32)
    r2row = p.sbuf("r2row", [1, TO], F32)

    psA = p.psum("psA", [128, 512])
    psI = p.psum("psI", [128, 512])
    psV = p.psum("psV", [128, 512])
    psB1 = p.psum("psB1", [128, 512])

    for s in ("ld", "vg", "mm", "sv", "ss", "wl", "so"):
        p.new_sem(s)

    p.begin()
    block = p.block
    sa_v = sa[:].rearrange("p (k t) -> p k t", k=8)
    hb_v = hb[:].rearrange("p (k t) -> p k t", k=8)
    halv = halot[:].rearrange("p (k t) -> p k t", k=8)
    NCH = TO // 512

    @block.gpsimd
    def _(g):
        for k in range(8):
            g.dma_start(out=sa_v[:, k, :], in_=sacc[k]).then_inc(p.sem["ld"], 16)
            g.dma_start(out=halv[:, k, :], in_=halo[k]).then_inc(p.sem["ld"], 16)
        for nm, src_ in (("pnwt", pnw), ("cbt", cb), ("dbt", db)):
            g.dma_start(out=smallw[nm][:], in_=src_).then_inc(p.sem["ld"], 16)
        g.dma_start(out=cwt[:], in_=cw).then_inc(p.sem["ld"], 16)
        g.dma_start(out=r2row[:], in_=rstd2).then_inc(p.sem["ld"], 16)
    p.cnt["ld"] += 16 * 21

    @block.vector
    def _(v):
        v.memset(onesr[:], 1.0)
        v.memset(mask[:], 1.0)
        v.memset(mask[:].rearrange("p (c k) -> p c k", k=64)[:, :, 0:1].squeeze(2), 0.0)
        ii = v.memset(zt[:], 0.0)
        p.inc(ii, "sv")

    for ch in range(NCH):
        @block.tensor
        def _(t, ch=ch):
            p.wait(t, "ld")
            p.wait(t, "sv")
            ii = t.matmul(psB1[:, :], onesr[:, :], r2row[:, ch * 512:(ch + 1) * 512],
                          start=True, stop=True)
            p.inc(ii, "mm")

        @block.vector
        def _(v, ch=ch):
            p.wait(v, "mm")
            ii = v.tensor_copy(rbc2[:, ch * 512:(ch + 1) * 512], psB1[:, :])
            p.inc(ii, "sv")

    for dt in range(8):
        @block.vector
        def _(v, dt=dt):
            p.wait(v, "ld")
            p.wait(v, "sv")
            if dt > 0:
                p.wait(v, "so")
            v.tensor_copy(rext[:, 0:3], halv[:, dt, :])
            v.tensor_scalar(out=rext[:, 3:3 + TO], in0=sa_v[:, dt, :],
                            scalar1=smallw["pnwt"][:, dt:dt + 1], scalar2=None, op0=ALU.mult)
            v.tensor_tensor(rext[:, 3:3 + TO], rext[:, 3:3 + TO], rbc2[:, :], ALU.mult)
            v.tensor_scalar(out=ac1[:, :], in0=rext[:, 0:TO],
                            scalar1=cwt[:, dt * 4:dt * 4 + 1], scalar2=None, op0=ALU.mult)
            v.scalar_tensor_tensor(out=ac2[:, :], in0=rext[:, 1:1 + TO],
                                   scalar=cwt[:, dt * 4 + 1:dt * 4 + 2], in1=ac1[:, :],
                                   op0=ALU.mult, op1=ALU.add)
            v.scalar_tensor_tensor(out=ac1[:, :], in0=rext[:, 2:2 + TO],
                                   scalar=cwt[:, dt * 4 + 2:dt * 4 + 3], in1=ac2[:, :],
                                   op0=ALU.mult, op1=ALU.add)
            v.scalar_tensor_tensor(out=ac2[:, :], in0=rext[:, 3:3 + TO],
                                   scalar=cwt[:, dt * 4 + 3:dt * 4 + 4], in1=ac1[:, :],
                                   op0=ALU.mult, op1=ALU.add)
            ii = v.tensor_scalar(out=hb_v[:, dt, :], in0=ac2[:, :],
                                 scalar1=smallw["cbt"][:, dt:dt + 1], scalar2=None, op0=ALU.add)
            p.inc(ii, "sv")

    A = [sa_v[:, k, :] for k in range(8)]
    c3 = lambda ap: ap.rearrange("p (c k) -> p c k", k=64)

    for ot in range(8):
        wbuf = wstr[ot % 2]
        wv3 = wbuf[:].rearrange("p (w k t) -> p w k t", w=3, k=8)

        @block.gpsimd
        def _(g, ot=ot, wbuf=wbuf, wv3=wv3):
            if ot < 2:
                p.wait(g, "sv")
            else:
                p.wait(g, "mm")
            for wi, wsrc in enumerate((waT, wiT, wvT)):
                for sub in range(8):
                    g.dma_start(out=wv3[:, wi, sub, :],
                                in_=wsrc[sub * 128:(sub + 1) * 128, ot * 128:(ot + 1) * 128]
                                ).then_inc(p.sem["wl"], 16)
        p.cnt["wl"] += 16 * 24

        @block.tensor
        def _(t, ot=ot, wv3=wv3):
            p.wait(t, "wl")
            p.wait(t, "sv")
            for ch in range(NCH):
                if ch > 0:
                    p.wait(t, "ss", p.cnt["ss"] + ch * 2)
                    p.wait(t, "sv", p.cnt["sv"] + ch)
                for sub in range(8):
                    t.matmul(psA[:, :], wv3[:, 0, sub, :], hb_v[:, sub, ch * 512:(ch + 1) * 512],
                             start=(sub == 0), stop=(sub == 7))
                for sub in range(8):
                    t.matmul(psI[:, :], wv3[:, 1, sub, :], hb_v[:, sub, ch * 512:(ch + 1) * 512],
                             start=(sub == 0), stop=(sub == 7))
                for sub in range(8):
                    ii = t.matmul(psV[:, :], wv3[:, 2, sub, :], hb_v[:, sub, ch * 512:(ch + 1) * 512],
                                  start=(sub == 0), stop=(sub == 7))
                p.inc(ii, "mm")

        @block.scalar
        def _(s, ot=ot):
            base = p.cnt["mm"] - NCH
            for ch in range(NCH):
                p.wait(s, "mm", base + ch + 1)
                s.activation(A[0][:, ch * 512:(ch + 1) * 512], psA[:, :], AF.Sigmoid,
                             bias=smallw["dbt"][:, ot:ot + 1], scale=1.0)
                ii = s.activation(A[1][:, ch * 512:(ch + 1) * 512], psI[:, :], AF.Sigmoid)
                p.inc(ii, "ss", 2)

        @block.vector
        def _(v, ot=ot):
            base = p.cnt["mm"] - NCH
            for ch in range(NCH):
                p.wait(v, "mm", base + ch + 1)
                ii = v.tensor_copy(A[2][:, ch * 512:(ch + 1) * 512], psV[:, :])
                p.inc(ii, "sv")

        @block.vector
        def _(v, ot=ot):
            p.wait(v, "ss")
            ii = v.tensor_scalar(out=A[3][:, :], in0=A[0][:, :], scalar1=1e-10,
                                 scalar2=None, op0=ALU.max)
            p.inc(ii, "sv")

        @block.scalar
        def _(s, ot=ot):
            p.wait(s, "sv")
            ii = s.activation(A[4][:, :], A[3][:, :], AF.Ln)
            p.inc(ii, "ss")

        @block.vector
        def _(v, ot=ot):
            p.wait(v, "ss")
            ii = v.tensor_tensor_scan(out=A[3][:, :], data0=mask[:, :], data1=A[4][:, :],
                                      initial=0.0, op0=ALU.mult, op1=ALU.add)
            p.inc(ii, "sv")

        @block.scalar
        def _(s, ot=ot):
            p.wait(s, "sv")
            ii = s.activation(A[4][:, :], A[3][:, :], AF.Exp)
            p.inc(ii, "ss")

        @block.vector
        def _(v, ot=ot):
            p.wait(v, "ss")
            v.tensor_scalar(out=A[3][:, :], in0=A[4][:, :], scalar1=1e-10,
                            scalar2=None, op0=ALU.max)
            v.reciprocal(A[5][:, :], A[3][:, :])
            v.tensor_tensor(A[3][:, :], A[0][:, :], A[0][:, :], ALU.mult)
            v.tensor_scalar(out=A[3][:, :], in0=A[3][:, :], scalar1=-1.0, scalar2=1.0,
                            op0=ALU.mult, op1=ALU.add)
            ii = v.tensor_scalar(out=A[3][:, :], in0=A[3][:, :], scalar1=1e-8,
                                 scalar2=None, op0=ALU.max)
            p.inc(ii, "sv")

        @block.scalar
        def _(s, ot=ot):
            p.wait(s, "sv")
            ii = s.activation(A[6][:, :], A[3][:, :], AF.Sqrt)
            p.inc(ii, "ss")

        @block.vector
        def _(v, ot=ot):
            p.wait(v, "ss")
            v.tensor_tensor(A[0][:, :], A[6][:, :], A[1][:, :], ALU.mult)
            v.tensor_tensor(A[0][:, :], A[0][:, :], A[2][:, :], ALU.mult)
            v.tensor_tensor(A[0][:, :], A[0][:, :], A[5][:, :], ALU.mult)
            v.tensor_tensor_scan(out=A[1][:, :], data0=mask[:, :], data1=A[0][:, :],
                                 initial=0.0, op0=ALU.mult, op1=ALU.add)
            v.tensor_tensor(A[2][:, :], A[4][:, :], A[1][:, :], ALU.mult)
            p.guard(v, v.tensor_tensor_scan(out=states[:, :], data0=c3(A[4])[:, :, 63],
                                            data1=c3(A[2])[:, :, 63], initial=0.0,
                                            op0=ALU.mult, op1=ALU.add))
            p.guard(v, v.memset(si[:, 0:1], 0.0))
            p.guard(v, v.tensor_copy(si[:, 1:NCHK], states[:, 0:NCHK - 1]))
            p.guard(v, v.tensor_tensor_scan(out=cpd[:, :], data0=c3(A[4])[:, :, 63],
                                            data1=zt[:, :], initial=1.0,
                                            op0=ALU.mult, op1=ALU.add))
            p.guard(v, v.memset(cpds[:, 0:1], 1.0))
            p.guard(v, v.tensor_copy(cpds[:, 1:NCHK], cpd[:, 0:NCHK - 1]))
            p.guard(v, v.tensor_copy(sot[:, ot:ot + 1], states[:, NCHK - 1:NCHK]))
            v.tensor_tensor(c3(A[0]), c3(A[4]),
                            si[:, :].unsqueeze(2).broadcast_to((128, NCHK, 64)), ALU.mult)
            if ot >= 1:
                p.wait(v, "so", 32 * ot)
            v.tensor_tensor(rtile[:, :], A[2][:, :], A[0][:, :], ALU.add)
            ii = v.tensor_tensor(c3(ftile), c3(A[4]),
                                 cpds[:, :].unsqueeze(2).broadcast_to((128, NCHK, 64)), ALU.mult)
            p.inc(ii, "sv")

        @block.gpsimd
        def _(g, ot=ot):
            p.wait(g, "sv")
            g.dma_start(out=recl[ot], in_=rtile[:, :]).then_inc(p.sem["so"], 16)
            g.dma_start(out=Fout[ot], in_=ftile[:, :]).then_inc(p.sem["so"], 16)
        p.cnt["so"] += 32

    @block.gpsimd
    def _(g):
        p.wait(g, "vg")
        g.dma_start(out=sout, in_=sot[:, :]).then_inc(p.sem["so"], 16)
        p.cnt["so"] += 16
        p.wait(g, "so")

    return p



def build_l1fix():
    p = P("l1fix")
    nc = p.nc
    sacc = nc.dram_tensor("sacc", [8, 128, TO], F32, kind="ExternalInput").ap()
    recl = nc.dram_tensor("recl", [8, 128, TO], BF16, kind="ExternalInput").ap()
    Fin = nc.dram_tensor("Fin", [8, 128, TO], BF16, kind="ExternalInput").ap()
    sin = nc.dram_tensor("sin", [128, 8], F32, kind="ExternalInput").ap()
    opT = nc.dram_tensor("opT", [D, D], F32, kind="ExternalInput").ap()
    fnw = nc.dram_tensor("fnw", [128, 8], F32, kind="ExternalInput").ap()
    wguT = nc.dram_tensor("wguT", [D, 2 * 4096], F32, kind="ExternalInput").ap()
    wdT = nc.dram_tensor("wdT", [4096, D], F32, kind="ExternalInput").ap()
    sfin = nc.dram_tensor("sfin", [8, 128, TO], F32, kind="ExternalOutput").ap()

    sa = p.sbuf("sa", [128, 8 * TO], F32)
    rb = p.sbuf("rb", [128, 8 * TO], BF16)
    hn = p.sbuf("hn", [128, 8 * TO], BF16)
    h2 = p.sbuf("h2", [128, 32 * 512], BF16)
    ft_in = p.sbuf("ft_in", [128, TO], BF16)
    op_sb = None
    sint = p.sbuf("sint", [128, 8], F32)
    fnwt = p.sbuf("fnwt", [128, 8], F32)
    sgt = p.sbuf("sgt", [128, 512], F32)
    sqt = [p.sbuf(f"sqt{i}", [128, 512], F32) for i in range(2)]
    onesc = p.sbuf("onesc", [128, 1], F32)
    onesr = p.sbuf("onesr", [1, 128], F32)
    rows = p.sbuf("rows", [1, TO], F32)
    rbc3 = p.sbuf("rbc3", [128, TO], F32)
    wgu = [p.sbuf(f"wgu{i}", [128, 8 * 256], BF16) for i in range(2)]
    wdd0 = p.sbuf("wdd0", [128, 32 * 128], BF16)
    wdd = [wdd0, wdd0]

    psP = [p.psum(f"psP{i}", [128, 512]) for i in range(2)]
    psG = p.psum("psG", [128, 512])
    psU = p.psum("psU", [128, 512])
    psQ = p.psum("psQ", [1, 512])
    psB = p.psum("psB", [128, 512])

    for s in ("ld", "vg", "mm", "m2", "sv", "ss", "wl", "so"):
        p.new_sem(s)

    p.begin()
    block = p.block
    sa_v = sa[:].rearrange("p (k t) -> p k t", k=8)
    rb_v = rb[:].rearrange("p (k t) -> p k t", k=8)
    hn_v = hn[:].rearrange("p (k t) -> p k t", k=8)
    h2_v = h2[:].rearrange("p (k t) -> p k t", k=32)
    op_v = h2[:, 0:8 * D].rearrange("p (k t) -> p k t", k=8)
    NCH = TO // 512

    @block.gpsimd
    def _(g):
        for k in range(8):
            g.dma_start(out=sa_v[:, k, :], in_=sacc[k]).then_inc(p.sem["ld"], 16)
            g.dma_start(out=op_v[:, k, :], in_=opT[k * 128:(k + 1) * 128, :]).then_inc(p.sem["ld"], 16)
        g.dma_start(out=sint[:], in_=sin).then_inc(p.sem["ld"], 16)
        g.dma_start(out=fnwt[:], in_=fnw).then_inc(p.sem["ld"], 16)
    p.cnt["ld"] += 16 * 18

    @block.vector
    def _(v):
        v.memset(onesc[:], 1.0)
        ii = v.memset(onesr[:], 1.0)
        p.inc(ii, "sv")

    for dt in range(8):
        @block.gpsimd
        def _(g, dt=dt):
            if dt >= 1:
                p.wait(g, "sv")
            g.dma_start(out=ft_in[:, :], in_=Fin[dt]).then_inc(p.sem["ld"], 16)
            g.dma_start(out=rb_v[:, dt, :], in_=recl[dt]).then_inc(p.sem["ld"], 16)
        p.cnt["ld"] += 32

        @block.vector
        def _(v, dt=dt):
            p.wait(v, "ld")
            ii = v.scalar_tensor_tensor(out=rb_v[:, dt, :], in0=ft_in[:, :],
                                        scalar=sint[:, dt:dt + 1], in1=rb_v[:, dt, :],
                                        op0=ALU.mult, op1=ALU.add)
            p.inc(ii, "sv")

    for ot in range(8):
        for ch in range(NCH):
            n = ot * NCH + ch

            @block.tensor
            def _(t, ot=ot, ch=ch, n=n):
                p.wait(t, "ld")
                p.wait(t, "sv")
                ps = psP[n % 2]
                for sub in range(8):
                    ii = t.matmul(ps[:, :], op_v[:, sub, ot * 128:(ot + 1) * 128],
                                  rb_v[:, sub, ch * 512:(ch + 1) * 512],
                                  start=(sub == 0), stop=(sub == 7))
                p.inc(ii, "mm")

            @block.vector
            def _(v, ot=ot, ch=ch, n=n):
                p.wait(v, "mm")
                ps = psP[n % 2]
                ii = v.tensor_tensor(sa_v[:, ot, ch * 512:(ch + 1) * 512],
                                     sa_v[:, ot, ch * 512:(ch + 1) * 512], ps[:, :], ALU.add)
                p.inc(ii, "sv")

    for ch in range(NCH):
        @block.scalar
        def _(s, ch=ch):
            p.wait(s, "sv")
            base = p.cnt["m2"]
            for sub in range(8):
                if sub >= 2:
                    p.wait(s, "m2", base + sub - 1)
                ii = s.activation(sqt[sub % 2][:, :],
                                  sa_v[:, sub, ch * 512:(ch + 1) * 512], AF.Square)
                p.inc(ii, "ss")

        @block.tensor
        def _(t, ch=ch):
            p.wait(t, "sv")
            base = p.cnt["ss"] - 8
            for sub in range(8):
                p.wait(t, "ss", base + sub + 1)
                ii = t.matmul(psQ[:, :], onesc[:, :], sqt[sub % 2][:, :],
                              start=(sub == 0), stop=(sub == 7))
                p.inc(ii, "m2")

        @block.vector
        def _(v, ch=ch):
            p.wait(v, "m2")
            ii = v.tensor_copy(rows[:, ch * 512:(ch + 1) * 512], psQ[:, :])
            p.inc(ii, "sv")

    @block.vector
    def _(v):
        p.wait(v, "sv")
        p.guard(v, v.tensor_scalar(out=rows[:, 0:TO], in0=rows[:, 0:TO],
                                   scalar1=1.0 / D, scalar2=1e-6,
                                   op0=ALU.mult, op1=ALU.add))

    @block.scalar
    def _(s):
        p.wait(s, "vg")
        ii = s.activation(rbc3[0:1, 0:TO], rows[:, 0:TO], AF.Sqrt)
        p.inc(ii, "ss")

    @block.vector
    def _(v):
        p.wait(v, "ss")
        p.guard(v, v.reciprocal(rows[:, 0:TO], rbc3[0:1, 0:TO]))

    for ch in range(NCH):
        @block.tensor
        def _(t, ch=ch):
            p.wait(t, "vg")
            if ch > 0:
                p.wait(t, "sv")
            ii = t.matmul(psB[:, :], onesr[:, :], rows[:, ch * 512:(ch + 1) * 512],
                          start=True, stop=True)
            p.inc(ii, "mm")

        @block.vector
        def _(v, ch=ch):
            p.wait(v, "mm")
            ii = v.tensor_copy(rbc3[:, ch * 512:(ch + 1) * 512], psB[:, :])
            p.inc(ii, "sv")

    for dt in range(8):
        @block.vector
        def _(v, dt=dt):
            p.wait(v, "sv")
            ii = v.tensor_scalar(out=hn_v[:, dt, :], in0=sa_v[:, dt, :],
                                 scalar1=fnwt[:, dt:dt + 1], scalar2=rbc3_slice(rbc3, dt),
                                 op0=ALU.mult, op1=ALU.mult) if False else None
            v.tensor_scalar(out=hn_v[:, dt, :], in0=sa_v[:, dt, :],
                            scalar1=fnwt[:, dt:dt + 1], scalar2=None, op0=ALU.mult)
            ii = v.tensor_tensor(hn_v[:, dt, :], hn_v[:, dt, :], rbc3[:, :], ALU.mult)
            p.inc(ii, "sv")

    for tc in range(NCH):
        for ft in range(32):
            wb = wgu[ft % 2]

            @block.gpsimd
            def _(g, tc=tc, ft=ft, wb=wb):
                p.wait(g, "mm")
                wv = wb[:].rearrange("p (k t) -> p k t", k=8)
                for sub in range(8):
                    g.dma_start(out=wv[:, sub, 0:128],
                                in_=wguT[sub * 128:(sub + 1) * 128, ft * 128:(ft + 1) * 128]
                                ).then_inc(p.sem["wl"], 16)
                    g.dma_start(out=wv[:, sub, 128:256],
                                in_=wguT[sub * 128:(sub + 1) * 128, 4096 + ft * 128:4096 + (ft + 1) * 128]
                                ).then_inc(p.sem["wl"], 16)
            p.cnt["wl"] += 16 * 16

            @block.tensor
            def _(t, tc=tc, ft=ft, wb=wb):
                p.wait(t, "wl")
                p.wait(t, "sv")
                wv = wb[:].rearrange("p (k t) -> p k t", k=8)
                for sub in range(8):
                    t.matmul(psG[:, :], wv[:, sub, 0:128].transpose([0, 1]) if False else wv[:, sub, 0:128],
                             hn_v[:, sub, tc * 512:(tc + 1) * 512],
                             start=(sub == 0), stop=(sub == 7))
                for sub in range(8):
                    ii = t.matmul(psU[:, :], wv[:, sub, 128:256],
                                  hn_v[:, sub, tc * 512:(tc + 1) * 512],
                                  start=(sub == 0), stop=(sub == 7))
                p.inc(ii, "mm")

            @block.scalar
            def _(s, tc=tc, ft=ft):
                p.wait(s, "mm")
                ii = s.activation(sgt[:, :], psG[:, :], AF.Silu)
                p.inc(ii, "ss")

            @block.vector
            def _(v, tc=tc, ft=ft):
                p.wait(v, "ss")
                ii = v.tensor_tensor(h2_v[:, ft, :], sgt[:, :], psU[:, :], ALU.mult)
                p.inc(ii, "sv")

        for dt in range(8):
            wd_c = wdd[dt % 2]

            @block.gpsimd
            def _(g, tc=tc, dt=dt, wd_c=wd_c):
                p.wait(g, "mm")
                wv = wd_c[:].rearrange("p (k t) -> p k t", k=32)
                for fs in range(32):
                    g.dma_start(out=wv[:, fs, :],
                                in_=wdT[fs * 128:(fs + 1) * 128, dt * 128:(dt + 1) * 128]
                                ).then_inc(p.sem["wl"], 16)
            p.cnt["wl"] += 16 * 32

            @block.tensor
            def _(t, tc=tc, dt=dt, wd_c=wd_c):
                p.wait(t, "wl")
                p.wait(t, "sv")
                ps = psP[dt % 2]
                wv = wd_c[:].rearrange("p (k t) -> p k t", k=32)
                for fs in range(32):
                    ii = t.matmul(ps[:, :], wv[:, fs, :], h2_v[:, fs, :],
                                  start=(fs == 0), stop=(fs == 31))
                p.inc(ii, "mm")

            @block.vector
            def _(v, tc=tc, dt=dt):
                p.wait(v, "mm")
                ps = psP[dt % 2]
                ii = v.tensor_tensor(sa_v[:, dt, tc * 512:(tc + 1) * 512],
                                     sa_v[:, dt, tc * 512:(tc + 1) * 512], ps[:, :], ALU.add)
                p.inc(ii, "sv")

    @block.sync
    def _(sy):
        p.wait(sy, "sv")
        for k in range(8):
            ii = sy.dma_start(out=sfin[k], in_=sa_v[:, k, :])
            p.inc(ii, "so", 16)
        p.wait(sy, "so")

    return p



def build_l1moe():
    p = P("l1moe")
    nc = p.nc
    xn1 = nc.dram_tensor("xn1", [8, 128, TO], BF16, kind="ExternalInput").ap()
    sfin = nc.dram_tensor("sfin", [8, 128, TO], F32, kind="ExternalInput").ap()
    x1 = nc.dram_tensor("x1", [8, 128, TO], F32, kind="ExternalInput").ap()
    wrg = nc.dram_tensor("wrg", [D, 16], F32, kind="ExternalInput").ap()
    mwg = nc.dram_tensor("mwg", [8, D, 4096], F32, kind="ExternalInput").ap()
    mwu = nc.dram_tensor("mwu", [8, D, 4096], F32, kind="ExternalInput").ap()
    mwd = nc.dram_tensor("mwd", [8, 4096, D], F32, kind="ExternalInput").ap()
    out = nc.dram_tensor("out", [8, 128, TO], F32, kind="ExternalOutput").ap()

    xn = p.sbuf("xn", [128, 8 * TO], BF16)
    macc = p.sbuf("macc", [128, 8 * TO], F32)
    h2 = p.sbuf("h2", [128, 32 * 512], BF16)
    wrgt = p.sbuf("wrgt", [128, 8 * 16], BF16)
    cwall = p.sbuf("cwall", [128, 16 * 8], F32)
    st8 = p.sbuf("st8", [128, 8], F32)
    cwfm = p.sbuf("cwfm", [8, TO], F32)
    cwrow = p.sbuf("cwrow", [1, TO], F32)
    cwb = p.sbuf("cwb", [128, TO], BF16)
    onesr = p.sbuf("onesr", [1, 128], F32)
    ident = p.sbuf("ident", [128, 128], F32)
    identi = p.sbuf("identi", [128, 128], I32)
    sgt = p.sbuf("sgt", [128, 512], F32)
    ub = p.sbuf("ub", [128, 512], BF16)
    wg_s = [p.sbuf(f"wg_s{i}", [128, 8 * 128], BF16) for i in range(2)]
    wu_s = [p.sbuf(f"wu_s{i}", [128, 8 * 128], BF16) for i in range(2)]
    wd_s = [p.sbuf(f"wd_s{i}", [128, 32 * 128], BF16) for i in range(2)]
    s_t = p.sbuf("s_t", [128, 512], F32)
    x_t = p.sbuf("x_t", [128, 512], F32)
    o_t = [p.sbuf(f"o_t{i}", [128, 512], F32) for i in range(2)]

    psG = p.psum("psG", [128, 512])
    psU = p.psum("psU", [128, 512])
    psD_ = [p.psum(f"psD{i}", [128, 512]) for i in range(2)]
    psR = p.psum("psR", [128, 16])
    psB = p.psum("psB", [128, 512])

    for s in ("ld", "vg", "mm", "sv", "ss", "wl", "gd", "so"):
        p.new_sem(s)

    p.begin()
    block = p.block
    xn_v = xn[:].rearrange("p (k t) -> p k t", k=8)
    ma_v = macc[:].rearrange("p (k t) -> p k t", k=8)
    h2_v = h2[:].rearrange("p (k t) -> p k t", k=32)
    wrg_v = wrgt[:].rearrange("p (k t) -> p k t", k=8)
    cw_v = cwall[:].rearrange("p (k t) -> p k t", k=16)
    NCH = TO // 512

    @block.gpsimd
    def _(g):
        for k in range(8):
            g.dma_start(out=xn_v[:, k, :], in_=xn1[k]).then_inc(p.sem["ld"], 16)
            g.dma_start(out=wrg_v[:, k, :], in_=wrg[k * 128:(k + 1) * 128, :]).then_inc(p.sem["ld"], 16)
        g.iota(identi[:], pattern=[[1, 128]], base=0, channel_multiplier=-1,
               allow_small_or_imprecise_dtypes=True)
        ii = g.tensor_scalar(out=ident[:], in0=identi[:], scalar1=0, scalar2=None, op0=ALU.is_equal)
        p.inc(ii, "gd")
    p.cnt["ld"] += 16 * 16

    @block.vector
    def _(v):
        ii = v.memset(onesr[:], 1.0)
        p.inc(ii, "sv")

    for tt in range(16):
        @block.tensor
        def _(t, tt=tt):
            p.wait(t, "ld")
            if tt > 0:
                p.wait(t, "ss")
            for sub in range(8):
                ii = t.matmul(psR[:, :], xn_v[:, sub, tt * 128:(tt + 1) * 128],
                              wrg_v[:, sub, :], start=(sub == 0), stop=(sub == 7))
            p.inc(ii, "mm")

        @block.vector
        def _(v, tt=tt):
            p.wait(v, "mm")
            p.guard(v, v.tensor_reduce(st8[:, 0:1], psR[:, 0:8], axis=AX.X, op=ALU.max, negate=True))
            p.guard(v, v.tensor_reduce(st8[:, 1:2], psR[:, 8:16], axis=AX.X, op=ALU.max, negate=True))
            p.guard(v, v.tensor_scalar(out=st8[:, 2:3], in0=st8[:, 1:2], scalar1=10.0,
                                       scalar2=None, op0=ALU.mult))

        @block.scalar
        def _(s, tt=tt):
            p.wait(s, "vg")
            s.activation(cw_v[:, tt, 0:8], psR[:, 0:8], AF.Exp, bias=st8[:, 0:1],
                         scale=1.0, accum_out=st8[:, 3:4])
            ii = s.activation(cw_v[:, tt, 8:16] if False else st8[:, 8:8], psR[:, 8:16], AF.Exp,
                              bias=st8[:, 2:3], scale=10.0,
                              accum_out=st8[:, 4:5]) if False else None
            ii = s.activation(cwall[:, tt * 8:(tt + 1) * 8].bitcast(F32) if False else cw_v[:, tt, 0:8],
                              psR[:, 0:8], AF.Exp, bias=st8[:, 0:1], scale=1.0,
                              accum_out=st8[:, 3:4]) if False else None
            p.inc(s.activation(sgt[:, 0:8], psR[:, 8:16], AF.Exp, bias=st8[:, 2:3],
                               scale=10.0, accum_out=st8[:, 4:5]), "ss")

        @block.vector
        def _(v, tt=tt):
            p.wait(v, "ss")
            p.guard(v, v.reciprocal(st8[:, 5:6], st8[:, 3:4]))
            p.guard(v, v.reciprocal(st8[:, 6:7], st8[:, 4:5]))
            p.guard(v, v.tensor_scalar(out=cw_v[:, tt, :], in0=cw_v[:, tt, :],
                                       scalar1=st8[:, 5:6], scalar2=None, op0=ALU.mult))
            p.guard(v, v.tensor_scalar(out=sgt[:, 0:8], in0=sgt[:, 0:8],
                                       scalar1=st8[:, 6:7], scalar2=None, op0=ALU.mult))
            p.guard(v, v.tensor_tensor(cw_v[:, tt, :], cw_v[:, tt, :], sgt[:, 0:8], ALU.mult))
            p.guard(v, v.tensor_reduce(st8[:, 7:8], cw_v[:, tt, :], axis=AX.X, op=ALU.add))
            p.guard(v, v.tensor_scalar(out=st8[:, 7:8], in0=st8[:, 7:8], scalar1=1e-9,
                                       scalar2=None, op0=ALU.add))
            p.guard(v, v.reciprocal(st8[:, 7:8], st8[:, 7:8]))
            p.guard(v, v.tensor_scalar(out=cw_v[:, tt, :], in0=cw_v[:, tt, :],
                                        scalar1=st8[:, 7:8], scalar2=None, op0=ALU.mult))
            ii = v.tensor_copy(st8[:, 0:1], st8[:, 0:1])
            p.inc(ii, "sv")

    for tt in range(16):
        @block.tensor
        def _(t, tt=tt):
            p.wait(t, "sv")
            p.wait(t, "gd", 1)
            if tt > 0:
                p.wait(t, "sv", p.cnt["sv"])
            ii = t.transpose(psB[0:8, 0:32].bitcast(F32) if False else psB[0:8, 0:128],
                             cw_v[:, tt, :], ident[:])
            p.inc(ii, "mm")

        @block.vector
        def _(v, tt=tt):
            p.wait(v, "mm")
            ii = v.tensor_copy(cwfm[:, tt * 128:(tt + 1) * 128], psB[0:8, 0:128])
            p.inc(ii, "sv")

    for e in range(8):
        @block.gpsimd
        def _(g, e=e):
            p.wait(g, "sv")
            if e > 0:
                p.wait(g, "mm")
            g.dma_start(out=cwrow[:, :], in_=cwfm[e:e + 1, :]).then_inc(p.sem["gd"], 16)
        p.cnt["gd"] += 16

        for ch in range(NCH):
            @block.tensor
            def _(t, e=e, ch=ch):
                p.wait(t, "gd")
                if ch > 0 or e > 0:
                    p.wait(t, "sv")
                ii = t.matmul(psB[:, :], onesr[:, :], cwrow[:, ch * 512:(ch + 1) * 512],
                              start=True, stop=True)
                p.inc(ii, "mm")

            @block.vector
            def _(v, e=e, ch=ch):
                p.wait(v, "mm")
                ii = v.tensor_copy(cwb[:, ch * 512:(ch + 1) * 512], psB[:, :])
                p.inc(ii, "sv")

        for tc in range(NCH):
            for ft in range(32):
                wgb = wg_s[ft % 2]
                wub = wu_s[ft % 2]

                @block.gpsimd
                def _(g, e=e, tc=tc, ft=ft, wgb=wgb, wub=wub):
                    p.wait(g, "mm")
                    wgv = wgb[:].rearrange("p (k t) -> p k t", k=8)
                    wuv = wub[:].rearrange("p (k t) -> p k t", k=8)
                    for sub in range(8):
                        g.dma_start(out=wgv[:, sub, :],
                                    in_=mwg[e][sub * 128:(sub + 1) * 128, ft * 128:(ft + 1) * 128]
                                    ).then_inc(p.sem["wl"], 16)
                        g.dma_start(out=wuv[:, sub, :],
                                    in_=mwu[e][sub * 128:(sub + 1) * 128, ft * 128:(ft + 1) * 128]
                                    ).then_inc(p.sem["wl"], 16)
                p.cnt["wl"] += 16 * 16

                @block.tensor
                def _(t, e=e, tc=tc, ft=ft, wgb=wgb, wub=wub):
                    p.wait(t, "wl")
                    p.wait(t, "sv")
                    wgv = wgb[:].rearrange("p (k t) -> p k t", k=8)
                    wuv = wub[:].rearrange("p (k t) -> p k t", k=8)
                    for sub in range(8):
                        t.matmul(psG[:, :], wgv[:, sub, :],
                                 xn_v[:, sub, tc * 512:(tc + 1) * 512],
                                 start=(sub == 0), stop=(sub == 7))
                    for sub in range(8):
                        ii = t.matmul(psU[:, :], wuv[:, sub, :],
                                      xn_v[:, sub, tc * 512:(tc + 1) * 512],
                                      start=(sub == 0), stop=(sub == 7))
                    p.inc(ii, "mm")

                @block.scalar
                def _(s, e=e, tc=tc, ft=ft):
                    p.wait(s, "mm")
                    ii = s.activation(sgt[:, :], psG[:, :], AF.Silu)
                    p.inc(ii, "ss")

                @block.vector
                def _(v, e=e, tc=tc, ft=ft):
                    p.wait(v, "ss")
                    v.tensor_tensor(ub[:, :], psU[:, :], cwb[:, tc * 512:(tc + 1) * 512], ALU.mult)
                    ii = v.tensor_tensor(h2_v[:, ft, :], sgt[:, :], ub[:, :], ALU.mult)
                    p.inc(ii, "sv")

            for dt in range(8):
                wdb = wd_s[dt % 2]

                @block.gpsimd
                def _(g, e=e, tc=tc, dt=dt, wdb=wdb):
                    p.wait(g, "mm")
                    wdv = wdb[:].rearrange("p (k t) -> p k t", k=32)
                    for fs in range(32):
                        g.dma_start(out=wdv[:, fs, :],
                                    in_=mwd[e][fs * 128:(fs + 1) * 128, dt * 128:(dt + 1) * 128]
                                    ).then_inc(p.sem["wl"], 16)
                p.cnt["wl"] += 16 * 32

                @block.tensor
                def _(t, e=e, tc=tc, dt=dt, wdb=wdb):
                    p.wait(t, "wl")
                    p.wait(t, "sv")
                    ps = psD_[dt % 2]
                    wdv = wdb[:].rearrange("p (k t) -> p k t", k=32)
                    for fs in range(32):
                        ii = t.matmul(ps[:, :], wdv[:, fs, :], h2_v[:, fs, :],
                                      start=(fs == 0), stop=(fs == 31))
                    p.inc(ii, "mm")

                @block.vector
                def _(v, e=e, tc=tc, dt=dt):
                    p.wait(v, "mm")
                    ps = psD_[dt % 2]
                    if e == 0:
                        ii = v.tensor_copy(ma_v[:, dt, tc * 512:(tc + 1) * 512], ps[:, :])
                    else:
                        ii = v.tensor_tensor(ma_v[:, dt, tc * 512:(tc + 1) * 512],
                                             ma_v[:, dt, tc * 512:(tc + 1) * 512], ps[:, :], ALU.add)
                    p.inc(ii, "sv")

    n = 0
    for dt in range(8):
        for tc in range(NCH):
            n += 1
            ob = o_t[n % 2]

            @block.gpsimd
            def _(g, dt=dt, tc=tc, n=n):
                p.wait(g, "sv")
                if n > 2:
                    p.wait(g, "so", 16 * (n - 2))
                g.dma_start(out=s_t[:, :], in_=sfin[dt][:, tc * 512:(tc + 1) * 512]).then_inc(p.sem["gd"], 16)
                g.dma_start(out=x_t[:, :], in_=x1[dt][:, tc * 512:(tc + 1) * 512]).then_inc(p.sem["gd"], 16)
            p.cnt["gd"] += 32

            @block.scalar
            def _(s, dt=dt, tc=tc):
                p.wait(s, "sv")
                ii = s.activation(sgt[:, :], ma_v[:, dt, tc * 512:(tc + 1) * 512], AF.Silu)
                p.inc(ii, "ss")

            @block.vector
            def _(v, dt=dt, tc=tc, ob=ob):
                p.wait(v, "ss")
                p.wait(v, "gd")
                v.tensor_tensor(ob[:, :], s_t[:, :], sgt[:, :], ALU.mult)
                ii = v.tensor_tensor(ob[:, :], ob[:, :], x_t[:, :], ALU.add)
                p.inc(ii, "sv")

            @block.sync
            def _(sy, dt=dt, tc=tc, ob=ob):
                p.wait(sy, "sv")
                ii = sy.dma_start(out=out[dt][:, tc * 512:(tc + 1) * 512], in_=ob[:, :])
                p.inc(ii, "so", 16)

    @block.sync
    def _(sy):
        p.wait(sy, "so")

    return p



TF = 4096


def build_mega(upto=None):
    p = P("mega")
    nc = p.nc
    xin = nc.dram_tensor("xin", [TF, D], F32, kind="ExternalInput").ap()
    xfm = nc.dram_tensor("xfm", [8, 128, TF], F32, kind="ExternalInput").ap()
    wqh = nc.dram_tensor("wqh", [D, D], F32, kind="ExternalInput").ap()
    wkh = nc.dram_tensor("wkh", [D, D], F32, kind="ExternalInput").ap()
    wvh = nc.dram_tensor("wvh", [D, D], F32, kind="ExternalInput").ap()
    wph = nc.dram_tensor("wph", [D, D], F32, kind="ExternalInput").ap()
    bph = nc.dram_tensor("bph", [128, 8], F32, kind="ExternalInput").ap()
    wg = nc.dram_tensor("wg", [FFI, D], F32, kind="ExternalInput").ap()
    wu = nc.dram_tensor("wu", [FFI, D], F32, kind="ExternalInput").ap()
    wd = nc.dram_tensor("wd", [D, FFI], F32, kind="ExternalInput").ap()
    pnw = nc.dram_tensor("pnw", [128, 8], F32, kind="ExternalInput").ap()
    cw = nc.dram_tensor("cw", [128, 32], F32, kind="ExternalInput").ap()
    cb = nc.dram_tensor("cb", [128, 8], F32, kind="ExternalInput").ap()
    db = nc.dram_tensor("db", [128, 8], F32, kind="ExternalInput").ap()
    waT = nc.dram_tensor("waT", [D, D], F32, kind="ExternalInput").ap()
    wiT = nc.dram_tensor("wiT", [D, D], F32, kind="ExternalInput").ap()
    wvT = nc.dram_tensor("wvT", [D, D], F32, kind="ExternalInput").ap()
    opT = nc.dram_tensor("opT", [D, D], F32, kind="ExternalInput").ap()
    fnw = nc.dram_tensor("fnw", [128, 8], F32, kind="ExternalInput").ap()
    wguT = nc.dram_tensor("wguT", [D, 2 * 4096], F32, kind="ExternalInput").ap()
    wdT = nc.dram_tensor("wdT", [4096, D], F32, kind="ExternalInput").ap()
    wrg = nc.dram_tensor("wrg", [D, 16], F32, kind="ExternalInput").ap()
    mwg = nc.dram_tensor("mwg", [8, D, 4096], F32, kind="ExternalInput").ap()
    mwu = nc.dram_tensor("mwu", [8, D, 4096], F32, kind="ExternalInput").ap()
    mwd = nc.dram_tensor("mwd", [8, 4096, D], F32, kind="ExternalInput").ap()
    hmask = nc.dram_tensor("hmask", [128, 8], F32, kind="ExternalInput").ap()
    out = nc.dram_tensor("out", [8, 128, TO], F32, kind="ExternalOutput").ap()
    wgq_d = nc.dram_tensor("wgq_d", [8, 128, FFI], BF16, kind="Internal").ap()
    wuq_d = nc.dram_tensor("wuq_d", [8, 128, FFI], BF16, kind="Internal").ap()
    wdq_d = nc.dram_tensor("wdq_d", [22, 128, D], BF16, kind="Internal").ap()
    x1_d = nc.dram_tensor("x1_d", [8, 128, TF], F32, kind="ExternalOutput").ap()
    sacc_d = nc.dram_tensor("sacc_d", [8, 128, TF], F32, kind="ExternalOutput").ap()
    sfin_d = nc.dram_tensor("sfin_d", [8, 128, TO], F32, kind="ExternalOutput").ap()
    rows_d = nc.dram_tensor("rows_d", [4, TF], F32, kind="Internal").ap()
    nmx_d = nc.dram_tensor("nmx_d", [32, 128], BF16, kind="Internal").ap()

    AR1 = p.sbuf("AR1", [128, 32768], BF16)
    AR2 = p.sbuf("AR2", [128, 32768], BF16)
    AR3 = p.sbuf("AR3", [128, 16384], F32)
    onesr = p.sbuf("onesr", [1, 128], F32)
    onesc = p.sbuf("onesc", [128, 1], F32)
    ident = p.sbuf("ident", [128, 128], BF16)
    identf = p.sbuf("identf", [128, 128], F32)
    smalls = {n: p.sbuf(n, [128, 8], F32)
              for n in ("bpt", "pnwt", "cbt", "dbt", "fnwt", "hmt", "sedge", "sinp")}
    cwt = p.sbuf("cwt", [128, 32], F32)
    redget = p.sbuf("redget", [128, 24], F32)
    halot = p.sbuf("halot", [128, 24], F32)
    mask64 = p.sbuf("mask64", [128, TO], F32)
    ztile = p.sbuf("ztile", [128, 32], F32)
    states = p.sbuf("states", [128, 32], F32)
    si = p.sbuf("si", [128, 32], F32)
    sgt = p.sbuf("sgt", [128, 512], F32)
    ub = p.sbuf("ub", [128, 512], BF16)
    st8 = p.sbuf("st8", [128, 8], F32)

    for s in ("ld", "vg", "mm", "m2", "pv", "sv", "ss", "gd", "wl", "tp", "so", "bar"):
        p.new_sem(s)

    p.begin()
    block = p.block

    def barrier():
        for name in ("tensor", "vector", "scalar", "gpsimd", "sync"):
            @getattr(block, name)
            def _(eng):
                for s in ("ld", "vg", "mm", "m2", "pv", "sv", "ss", "gd", "wl", "tp", "so"):
                    p.wait(eng, s)
                ii = eng.nop()
                p.inc(ii, "bar")
        tgt = p.cnt["bar"]
        for name in ("tensor", "vector", "scalar", "gpsimd", "sync"):
            @getattr(block, name)
            def _(eng):
                eng.wait_ge(p.sem["bar"], tgt)

    @block.gpsimd
    def _(g):
        for nm, srcd in (("bpt", bph), ("pnwt", pnw), ("cbt", cb), ("dbt", db),
                         ("fnwt", fnw), ("hmt", hmask)):
            g.dma_start(out=smalls[nm][:], in_=srcd).then_inc(p.sem["ld"], 16)
        g.dma_start(out=cwt[:], in_=cw).then_inc(p.sem["ld"], 16)
        g.iota(identf[:].bitcast(I32), pattern=[[1, 128]], base=0, channel_multiplier=-1,
               allow_small_or_imprecise_dtypes=True)
        ii = g.tensor_scalar(out=ident[:], in0=identf[:].bitcast(I32), scalar1=0,
                             scalar2=None, op0=ALU.is_equal)
        p.inc(ii, "gd")
        ii = g.tensor_scalar(out=identf[:], in0=identf[:].bitcast(I32), scalar1=0,
                             scalar2=None, op0=ALU.is_equal)
        p.inc(ii, "gd")
    p.cnt["ld"] += 16 * 7

    @block.vector
    def _(v):
        v.memset(onesr[:], 1.0)
        v.memset(onesc[:], 1.0)
        v.memset(mask64[:], 1.0)
        v.memset(mask64[:].rearrange("p (c k) -> p c k", k=64)[:, :, 0:1].squeeze(2), 0.0)
        ii = v.memset(ztile[:], 0.0)
        p.inc(ii, "sv")

    a3f = lambda o, n: AR3[:, o:o + n]
    a3b = lambda o, n: AR3[:, o:o + (n + 1) // 2].bitcast(BF16)
    xn_v = AR1[:].rearrange("p (k t) -> p k t", k=8)
    ao_v = AR2[:].rearrange("p (k t) -> p k t", k=8)

    W_X0, W_X1, W_SQ, W_XNT, W_ST = 0, 1024, 2048, 3072, 4096
    W_XNB = 4200
    xts = [a3f(W_X0, 1024), a3f(W_X1, 1024)]
    sq_t = a3f(W_SQ, 1024)
    xn_t = a3f(W_XNT, 1024)
    st1 = a3f(W_ST, 8)
    stA = a3f(W_ST, 1); stB = a3f(W_ST + 1, 1); stC = a3f(W_ST + 2, 1)
    stD = a3f(W_ST + 3, 1); stE = a3f(W_ST + 4, 1); stF = a3f(W_ST + 5, 1)
    stG = a3f(W_ST + 6, 1); stH = a3f(W_ST + 7, 1)
    xnb = a3b(W_XNB, 1024)

    for i in range(TF // 128):
        buf = xts[i % 2]

        @block.gpsimd
        def _(g, i=i, buf=buf):
            if i >= 2:
                p.wait(g, "sv")
            g.dma_start(out=buf, in_=xin[i * 128:(i + 1) * 128, :]).then_inc(p.sem["ld"], 16)
        p.cnt["ld"] += 16

        @block.vector
        def _(v, i=i, buf=buf):
            p.wait(v, "ld")
            p.guard(v, v.tensor_reduce(stA, buf, axis=AX.X, op=ALU.add))
            v.tensor_tensor(sq_t, buf, buf, ALU.mult)
            p.guard(v, v.tensor_reduce(stB, sq_t, axis=AX.X, op=ALU.add))
            p.guard(v, v.tensor_scalar(out=stC, in0=stA, scalar1=1.0 / D, scalar2=None, op0=ALU.mult))
            p.guard(v, v.tensor_scalar(out=stD, in0=stB, scalar1=1.0 / D, scalar2=None, op0=ALU.mult))
            p.guard(v, v.tensor_tensor(stE, stC, stC, ALU.mult))
            p.guard(v, v.tensor_tensor(stF, stD, stE, ALU.subtract))
            p.guard(v, v.tensor_scalar(out=stF, in0=stF, scalar1=1e-5, scalar2=None, op0=ALU.add))

        @block.scalar
        def _(s, i=i):
            p.wait(s, "vg")
            ii = s.activation(stG, stF, AF.Sqrt)
            p.inc(ii, "ss")

        @block.vector
        def _(v, i=i, buf=buf):
            p.wait(v, "ss")
            p.wait(v, "tp")
            p.guard(v, v.reciprocal(stH, stG))
            v.tensor_scalar(out=xn_t, in0=buf, scalar1=stC, scalar2=stH,
                            op0=ALU.subtract, op1=ALU.mult)
            ii = v.tensor_copy(xnb, xn_t)
            p.inc(ii, "sv")

        @block.scalar
        def _(s, i=i):
            p.wait(s, "sv")
            for k in range(8):
                ii = s.dma_start_transpose(xn_v[:, k, i * 128:(i + 1) * 128],
                                           xnb[:, k * 128:(k + 1) * 128])
                p.inc(ii, "tp", 16)

    barrier()
    if upto == "p1":
        return p

    wq_wt = a3f(0, 2816)
    wq_qb = a3b(2816, 2816)
    wq_asm = a3b(4224, 4096)
    wq_tmp = {"y2": a3f(6272, 1408), "qlo": a3f(7680, 1408), "qhi": a3f(9088, 1408),
              "sabs": a3f(10496, 1408), "qmask": a3f(11904, 1408).bitcast(I32),
              "bm": a3f(13312, 88), "rs2": a3f(13400, 88), "rsh": a3f(13488, 88)}
    wq_tmp["qsel"] = wq_tmp["y2"]

    for src_, dst_, R, C in ((wg, wgq_d, FFI, D), (wu, wuq_d, FFI, D), (wd, wdq_d, D, FFI)):
        nr = R // 128
        nk = C // 128
        asm_v = wq_asm[:, 0:C].rearrange("p (k r) -> p k r", k=nk)
        for r in range(nr):
            @block.gpsimd
            def _(g, src_=src_, r=r, C=C):
                p.wait(g, "sv")
                g.dma_start(out=wq_wt[:, :C], in_=src_[r * 128:(r + 1) * 128, :]).then_inc(p.sem["ld"], 16)
            p.cnt["ld"] += 16

            @block.vector
            def _(v, r=r, C=C):
                p.wait(v, "ld")
                p.wait(v, "tp")
                for h0 in range(0, C, 1408):
                    hw = min(1408, C - h0)
                    ii = emit_quant(p, v, wq_wt[:, h0:h0 + hw], wq_qb[:, h0:h0 + hw],
                                    wq_tmp, hw)
                p.inc(ii, "sv")

            @block.scalar
            def _(s, r=r, nk=nk, asm_v=asm_v):
                p.wait(s, "sv")
                p.wait(s, "so")
                for k in range(nk):
                    ii = s.dma_start_transpose(asm_v[:, k, :], wq_qb[:, k * 128:(k + 1) * 128])
                    p.inc(ii, "tp", 16)

            @block.gpsimd
            def _(g, r=r, nk=nk, asm_v=asm_v, dst_=dst_):
                p.wait(g, "tp")
                for k in range(nk):
                    ii = g.dma_start(out=dst_[k][:, r * 128:(r + 1) * 128], in_=asm_v[:, k, :])
                    p.inc(ii, "so", 16)
    barrier()
    if upto == "wq":
        return p

    psA = p.psum("psA", [128, 512])
    psB = p.psum("psB", [128, 512])
    psC = p.psum("psC", [128, 1024])
    psD = p.psum("psD", [128, 1024])
    psE = p.psum("psE", [128, 1024])

    Kb = [a3b(0, 4096), a3b(2048, 4096)]
    Qb = [a3b(4096, 4096), a3b(6144, 4096)]
    Vb = a3b(8192, 4160)
    att = [a3b(10272, 1024), a3b(10784, 1024)]
    bc_s = a3f(11296, 1024)[0:64, :]
    r1 = a3f(12320, 1024)[0:1, :]
    mx8 = a3f(13344, 8)
    nmx = a3b(13352, 32)
    nmxt = a3b(13368, 128)[0:32, :]
    wpair = a3b(13440, 3072)
    Vb_v = Vb.rearrange("p (m c) -> p m c", c=130)
    wpv = wpair.rearrange("p (w k e) -> p w k e", w=3, k=8)
    NQT = TF // 128
    NQC = TF // 1024

    @block.vector
    def _(v):
        p.wait(v, "bar")
        v.memset(Vb_v[:, :, 64:65].squeeze(2), 1.0)
        ii = v.memset(Vb_v[:, :, 129:130].squeeze(2), 1.0)
        p.inc(ii, "sv")

    for hp in range(8):
        @block.gpsimd
        def _(g, hp=hp):
            p.wait(g, "mm")
            p.wait(g, "sv")
            for sub in range(8):
                g.dma_start(out=wpv[:, 0, sub, :],
                            in_=wvh[sub * 128:(sub + 1) * 128, hp * 128:(hp + 1) * 128]
                            ).then_inc(p.sem["wl"], 16)
                g.dma_start(out=wpv[:, 1, sub, :],
                            in_=wkh[sub * 128:(sub + 1) * 128, hp * 128:(hp + 1) * 128]
                            ).then_inc(p.sem["wl"], 16)
                g.dma_start(out=wpv[:, 2, sub, :],
                            in_=wqh[sub * 128:(sub + 1) * 128, hp * 128:(hp + 1) * 128]
                            ).then_inc(p.sem["wl"], 16)
        p.cnt["wl"] += 16 * 24

        @block.tensor
        def _(t, hp=hp):
            p.wait(t, "wl")
            p.wait(t, "sv")
            for mt in range(32):
                dst = psC[:, 0:128] if mt % 2 == 0 else psC[:, 512:640]
                if mt >= 2:
                    p.wait(t, "sv", p.cnt["sv"] + mt - 1)
                for sub in range(8):
                    ii = t.matmul(dst, xn_v[:, sub, mt * 128:(mt + 1) * 128],
                                  wpv[:, 0, sub, :], start=(sub == 0), stop=(sub == 7))
                p.inc(ii, "mm")

        @block.vector
        def _(v, hp=hp):
            base = p.cnt["mm"] - 32
            for mt in range(32):
                srcp = psC[:, 0:128] if mt % 2 == 0 else psC[:, 512:640]
                p.wait(v, "mm", base + mt + 1)
                v.tensor_copy(Vb_v[:, mt, 0:64], srcp[:, 0:64])
                ii = v.tensor_copy(Vb_v[:, mt, 65:129], srcp[:, 64:128])
                p.inc(ii, "sv")

        for hh in range(2):
            h = hp * 2 + hh
            K_h = Kb[hh]
            Q_h = Qb[hh]
            eoff = 65 * hh

            @block.vector
            def _(v, K_h=K_h):
                ii = v.memset(K_h[64:65, :], 1.0)
                p.inc(ii, "sv")

            @block.tensor
            def _(t, h=h, hh=hh):
                p.wait(t, "sv")
                p.wait(t, "ss")
                for mc in range(8):
                    ps = psA if mc % 2 == 0 else psB
                    if mc >= 2:
                        p.wait(t, "ss", p.cnt["ss"] + mc - 1)
                    for sub in range(8):
                        ii = t.matmul(ps[0:64, :], wpv[:, 1, sub, hh * 64:(hh + 1) * 64],
                                      xn_v[:, sub, mc * 512:(mc + 1) * 512],
                                      start=(sub == 0), stop=(sub == 7))
                    p.inc(ii, "mm")
                for qc in range(TF // 512):
                    ps = psA if qc % 2 == 0 else psB
                    p.wait(t, "ss", p.cnt["ss"] + 6 + qc + 1)
                    for sub in range(8):
                        ii = t.matmul(ps[0:64, :], wpv[:, 2, sub, hh * 64:(hh + 1) * 64],
                                      xn_v[:, sub, qc * 512:(qc + 1) * 512],
                                      start=(sub == 0), stop=(sub == 7))
                    p.inc(ii, "mm")

            @block.scalar
            def _(s, h=h, K_h=K_h, Q_h=Q_h):
                base = p.cnt["mm"] - 8 - TF // 512
                for mc in range(8):
                    ps = psA if mc % 2 == 0 else psB
                    p.wait(s, "mm", base + 1 + mc)
                    ii = s.copy(out=K_h[0:64, mc * 512:(mc + 1) * 512], in_=ps[0:64, :])
                    p.inc(ii, "ss")
                for qc in range(TF // 512):
                    ps = psA if qc % 2 == 0 else psB
                    p.wait(s, "mm", base + 9 + qc)
                    ii = s.activation(Q_h[0:64, qc * 512:(qc + 1) * 512], ps[0:64, :],
                                      AF.Copy, scale=0.125)
                    p.inc(ii, "ss")

            @block.tensor
            def _(t, h=h, K_h=K_h, Q_h=Q_h):
                p.wait(t, "ss")
                for qt in range(NQT):
                    for mc in range(8):
                        n = qt * 8 + mc
                        ps = psA if n % 2 == 0 else psB
                        if n >= 2:
                            p.wait(t, "sv", p.cnt["sv"] + n - 1)
                        ii = t.matmul(ps[:, :], Q_h[0:64, qt * 128:(qt + 1) * 128],
                                      K_h[0:64, mc * 512:(mc + 1) * 512], start=True, stop=True)
                        p.inc(ii, "mm")

            @block.vector
            def _(v, h=h):
                base = p.cnt["mm"] - 8 * NQT
                n = 0
                for qt in range(NQT):
                    for mc in range(8):
                        n += 1
                        ps = psA if (n - 1) % 2 == 0 else psB
                        p.wait(v, "mm", base + n)
                        ii = v.tensor_reduce(mx8[:, mc:mc + 1], ps[:, :], axis=AX.X, op=ALU.max)
                        p.inc(ii, "sv")
                        v.wait_ge(p.sem["sv"], p.cnt["sv"])
                    p.guard(v, v.tensor_reduce(nmx[:, qt:qt + 1], mx8[:, :],
                                               axis=AX.X, op=ALU.max, negate=True))

            @block.tensor
            def _(t, h=h):
                p.wait(t, "vg")
                p.wait(t, "gd", 1)
                psT = psA[0:32, 0:64].bitcast(BF16)
                ii = t.transpose(psT, nmx[:, 0:NQT], ident[:])
                p.inc(ii, "mm")

            @block.vector
            def _(v, h=h):
                p.wait(v, "mm")
                ii = v.tensor_copy(nmxt[0:NQT, :], psA[0:32, 0:64].bitcast(BF16))
                p.inc(ii, "sv")

            @block.gpsimd
            def _(g, h=h, Q_h=Q_h):
                p.wait(g, "sv")
                ii = g.dma_start(out=nmx_d, in_=nmxt[0:NQT, :])
                p.inc(ii, "gd", 16)
                p.wait(g, "gd")
                ii = g.dma_start(out=Q_h[64:65, :], in_=nmx_d.rearrange("a b -> (a b)").unsqueeze(0))
                p.inc(ii, "gd", 16)

            for qc in range(NQC):
                qk_base = p.cnt["mm"]
                ss_base = p.cnt["ss"]
                pv_base = p.cnt["pv"]
                sv_base = p.cnt["sv"]

                @block.tensor
                def _(t, h=h, qc=qc, K_h=K_h, Q_h=Q_h, eoff=eoff,
                      qk_base=qk_base, ss_base=ss_base, sv_base=sv_base):
                    if qc == 0:
                        p.wait(t, "gd")
                    else:
                        p.wait(t, "sv", sv_base)

                    def qk(mt):
                        psl = psC if mt % 2 == 0 else psD
                        for j in range(2):
                            ii = t.matmul(psl[:, j * 512:(j + 1) * 512],
                                          K_h[0:65, mt * 128:(mt + 1) * 128],
                                          Q_h[0:65, qc * 1024 + j * 512: qc * 1024 + (j + 1) * 512],
                                          start=True, stop=True)
                        p.inc(ii, "mm")
                    qk(0)
                    qk(1)
                    for mt in range(32):
                        p.wait(t, "ss", ss_base + mt + 1)
                        ab = att[mt % 2]
                        for j in range(2):
                            ii = t.matmul(psE[0:65, j * 512:(j + 1) * 512],
                                          Vb_v[:, mt, eoff:eoff + 65],
                                          ab[:, j * 512:(j + 1) * 512],
                                          start=(mt == 0), stop=(mt == 31))
                        p.inc(ii, "pv")
                        if mt + 2 < 32:
                            qk(mt + 2)

                @block.scalar
                def _(s, h=h, qc=qc, qk_base=qk_base, pv_base=pv_base):
                    for mt in range(32):
                        psl = psC if mt % 2 == 0 else psD
                        ab = att[mt % 2]
                        p.wait(s, "mm", qk_base + mt + 1)
                        if mt >= 2:
                            p.wait(s, "pv", pv_base + mt - 1)
                        ii = s.activation(ab[:, :], psl[:, :], AF.Exp)
                        p.inc(ii, "ss")

                @block.vector
                def _(v, h=h, qc=qc):
                    p.wait(v, "pv")
                    p.guard(v, v.reciprocal(r1[:, :], psE[64:65, 0:1024]))

                @block.tensor
                def _(t, h=h, qc=qc):
                    p.wait(t, "vg")
                    for j in range(2):
                        ii = t.matmul(psA[0:64, :] if j == 0 else psB[0:64, :],
                                      onesr[:, 0:64], r1[:, j * 512:(j + 1) * 512],
                                      start=True, stop=True)
                    p.inc(ii, "mm")

                @block.vector
                def _(v, h=h, qc=qc):
                    p.wait(v, "mm")
                    v.tensor_copy(bc_s[:, 0:512], psA[0:64, :])
                    v.tensor_copy(bc_s[:, 512:1024], psB[0:64, :])
                    ii = v.tensor_tensor(
                        ao_v[:, h // 2, qc * 1024:(qc + 1) * 1024][64 * (h % 2):64 * (h % 2) + 64, :],
                        psE[0:64, :], bc_s[:, :], ALU.mult)
                    p.inc(ii, "sv")

    barrier()
    if upto == "p2":
        return p

    q_x = [a3f(0, 1024), a3f(1024, 1024)]
    q_sq = a3f(2048, 1024)
    q_xnt = a3f(3072, 1024)
    q_st = a3f(4096, 8)
    q_tmp = {"y2": a3f(4104, 1024), "qlo": a3f(5128, 1024), "qhi": a3f(6152, 1024),
             "sabs": a3f(7176, 1024), "qmask": a3f(8200, 1024).bitcast(I32),
             "bm": a3f(9224, 64), "rs2": a3f(9288, 64), "rsh": a3f(9352, 64)}
    q_tmp["qsel"] = q_tmp["y2"]
    q_xqb = a3b(9416, 1024)
    xq_v = xn_v

    for i in range(TF // 128):
        buf = q_x[i % 2]

        @block.gpsimd
        def _(g, i=i, buf=buf):
            if i >= 2:
                p.wait(g, "sv")
            g.dma_start(out=buf, in_=xin[i * 128:(i + 1) * 128, :]).then_inc(p.sem["ld"], 16)
        p.cnt["ld"] += 16

        @block.vector
        def _(v, i=i, buf=buf):
            p.wait(v, "ld")
            p.guard(v, v.tensor_reduce(q_st[:, 0:1], buf, axis=AX.X, op=ALU.add))
            v.tensor_tensor(q_sq, buf, buf, ALU.mult)
            p.guard(v, v.tensor_reduce(q_st[:, 1:2], q_sq, axis=AX.X, op=ALU.add))
            p.guard(v, v.tensor_scalar(out=q_st[:, 2:3], in0=q_st[:, 0:1], scalar1=1.0 / D, scalar2=None, op0=ALU.mult))
            p.guard(v, v.tensor_scalar(out=q_st[:, 3:4], in0=q_st[:, 1:2], scalar1=1.0 / D, scalar2=None, op0=ALU.mult))
            p.guard(v, v.tensor_tensor(q_st[:, 4:5], q_st[:, 2:3], q_st[:, 2:3], ALU.mult))
            p.guard(v, v.tensor_tensor(q_st[:, 5:6], q_st[:, 3:4], q_st[:, 4:5], ALU.subtract))
            p.guard(v, v.tensor_scalar(out=q_st[:, 5:6], in0=q_st[:, 5:6], scalar1=1e-5, scalar2=None, op0=ALU.add))

        @block.scalar
        def _(s, i=i):
            p.wait(s, "vg")
            ii = s.activation(q_st[:, 6:7], q_st[:, 5:6], AF.Sqrt)
            p.inc(ii, "ss")

        @block.vector
        def _(v, i=i, buf=buf):
            p.wait(v, "ss")
            p.wait(v, "tp")
            p.guard(v, v.reciprocal(q_st[:, 7:8], q_st[:, 6:7]))
            v.tensor_scalar(out=q_xnt, in0=buf, scalar1=q_st[:, 2:3], scalar2=q_st[:, 7:8],
                            op0=ALU.subtract, op1=ALU.mult)
            ii = emit_quant(p, v, q_xnt, q_xqb, q_tmp, 1024)
            p.inc(ii, "sv")

        @block.scalar
        def _(s, i=i):
            p.wait(s, "sv")
            for k in range(8):
                ii = s.dma_start_transpose(xq_v[:, k, i * 128:(i + 1) * 128],
                                           q_xqb[:, k * 128:(k + 1) * 128])
                p.inc(ii, "tp", 16)

    barrier()
    if upto == "p3":
        return p

    f_wg = a3b(0, 4096)
    f_wu = a3b(2048, 4096)
    f_sg = a3f(4096, 512)
    f_h = a3f(4608, 512)
    f_tmp = {"y2": a3f(5120, 512), "qlo": a3f(5632, 512), "qhi": a3f(6144, 512),
             "sabs": a3f(6656, 512), "qmask": a3f(7168, 512).bitcast(I32),
             "bm": a3f(7680, 32), "rs2": a3f(7712, 32), "rsh": a3f(7744, 32)}
    f_tmp["qsel"] = f_tmp["y2"]
    f_hqb = a3b(7776, 512)
    f_hq = a3b(8192, 11264)
    f_wd = a3b(13824, 2816)
    f_wp = a3b(15232, 1024)
    f_xt = a3f(15744, 512)
    f_gt = a3b(6144, 512)
    f_tmps = f_tmp["qlo"]
    f_hq_v = f_hq.rearrange("p (k t) -> p k t", k=22)
    FCH = [(i * 512, min(512, FFI - i * 512)) for i in range((FFI + 511) // 512)]

    for tc in range(TF // 512):
        for fi, (f0, fw) in enumerate(FCH):
            @block.gpsimd
            def _(g, tc=tc, fi=fi, f0=f0, fw=fw):
                p.wait(g, "mm")
                wgv = f_wg.rearrange("p (k t) -> p k t", k=8)
                wuv = f_wu.rearrange("p (k t) -> p k t", k=8)
                for k in range(8):
                    g.dma_start(out=wgv[:, k, 0:fw], in_=wgq_d[k][:, f0:f0 + fw]).then_inc(p.sem["wl"], 16)
                    g.dma_start(out=wuv[:, k, 0:fw], in_=wuq_d[k][:, f0:f0 + fw]).then_inc(p.sem["wl"], 16)
            p.cnt["wl"] += 16 * 16

            for tt in range(4):
                @block.tensor
                def _(t, tc=tc, fi=fi, f0=f0, fw=fw, tt=tt):
                    p.wait(t, "wl")
                    p.wait(t, "sv")
                    wgv = f_wg.rearrange("p (k t) -> p k t", k=8)
                    wuv = f_wu.rearrange("p (k t) -> p k t", k=8)
                    gt = tc * 4 + tt
                    for sub in range(8):
                        t.matmul(psA[:, 0:fw], xq_v[:, sub, gt * 128:(gt + 1) * 128],
                                 wgv[:, sub, 0:fw], start=(sub == 0), stop=(sub == 7))
                    for sub in range(8):
                        ii = t.matmul(psB[:, 0:fw], xq_v[:, sub, gt * 128:(gt + 1) * 128],
                                      wuv[:, sub, 0:fw], start=(sub == 0), stop=(sub == 7))
                    p.inc(ii, "mm")

                @block.scalar
                def _(s, fw=fw):
                    p.wait(s, "mm")
                    ii = s.activation(f_sg[:, 0:fw], psA[:, 0:fw], AF.Silu)
                    p.inc(ii, "ss")

                @block.vector
                def _(v, f0=f0, fw=fw, tt=tt):
                    p.wait(v, "ss")
                    p.wait(v, "tp")
                    i0 = v.tensor_tensor(f_h[:, 0:fw], f_sg[:, 0:fw], psB[:, 0:fw], ALU.mult)
                    p.inc(i0, "sv")
                    i1 = emit_quant(p, v, f_h[:, 0:fw], f_hqb[:, 0:fw], f_tmp, fw)
                    p.inc(i1, "sv")

                @block.scalar
                def _(s, f0=f0, fw=fw, tt=tt):
                    p.wait(s, "sv")
                    for j in range(fw // 128):
                        ii = s.dma_start_transpose(
                            f_hq_v[:, f0 // 128 + j, tt * 128:(tt + 1) * 128],
                            f_hqb[:, j * 128:(j + 1) * 128])
                        p.inc(ii, "tp", 16)

        for dt in range(8):
            @block.gpsimd
            def _(g, tc=tc, dt=dt):
                p.wait(g, "mm")
                wdv = f_wd.rearrange("p (k t) -> p k t", k=22)
                wpv2 = f_wp.rearrange("p (k t) -> p k t", k=8)
                for fs in range(22):
                    g.dma_start(out=wdv[:, fs, :], in_=wdq_d[fs][:, dt * 128:(dt + 1) * 128]
                                ).then_inc(p.sem["wl"], 16)
                for sub in range(8):
                    g.dma_start(out=wpv2[:, sub, :], in_=wph[sub * 128:(sub + 1) * 128,
                                                            dt * 128:(dt + 1) * 128]
                                ).then_inc(p.sem["wl"], 16)
                ii = g.dma_start(out=f_xt, in_=xfm[dt][:, tc * 512:(tc + 1) * 512])
                p.inc(ii, "gd", 16)
            p.cnt["wl"] += 16 * 30

            @block.tensor
            def _(t, tc=tc, dt=dt):
                p.wait(t, "wl")
                if dt == 0:
                    p.wait(t, "tp")
                p.wait(t, "sv")
                wdv = f_wd.rearrange("p (k t) -> p k t", k=22)
                wpv2 = f_wp.rearrange("p (k t) -> p k t", k=8)
                for fs in range(22):
                    ii = t.matmul(psD[:, 0:512], wdv[:, fs, :], f_hq_v[:, fs, :],
                                  start=(fs == 0), stop=(fs == 21))
                p.inc(ii, "mm")
                for sub in range(8):
                    ii = t.matmul(psE[:, 0:512], wpv2[:, sub, :],
                                  ao_v[:, sub, tc * 512:(tc + 1) * 512],
                                  start=(sub == 0), stop=(sub == 7))
                p.inc(ii, "mm")

            @block.scalar
            def _(s, tc=tc, dt=dt):
                p.wait(s, "mm", p.cnt["mm"] - 1)
                ii = s.activation(f_gt, psD[:, 0:512], AF.Silu)
                p.inc(ii, "ss")

            @block.vector
            def _(v, tc=tc, dt=dt):
                p.wait(v, "mm")
                p.wait(v, "ss")
                p.wait(v, "gd")
                v.tensor_scalar(out=f_tmps, in0=psE[:, 0:512],
                                scalar1=smalls["bpt"][:, dt:dt + 1], scalar2=None, op0=ALU.add)
                v.tensor_tensor(f_tmps, f_tmps, f_gt, ALU.mult)
                ii = v.tensor_tensor(f_tmps, f_tmps, f_xt, ALU.add)
                p.inc(ii, "sv")

            @block.gpsimd
            def _(g, tc=tc, dt=dt):
                p.wait(g, "sv")
                ii = g.dma_start(out=x1_d[dt][:, tc * 512:(tc + 1) * 512], in_=f_tmps)
                p.inc(ii, "so", 16)

    barrier()
    if upto == "p4":
        return p

    AR1F = AR1[:].bitcast(F32)
    r_sum = AR3[0:1, 0:TF]
    r_sq = AR3[0:1, TF:2 * TF]
    r_mean = AR3[0:1, 2 * TF:3 * TF]
    r_rstd = AR3[0:1, 3 * TF:4 * TF]
    l_xt = [AR1F[:, 0:512], AR1F[:, 512:1024]]
    l_sq = AR1F[:, 1024:1536]
    l_mb = AR1F[:, 1536:2048]
    l_rb = AR1F[:, 2048:2560]
    l_sa = AR1F[:, 2560:3072]
    xn1_v = ao_v

    for ch in range(TF // 512):
        for sub in range(8):
            n = ch * 8 + sub
            buf = l_xt[n % 2]

            @block.gpsimd
            def _(g, ch=ch, sub=sub, buf=buf, n=n):
                if n >= 2:
                    p.wait(g, "m2")
                g.dma_start(out=buf, in_=x1_d[sub][:, ch * 512:(ch + 1) * 512]).then_inc(p.sem["ld"], 16)
            p.cnt["ld"] += 16

            @block.tensor
            def _(t, ch=ch, sub=sub, buf=buf, n=n):
                p.wait(t, "ld")
                if n % 8 == 0 and n > 0:
                    p.wait(t, "sv")
                t.matmul(psA[0:1, :], onesc[:, :], buf, start=(sub == 0), stop=(sub == 7))
                p.inc(t.nop(), "mm")

            @block.scalar
            def _(s, ch=ch, sub=sub, buf=buf, n=n):
                p.wait(s, "mm")
                p.wait(s, "m2")
                ii = s.activation(l_sq, buf, AF.Square)
                p.inc(ii, "ss")

            @block.tensor
            def _(t, ch=ch, sub=sub, n=n):
                p.wait(t, "ss")
                if sub == 0:
                    p.wait(t, "sv")
                ii = t.matmul(psB[0:1, :], onesc[:, :], l_sq, start=(sub == 0), stop=(sub == 7))
                p.inc(ii, "m2")

        @block.vector
        def _(v, ch=ch):
            p.wait(v, "m2")
            v.tensor_copy(r_sum[:, ch * 512:(ch + 1) * 512], psA[0:1, :])
            ii = v.tensor_copy(r_sq[:, ch * 512:(ch + 1) * 512], psB[0:1, :])
            p.inc(ii, "sv")

    @block.vector
    def _(v):
        p.wait(v, "sv")
        p.guard(v, v.tensor_scalar(out=r_mean, in0=r_sum, scalar1=1.0 / D, scalar2=None, op0=ALU.mult))
        p.guard(v, v.tensor_scalar(out=r_rstd, in0=r_sq, scalar1=1.0 / D, scalar2=None, op0=ALU.mult))
        p.guard(v, v.tensor_tensor(r_sum, r_mean, r_mean, ALU.mult))
        p.guard(v, v.tensor_tensor(r_rstd, r_rstd, r_sum, ALU.subtract))
        p.guard(v, v.tensor_scalar(out=r_rstd, in0=r_rstd, scalar1=1e-5, scalar2=None, op0=ALU.add))

    @block.scalar
    def _(s):
        p.wait(s, "vg")
        ii = s.activation(r_sum, r_rstd, AF.Sqrt)
        p.inc(ii, "ss")

    @block.vector
    def _(v):
        p.wait(v, "ss")
        p.guard(v, v.reciprocal(r_rstd, r_sum))

    for ch in range(TF // 512):
        @block.tensor
        def _(t, ch=ch):
            p.wait(t, "vg")
            p.wait(t, "sv")
            t.matmul(psA[:, :], onesr[:, :], r_mean[:, ch * 512:(ch + 1) * 512], start=True, stop=True)
            ii = t.matmul(psB[:, :], onesr[:, :], r_rstd[:, ch * 512:(ch + 1) * 512], start=True, stop=True)
            p.inc(ii, "mm")

        @block.vector
        def _(v, ch=ch):
            p.wait(v, "mm")
            v.tensor_copy(l_mb, psA[:, :])
            ii = v.tensor_copy(l_rb, psB[:, :])
            p.inc(ii, "sv")

        for sub in range(8):
            n = ch * 8 + sub
            buf = l_xt[n % 2]

            @block.gpsimd
            def _(g, ch=ch, sub=sub, buf=buf, n=n):
                p.wait(g, "m2")
                g.dma_start(out=buf, in_=x1_d[sub][:, ch * 512:(ch + 1) * 512]).then_inc(p.sem["ld"], 16)
            p.cnt["ld"] += 16

            @block.vector
            def _(v, ch=ch, sub=sub, buf=buf, n=n):
                p.wait(v, "ld")
                p.wait(v, "so")
                p.wait(v, "ss")
                v.tensor_tensor(l_sa, buf, l_mb, ALU.subtract)
                v.tensor_tensor(l_sa, l_sa, l_rb, ALU.mult)
                ii = v.tensor_copy(xn1_v[:, sub, ch * 512:(ch + 1) * 512], l_sa)
                p.inc(ii, "sv")

            @block.gpsimd
            def _(g, ch=ch, sub=sub):
                p.wait(g, "sv")
                ii = g.dma_start(out=sacc_d[sub][:, ch * 512:(ch + 1) * 512], in_=l_sa)
                p.inc(ii, "so", 16)

            @block.scalar
            def _(s, ch=ch, sub=sub, n=n):
                p.wait(s, "sv")
                p.wait(s, "m2")
                ii = s.activation(l_sq, l_sa, AF.Square)
                p.inc(ii, "ss")

            @block.tensor
            def _(t, ch=ch, sub=sub, n=n):
                p.wait(t, "ss")
                if sub == 0:
                    p.wait(t, "sv")
                ii = t.matmul(psC[0:1, 0:512], onesc[:, :], l_sq, start=(sub == 0), stop=(sub == 7))
                p.inc(ii, "m2")

        @block.vector
        def _(v, ch=ch):
            p.wait(v, "m2")
            ii = v.tensor_copy(r_sq[:, ch * 512:(ch + 1) * 512], psC[0:1, 0:512])
            p.inc(ii, "sv")

    @block.vector
    def _(v):
        p.wait(v, "sv")
        p.guard(v, v.tensor_scalar(out=r_sq, in0=r_sq, scalar1=1.0 / D, scalar2=1e-6,
                                   op0=ALU.mult, op1=ALU.add))

    @block.scalar
    def _(s):
        p.wait(s, "vg")
        ii = s.activation(r_mean, r_sq, AF.Sqrt)
        p.inc(ii, "ss")

    @block.vector
    def _(v):
        p.wait(v, "ss")
        p.guard(v, v.reciprocal(r_sq, r_mean))

    @block.gpsimd
    def _(g):
        p.wait(g, "vg")
        ii = g.dma_start(out=rows_d[0:1, :], in_=r_sq)
        p.inc(ii, "so", 16)

    barrier()
    if upto == "p5":
        return p

    rec_d = nc.dram_tensor("rec_d", [8, 128, TO], BF16, kind="ExternalOutput").ap()
    A = [a3f(k * 2048, 2048) for k in range(7)]
    rn_ext = a3f(6144, 2052)
    ac1 = a3f(8208, 2048)
    ac2 = a3f(10256, 2048)
    r2bc = a3f(14336, 2048)
    r2row = AR3[0:1, 14336:16384]
    h_v = AR1[:, 0:16384].rearrange("p (k t) -> p k t", k=8)
    wstr = [AR1[:, 16384 + i * 3072: 16384 + (i + 1) * 3072] for i in range(2)]
    rectile = AR1[:, 22528:24576]
    c3 = lambda ap: ap.rearrange("p (c k) -> p c k", k=64)

    for hf in (1, 0):
        cols0 = 2048 * hf

        @block.gpsimd
        def _(g, hf=hf, cols0=cols0):
            p.wait(g, "sv")
            p.wait(g, "mm")
            ii = g.dma_start(out=r2row, in_=rows_d[0:1, cols0:cols0 + 2048])
            p.inc(ii, "gd", 16)

        for j in range(4):
            @block.tensor
            def _(t, j=j):
                p.wait(t, "gd")
                if j > 0:
                    p.wait(t, "sv")
                ii = t.matmul(psA[:, :], onesr[:, :], r2row[:, j * 512:(j + 1) * 512],
                              start=True, stop=True)
                p.inc(ii, "mm")

            @block.vector
            def _(v, j=j):
                p.wait(v, "mm")
                ii = v.tensor_copy(r2bc[:, j * 512:(j + 1) * 512], psA[:, :])
                p.inc(ii, "sv")

        for dt in range(8):
            @block.gpsimd
            def _(g, hf=hf, cols0=cols0, dt=dt):
                p.wait(g, "sv")
                ii = g.dma_start(out=rn_ext[:, 3:2051], in_=sacc_d[dt][:, cols0:cols0 + 2048])
                p.inc(ii, "gd", 16)

            @block.vector
            def _(v, hf=hf, dt=dt):
                p.wait(v, "gd")
                v.tensor_scalar(out=rn_ext[:, 3:2051], in0=rn_ext[:, 3:2051],
                                scalar1=smalls["pnwt"][:, dt:dt + 1], scalar2=None, op0=ALU.mult)
                v.tensor_tensor(rn_ext[:, 3:2051], rn_ext[:, 3:2051], r2bc[:, :], ALU.mult)
                if hf == 1:
                    v.memset(rn_ext[:, 0:3], 0.0)
                    p.guard(v, v.tensor_copy(redget[:, dt * 3:(dt + 1) * 3], rn_ext[:, 2048:2051]))
                else:
                    p.guard(v, v.tensor_copy(rn_ext[:, 0:3], halot[:, dt * 3:(dt + 1) * 3]))
                v.tensor_scalar(out=ac1[:, :], in0=rn_ext[:, 0:2048],
                                scalar1=cwt[:, dt * 4:dt * 4 + 1], scalar2=None, op0=ALU.mult)
                v.scalar_tensor_tensor(out=ac2[:, :], in0=rn_ext[:, 1:2049],
                                       scalar=cwt[:, dt * 4 + 1:dt * 4 + 2], in1=ac1[:, :],
                                       op0=ALU.mult, op1=ALU.add)
                v.scalar_tensor_tensor(out=ac1[:, :], in0=rn_ext[:, 2:2050],
                                       scalar=cwt[:, dt * 4 + 2:dt * 4 + 3], in1=ac2[:, :],
                                       op0=ALU.mult, op1=ALU.add)
                v.scalar_tensor_tensor(out=ac2[:, :], in0=rn_ext[:, 3:2051],
                                       scalar=cwt[:, dt * 4 + 3:dt * 4 + 4], in1=ac1[:, :],
                                       op0=ALU.mult, op1=ALU.add)
                ii = v.tensor_scalar(out=h_v[:, dt, :], in0=ac2[:, :],
                                     scalar1=smalls["cbt"][:, dt:dt + 1], scalar2=None, op0=ALU.add)
                p.inc(ii, "sv")

        for ot in range(8):
            wbuf = wstr[ot % 2]
            wv3 = wbuf.rearrange("p (w k t) -> p w k t", w=3, k=8)

            @block.gpsimd
            def _(g, hf=hf, ot=ot, wv3=wv3):
                p.wait(g, "mm")
                for wi, wsrc in enumerate((waT, wiT, wvT)):
                    for sub in range(8):
                        g.dma_start(out=wv3[:, wi, sub, :],
                                    in_=wsrc[sub * 128:(sub + 1) * 128, ot * 128:(ot + 1) * 128]
                                    ).then_inc(p.sem["wl"], 16)
            p.cnt["wl"] += 16 * 24

            @block.tensor
            def _(t, hf=hf, ot=ot, wv3=wv3):
                p.wait(t, "wl")
                p.wait(t, "sv")
                for ch in range(4):
                    if ch > 0:
                        p.wait(t, "ss", p.cnt["ss"] + ch * 2)
                        p.wait(t, "sv", p.cnt["sv"] + ch)
                    for sub in range(8):
                        t.matmul(psA[:, :], wv3[:, 0, sub, :], h_v[:, sub, ch * 512:(ch + 1) * 512],
                                 start=(sub == 0), stop=(sub == 7))
                    for sub in range(8):
                        t.matmul(psB[:, :], wv3[:, 1, sub, :], h_v[:, sub, ch * 512:(ch + 1) * 512],
                                 start=(sub == 0), stop=(sub == 7))
                    for sub in range(8):
                        ii = t.matmul(psC[:, 0:512], wv3[:, 2, sub, :], h_v[:, sub, ch * 512:(ch + 1) * 512],
                                      start=(sub == 0), stop=(sub == 7))
                    p.inc(ii, "mm")

            @block.scalar
            def _(s, hf=hf, ot=ot):
                base = p.cnt["mm"] - 4
                for ch in range(4):
                    p.wait(s, "mm", base + ch + 1)
                    s.activation(A[0][:, ch * 512:(ch + 1) * 512], psA[:, :], AF.Sigmoid,
                                 bias=smalls["dbt"][:, ot:ot + 1], scale=1.0)
                    ii = s.activation(A[1][:, ch * 512:(ch + 1) * 512], psB[:, :], AF.Sigmoid)
                    p.inc(ii, "ss", 2)

            @block.vector
            def _(v, hf=hf, ot=ot):
                base = p.cnt["mm"] - 4
                for ch in range(4):
                    p.wait(v, "mm", base + ch + 1)
                    ii = v.tensor_copy(A[2][:, ch * 512:(ch + 1) * 512], psC[:, 0:512])
                    p.inc(ii, "sv")

            @block.vector
            def _(v, hf=hf, ot=ot):
                p.wait(v, "ss")
                ii = v.tensor_scalar(out=A[3][:, :], in0=A[0][:, :], scalar1=1e-10,
                                     scalar2=None, op0=ALU.max)
                p.inc(ii, "sv")

            @block.scalar
            def _(s, hf=hf, ot=ot):
                p.wait(s, "sv")
                ii = s.activation(A[4][:, :], A[3][:, :], AF.Ln)
                p.inc(ii, "ss")

            @block.vector
            def _(v, hf=hf, ot=ot):
                p.wait(v, "ss")
                ii = v.tensor_tensor_scan(out=A[3][:, :], data0=mask64[:, :], data1=A[4][:, :],
                                          initial=0.0, op0=ALU.mult, op1=ALU.add)
                p.inc(ii, "sv")

            @block.scalar
            def _(s, hf=hf, ot=ot):
                p.wait(s, "sv")
                ii = s.activation(A[4][:, :], A[3][:, :], AF.Exp)
                p.inc(ii, "ss")

            @block.vector
            def _(v, hf=hf, ot=ot):
                p.wait(v, "ss")
                v.tensor_scalar(out=A[3][:, :], in0=A[4][:, :], scalar1=1e-10,
                                scalar2=None, op0=ALU.max)
                v.reciprocal(A[5][:, :], A[3][:, :])
                v.tensor_tensor(A[3][:, :], A[0][:, :], A[0][:, :], ALU.mult)
                v.tensor_scalar(out=A[3][:, :], in0=A[3][:, :], scalar1=-1.0, scalar2=1.0,
                                op0=ALU.mult, op1=ALU.add)
                ii = v.tensor_scalar(out=A[3][:, :], in0=A[3][:, :], scalar1=1e-8,
                                     scalar2=None, op0=ALU.max)
                p.inc(ii, "sv")

            @block.scalar
            def _(s, hf=hf, ot=ot):
                p.wait(s, "sv")
                ii = s.activation(A[6][:, :], A[3][:, :], AF.Sqrt)
                p.inc(ii, "ss")

            @block.vector
            def _(v, hf=hf, ot=ot):
                p.wait(v, "ss")
                v.tensor_tensor(A[0][:, :], A[6][:, :], A[1][:, :], ALU.mult)
                v.tensor_tensor(A[0][:, :], A[0][:, :], A[2][:, :], ALU.mult)
                v.tensor_tensor(A[0][:, :], A[0][:, :], A[5][:, :], ALU.mult)
                v.tensor_tensor_scan(out=A[1][:, :], data0=mask64[:, :], data1=A[0][:, :],
                                     initial=0.0, op0=ALU.mult, op1=ALU.add)
                v.tensor_tensor(A[2][:, :], A[4][:, :], A[1][:, :], ALU.mult)
                init = 0.0 if hf == 1 else smalls["sinp"][:, ot:ot + 1]
                p.guard(v, v.tensor_tensor_scan(out=states[:, :], data0=c3(A[4])[:, :, 63],
                                                data1=c3(A[2])[:, :, 63], initial=init,
                                                op0=ALU.mult, op1=ALU.add))
                if hf == 1:
                    ii = p.guard(v, v.tensor_copy(smalls["sedge"][:, ot:ot + 1],
                                                  states[:, 31:32]))
                    p.inc(v.tensor_copy(st8[:, 0:1], st8[:, 0:1]), "sv")
                else:
                    if hf == 0:
                        pass
                    p.guard(v, v.tensor_copy(si[:, 0:1], smalls["sinp"][:, ot:ot + 1]))
                    p.guard(v, v.tensor_copy(si[:, 1:32], states[:, 0:31]))
                    v.tensor_tensor(c3(A[0]), c3(A[4]),
                                    si[:, :].unsqueeze(2).broadcast_to((128, 32, 64)), ALU.mult)
                    if ot >= 1:
                        p.wait(v, "so", p.cnt["so"])
                    ii = v.tensor_tensor(rectile[:, :], A[2][:, :], A[0][:, :], ALU.add)
                    p.inc(ii, "sv")

            if hf == 0:
                @block.gpsimd
                def _(g, ot=ot):
                    p.wait(g, "sv")
                    ii = g.dma_start(out=rec_d[ot], in_=rectile[:, :])
                    p.inc(ii, "so", 16)

        if hf == 1:
            @block.vector
            def _(v):
                p.wait(v, "vg")
                p.guard(v, v.tensor_scalar(out=smalls["sinp"][:, :], in0=smalls["sedge"][:, :],
                                           scalar1=smalls["hmt"][:, 0:1], scalar2=None, op0=ALU.mult))
                p.guard(v, v.tensor_scalar(out=halot[:, :], in0=redget[:, :],
                                           scalar1=smalls["hmt"][:, 0:1], scalar2=None, op0=ALU.mult))

    barrier()
    if upto == "p6":
        return p

    rec_v = AR1[:, 0:16384].rearrange("p (k t) -> p k t", k=8)
    hn_v = rec_v
    p7_h2 = AR1[:, 16384:24576].rearrange("p (k t) -> p k t", k=32)
    p7_op = [AR1[:, 16384 + i * 1024: 16384 + (i + 1) * 1024] for i in range(2)]
    p7_sq = AR1[:, 18432:20480].bitcast(F32)
    p7_row = AR1[:, 20480:24576].bitcast(F32)
    r3_ss = p7_row[0:1, 0:2048]
    p7_bc = AR1[:, 24576:28672].bitcast(F32)
    r3_tmp = p7_bc[0:1, 1024:1536][0:1, 0:512] if False else p7_bc[0:1, 512:2048][0:1, 0:1536]
    r3_tmp = p7_bc[0:1, 0:2048]
    p7_wgu = [AR1[:, 24576 + i * 2048: 24576 + (i + 1) * 2048] for i in range(2)]
    p7_wd = AR1[:, 28672:32768]
    sa8 = [a3f(k * 2048, 2048) for k in range(8)]

    @block.gpsimd
    def _(g):
        p.wait(g, "sv")
        p.wait(g, "ss")
        for sub in range(8):
            g.dma_start(out=sa8[sub], in_=sacc_d[sub][:, 0:TO]).then_inc(p.sem["ld"], 16)
            g.dma_start(out=rec_v[:, sub, :], in_=rec_d[sub]).then_inc(p.sem["ld"], 16)
    p.cnt["ld"] += 16 * 16

    for ot in range(8):
        wb = p7_op[ot % 2]

        @block.gpsimd
        def _(g, ot=ot, wb=wb):
            p.wait(g, "mm")
            wv = wb.rearrange("p (k t) -> p k t", k=8)
            for sub in range(8):
                g.dma_start(out=wv[:, sub, :],
                            in_=opT[sub * 128:(sub + 1) * 128, ot * 128:(ot + 1) * 128]
                            ).then_inc(p.sem["wl"], 16)
        p.cnt["wl"] += 16 * 8

        for ch in range(4):
            @block.tensor
            def _(t, ot=ot, ch=ch, wb=wb):
                p.wait(t, "wl")
                p.wait(t, "ld")
                p.wait(t, "sv")
                wv = wb.rearrange("p (k t) -> p k t", k=8)
                ps = psA if ch % 2 == 0 else psB
                for sub in range(8):
                    ii = t.matmul(ps[:, :], wv[:, sub, :], rec_v[:, sub, ch * 512:(ch + 1) * 512],
                                  start=(sub == 0), stop=(sub == 7))
                p.inc(ii, "mm")

            @block.vector
            def _(v, ot=ot, ch=ch):
                p.wait(v, "mm")
                ps = psA if ch % 2 == 0 else psB
                ii = v.tensor_tensor(sa8[ot][:, ch * 512:(ch + 1) * 512],
                                     sa8[ot][:, ch * 512:(ch + 1) * 512], ps[:, :], ALU.add)
                p.inc(ii, "sv")

    for ch in range(4):
        for sub in range(8):
            @block.scalar
            def _(s, ch=ch, sub=sub):
                p.wait(s, "sv")
                p.wait(s, "m2")
                ii = s.activation(p7_sq[:, 0:512], sa8[sub][:, ch * 512:(ch + 1) * 512], AF.Square)
                p.inc(ii, "ss")

            @block.tensor
            def _(t, ch=ch, sub=sub):
                p.wait(t, "ss")
                if sub == 0:
                    p.wait(t, "sv")
                ii = t.matmul(psC[0:1, 0:512], onesc[:, :], p7_sq[:, 0:512],
                              start=(sub == 0), stop=(sub == 7))
                p.inc(ii, "m2")

        @block.vector
        def _(v, ch=ch):
            p.wait(v, "m2")
            ii = v.tensor_copy(r3_ss[:, ch * 512:(ch + 1) * 512], psC[0:1, 0:512])
            p.inc(ii, "sv")

    @block.vector
    def _(v):
        p.wait(v, "sv")
        p.guard(v, v.tensor_scalar(out=r3_ss, in0=r3_ss, scalar1=1.0 / D, scalar2=1e-6,
                                   op0=ALU.mult, op1=ALU.add))

    @block.scalar
    def _(s):
        p.wait(s, "vg")
        ii = s.activation(r3_tmp, r3_ss, AF.Sqrt)
        p.inc(ii, "ss")

    @block.vector
    def _(v):
        p.wait(v, "ss")
        p.guard(v, v.reciprocal(r3_ss, r3_tmp))

    for ch in range(4):
        @block.tensor
        def _(t, ch=ch):
            p.wait(t, "vg")
            p.wait(t, "sv")
            ii = t.matmul(psC[:, 0:512], onesr[:, :], r3_ss[:, ch * 512:(ch + 1) * 512],
                          start=True, stop=True)
            p.inc(ii, "mm")

        @block.vector
        def _(v, ch=ch):
            p.wait(v, "mm")
            ii = v.tensor_copy(p7_bc[:, 0:512], psC[:, 0:512])
            p.inc(ii, "sv")
            for sub in range(8):
                v.tensor_scalar(out=p7_sq[:, 0:512], in0=sa8[sub][:, ch * 512:(ch + 1) * 512],
                                scalar1=smalls["fnwt"][:, sub:sub + 1], scalar2=None, op0=ALU.mult)
                ii = v.tensor_tensor(hn_v[:, sub, ch * 512:(ch + 1) * 512],
                                     p7_sq[:, 0:512], p7_bc[:, 0:512], ALU.mult)
            p.inc(ii, "sv")

    for tc in range(TO // 256):
        for ft in range(32):
            wb = p7_wgu[ft % 2]

            @block.gpsimd
            def _(g, tc=tc, ft=ft, wb=wb):
                p.wait(g, "mm")
                p.wait(g, "sv")
                wv = wb.rearrange("p (k t) -> p k t", k=8)
                for sub in range(8):
                    g.dma_start(out=wv[:, sub, 0:128],
                                in_=wguT[sub * 128:(sub + 1) * 128, ft * 128:(ft + 1) * 128]
                                ).then_inc(p.sem["wl"], 16)
                    g.dma_start(out=wv[:, sub, 128:256],
                                in_=wguT[sub * 128:(sub + 1) * 128, 4096 + ft * 128:4096 + (ft + 1) * 128]
                                ).then_inc(p.sem["wl"], 16)
            p.cnt["wl"] += 16 * 16

            @block.tensor
            def _(t, tc=tc, ft=ft, wb=wb):
                p.wait(t, "wl")
                p.wait(t, "sv")
                wv = wb.rearrange("p (k t) -> p k t", k=8)
                for sub in range(8):
                    t.matmul(psA[:, 0:256], wv[:, sub, 0:128],
                             hn_v[:, sub, tc * 256:(tc + 1) * 256],
                             start=(sub == 0), stop=(sub == 7))
                for sub in range(8):
                    ii = t.matmul(psB[:, 0:256], wv[:, sub, 128:256],
                                  hn_v[:, sub, tc * 256:(tc + 1) * 256],
                                  start=(sub == 0), stop=(sub == 7))
                p.inc(ii, "mm")

            @block.scalar
            def _(s, tc=tc, ft=ft):
                p.wait(s, "mm")
                ii = s.activation(sgt[:, 0:256], psA[:, 0:256], AF.Silu)
                p.inc(ii, "ss")

            @block.vector
            def _(v, tc=tc, ft=ft):
                p.wait(v, "ss")
                ii = v.tensor_tensor(p7_h2[:, ft, tc % 1 * 0:256] if False else p7_h2[:, ft, 0:256],
                                     sgt[:, 0:256], psB[:, 0:256], ALU.mult)
                p.inc(ii, "sv")

        for dt in range(8):
            @block.gpsimd
            def _(g, tc=tc, dt=dt):
                p.wait(g, "mm")
                wv = p7_wd.rearrange("p (k t) -> p k t", k=32)
                for fs in range(32):
                    g.dma_start(out=wv[:, fs, :],
                                in_=wdT[fs * 128:(fs + 1) * 128, dt * 128:(dt + 1) * 128]
                                ).then_inc(p.sem["wl"], 16)
            p.cnt["wl"] += 16 * 32

            @block.tensor
            def _(t, tc=tc, dt=dt):
                p.wait(t, "wl")
                p.wait(t, "sv")
                wv = p7_wd.rearrange("p (k t) -> p k t", k=32)
                ps = psA if dt % 2 == 0 else psB
                for fs in range(32):
                    ii = t.matmul(ps[:, 0:256], wv[:, fs, :], p7_h2[:, fs, 0:256],
                                  start=(fs == 0), stop=(fs == 31))
                p.inc(ii, "mm")

            @block.vector
            def _(v, tc=tc, dt=dt):
                p.wait(v, "mm")
                ps = psA if dt % 2 == 0 else psB
                ii = v.tensor_tensor(sa8[dt][:, tc * 256:(tc + 1) * 256],
                                     sa8[dt][:, tc * 256:(tc + 1) * 256], ps[:, 0:256], ALU.add)
                p.inc(ii, "sv")

    @block.gpsimd
    def _(g):
        p.wait(g, "sv")
        for sub in range(8):
            ii = g.dma_start(out=sfin_d[sub], in_=sa8[sub])
            p.inc(ii, "so", 16)

    barrier()
    if upto == "p7":
        return p

    cwfm_d = nc.dram_tensor("cwfm_d", [8, TO], F32, kind="Internal").ap()
    m_h2 = AR1[:, 0:8192].rearrange("p (k t) -> p k t", k=32)
    m_wg = [AR1[:, 8192 + i * 1024: 8192 + (i + 1) * 1024] for i in range(2)]
    m_wu = [AR1[:, 10240 + i * 1024: 10240 + (i + 1) * 1024] for i in range(2)]
    m_wd = [AR1[:, 12288 + i * 4096: 12288 + (i + 1) * 4096] for i in range(2)]
    m_cw = AR1[:, 30720:32768].bitcast(F32)
    macc8 = [a3f(k * 2048, 2048) for k in range(8)]
    wrgt_v = AR1[:, 8192:9216][:, 0:128].rearrange("p (k t) -> p k t", k=8)
    cw_v2 = m_cw[:, 0:128].rearrange("p (k t) -> p k t", k=16)
    mtp = AR1[:, 9216:9472]
    NQT2 = TO // 128

    @block.gpsimd
    def _(g):
        p.wait(g, "bar")
        for sub in range(8):
            g.dma_start(out=wrgt_v[:, sub, :], in_=wrg[sub * 128:(sub + 1) * 128, :]).then_inc(p.sem["ld"], 16)
    p.cnt["ld"] += 16 * 8

    for tt in range(NQT2):
        @block.tensor
        def _(t, tt=tt):
            p.wait(t, "ld")
            if tt > 0:
                p.wait(t, "ss")
            for sub in range(8):
                ii = t.matmul(psA[:, 0:16], xn1_v[:, sub, tt * 128:(tt + 1) * 128],
                              wrgt_v[:, sub, :], start=(sub == 0), stop=(sub == 7))
            p.inc(ii, "mm")

        @block.vector
        def _(v, tt=tt):
            p.wait(v, "mm")
            p.guard(v, v.tensor_reduce(st8[:, 0:1], psA[:, 0:8], axis=AX.X, op=ALU.max, negate=True))
            p.guard(v, v.tensor_reduce(st8[:, 1:2], psA[:, 8:16], axis=AX.X, op=ALU.max, negate=True))
            p.guard(v, v.tensor_scalar(out=st8[:, 2:3], in0=st8[:, 1:2], scalar1=10.0,
                                       scalar2=None, op0=ALU.mult))

        @block.scalar
        def _(s, tt=tt):
            p.wait(s, "vg")
            s.activation(cw_v2[:, tt, :], psA[:, 0:8], AF.Exp, bias=st8[:, 0:1],
                         scale=1.0, accum_out=st8[:, 3:4])
            ii = s.activation(sgt[:, 0:8], psA[:, 8:16], AF.Exp, bias=st8[:, 2:3],
                              scale=10.0, accum_out=st8[:, 4:5])
            p.inc(ii, "ss")

        @block.vector
        def _(v, tt=tt):
            p.wait(v, "ss")
            p.guard(v, v.reciprocal(st8[:, 5:6], st8[:, 3:4]))
            p.guard(v, v.reciprocal(st8[:, 6:7], st8[:, 4:5]))
            p.guard(v, v.tensor_scalar(out=cw_v2[:, tt, :], in0=cw_v2[:, tt, :],
                                       scalar1=st8[:, 5:6], scalar2=None, op0=ALU.mult))
            p.guard(v, v.tensor_scalar(out=sgt[:, 0:8], in0=sgt[:, 0:8],
                                       scalar1=st8[:, 6:7], scalar2=None, op0=ALU.mult))
            p.guard(v, v.tensor_tensor(cw_v2[:, tt, :], cw_v2[:, tt, :], sgt[:, 0:8], ALU.mult))
            p.guard(v, v.tensor_reduce(st8[:, 7:8], cw_v2[:, tt, :], axis=AX.X, op=ALU.add))
            p.guard(v, v.tensor_scalar(out=st8[:, 7:8], in0=st8[:, 7:8], scalar1=1e-9,
                                       scalar2=None, op0=ALU.add))
            p.guard(v, v.reciprocal(st8[:, 7:8], st8[:, 7:8]))
            p.guard(v, v.tensor_scalar(out=cw_v2[:, tt, :], in0=cw_v2[:, tt, :],
                                       scalar1=st8[:, 7:8], scalar2=None, op0=ALU.mult))
            ii = v.tensor_copy(st8[:, 0:1], st8[:, 0:1])
            p.inc(ii, "sv")

    for tt in range(NQT2):
        @block.tensor
        def _(t, tt=tt):
            p.wait(t, "sv")
            p.wait(t, "gd", 2)
            psT8 = psB[0:8, 0:128]
            ii = t.transpose(psT8, cw_v2[:, tt, :], identf[:])
            p.inc(ii, "mm")

        @block.vector
        def _(v, tt=tt):
            p.wait(v, "mm")
            p.wait(v, "so")
            ii = v.tensor_copy(mtp.bitcast(F32)[0:8, 0:128], psB[0:8, 0:128])
            p.inc(ii, "sv")

        @block.gpsimd
        def _(g, tt=tt):
            p.wait(g, "sv")
            ii = g.dma_start(out=cwfm_d[:, tt * 128:(tt + 1) * 128],
                             in_=mtp.bitcast(F32)[0:8, 0:128])
            p.inc(ii, "so", 16)

    if upto == "p8r":
        barrier()
        return p
    dead = [ao_v[:, k, 2048:4096] for k in range(8)]
    wgv2 = [dead[0].rearrange("p (k t) -> p k t", k=8)[:, :, 0:128],
            dead[0].rearrange("p (k t) -> p k t", k=8)[:, :, 128:256]]
    wuv2 = [dead[1].rearrange("p (k t) -> p k t", k=8)[:, :, 0:128],
            dead[1].rearrange("p (k t) -> p k t", k=8)[:, :, 128:256]]
    wdv2 = [(dead[2].rearrange("p (k t) -> p k t", k=16),
             dead[3].rearrange("p (k t) -> p k t", k=16)),
            (dead[4].rearrange("p (k t) -> p k t", k=16),
             dead[5].rearrange("p (k t) -> p k t", k=16))]
    cwb2 = dead[6]
    d7f = dead[7][:].bitcast(F32) if False else dead[7].bitcast(F32)
    cwrow = d7f[0:1, 0:256]
    cwrow2 = d7f[0:1, 256:512]
    ub2 = dead[7][:, 1024:2048]
    m_sx = d7f[:, 0:512]
    m_xx = d7f[:, 512:1024]
    m_oo = dead[0][:, 0:1024].bitcast(F32)
    h2f = AR1[:].rearrange("p (k t) -> p k t", k=32)

    n_experts = 1 if upto in ('p8e', 'p8x') else 8
    for e in range(n_experts):
        for tcb in range(4):
            @block.gpsimd
            def _(g, e=e, tcb=tcb):
                p.wait(g, "mm")
                ii = g.dma_start(out=cwrow[:, 0:256], in_=cwfm_d[e:e + 1, tcb * 512:tcb * 512 + 256])
                p.inc(ii, "gd", 16)
                p.wait(g, "gd")
                ii = g.dma_start(out=cwrow2[:, 0:256], in_=cwfm_d[e:e + 1, tcb * 512 + 256:(tcb + 1) * 512])
                p.inc(ii, "gd", 16)

            @block.tensor
            def _(t, e=e, tcb=tcb):
                p.wait(t, "gd")
                p.wait(t, "sv")
                t.matmul(psC[:, 0:256], onesr[:, :], cwrow[:, 0:256], start=True, stop=True)
                ii = t.matmul(psC[:, 256:512], onesr[:, :], cwrow2[:, 0:256], start=True, stop=True)
                p.inc(ii, "mm")

            @block.vector
            def _(v, e=e, tcb=tcb):
                p.wait(v, "mm")
                ii = v.tensor_copy(cwb2[:, tcb * 512:(tcb + 1) * 512], psC[:, 0:512])
                p.inc(ii, "sv")

        if upto == "p8c":
            barrier()
            return p
        for tc in range(2):
            if upto == "p8g" and tc == 1:
                break
            for ft in range(32):
                wgb = wgv2[ft % 2]
                wub = wuv2[ft % 2]

                @block.gpsimd
                def _(g, e=e, tc=tc, ft=ft, wgb=wgb, wub=wub):
                    p.wait(g, "mm", max(0, p.cnt["mm"] - 1))
                    g.dma_start(out=wgb, in_=mwg[e][:, ft * 128:(ft + 1) * 128]
                                .rearrange("(k p) t -> p k t", p=128)).then_inc(p.sem["wl"], 16)
                    g.dma_start(out=wub, in_=mwu[e][:, ft * 128:(ft + 1) * 128]
                                .rearrange("(k p) t -> p k t", p=128)).then_inc(p.sem["wl"], 16)
                p.cnt["wl"] += 32

                @block.tensor
                def _(t, e=e, tc=tc, ft=ft, wgb=wgb, wub=wub):
                    p.wait(t, "wl")
                    p.wait(t, "sv")
                    for j in range(2):
                        for sub in range(8):
                            t.matmul(psC[:, j * 512:(j + 1) * 512], wgb[:, sub, :],
                                     xn1_v[:, sub, tc * 1024 + j * 512: tc * 1024 + (j + 1) * 512],
                                     start=(sub == 0), stop=(sub == 7))
                    for j in range(2):
                        for sub in range(8):
                            ii = t.matmul(psD[:, j * 512:(j + 1) * 512], wub[:, sub, :],
                                          xn1_v[:, sub, tc * 1024 + j * 512: tc * 1024 + (j + 1) * 512],
                                          start=(sub == 0), stop=(sub == 7))
                    p.inc(ii, "mm")

                @block.scalar
                def _(s, e=e, tc=tc, ft=ft):
                    p.wait(s, "mm")
                    ii = s.activation(h2f[:, ft, :], psC[:, 0:1024], AF.Silu)
                    p.inc(ii, "ss")

                @block.vector
                def _(v, e=e, tc=tc, ft=ft):
                    p.wait(v, "ss")
                    v.tensor_tensor(ub2[:, 0:1024], psD[:, 0:1024],
                                    cwb2[:, tc * 1024:(tc + 1) * 1024], ALU.mult)
                    ii = v.tensor_tensor(h2f[:, ft, :], h2f[:, ft, :], ub2[:, 0:1024], ALU.mult)
                    p.inc(ii, "sv")

            if upto == "p8g":
                barrier()
                return p
            for dt in range(8):
                wdb = wdv2[dt % 2]

                @block.gpsimd
                def _(g, e=e, tc=tc, dt=dt, wdb=wdb):
                    p.wait(g, "mm", max(0, p.cnt["mm"] - 2))
                    g.dma_start(out=wdb[0], in_=mwd[e][0:2048, dt * 128:(dt + 1) * 128]
                                .rearrange("(k p) t -> p k t", p=128)).then_inc(p.sem["wl"], 16)
                    g.dma_start(out=wdb[1], in_=mwd[e][2048:4096, dt * 128:(dt + 1) * 128]
                                .rearrange("(k p) t -> p k t", p=128)).then_inc(p.sem["wl"], 16)
                p.cnt["wl"] += 32

                @block.tensor
                def _(t, e=e, tc=tc, dt=dt, wdb=wdb):
                    p.wait(t, "wl")
                    p.wait(t, "sv")
                    for j in range(2):
                        ps = psA if j == 0 else psB
                        for fs in range(32):
                            wv = wdb[fs // 16][:, fs % 16, :]
                            ii = t.matmul(ps[:, :], wv, h2f[:, fs, j * 512:(j + 1) * 512],
                                          start=(fs == 0), stop=(fs == 31))
                        p.inc(ii, "mm")

                @block.vector
                def _(v, e=e, tc=tc, dt=dt):
                    base = p.cnt["mm"] - 2
                    for j in range(2):
                        ps = psA if j == 0 else psB
                        p.wait(v, "mm", base + j + 1)
                        col = tc * 1024 + j * 512
                        if e == 0:
                            ii = v.tensor_copy(macc8[dt][:, col:col + 512], ps[:, :])
                        else:
                            ii = v.tensor_tensor(macc8[dt][:, col:col + 512],
                                                 macc8[dt][:, col:col + 512], ps[:, :], ALU.add)
                        p.inc(ii, "sv")

    if upto in ("p8e", "p8f"):
        barrier()
        return p
    barrier()
    for dt in range(8):
        for tc in range(4):
            @block.gpsimd
            def _(g, dt=dt, tc=tc):
                p.wait(g, "so")
                g.dma_start(out=m_sx[:, 0:512], in_=sfin_d[dt][:, tc * 512:(tc + 1) * 512]).then_inc(p.sem["gd"], 16)
                g.dma_start(out=m_xx[:, 0:512], in_=x1_d[dt][:, tc * 512:(tc + 1) * 512]).then_inc(p.sem["gd"], 16)
            p.cnt["gd"] += 32

            @block.scalar
            def _(s, dt=dt, tc=tc):
                p.wait(s, "sv")
                p.wait(s, "so")
                ii = s.activation(sgt[:, :], macc8[dt][:, tc * 512:(tc + 1) * 512], AF.Silu)
                p.inc(ii, "ss")

            @block.vector
            def _(v, dt=dt, tc=tc):
                p.wait(v, "ss")
                p.wait(v, "gd")
                v.tensor_tensor(m_oo[:, 0:512], m_sx[:, 0:512], sgt[:, :], ALU.mult)
                ii = v.tensor_tensor(m_oo[:, 0:512], m_oo[:, 0:512], m_xx[:, 0:512], ALU.add)
                p.inc(ii, "sv")

            @block.sync
            def _(sy, dt=dt, tc=tc):
                p.wait(sy, "sv")
                ii = sy.dma_start(out=out[dt][:, tc * 512:(tc + 1) * 512], in_=m_oo[:, 0:512])
                p.inc(ii, "so", 16)

    @block.sync
    def _(sy):
        p.wait(sy, "so")

    return p


_cache = {}


def _get(name, builder):
    if name not in _cache:
        _cache[name] = builder()
    return _cache[name]


def run_launch(name, builder, in_maps):
    p = _get(name, builder)
    res = run_bass_kernel_spmd(p.nc, in_maps, core_ids=list(range(NCORES)))
    return res.results


def run_l0prep(x_by_core):
    return run_launch("l0prep", build_l0prep, [{"xin": x_by_core[c]} for c in range(NCORES)])


def _vec_pk(w):
    return np.ascontiguousarray(np.asarray(w, np.float32).reshape(8, 128).T)


MEGA = True


def _kernel_mega(inp, xs, wq2, wk2, wv2, wp2, bp2, cw2, waT, wiT, wvT,
                 opT, wguT, wdT, wrg, mwg, mwu, mwd):
    xfm_full = []
    for c in range(NCORES):
        xfm_full.append(np.ascontiguousarray(xs[c].T.reshape(8, 128, TF)))
    maps = []
    for c in range(NCORES):
        hm = np.full((128, 8), 0.0 if c % 2 == 0 else 1.0, np.float32)
        maps.append({
            "xin": xs[c], "xfm": xfm_full[c],
            "wqh": wq2, "wkh": wk2, "wvh": wv2, "wph": wp2, "bph": bp2,
            "wg": np.asarray(inp["L0_ffn_w_gate"], np.float32),
            "wu": np.asarray(inp["L0_ffn_w_up"], np.float32),
            "wd": np.asarray(inp["L0_ffn_w_down"], np.float32),
            "pnw": _vec_pk(inp["L1_attn_pre_norm_w"]), "cw": cw2,
            "cb": _vec_pk(inp["L1_attn_conv_b"]), "db": _vec_pk(inp["L1_attn_decay_bias"]),
            "waT": waT, "wiT": wiT, "wvT": wvT, "opT": opT,
            "fnw": _vec_pk(inp["L1_attn_ffn_norm_w"]),
            "wguT": wguT, "wdT": wdT, "wrg": wrg,
            "mwg": mwg, "mwu": mwu, "mwd": mwd, "hmask": hm,
        })
    res = run_launch("mega", build_mega, maps)
    out = np.empty((B, T, D), np.float32)
    for c in range(NCORES):
        b, h = c // 2, c % 2
        o = np.asarray(res[c]["out"])
        out[b, h * TO:(h + 1) * TO] = o.reshape(D, TO).T
    return out


def kernel(**inp):
    x = np.asarray(inp["L0_attn_query"], np.float32)
    xs = []
    xfm = []
    for c in range(NCORES):
        b, h = c // 2, c % 2
        own = x[b, h * TO:(h + 1) * TO]
        oth = x[b, (1 - h) * TO:(2 - h) * TO]
        xs.append(np.ascontiguousarray(np.concatenate([own, oth], 0)))
        xfm.append(np.ascontiguousarray(own.T.reshape(8, 128, TO)))
    wq2 = np.ascontiguousarray(np.asarray(inp["L0_attn_Wq"], np.float32).transpose(1, 0, 2).reshape(D, D))
    wk2 = np.ascontiguousarray(np.asarray(inp["L0_attn_Wk"], np.float32).transpose(1, 0, 2).reshape(D, D))
    wv2 = np.ascontiguousarray(np.asarray(inp["L0_attn_Wv"], np.float32).transpose(1, 0, 2).reshape(D, D))
    idx = (np.arange(D) % 64) * 16 + np.arange(D) // 64
    wp2 = np.ascontiguousarray(np.asarray(inp["L0_attn_Wp"], np.float32)[idx])
    bp2 = _vec_pk(inp["L0_attn_bp"])
    cw2 = np.ascontiguousarray(np.asarray(inp["L1_attn_conv_w"], np.float32)[:, 0, :]
                               .reshape(8, 128, 4).transpose(1, 0, 2).reshape(128, 32))
    waT = np.ascontiguousarray(np.asarray(inp["L1_attn_w_a"], np.float32).T)
    wiT = np.ascontiguousarray(np.asarray(inp["L1_attn_w_i"], np.float32).T)
    wvT = np.ascontiguousarray(np.asarray(inp["L1_attn_w_v"], np.float32).T)
    opT = np.ascontiguousarray(np.asarray(inp["L1_attn_out_proj_w"], np.float32).T)
    wguT = np.ascontiguousarray(np.asarray(inp["L1_attn_w_gate_up"], np.float32).T)
    wdT = np.ascontiguousarray(np.asarray(inp["L1_attn_w_down"], np.float32).T)
    wrg = np.ascontiguousarray(np.concatenate([np.asarray(inp["L1_ffn_w_router"], np.float32),
                                               np.asarray(inp["L1_ffn_w_gating"], np.float32)], 1))
    mwg = np.ascontiguousarray(np.asarray(inp["L1_ffn_w_gate"], np.float32))
    mwu = np.ascontiguousarray(np.asarray(inp["L1_ffn_w_up"], np.float32))
    mwd = np.ascontiguousarray(np.asarray(inp["L1_ffn_w_down"], np.float32))

    if MEGA:
        return _kernel_mega(inp, xs, wq2, wk2, wv2, wp2, bp2, cw2, waT, wiT, wvT,
                            opT, wguT, wdT, wrg, mwg, mwu, mwd)
    pre = run_launch("l0prep", build_l0prep, [{"xin": xs[c]} for c in range(NCORES)])
    att = run_launch("l0attn", build_l0attn,
                     [{"xnfm": np.asarray(pre[c]["xnfm"]), "wqh": wq2, "wkh": wk2,
                       "wvh": wv2} for c in range(NCORES)])
    wqz = run_launch("l0wq", build_l0wq,
                     [{"wg": np.asarray(inp["L0_ffn_w_gate"], np.float32),
                       "wu": np.asarray(inp["L0_ffn_w_up"], np.float32),
                       "wd": np.asarray(inp["L0_ffn_w_down"], np.float32)} for c in range(NCORES)])
    ffn = run_launch("l0ffn", build_l0ffn,
                     [{"xqfm": np.asarray(pre[c]["xqfm"]),
                       "wgq": np.asarray(wqz[c]["wgq"]),
                       "wuq": np.asarray(wqz[c]["wuq"]),
                       "wdq": np.asarray(wqz[c]["wdq"])} for c in range(NCORES)])
    comb = run_launch("l0comb", build_l0comb,
                      [{"aoin": np.asarray(att[c]["aout"]), "gin": np.asarray(ffn[c]["gout"]),
                        "xfm": xfm[c], "wph": wp2, "bph": bp2} for c in range(NCORES)])
    x1 = [np.asarray(comb[c]["x1"]) for c in range(NCORES)]
    ln = run_launch("l1ln", build_l1ln,
                    [{"x1": x1[c], "pnw": _vec_pk(inp["L1_attn_pre_norm_w"])}
                     for c in range(NCORES)])
    maps = []
    for c in range(NCORES):
        halo = (np.zeros((8, 128, 3), np.float32) if c % 2 == 0
                else np.asarray(ln[c - 1]["redge"]))
        maps.append({
            "sacc": np.asarray(ln[c]["sacc"]), "rstd2": np.asarray(ln[c]["rstd2"]),
            "halo": halo, "pnw": _vec_pk(inp["L1_attn_pre_norm_w"]),
            "cw": cw2, "cb": _vec_pk(inp["L1_attn_conv_b"]),
            "db": _vec_pk(inp["L1_attn_decay_bias"]),
            "waT": waT, "wiT": wiT, "wvT": wvT,
        })
    rec = run_launch("l1rec", build_l1rec, maps)
    maps2 = []
    for c in range(NCORES):
        s_in = (np.zeros((128, 8), np.float32) if c % 2 == 0
                else np.asarray(rec[c - 1]["sout"]))
        maps2.append({
            "sacc": np.asarray(ln[c]["sacc"]),
            "recl": np.asarray(rec[c]["recl"]), "Fin": np.asarray(rec[c]["Fout"]),
            "sin": s_in, "opT": opT, "fnw": _vec_pk(inp["L1_attn_ffn_norm_w"]),
            "wguT": wguT, "wdT": wdT,
        })
    fix = run_launch("l1fix", build_l1fix, maps2)
    moe = run_launch("l1moe", build_l1moe,
                     [{"xn1": np.asarray(ln[c]["xn1"]), "sfin": np.asarray(fix[c]["sfin"]),
                       "x1": x1[c], "wrg": wrg, "mwg": mwg, "mwu": mwu, "mwd": mwd}
                      for c in range(NCORES)])

    out = np.empty((B, T, D), np.float32)
    for c in range(NCORES):
        b, h = c // 2, c % 2
        o = np.asarray(moe[c]["out"])
        out[b, h * TO:(h + 1) * TO] = o.reshape(D, TO).T
    return out
